# version 05
# baseline (speedup 1.0000x reference)
import contextlib
import math
import numpy as np

import concourse.bass as bass
import concourse.mybir as mybir
from concourse.bass_utils import run_bass_kernel_spmd

F32 = mybir.dt.float32
BF16 = mybir.dt.bfloat16
AF = mybir.ActivationFunctionType
ALU = mybir.AluOpType
AX = mybir.AxisListType

D = 1024
T = 4096
NT = 32
N_CORES = 8

ENGS = ("tensor", "vector", "scalar", "gpsimd", "sync")
DEBUG_L0AB_HGS = 2
DEBUG_L0AB_NQT = NT
DEBUG_L0AB_SKIP = set()
DEBUG_L0AB_BSTEP = 9
DEBUG_L0AB_TCN = 8
DEBUG_L0AB_NOCOPY = False

_NAME_N = [0]


def _sb(nc, name, shape, dtype):
    _NAME_N[0] += 1
    return nc.sbuf_tensor(f"{name}_{_NAME_N[0]}", shape, dtype)


def _ps(nc, name, shape, dtype):
    _NAME_N[0] += 1
    return nc.psum_tensor(f"{name}_{_NAME_N[0]}", shape, dtype)



class Stream:

    def __init__(self, sem, name, mode):
        self.sem = sem
        self.name = name
        self.mode = mode
        self.cnt = 0
        self.waited_epoch = False


class Prog:

    def __init__(self, nc, block, es):
        self.nc = nc
        self.block = block
        self.es = es
        self.pend = {e: [] for e in ENGS}
        self.cnt = {e: 0 for e in ENGS}
        self.waited = {e: {} for e in ENGS}
        self.sem = {e: es.enter_context(nc.semaphore(f"sem_{e}")) for e in ENGS}
        self.streams = {}

    def stream(self, name, mode="exact"):
        s = self.streams.get(name)
        if s is None:
            s = Stream(self.es.enter_context(self.nc.semaphore(f"st_{name}")), name, mode)
            self.streams[name] = s
        assert s.mode == mode, name
        return s

    def _emit_waits(self, eng, deps):
        for d in deps:
            if d is None:
                continue
            if isinstance(d[0], Stream):
                s, val = d
                if s.mode == "exact":
                    assert val == s.cnt, f"stream {s.name}: wait {val} != latest {s.cnt}"
                else:
                    val = s.cnt
                s.waited_epoch = True
                w = self.waited[eng]
                key = ("S", s.name)
                if w.get(key, 0) < val:
                    w[key] = val
                    self.pend[eng].append(lambda h, ss=s.sem, v=val: h.wait_ge(ss, v))
            else:
                pe, val = d
                if val <= 0:
                    continue
                if pe == eng and val > self.cnt[eng]:
                    raise AssertionError(f"same-engine dep on future op {pe} {val} > {self.cnt[eng]}")
                w = self.waited[eng]
                if w.get(pe, 0) < val:
                    w[pe] = val
                    sem = self.sem[pe]
                    self.pend[eng].append(lambda h, s=sem, v=val: h.wait_ge(s, v))

    def op(self, eng, fn, deps=(), inc=0):
        self._emit_waits(eng, deps)
        if inc:
            self.cnt[eng] += inc
            sem = self.sem[eng]
            self.pend[eng].append(lambda h, f=fn, s=sem, a=inc: f(h).then_inc(s, a))
            return (eng, self.cnt[eng])
        else:
            self.pend[eng].append(fn)
            return (eng, self.cnt[eng])

    def dma(self, eng, stream, fn, deps=()):
        if stream.mode == "bulk" and stream.waited_epoch:
            raise AssertionError(f"bulk stream {stream.name}: DMA after wait (need barrier)")
        self._emit_waits(eng, deps)
        stream.cnt += 16
        self.pend[eng].append(lambda h, f=fn, s=stream.sem: f(h).then_inc(s, 16))
        return (stream, stream.cnt)

    def tok(self, eng):
        return (eng, self.cnt[eng])

    def barrier(self):
        toks = [(e, self.cnt[e]) for e in ENGS if self.cnt[e] > 0]
        toks += [(s, s.cnt) for s in self.streams.values() if s.cnt > 0]
        for e in ENGS:
            self._save = e
            for d in toks:
                if not isinstance(d[0], Stream) and d[0] == e:
                    continue
                self._emit_waits(e, [d])
        for s in self.streams.values():
            s.waited_epoch = False

    def flush(self):
        for e in ENGS:
            q = self.pend[e]
            if not q:
                continue
            self.pend[e] = []
            body = getattr(self.block, e)

            def run(h, q=q):
                for f in q:
                    f(h)
            body(run)

    def finish(self):
        self.barrier()
        self.flush()




def ap3(base_ap, dims):
    return bass.AP(base_ap.tensor, base_ap.offset, dims)


def rearr(ap, spec, **kw):
    return ap.rearrange(spec, **kw)


class Phase:

    def __init__(self, P, es):
        self.P = P
        self.es = es


def const_setup(P, es):
    nc = P.nc
    c = {}
    c["ones_b"] = es.enter_context(_sb(nc, "c_ones_b", [1, 512], BF16))
    c["ones_f"] = es.enter_context(_sb(nc, "c_ones_f", [1, 512], F32))
    c["onescol_b"] = es.enter_context(_sb(nc, "c_onescol_b", [128, 1], BF16))
    c["onescol_f"] = es.enter_context(_sb(nc, "c_onescol_f", [128, 1], F32))
    c["ident_b"] = es.enter_context(_sb(nc, "c_ident_b", [128, 128], BF16))
    c["ident_f"] = es.enter_context(_sb(nc, "c_ident_f", [128, 128], F32))
    io = es.enter_context(_sb(nc, "c_iota_fp", [128, 128], F32))
    t1 = P.op("gpsimd", lambda h: h.memset(c["ones_b"][:, :], 1.0), inc=1)
    t2 = P.op("gpsimd", lambda h: h.memset(c["ones_f"][:, :], 1.0), inc=1)
    t2b = P.op("gpsimd", lambda h: h.memset(c["onescol_b"][:, :], 1.0), inc=1)
    t2c = P.op("gpsimd", lambda h: h.memset(c["onescol_f"][:, :], 1.0), inc=1)
    t3 = P.op("gpsimd", lambda h: h.iota(io[:, :], [[1, 128]], channel_multiplier=-1,
                                         allow_small_or_imprecise_dtypes=True), inc=1)
    t4 = P.op("vector", lambda h: h.tensor_scalar(c["ident_f"][:, :], io[:, :], 0.0, None, ALU.is_equal),
              deps=[t3], inc=1)
    t5 = P.op("vector", lambda h: h.tensor_copy(c["ident_b"][:, :], c["ident_f"][:, :]),
              deps=[t4], inc=1)
    c["iota_fp"] = io
    c["tok"] = [t1, t2, t2b, t2c, t4, t5]
    return c


def load_w_bf16(P, name, dram_ap, shape_sb, es, rearrange=None, stream=None):
    t = es.enter_context(_sb(P.nc, name, shape_sb, BF16))
    src = dram_ap if rearrange is None else dram_ap.rearrange(*rearrange[0], **rearrange[1])
    tok = P.dma("gpsimd", stream if stream is not None else P.stream("bulk", "bulk"),
                lambda h: h.dma_start(out=t[...], in_=src))
    return t, tok


def load_row(P, name, dram_vec_ap, n, es, dtype=F32, stream=None):
    t = es.enter_context(_sb(P.nc, name, [1, n], dtype))
    src = dram_vec_ap.rearrange("(a n) -> a n", a=1)
    eng = "gpsimd" if dtype != F32 else "sync"
    default = "bulk" if eng == "gpsimd" else "bulks"
    tok = P.dma(eng, stream if stream is not None else P.stream(default, "bulk"),
                lambda h: h.dma_start(out=t[:, :], in_=src))
    return t, tok


def bcast_row_f32(P, row_ap, n, psum, out_sb, ones_f, deps):
    nchunks = (n + 511) // 512
    mts = []
    for j in range(nchunks):
        lo, hi = j * 512, min(n, (j + 1) * 512)
        mts.append(P.op("tensor",
                        lambda h, lo=lo, hi=hi: h.matmul(psum[:, lo:hi], ones_f[:, 0:128],
                                                         row_ap[:, lo:hi], start=True, stop=True),
                        deps=deps, inc=(1 if hi == n else 0)))
    tok = P.op("vector", lambda h: h.tensor_copy(out_sb[:, 0:n], psum[:, 0:n]), deps=[mts[-1]], inc=1)
    return tok


def ln_tile(P, x_ap, F, scr, eps, out_ap, out_tok_extra_deps=(), scale_row=None, bias_row=None,
            x_deps=(), out_dtype_pass=None):
    junk, s1, s2, s3, sq = scr["junk"], scr["s1"], scr["s2"], scr["s3"], scr["sq"]
    t1 = P.op("vector", lambda h: h.tensor_scalar(junk[:, 0:F], x_ap, 1.0, 0.0, ALU.mult, ALU.add,
                                                  accum_out=s1[:, :]), deps=list(x_deps), inc=1)
    t2 = P.op("vector", lambda h: h.tensor_scalar(s1[:, :], s1[:, :], -1.0 / F, None, ALU.mult),
              deps=[t1], inc=1)
    t3 = P.op("scalar", lambda h: h.activation(out=sq[:, 0:F], in_=x_ap, func=AF.Square,
                                               bias=s1[:, 0:1], scale=1.0, accum_out=s2[:, :]),
              deps=[t2] + list(x_deps), inc=1)
    t4 = P.op("vector", lambda h: h.tensor_scalar(s2[:, :], s2[:, :], 1.0 / F, eps, ALU.mult, ALU.add),
              deps=[t3], inc=1)
    t5 = P.op("scalar", lambda h: h.activation(out=s2[:, :], in_=s2[:, :], func=AF.Sqrt),
              deps=[t4], inc=1)
    t6 = P.op("vector", lambda h: h.reciprocal(s2[:, :], s2[:, :]), deps=[t5], inc=1)
    t7 = P.op("vector", lambda h: h.tensor_mul(s3[:, :], s1[:, :], s2[:, :]), deps=[t6, t2], inc=1)
    t8 = P.op("scalar", lambda h: h.activation(out=out_ap, in_=x_ap, func=AF.Identity,
                                               bias=s3[:, 0:1], scale=s2[:, 0:1]),
              deps=[t7] + list(x_deps) + list(out_tok_extra_deps), inc=1)
    return t8


def transpose_tile(P, src_bf16_ap_fn, ps_t, out_sb, nchunks, ident_b, deps, psum_free_tok=None,
                   out_ap=None):
    tts = []
    dd = list(deps) + ([psum_free_tok] if psum_free_tok else [])
    for j in range(nchunks):
        tts.append(P.op("tensor",
                        lambda h, j=j: h.transpose(ps_t[:, j * 128:(j + 1) * 128],
                                                   src_bf16_ap_fn(j), ident_b[:, :]),
                        deps=dd, inc=(1 if j == nchunks - 1 else 0)))
    if out_ap is not None:
        ev = P.op("vector", lambda h: h.tensor_copy(
            out_ap, ps_t[:, 0:nchunks * 128].rearrange("p (j t) -> p j t", j=nchunks)),
            deps=[tts[-1]], inc=1)
    else:
        ev = P.op("vector", lambda h: h.tensor_copy(out_sb[:, 0:nchunks * 128], ps_t[:, 0:nchunks * 128]),
                  deps=[tts[-1]], inc=1)
    return ev




def build_l0_attn_a(P, es, aps, consts):
    nc = P.nc
    xin, x2buf, xout = aps["xin"], aps["x2buf"], aps["xout"]

    with contextlib.ExitStack() as ph:
        wq, tw1 = load_w_bf16(P, "wq", aps["Wq"], [128, 8, 1024], ph,
                              rearrange=(("(ci p) o -> p ci o",), dict(p=128)))
        wk, tw2 = load_w_bf16(P, "wk", aps["Wk"], [128, 8, 1024], ph,
                              rearrange=(("(ci p) o -> p ci o",), dict(p=128)))
        wv, tw3 = load_w_bf16(P, "wv", aps["Wv"], [128, 8, 1024], ph,
                              rearrange=(("(ci p) o -> p ci o",), dict(p=128)))
        wg, tw4 = load_w_bf16(P, "wg", aps["Wg"], [128, 8, 1024], ph,
                              rearrange=(("(ci p) o -> p ci o",), dict(p=128)))
        wo, tw5 = load_w_bf16(P, "wo", aps["Wo"], [128, 8, 1024], ph,
                              rearrange=(("(ci p) o -> p ci o",), dict(p=128)))
        wtoks = [tw1, tw2, tw3, tw4, tw5]
        rows = {}
        rtoks = []
        for nm in ("bq", "bk", "bv", "bg", "bo"):
            rows[nm], tk = load_row(P, "r_" + nm, aps[nm], D, ph, BF16)
            rtoks.append(tk)
        for nm in ("ln1_g", "ln1_b"):
            rows[nm], tk = load_row(P, "r_" + nm, aps[nm], D, ph, F32)
            rtoks.append(tk)
        dlrow, tdl = load_row(P, "r_dl", aps["decay_logit"], 16, ph, F32)
        rtoks.append(tdl)

        ident_b = consts["ident_b"]
        ones_b = consts["ones_b"]
        ones_f = consts["ones_f"]
        iota_fp = consts["iota_fp"]

        ps_a = ph.enter_context(_ps(nc, "ps_a", [128, 1024], F32))
        ps_b = ph.enter_context(_ps(nc, "ps_b", [128, 1024], F32))
        ps_c = ph.enter_context(_ps(nc, "ps_c", [128, 1024], F32))
        ps_t = ph.enter_context(_ps(nc, "ps_t", [128, 1024], BF16))

        drow = ph.enter_context(_sb(nc, "drow", [1, 16], F32))
        lndr = ph.enter_context(_sb(nc, "lndr", [1, 16], F32))
        lndc = ph.enter_context(_sb(nc, "lndc", [1, 1024], F32))
        lndcol = ph.enter_context(_sb(nc, "lndcol", [128, 16], F32))
        Lh = ph.enter_context(_sb(nc, "Lh", [128, 16 * 128], BF16))
        Lcol = ph.enter_context(_sb(nc, "Lcol", [128, 16], BF16))
        Ppow = ph.enter_context(_sb(nc, "Ppow", [128, 1024], F32))
        d128 = ph.enter_context(_sb(nc, "d128", [1, 1024], F32))
        mask_lt = ph.enter_context(_sb(nc, "mask_lt", [128, 128], F32))
        scr128 = ph.enter_context(_sb(nc, "scr128", [128, 128], F32))
        r1row = ph.enter_context(_sb(nc, "r1row", [1, 128], F32))
        r2row = ph.enter_context(_sb(nc, "r2row", [1, 128], F32))

        t = P.op("scalar", lambda h: h.activation(out=drow[:, :], in_=dlrow[:, :], func=AF.Sigmoid),
                 deps=[tdl], inc=1)
        t = P.op("scalar", lambda h: h.activation(out=lndr[:, :], in_=drow[:, :], func=AF.Ln),
                 deps=[t], inc=1)
        tln = t
        src = ap3(lndr[:, :], [lndr[:, :].ap[0], [1, 16], [0, 64]])
        tlc = P.op("vector", lambda h: h.tensor_copy(rearr(lndc[:, :], "a (h e) -> a h e", h=16), src),
                   deps=[tln], inc=1)
        tm = P.op("tensor", lambda h: h.matmul(ps_a[:, 0:16], ones_f[:, 0:128], lndr[:, :],
                                               start=True, stop=True), deps=[tln] + consts["tok"], inc=1)
        tlcol = P.op("vector", lambda h: h.tensor_copy(lndcol[:, :], ps_a[:, 0:16]), deps=[tm], inc=1)
        tmask = P.op("vector", lambda h: h.tensor_scalar(mask_lt[:, :], iota_fp[:, :], 0.0, None, ALU.is_ge),
                     deps=consts["tok"], inc=1)
        tio1 = P.op("gpsimd", lambda h: h.iota(r1row[:, :], [[1, 128]], channel_multiplier=0,
                                               allow_small_or_imprecise_dtypes=True), inc=1)
        tio1b = P.op("vector", lambda h: h.tensor_scalar(r1row[:, :], r1row[:, :], 1.0, None, ALU.add),
                     deps=[tio1], inc=1)
        tio2 = P.op("gpsimd", lambda h: h.iota(r2row[:, :], [[-1, 128]], channel_multiplier=0,
                                               allow_small_or_imprecise_dtypes=True), inc=1)
        tio2b = P.op("vector", lambda h: h.tensor_scalar(r2row[:, :], r2row[:, :], 127.0, None, ALU.add),
                     deps=[tio2], inc=1)
        ltoks = []
        for hd in range(16):
            t1 = P.op("vector", lambda h, hd=hd: h.tensor_scalar(scr128[:, :], iota_fp[:, :],
                                                                 lndcol[:, hd:hd + 1], None, ALU.mult),
                      deps=[tlcol, tmask] + (ltoks[-1:] if ltoks else []), inc=1)
            t2 = P.op("vector", lambda h: h.tensor_scalar_min(scr128[:, :], scr128[:, :], 0.0),
                      deps=[t1], inc=1)
            t3 = P.op("scalar", lambda h: h.activation(out=scr128[:, :], in_=scr128[:, :], func=AF.Exp),
                      deps=[t2], inc=1)
            t4 = P.op("vector", lambda h, hd=hd: h.tensor_tensor(Lh[:, hd * 128:(hd + 1) * 128],
                                                                 scr128[:, :], mask_lt[:, :], ALU.mult),
                      deps=[t3], inc=1)
            ltoks.append(t4)
        tp1 = P.op("tensor", lambda h: h.matmul(ps_b[:, 0:512], r1row[:, :], lndc[:, 0:512],
                                                start=True, stop=True), deps=[tio1b, tlc], inc=0)
        tp2 = P.op("tensor", lambda h: h.matmul(ps_b[:, 512:1024], r1row[:, :], lndc[:, 512:1024],
                                                start=True, stop=True), deps=[tio1b, tlc], inc=1)
        tp3 = P.op("scalar", lambda h: h.activation(out=Ppow[:, :], in_=ps_b[:, :], func=AF.Exp),
                   deps=[tp2], inc=1)
        tq1 = P.op("tensor", lambda h: h.matmul(ps_a[:, 16:32], r2row[:, :], lndr[:, :],
                                                start=True, stop=True), deps=[tio2b, tln, tlcol], inc=1)
        tq2 = P.op("scalar", lambda h: h.activation(out=Lcol[:, :], in_=ps_a[:, 16:32], func=AF.Exp),
                   deps=[tq1], inc=1)
        td1 = P.op("vector", lambda h: h.tensor_scalar(d128[:, :], lndc[:, :], 128.0, None, ALU.mult),
                   deps=[tlc], inc=1)
        td2 = P.op("scalar", lambda h: h.activation(out=d128[:, :], in_=d128[:, :], func=AF.Exp),
                   deps=[td1], inc=1)

        g1b = ph.enter_context(_sb(nc, "g1b", [128, 1024], F32))
        b1b = ph.enter_context(_sb(nc, "b1b", [128, 1024], F32))
        tg = bcast_row_f32(P, rows["ln1_g"][:, :], D, ps_c, g1b, ones_f, [rtoks[5], tq1])
        tb = bcast_row_f32(P, rows["ln1_b"][:, :], D, ps_c, b1b, ones_f, [rtoks[6], tg])

        xt = ph.enter_context(_sb(nc, "xt", [128, 1024], F32))
        y_sb = ph.enter_context(_sb(nc, "y_sb", [128, 1024], F32))
        y1 = ph.enter_context(_sb(nc, "y1", [128, 1024], F32))
        y1b_ = ph.enter_context(_sb(nc, "y1b", [128, 1024], BF16))
        yT = ph.enter_context(_sb(nc, "yT", [128, 1024], BF16))
        u_sb = ph.enter_context(_sb(nc, "u_sb", [128, 1024], BF16))
        k_sb = ph.enter_context(_sb(nc, "k_sb", [128, 1024], F32))
        qg_sb = ph.enter_context(_sb(nc, "qg_sb", [128, 1024], F32))
        sg_sb = ph.enter_context(_sb(nc, "sg_sb", [128, 1024], F32))
        st0tmp = ph.enter_context(_sb(nc, "st0tmp", [128, 1024], F32))
        state_sb = ph.enter_context(_sb(nc, "state_sb", [128, 1024], F32))
        o_sb = ph.enter_context(_sb(nc, "o_sb", [128, 1024], BF16))
        oT = ph.enter_context(_sb(nc, "oT", [128, 1024], BF16))
        x2_sb = ph.enter_context(_sb(nc, "x2_sb", [128, 1024], F32))
        st0 = ph.enter_context(_sb(nc, "st0", [1, 2, 1024], F32))
        st0t = ph.enter_context(_sb(nc, "st0t", [1, 1024], F32))
        scr = dict(
            junk=ph.enter_context(_sb(nc, "lnjunk", [128, 1024], F32)),
            s1=ph.enter_context(_sb(nc, "lns1", [128, 1], F32)),
            s2=ph.enter_context(_sb(nc, "lns2", [128, 1], F32)),
            s3=ph.enter_context(_sb(nc, "lns3", [128, 1], F32)),
            sq=ph.enter_context(_sb(nc, "lnsq", [128, 1024], F32)),
        )

        tst0 = P.op("gpsimd", lambda h: h.memset(st0[:, 0, :], 0.0), inc=1)
        P.barrier()

        xin_t = xin.rearrange("(n p) d -> n p d", p=128)
        x2_t = x2buf.rearrange("(n p) d -> n p d", p=128)

        prev = dict(x=None, y=tg, y1=tb, y1b=None, yT=None, u=None, k=None, qg=None,
                    sg=None, st=tst0, o=None, oT=None, x2=None, state=None, stt=None)
        st0_tok = tst0
        ps_a_free = tq2
        ps_b_free = tp3
        ps_c_free = tb
        ps_t_free = (P.tok("tensor"))

        for i in range(NT):
            par = i % 2
            tload = P.dma("sync", P.stream("a_x"), lambda h, i=i: h.dma_start(
                out=xt[:, :], in_=xin_t[i, :, :]), deps=[prev["y"], prev["y1"]])
            ty = ln_tile(P, xt[:, :], D, scr, 1e-5, y_sb[:, :],
                         x_deps=[tload], out_tok_extra_deps=[prev["x2"]])
            ty1a = ln_tile(P, y_sb[:, :], D, scr, 1e-5, y1[:, :], x_deps=[ty],
                           out_tok_extra_deps=[prev["y1b"]])
            ty1g = P.op("vector", lambda h: h.tensor_tensor(y1[:, :], y1[:, :], g1b[:, :], ALU.mult),
                        deps=[ty1a, tg], inc=1)
            ty1 = P.op("vector", lambda h: h.tensor_tensor(y1[:, :], y1[:, :], b1b[:, :], ALU.add),
                       deps=[ty1g, tb], inc=1)
            tyb = P.op("vector", lambda h: h.tensor_copy(y1b_[:, :], y1[:, :]),
                       deps=[ty1, prev["yT"]], inc=1)
            tyT = transpose_tile(P, lambda j: y1b_[:, j * 128:(j + 1) * 128], ps_t, yT, 8,
                                 ident_b, [tyb], psum_free_tok=ps_t_free)
            mm = None
            for w_sb, wtok, brow, half in ((wk, wtoks[1], rows["bk"], 0), (wv, wtoks[2], rows["bv"], 1)):
                for oc in range(2):
                    sl = slice(half * 0 + oc * 512, oc * 512 + 512)
                    dst = ps_a if half == 0 else ps_b
                    for ci in range(8):
                        mm = P.op("tensor", lambda h, dst=dst, sl=sl, ci=ci, w_sb=w_sb:
                                  h.matmul(dst[:, sl], yT[:, ci * 128:(ci + 1) * 128],
                                           w_sb[:, ci, sl], start=(ci == 0), stop=False),
                                  deps=[tyT, wtok, ps_a_free if half == 0 else ps_b_free], inc=0)
                    mm = P.op("tensor", lambda h, dst=dst, sl=sl, brow=brow:
                              h.matmul(dst[:, sl], ones_b[:, 0:128], brow[:, sl], start=False, stop=True),
                              deps=rtoks[:2], inc=(1 if (half, oc) == (1, 1) else 0))
            tkv = mm
            tke = P.op("vector", lambda h: h.tensor_copy(k_sb[:, :], ps_a[:, :]),
                       deps=[tkv, prev["u"]], inc=1)
            tu = P.op("vector", lambda h: h.tensor_tensor(u_sb[:, :], ps_b[:, :], k_sb[:, :], ALU.mult),
                      deps=[tke], inc=1)
            ps_a_free = tke
            ps_b_free = tu
            mm = None
            for w_sb, wtok, brow, half in ((wq, wtoks[0], rows["bq"], 0), (wg, wtoks[3], rows["bg"], 1)):
                for oc in range(2):
                    sl = slice(oc * 512, oc * 512 + 512)
                    dst = ps_a if half == 0 else ps_b
                    for ci in range(8):
                        mm = P.op("tensor", lambda h, dst=dst, sl=sl, ci=ci, w_sb=w_sb:
                                  h.matmul(dst[:, sl], yT[:, ci * 128:(ci + 1) * 128],
                                           w_sb[:, ci, sl], start=(ci == 0), stop=False),
                                  deps=[tyT, wtok, ps_a_free if half == 0 else ps_b_free], inc=0)
                    mm = P.op("tensor", lambda h, dst=dst, sl=sl, brow=brow:
                              h.matmul(dst[:, sl], ones_b[:, 0:128], brow[:, sl], start=False, stop=True),
                              deps=rtoks[:4], inc=(1 if (half, oc) == (1, 1) else 0))
            tqgmm = mm
            tsg = P.op("scalar", lambda h: h.activation(out=sg_sb[:, :], in_=ps_b[:, :], func=AF.Sigmoid),
                       deps=[tqgmm, prev["qg"]], inc=1)
            tqg = P.op("vector", lambda h: h.scalar_tensor_tensor(qg_sb[:, :], ps_a[:, :], 0.125,
                                                                  sg_sb[:, :], ALU.mult, ALU.mult),
                       deps=[tsg], inc=1)
            ps_a_free = tqg
            ps_b_free = tqg
            mm = None
            for hd in range(16):
                mm = P.op("tensor", lambda h, hd=hd: h.matmul(ps_c[:, hd * 64:(hd + 1) * 64],
                                                              Lh[:, hd * 128:(hd + 1) * 128],
                                                              u_sb[:, hd * 64:(hd + 1) * 64],
                                                              start=True, stop=True),
                          deps=[tu, ltoks[-1], ps_c_free], inc=(1 if hd == 15 else 0))
            tS = mm
            tb1 = P.op("tensor", lambda h, par=par: h.matmul(ps_b[:, 0:512], ones_f[:, 0:128],
                                                             st0[:, par, 0:512], start=True, stop=True),
                       deps=[st0_tok, ps_b_free], inc=0)
            tb2 = P.op("tensor", lambda h, par=par: h.matmul(ps_b[:, 512:1024], ones_f[:, 0:128],
                                                             st0[:, par, 512:1024], start=True, stop=True),
                       deps=[st0_tok, ps_b_free], inc=1)
            ts1 = P.op("vector", lambda h: h.tensor_tensor(st0tmp[:, :], ps_b[:, :], Ppow[:, :], ALU.mult),
                       deps=[tb2, tp3, prev["state"]], inc=1)
            ts2 = P.op("vector", lambda h: h.tensor_tensor(state_sb[:, :], ps_c[:, :], st0tmp[:, :], ALU.add),
                       deps=[ts1, tS], inc=1)
            to_ = P.op("vector", lambda h: h.tensor_tensor(o_sb[:, :], state_sb[:, :], qg_sb[:, :], ALU.mult),
                       deps=[ts2, tqg, prev["oT"]], inc=1)
            ps_c_free = ts2
            mm = None
            for hd in range(16):
                mm = P.op("tensor", lambda h, hd=hd: h.matmul(ps_a[0:1, hd * 64:(hd + 1) * 64],
                                                              Lcol[:, hd:hd + 1],
                                                              u_sb[:, hd * 64:(hd + 1) * 64],
                                                              start=True, stop=True),
                          deps=[tu, tq2, ps_a_free], inc=(1 if hd == 15 else 0))
            tsr = mm
            tn1 = P.op("vector", lambda h, par=par: h.tensor_tensor(st0t[:, :], st0[:, par, :],
                                                                    d128[:, :], ALU.mult),
                       deps=[st0_tok, td2], inc=1)
            st0_tok = P.op("vector", lambda h, par=par: h.tensor_tensor(st0[:, 1 - par, :],
                                                                        ps_a[0:1, :], st0t[:, :], ALU.add),
                           deps=[tn1, tsr], inc=1)
            ps_a_free = st0_tok
            toT = transpose_tile(P, lambda j: o_sb[:, j * 128:(j + 1) * 128], ps_t, oT, 8,
                                 ident_b, [to_], psum_free_tok=tyT)
            ps_t_free = toT
            mm = None
            for oc in range(2):
                sl = slice(oc * 512, oc * 512 + 512)
                for ci in range(8):
                    mm = P.op("tensor", lambda h, sl=sl, ci=ci: h.matmul(ps_a[:, sl],
                                                                         oT[:, ci * 128:(ci + 1) * 128],
                                                                         wo[:, ci, sl], start=(ci == 0),
                                                                         stop=False),
                              deps=[toT, wtoks[4], ps_a_free], inc=0)
                mm = P.op("tensor", lambda h, sl=sl: h.matmul(ps_a[:, sl], ones_b[:, 0:128],
                                                              rows["bo"][:, sl], start=False, stop=True),
                          deps=[rtoks[4]], inc=(1 if oc == 1 else 0))
            tx2 = P.op("vector", lambda h: h.tensor_tensor(x2_sb[:, :], ps_a[:, :], y_sb[:, :], ALU.add),
                       deps=[mm, ty], inc=1)
            ps_a_free = tx2
            tst = P.dma("sync", P.stream("a_x2st"), lambda h, i=i: h.dma_start(
                out=x2_t[i, :, :], in_=x2_sb[:, :]), deps=[tx2])
            prev = dict(x=tload, y=ty, y1=ty1, y1b=tyT, yT=tkv if False else tqgmm, u=tS, k=tu,
                        qg=to_, sg=tqg, st=st0_tok, o=toT, oT=mm, x2=tst, state=to_, stt=ts2)

        P.barrier()
        P.flush()

    with contextlib.ExitStack() as ph:
        rows = {}
        rtoks = []
        for nm in ("ln2_g", "ln2_b"):
            rows[nm], tk = load_row(P, "f_" + nm, aps[nm], D, ph, F32)
            rtoks.append(tk)
        b2row, tb2r = load_row(P, "f_b2", aps["b2"], D, ph, BF16)
        b1row, tb1r = load_row(P, "f_b1", aps["b1"], 4096, ph, BF16)

        ps_f = [ph.enter_context(_ps(nc, f"ps_f{j}", [128, 512], F32)) for j in range(2)]
        ps_w2 = ph.enter_context(_ps(nc, "ps_w2", [128, 1024], F32))
        ps_g = ph.enter_context(_ps(nc, "ps_g", [128, 1024], F32))
        ps_t = ph.enter_context(_ps(nc, "ps_t2", [128, 1024], BF16))

        g2b = ph.enter_context(_sb(nc, "g2b", [128, 1024], F32))
        b2b = ph.enter_context(_sb(nc, "b2b", [128, 1024], F32))
        tg = bcast_row_f32(P, rows["ln2_g"][:, :], D, ps_g, g2b, consts["ones_f"], [rtoks[0]])
        tb = bcast_row_f32(P, rows["ln2_b"][:, :], D, ps_g, b2b, consts["ones_f"], [rtoks[1], tg])

        w1h = ph.enter_context(_sb(nc, "w1h", [128, 8, 2048], BF16))
        w2h = ph.enter_context(_sb(nc, "w2h", [128, 16, 1024], BF16))

        xt = ph.enter_context(_sb(nc, "fxt", [128, 1024], F32))
        x2t = ph.enter_context(_sb(nc, "fx2t", [128, 1024], F32))
        hh = ph.enter_context(_sb(nc, "fh", [128, 1024], F32))
        hb_sb = ph.enter_context(_sb(nc, "fhb", [128, 1024], BF16))
        hT = [ph.enter_context(_sb(nc, f"fhT{j}", [128, 1024], BF16)) for j in range(8)]
        xnew = [ph.enter_context(_sb(nc, f"fxn{j}", [128, 1024], F32)) for j in range(8)]
        ff1 = ph.enter_context(_sb(nc, "ff1", [128, 512], BF16))
        scr = dict(
            junk=ph.enter_context(_sb(nc, "f_junk", [128, 1024], F32)),
            s1=ph.enter_context(_sb(nc, "f_s1", [128, 1], F32)),
            s2=ph.enter_context(_sb(nc, "f_s2", [128, 1], F32)),
            s3=ph.enter_context(_sb(nc, "f_s3", [128, 1], F32)),
            sq=ph.enter_context(_sb(nc, "f_sq", [128, 1024], F32)),
        )

        xin_t = xin.rearrange("(n p) d -> n p d", p=128)
        x2_t = x2buf.rearrange("(n p) d -> n p d", p=128)
        xout_t = xout.rearrange("(n p) d -> n p d", p=128)
        W1 = aps["W1"]
        W2 = aps["W2"]

        P.barrier()
        last_store = None
        pf_free = [None, None]
        ff1_free = None
        ps_w2_free = None
        for tg_i in range(4):
            tiles = list(range(tg_i * 8, tg_i * 8 + 8))
            hT_toks = {}
            xnew_toks = {-1: None}
            for jj, i in enumerate(tiles):
                tl1 = P.dma("sync", P.stream("f_x"), lambda h, i=i: h.dma_start(
                    out=xt[:, :], in_=xin_t[i, :, :]), deps=[xnew_toks[jj - 1], last_store])
                tl2 = P.dma("sync", P.stream("f_x2"), lambda h, i=i: h.dma_start(
                    out=x2t[:, :], in_=x2_t[i, :, :]), deps=[xnew_toks[jj - 1]])
                th = ln_tile(P, x2t[:, :], D, scr, 1e-5, hh[:, :], x_deps=[tl2])
                th1 = P.op("vector", lambda h: h.tensor_tensor(hh[:, :], hh[:, :], g2b[:, :], ALU.mult),
                           deps=[th, tg], inc=1)
                th2 = P.op("vector", lambda h: h.tensor_tensor(hh[:, :], hh[:, :], b2b[:, :], ALU.add),
                           deps=[th1, tb], inc=1)
                thb = P.op("vector", lambda h: h.tensor_copy(hb_sb[:, :], hh[:, :]), deps=[th2], inc=1)
                tT = transpose_tile(P, lambda j: hb_sb[:, j * 128:(j + 1) * 128], ps_t, hT[jj], 8,
                                    consts["ident_b"], [thb], psum_free_tok=hT_toks.get(jj - 1))
                hT_toks[jj] = tT
                txn = P.op("vector", lambda h, jj=jj: h.tensor_tensor(xnew[jj][:, :], xt[:, :],
                                                                      x2t[:, :], ALU.add),
                           deps=[tl1, tl2, tT], inc=1)
                xnew_toks[jj] = txn
            accum_toks = {jj: xnew_toks[jj] for jj in range(8)}
            for half in range(2):
                wfree = P.tok("tensor")
                tw1 = P.dma("gpsimd", P.stream("f_w1"), lambda h, half=half: h.dma_start(
                    out=w1h[:, :, :],
                    in_=W1[:, half * 2048:(half + 1) * 2048].rearrange("(ci p) n -> p ci n", p=128)),
                    deps=[wfree])
                tw2 = P.dma("gpsimd", P.stream("f_w2"), lambda h, half=half: h.dma_start(
                    out=w2h[:, :, :],
                    in_=W2[half * 2048:(half + 1) * 2048, :].rearrange("(ci p) n -> p ci n", p=128)),
                    deps=[wfree])
                for jj in range(8):
                    for hb_i in range(4):
                        pf = ps_f[hb_i % 2]
                        mm = None
                        for hc in range(4):
                            hcg = hb_i * 4 + hc
                            for ci in range(8):
                                mm = P.op("tensor", lambda h, pf=pf, hc=hc, hcg=hcg, ci=ci, jj=jj:
                                          h.matmul(pf[:, hc * 128:(hc + 1) * 128],
                                                   w1h[:, ci, hcg * 128:(hcg + 1) * 128],
                                                   hT[jj][:, ci * 128:(ci + 1) * 128],
                                                   start=(ci == 0), stop=False),
                                          deps=[tw1, hT_toks[jj], pf_free[hb_i % 2]], inc=0)
                            mm = P.op("tensor", lambda h, pf=pf, hc=hc, hcg=hcg, half=half:
                                      h.matmul(pf[:, hc * 128:(hc + 1) * 128],
                                               b1row[:, half * 2048 + hcg * 128:half * 2048 + (hcg + 1) * 128],
                                               consts["ones_b"][:, 0:128], start=False, stop=True),
                                      deps=[tb1r], inc=(1 if hc == 3 else 0))
                        tgel = P.op("scalar", lambda h, pf=pf: h.activation(out=ff1[:, :], in_=pf[:, :],
                                                                            func=AF.Gelu),
                                    deps=[mm, ff1_free], inc=1)
                        pf_free[hb_i % 2] = tgel
                        mm2 = None
                        for oc in range(2):
                            for hc in range(4):
                                hcg = hb_i * 4 + hc
                                is_first = (hb_i == 0 and hc == 0)
                                is_last = (half == 0 and hb_i == 3 and hc == 3)
                                mm2 = P.op("tensor", lambda h, hc=hc, hcg=hcg, oc=oc, is_first=is_first, is_last=is_last:
                                           h.matmul(ps_w2[:, oc * 512:(oc + 1) * 512],
                                                    ff1[:, hc * 128:(hc + 1) * 128],
                                                    w2h[:, hcg, oc * 512:(oc + 1) * 512],
                                                    start=is_first, stop=is_last),
                                           deps=[tgel, tw2, ps_w2_free], inc=(1 if is_last and oc == 1 else 0))
                        ff1_free = mm2
                    if half == 1:
                        mmb = None
                        for oc in range(2):
                            mmb = P.op("tensor", lambda h, oc=oc:
                                       h.matmul(ps_w2[:, oc * 512:(oc + 1) * 512],
                                                consts["ones_b"][:, 0:128],
                                                b2row[:, oc * 512:(oc + 1) * 512],
                                                start=False, stop=True),
                                       deps=[tb2r], inc=(1 if oc == 1 else 0))
                        tadd = P.op("vector", lambda h, jj=jj: h.tensor_tensor(xnew[jj][:, :], xnew[jj][:, :],
                                                                               ps_w2[:, :], ALU.add),
                                    deps=[mmb, accum_toks[jj]], inc=1)
                    else:
                        tadd = P.op("vector", lambda h, jj=jj: h.tensor_tensor(xnew[jj][:, :], xnew[jj][:, :],
                                                                               ps_w2[:, :], ALU.add),
                                    deps=[mm2, accum_toks[jj]], inc=1)
                    accum_toks[jj] = tadd
                    ps_w2_free = tadd
            for jj, i in enumerate(tiles):
                tst = P.dma("sync", P.stream("f_st"), lambda h, i=i, jj=jj: h.dma_start(
                    out=xout_t[i, :, :], in_=xnew[jj][:, :]), deps=[accum_toks[jj]])
                last_store = tst
        P.barrier()
        P.flush()
    return last_store




L0AA_W = [("Wq", [D, D]), ("Wk", [D, D]), ("Wv", [D, D]), ("Wg", [D, D]), ("Wo", [D, D]),
          ("bq", [D]), ("bk", [D]), ("bv", [D]), ("bg", [D]), ("bo", [D]),
          ("ln1_g", [D]), ("ln1_b", [D]), ("ln2_g", [D]), ("ln2_b", [D]),
          ("decay_logit", [16]), ("W1", [D, 4096]), ("b1", [4096]), ("W2", [4096, D]), ("b2", [D])]


def _mk_prog():
    nc = bass.Bass("TRN2", target_bir_lowering=False)
    return nc


def build_program_l0_attn_a():
    nc = _mk_prog()
    aps = {}
    aps["xin"] = nc.dram_tensor("xin", [T, D], F32, kind="ExternalInput").ap()
    aps["x2buf"] = nc.dram_tensor("x2buf", [T, D], F32, kind="Internal").ap()
    aps["xout"] = nc.dram_tensor("xout", [T, D], F32, kind="ExternalOutput").ap()
    for nm, shp in L0AA_W:
        aps[nm] = nc.dram_tensor(nm, shp, F32, kind="ExternalInput").ap()
    with contextlib.ExitStack() as es:
        P = Prog(nc, None, es)
        block = es.enter_context(nc.Block())
        P.block = block
        consts = const_setup(P, es)
        build_l0_attn_a(P, es, aps, consts)
        P.finish()
    return nc


def run_sublayer_l0aa(x_full, wd):
    nc = build_program_l0_attn_a()
    in_maps = []
    for c in range(N_CORES):
        m = {"xin": np.ascontiguousarray(x_full[c])}
        for nm, _ in L0AA_W:
            m[nm] = wd[nm]
        in_maps.append(m)
    res = run_bass_kernel_spmd(nc, in_maps, core_ids=list(range(N_CORES)))
    return np.stack([r["xout"] for r in res.results])




def build_l0_ffn(P, es, aps, consts):
    nc = P.nc
    xin, xout = aps["xin"], aps["xout"]
    xin_t = xin.rearrange("(n p) d -> n p d", p=128)
    xout_t = xout.rearrange("(n p) d -> n p d", p=128)

    tcopy = P.dma("sync", P.stream("d_copy", "bulk"), lambda h: h.dma_start(out=xout, in_=xin))

    with contextlib.ExitStack() as ph:
        xlnT = ph.enter_context(_sb(nc, "xlnT", [128, 8, 4096], BF16))
        mask = ph.enter_context(_sb(nc, "mask", [128, 32, 9], F32))
        p0 = contextlib.ExitStack()
        routerT = p0.enter_context(_sb(nc, "sb_routerT", [128, 8, 8], F32))
        ps_t = p0.enter_context(_ps(nc, "d_ps_t", [128, 1024], BF16))
        ps_tf = p0.enter_context(_ps(nc, "d_ps_tf", [128, 1024], F32))
        ps_r = p0.enter_context(_ps(nc, "d_ps_r", [128, 8], F32))
        xt = p0.enter_context(_sb(nc, "d_xt", [128, 1024], F32))
        xln = p0.enter_context(_sb(nc, "d_xln", [128, 1024], F32))
        xlnb = p0.enter_context(_sb(nc, "d_xlnb", [128, 1024], BF16))
        xT32 = p0.enter_context(_sb(nc, "d_xT32", [128, 1024], F32))
        mx = p0.enter_context(_sb(nc, "d_mx", [128, 1], F32))
        scr = dict(
            junk=p0.enter_context(_sb(nc, "d_junk", [128, 1024], F32)),
            s1=p0.enter_context(_sb(nc, "d_s1", [128, 1], F32)),
            s2=p0.enter_context(_sb(nc, "d_s2", [128, 1], F32)),
            s3=p0.enter_context(_sb(nc, "d_s3", [128, 1], F32)),
            sq=p0.enter_context(_sb(nc, "d_sq", [128, 1024], F32)),
        )
        trt = P.dma("sync", P.stream("bulks", "bulk"), lambda h: h.dma_start(
            out=routerT[:, :, :], in_=aps["routerT"].rearrange("(ci p) e -> p ci e", p=128)))
        tmz = P.op("gpsimd", lambda h: h.memset(mask[:, :, 8:9], 1.0), inc=1)

        prev_ln = None
        prev_T = None
        prev_T32 = None
        prev_r = None
        prev_mask = None
        for i in range(NT):
            tl = P.dma("sync", P.stream("d_x"), lambda h, i=i: h.dma_start(
                out=xt[:, :], in_=xin_t[i, :, :]), deps=[prev_ln])
            tln = ln_tile(P, xt[:, :], D, scr, 1e-5, xln[:, :], x_deps=[tl],
                          out_tok_extra_deps=[prev_T32, prev_T])
            tcb = P.op("vector", lambda h: h.tensor_copy(xlnb[:, :], xln[:, :]),
                       deps=[tln, prev_T], inc=1)
            tT = transpose_tile(P, lambda j: xlnb[:, j * 128:(j + 1) * 128], ps_t,
                                None, 8, consts["ident_b"], [tcb], psum_free_tok=prev_T,
                                out_ap=xlnT[:, :, i * 128:(i + 1) * 128])
            tts = []
            for j in range(8):
                tts.append(P.op("tensor", lambda h, j=j: h.transpose(ps_tf[:, j * 128:(j + 1) * 128],
                                                                     xln[:, j * 128:(j + 1) * 128],
                                                                     consts["ident_f"][:, :]),
                                deps=[tln, prev_T32] + consts["tok"], inc=(1 if j == 7 else 0)))
            tev = P.op("vector", lambda h: h.tensor_copy(xT32[:, :], ps_tf[:, :]), deps=[tts[-1]], inc=1)
            mmr = None
            for ci in range(8):
                mmr = P.op("tensor", lambda h, ci=ci: h.matmul(ps_r[:, :], xT32[:, ci * 128:(ci + 1) * 128],
                                                               routerT[:, ci, :], start=(ci == 0),
                                                               stop=(ci == 7)),
                           deps=[tev, trt, prev_r], inc=(1 if ci == 7 else 0))
            tmx = P.op("vector", lambda h: h.reduce_max(mx[:, :], ps_r[:, :], axis=AX.X),
                       deps=[mmr, prev_mask], inc=1)
            tms = P.op("vector", lambda h, i=i: h.tensor_scalar(mask[:, i, 0:8], ps_r[:, :],
                                                                mx[:, 0:1], None, ALU.is_equal),
                       deps=[tmx], inc=1)
            prev_ln = tln
            prev_T = tT
            prev_T32 = mmr
            prev_r = tms
            prev_mask = tms
        P.barrier()
        P.flush()
        p0.close()

        with contextlib.ExitStack() as pe:
            upsl = pe.enter_context(_sb(nc, "upsl", [128, 8, 2048], BF16))
            gtsl = pe.enter_context(_sb(nc, "gtsl", [128, 8, 2048], BF16))
            dwn = pe.enter_context(_sb(nc, "dwn", [128, 16, 1024], BF16))
            hact = pe.enter_context(_sb(nc, "hact", [128, 16, 1024], BF16))
            sg = [pe.enter_context(_sb(nc, f"d_sg{j}", [128, 512], BF16)) for j in range(2)]
            sg_free = {0: None, 1: None}
            contrib = [pe.enter_context(_sb(nc, f"d_ctr{j}", [128, 1024], F32)) for j in range(2)]
            ps_u = [pe.enter_context(_ps(nc, f"d_ps_u{j}", [128, 512], F32)) for j in range(2)]
            ps_g = [pe.enter_context(_ps(nc, f"d_ps_g{j}", [128, 512], F32)) for j in range(2)]
            ps_y = [pe.enter_context(_ps(nc, f"d_ps_y{j}", [128, 1024], F32)) for j in range(2)]

            tdw = None
            tu_free = {0: None, 1: None}
            tg_free = {0: None, 1: None}
            ctr_free = {0: None, 1: None}
            ps_y_free = {0: None, 1: None}
            hact_free = None
            upsl_free = None
            gtsl_free = None
            acc_last = tcopy
            for e in range(9):
                dsrc = (aps["downT"][e] if e < 8 else aps["downTs"])
                tdw = P.dma("gpsimd", P.stream("d_dw"), lambda h, dsrc=dsrc: h.dma_start(
                    out=dwn[:, :, :], in_=dsrc.rearrange("(ci p) n -> p ci n", p=128)),
                    deps=[P.tok("tensor")])
                usrc_f = (aps["upT"][e] if e < 8 else aps["upTs"])
                gsrc_f = (aps["gateT"][e] if e < 8 else aps["gateTs"])
                tlu = P.dma("gpsimd", P.stream("d_up0"), lambda h, usrc_f=usrc_f: h.dma_start(
                    out=upsl[:, :, :], in_=usrc_f.rearrange("(ci p) n -> p ci n", p=128)),
                    deps=[upsl_free])
                tlg = P.dma("gpsimd", P.stream("d_gt0"), lambda h, gsrc_f=gsrc_f: h.dma_start(
                    out=gtsl[:, :, :], in_=gsrc_f.rearrange("(ci p) n -> p ci n", p=128)),
                    deps=[gtsl_free])
                for jg in range(4):
                    jt0 = jg * 1024
                    silu_toks = []
                    for hc in range(16):
                        mmu = mmg = None
                        for j2 in range(2):
                            bk = (hc * 2 + j2) % 2
                            tsl = slice(jt0 + j2 * 512, jt0 + (j2 + 1) * 512)
                            for ci in range(8):
                                mmu = P.op("tensor", lambda h, ci=ci, hc=hc, tsl=tsl, bk=bk:
                                           h.matmul(ps_u[bk][:, :], upsl[:, ci, hc * 128:(hc + 1) * 128],
                                                    xlnT[:, ci, tsl], start=(ci == 0), stop=(ci == 7)),
                                           deps=[tlu, tu_free[bk]], inc=(1 if ci == 7 else 0))
                            for ci in range(8):
                                mmg = P.op("tensor", lambda h, ci=ci, hc=hc, tsl=tsl, bk=bk:
                                           h.matmul(ps_g[bk][:, :], gtsl[:, ci, hc * 128:(hc + 1) * 128],
                                                    xlnT[:, ci, tsl], start=(ci == 0), stop=(ci == 7)),
                                           deps=[tlg, tg_free[bk]], inc=(1 if ci == 7 else 0))
                            tsi = P.op("scalar", lambda h, bk=bk: h.activation(
                                out=sg[bk][:, :], in_=ps_g[bk][:, :], func=AF.Silu),
                                deps=[mmg, sg_free[bk]], inc=1)
                            tmu = P.op("vector", lambda h, hc=hc, j2=j2, bk=bk: h.tensor_tensor(
                                hact[:, hc, j2 * 512:(j2 + 1) * 512], ps_u[bk][:, :], sg[bk][:, :],
                                ALU.mult),
                                deps=[tsi, mmu, hact_free], inc=1)
                            tu_free[bk] = tmu
                            tg_free[bk] = tsi
                            sg_free[bk] = tmu
                            silu_toks.append(tmu)
                        upsl_free = mmu
                        gtsl_free = mmg
                    down_last = None
                    for tsub in range(8):
                        tile_i = jg * 8 + tsub
                        pyp = tsub % 2
                        mm = None
                        for hc in range(16):
                            for oc in range(2):
                                mm = P.op("tensor", lambda h, hc=hc, oc=oc, tsub=tsub, pyp=pyp:
                                          h.matmul(ps_y[pyp][:, oc * 512:(oc + 1) * 512],
                                                   hact[:, hc, tsub * 128:(tsub + 1) * 128],
                                                   dwn[:, hc, oc * 512:(oc + 1) * 512],
                                                   start=(hc == 0), stop=(hc == 15)),
                                          deps=[silu_toks[-1], tdw, ps_y_free[pyp]],
                                          inc=(1 if (hc == 15 and oc == 1) else 0))
                        tev = P.op("vector", lambda h, tile_i=tile_i, tsub=tsub, e=e, pyp=pyp:
                                   h.tensor_scalar(contrib[tsub % 2][:, :], ps_y[pyp][:, :],
                                                   mask[:, tile_i, e:e + 1], None, ALU.mult),
                                   deps=[mm, ctr_free[tsub % 2], prev_mask], inc=1)
                        ps_y_free[pyp] = tev
                        tac = P.dma("gpsimd", P.stream(f"d_ac{tsub % 2}"),
                            lambda h, tile_i=tile_i, tsub=tsub: h.dma_start(
                                out=xout_t[tile_i, :, :], in_=contrib[tsub % 2][:, :],
                                accum_op=ALU.add),
                            deps=[tev, tcopy])
                        ctr_free[tsub % 2] = tac
                        down_last = tac
                    hact_free = mm
            P.barrier()
            P.flush()
            return down_last


L0FFN_W = [("upT", [8, D, 2048]), ("gateT", [8, D, 2048]), ("downT", [8, 2048, D]),
           ("routerT", [D, 8]), ("upTs", [D, 2048]), ("gateTs", [D, 2048]), ("downTs", [2048, D])]


def build_program_l0_ffn():
    nc = _mk_prog()
    aps = {}
    aps["xin"] = nc.dram_tensor("xin", [T, D], F32, kind="ExternalInput").ap()
    aps["xout"] = nc.dram_tensor("xout", [T, D], F32, kind="ExternalOutput").ap()
    for nm, shp in L0FFN_W:
        aps[nm] = nc.dram_tensor(nm, shp, F32, kind="ExternalInput").ap()
    with contextlib.ExitStack() as es:
        P = Prog(nc, None, es)
        block = es.enter_context(nc.Block())
        P.block = block
        consts = const_setup(P, es)
        build_l0_ffn(P, es, aps, consts)
        P.finish()
    return nc


def prep_l0ffn_weights(ins, pfx="L0_ffn_"):
    return {
        "upT": np.ascontiguousarray(ins[pfx + "up"].transpose(0, 2, 1)),
        "gateT": np.ascontiguousarray(ins[pfx + "gate"].transpose(0, 2, 1)),
        "downT": np.ascontiguousarray(ins[pfx + "down"].transpose(0, 2, 1)),
        "routerT": np.ascontiguousarray(ins[pfx + "router"].T),
        "upTs": np.ascontiguousarray(ins[pfx + "w_up_s"].T),
        "gateTs": np.ascontiguousarray(ins[pfx + "w_gate_s"].T),
        "downTs": np.ascontiguousarray(ins[pfx + "w_down_s"].T),
    }


def run_sublayer_l0ffn(x_full, wd):
    nc = build_program_l0_ffn()
    in_maps = []
    for c in range(N_CORES):
        m = {"xin": np.ascontiguousarray(x_full[c])}
        m.update(wd)
        in_maps.append(m)
    res = run_bass_kernel_spmd(nc, in_maps, core_ids=list(range(N_CORES)))
    return np.stack([r["xout"] for r in res.results])




def build_l0_attn_b(P, es, aps, consts):
    nc = P.nc
    xin, xout, znT_d = aps["xin"], aps["xout"], aps["znT"]
    xin_t = xin.rearrange("(n p) d -> n p d", p=128)
    xout_t = xout.rearrange("(n p) d -> n p d", p=128)

    with contextlib.ExitStack() as ph:
        ps_t = ph.enter_context(_ps(nc, "b_ps_t", [128, 1024], BF16))
        xt = ph.enter_context(_sb(nc, "b_xt", [128, 1024], F32))
        y = ph.enter_context(_sb(nc, "b_y", [128, 1024], F32))
        zn = ph.enter_context(_sb(nc, "b_zn", [128, 1024], F32))
        znb = ph.enter_context(_sb(nc, "b_znb", [128, 1024], BF16))
        znTt = ph.enter_context(_sb(nc, "b_znTt", [128, 8, 128], BF16))
        xy = ph.enter_context(_sb(nc, "b_xy", [128, 1024], F32))
        gwb = ph.enter_context(_sb(nc, "b_gwb", [128, 1024], F32))
        bwb = ph.enter_context(_sb(nc, "b_bwb", [128, 1024], F32))
        ps_bc = ph.enter_context(_ps(nc, "b_ps_bc", [128, 1024], F32))
        scr = dict(
            junk=ph.enter_context(_sb(nc, "b_junk", [128, 1024], F32)),
            s1=ph.enter_context(_sb(nc, "b_s1", [128, 1], F32)),
            s2=ph.enter_context(_sb(nc, "b_s2", [128, 1], F32)),
            s3=ph.enter_context(_sb(nc, "b_s3", [128, 1], F32)),
            sq=ph.enter_context(_sb(nc, "b_sq", [128, 1024], F32)),
        )
        lnw, t1 = load_row(P, "b_lnw", aps["ln_w"], D, ph, F32)
        lnb, t2 = load_row(P, "b_lnb", aps["ln_b"], D, ph, F32)
        tg = bcast_row_f32(P, lnw[:, :], D, ps_bc, gwb, consts["ones_f"], [t1] + consts["tok"])
        tb = bcast_row_f32(P, lnb[:, :], D, ps_bc, bwb, consts["ones_f"], [t2, tg])

        prev = {}
        for i in range(NT):
            tl = P.dma("sync", P.stream("b_x"), lambda h, i=i: h.dma_start(
                out=xt[:, :], in_=xin_t[i, :, :]), deps=[prev.get("xy")])
            ty = ln_tile(P, xt[:, :], D, scr, 1e-5, y[:, :], x_deps=[tl],
                         out_tok_extra_deps=[prev.get("xy2")])
            tz = ln_tile(P, y[:, :], D, scr, 1e-5, zn[:, :], x_deps=[ty],
                         out_tok_extra_deps=[prev.get("znb")])
            tz1 = P.op("vector", lambda h: h.tensor_tensor(zn[:, :], zn[:, :], gwb[:, :], ALU.mult),
                       deps=[tz, tg], inc=1)
            tz2 = P.op("vector", lambda h: h.tensor_tensor(zn[:, :], zn[:, :], bwb[:, :], ALU.add),
                       deps=[tz1, tb], inc=1)
            tzb = P.op("vector", lambda h: h.tensor_copy(znb[:, :], zn[:, :]),
                       deps=[tz2, prev.get("znT")], inc=1)
            tT = transpose_tile(P, lambda j: znb[:, j * 128:(j + 1) * 128], ps_t, None, 8,
                                consts["ident_b"], [tzb], psum_free_tok=prev.get("znT"),
                                out_ap=znTt[:, :, :])
            tst1 = P.dma("sync", P.stream("b_znst"), lambda h, i=i: h.dma_start(
                out=znT_d[:, :, i * 128:(i + 1) * 128], in_=znTt[:, :, :]), deps=[tT])
            txy = P.op("vector", lambda h: h.tensor_tensor(xy[:, :], xt[:, :], y[:, :], ALU.add),
                       deps=[ty, tl, prev.get("xyst")], inc=1)
            tst2 = P.dma("sync", P.stream("b_xyst"), lambda h, i=i: h.dma_start(
                out=xout_t[i, :, :], in_=xy[:, :]), deps=[txy])
            prev = dict(xy=ty, xy2=txy, znb=tT, znT=tst1, xyst=tst2)
        P.barrier()
        P.flush()
        b0_done = prev["xyst"]

    acc_last = b0_done
    for hg in range(DEBUG_L0AB_HGS):
        with contextlib.ExitStack() as ph:
            QT = ph.enter_context(_sb(nc, f"b_QT{hg}", [128, 4, 4096], BF16))
            KT = ph.enter_context(_sb(nc, f"b_KT{hg}", [128, 4, 4096], BF16))
            Vp = ph.enter_context(_sb(nc, f"b_Vp{hg}", [128, 32, 520], BF16))
            negB = ph.enter_context(_sb(nc, f"b_negB{hg}", [128, 8], F32))
            wq = ph.enter_context(_sb(nc, f"b_wq{hg}", [128, 8, 512], BF16))
            wk = ph.enter_context(_sb(nc, f"b_wk{hg}", [128, 8, 512], BF16))
            wv = ph.enter_context(_sb(nc, f"b_wv{hg}", [128, 8, 512], BF16))
            wp = ph.enter_context(_sb(nc, f"b_wp{hg}", [128, 4, 1024], BF16))
            brow, tbr = load_row(P, f"b_brow{hg}", aps["b_qkv"], 3072, ph, BF16)
            borow, tbo = load_row(P, f"b_borow{hg}", aps["b_proj"], D, ph, BF16)
            W = aps["W_qkv"]
            twq = P.dma("gpsimd", P.stream("bulk", "bulk"), lambda h, hg=hg: h.dma_start(
                out=wq[:, :, :],
                in_=W[:, hg * 512:hg * 512 + 512].rearrange("(ci p) n -> p ci n", p=128)))
            twk = P.dma("gpsimd", P.stream("bulk", "bulk"), lambda h, hg=hg: h.dma_start(
                out=wk[:, :, :],
                in_=W[:, 1024 + hg * 512:1024 + hg * 512 + 512].rearrange("(ci p) n -> p ci n", p=128)))
            twv = P.dma("gpsimd", P.stream("bulk", "bulk"), lambda h, hg=hg: h.dma_start(
                out=wv[:, :, :],
                in_=W[:, 2048 + hg * 512:2048 + hg * 512 + 512].rearrange("(ci p) n -> p ci n", p=128)))
            twp = P.dma("gpsimd", P.stream("bulk", "bulk"), lambda h, hg=hg: h.dma_start(
                out=wp[:, :, :],
                in_=aps["W_proj"][hg * 512:hg * 512 + 512, :].rearrange("(ci p) n -> p ci n", p=128)))

            with contextlib.ExitStack() as pp:
                znt = [pp.enter_context(_sb(nc, f"b_znt{hg}{j}", [128, 8, 512], BF16))
                       for j in range(2)]
                ps_q = pp.enter_context(_ps(nc, "b_ps_q", [128, 512], F32))
                ps_v = pp.enter_context(_ps(nc, "b_ps_v", [128, 512], F32))
                ps_n = pp.enter_context(_ps(nc, "b_ps_n", [1, 512], F32))
                ps_nb = pp.enter_context(_ps(nc, "b_ps_nb", [128, 8], F32))
                sq_s = pp.enter_context(_sb(nc, "b_sq_s", [128, 4096], F32))
                nrmQ = pp.enter_context(_sb(nc, "b_nrmQ", [1, 8], F32))
                nrmK = pp.enter_context(_sb(nc, "b_nrmK", [1, 8], F32))
                ntmp = pp.enter_context(_sb(nc, "b_ntmp", [1, 512], F32))
                ntm1 = pp.enter_context(_sb(nc, "b_ntm1", [1, 1], F32))
                znt_free = {0: None, 1: None}
                ps_q_free = None
                ps_v_free = None
                qkv_done = []
                for tt in range(8):
                    par = tt % 2
                    tsl = slice(tt * 512, (tt + 1) * 512)
                    tld = P.dma("gpsimd", P.stream(f"b_zl{par}"), lambda h, tsl=tsl, par=par:
                        h.dma_start(out=znt[par][:, :, :], in_=znT_d[:, :, tsl]),
                        deps=[znt_free[par], b0_done])
                    last_read = None
                    for wsb, wtok, dst, boff in (() if "qk" in DEBUG_L0AB_SKIP else ((wq, twq, QT, 0), (wk, twk, KT, 1024))):
                        for ch in range(4):
                            mm = None
                            for ci in range(8):
                                mm = P.op("tensor", lambda h, ch=ch, ci=ci, par=par, wsb=wsb:
                                          h.matmul(ps_q[:, :], wsb[:, ci, ch * 128:(ch + 1) * 128],
                                                   znt[par][:, ci, :], start=(ci == 0), stop=False),
                                          deps=[tld, wtok, ps_q_free], inc=0)
                            mm = P.op("tensor", lambda h, ch=ch, boff=boff, hg=hg:
                                      h.matmul(ps_q[:, :],
                                               brow[:, boff + hg * 512 + ch * 128:boff + hg * 512 + (ch + 1) * 128],
                                               consts["ones_b"][:, 0:512], start=False, stop=True),
                                      deps=[tbr], inc=1)
                            tev = P.op("vector", lambda h, ch=ch, tsl=tsl, dst=dst:
                                       h.tensor_copy(dst[:, ch, tsl], ps_q[:, :]), deps=[mm], inc=1)
                            ps_q_free = tev
                            last_read = mm
                            qkv_done.append(tev)
                    for tsub in (() if "v" in DEBUG_L0AB_SKIP else range(4)):
                        kc = tt * 4 + tsub
                        mm = None
                        for ci in range(8):
                            mm = P.op("tensor", lambda h, tsub=tsub, ci=ci, par=par:
                                      h.matmul(ps_v[:, :], znt[par][:, ci, tsub * 128:(tsub + 1) * 128],
                                               wv[:, ci, :], start=(ci == 0), stop=False),
                                      deps=[tld, twv, ps_v_free], inc=0)
                        mm = P.op("tensor", lambda h, hg=hg:
                                  h.matmul(ps_v[:, :], consts["ones_b"][:, 0:128],
                                           brow[:, 2048 + hg * 512:2048 + hg * 512 + 512],
                                           start=False, stop=True),
                                  deps=[tbr], inc=1)
                        vap = Vp[:, kc, :]
                        vout = ap3(vap, [vap.ap[0], [65, 8], [1, 64]])
                        tev = P.op("vector", lambda h, vout=vout: h.tensor_copy(
                            vout, ps_v[:, :].rearrange("p (e f) -> p e f", e=8)), deps=[mm], inc=1)
                        vone = ap3(vap, [vap.ap[0], [65, 8], [1, 1]])
                        vone = bass.AP(vone.tensor, vone.offset + 64, vone.ap)
                        tmo = P.op("vector", lambda h, vone=vone: h.memset(vone, 1.0),
                                   deps=[mm], inc=1)
                        ps_v_free = tev
                        qkv_done.append(tev)
                        qkv_done.append(tmo)
                        last_read = mm
                    znt_free[par] = last_read
                nz = P.op("gpsimd", lambda h: h.memset(nrmQ[:, :], 0.0), inc=1)
                nz2 = P.op("gpsimd", lambda h: h.memset(nrmK[:, :], 0.0), inc=1)
                qlast = qkv_done[-1] if qkv_done else None
                sq_free = None
                mprev = None
                ntm1_free = None
                for which, src_t, nrm_t, nztok in (() if "bounds" in DEBUG_L0AB_SKIP else (("q", QT, nrmQ, nz), ("k", KT, nrmK, nz2))):
                    for ch in range(4):
                        tsq = P.op("scalar", lambda h, ch=ch, src_t=src_t: h.activation(
                            out=sq_s[:, :], in_=src_t[:, ch, :], func=AF.Square),
                            deps=[qlast, sq_free], inc=1)
                        for half in (range(2) if DEBUG_L0AB_BSTEP >= 2 else ()):
                            hd = ch * 2 + half
                            tac = nztok
                            for tc in range(DEBUG_L0AB_TCN):
                                mm = P.op("tensor", lambda h, half=half, tc=tc: h.matmul(
                                    ps_n[:, :], consts["onescol_f"][half * 64:half * 64 + 64, 0:1],
                                    sq_s[half * 64:half * 64 + 64, tc * 512:(tc + 1) * 512],
                                    start=True, stop=True),
                                    deps=[tsq, mprev], inc=1)
                                tmx = P.op("vector", lambda h: h.reduce_max(ntm1[:, :], ps_n[:, :],
                                                                            axis=AX.X),
                                           deps=[mm, ntm1_free], inc=1)
                                tac = P.op("vector", lambda h, hd=hd, nrm_t=nrm_t: h.tensor_tensor(
                                    nrm_t[:, hd:hd + 1], nrm_t[:, hd:hd + 1], ntm1[:, :], ALU.max),
                                    deps=[tmx, tac], inc=1)
                                ntm1_free = tac
                                mprev = tmx
                            sq_free = mm
                tn1 = P.op("vector", lambda h: h.tensor_tensor(nrmQ[:, :], nrmQ[:, :], nrmK[:, :],
                                                               ALU.mult),
                           deps=[P.tok("vector")], inc=1)
                tn2 = P.op("scalar", lambda h: h.activation(out=nrmQ[:, :], in_=nrmQ[:, :], func=AF.Sqrt),
                           deps=[tn1], inc=1)
                tn3 = P.op("vector", lambda h: h.tensor_scalar(nrmQ[:, :], nrmQ[:, :], -0.125, None,
                                                               ALU.mult),
                           deps=[tn2], inc=1)
                tn4 = P.op("tensor", lambda h: h.matmul(ps_nb[:, :], consts["ones_f"][:, 0:128],
                                                        nrmQ[:, :], start=True, stop=True),
                           deps=[tn3], inc=1)
                tn5 = P.op("vector", lambda h: h.tensor_copy(negB[:, :], ps_nb[:, :]), deps=[tn4], inc=1)
                P.barrier()
                P.flush()

            with contextlib.ExitStack() as pa:
                sc = [pa.enter_context(_ps(nc, f"b_sc{j}", [128, 512], F32)) for j in range(2)]
                av = [pa.enter_context(_ps(nc, f"b_av{j}", [128, 128], F32)) for j in range(2)]
                ps_p = pa.enter_context(_ps(nc, "b_ps_p", [128, 1024], F32))
                ps_t2 = pa.enter_context(_ps(nc, "b_ps_t2", [128, 512], BF16))
                attnT = pa.enter_context(_sb(nc, "b_attnT", [128, 4096], BF16))
                ao = pa.enter_context(_sb(nc, "b_ao", [128, 512], BF16))
                aoT = pa.enter_context(_sb(nc, "b_aoT", [128, 512], BF16))
                rcol = pa.enter_context(_sb(nc, "b_rcol", [128, 1], F32))
                ctr = [pa.enter_context(_sb(nc, f"b_ctr{j}", [128, 1024], F32)) for j in range(2)]
                sc_free = {0: None, 1: None}
                av_free = {0: None, 1: None}
                attnT_free = None
                ao_free = None
                rcol_free = None
                aoT_free = None
                ctr_free = {0: None, 1: None}
                ps_p_free = None
                for qt in range(DEBUG_L0AB_NQT):
                    ao_toks = []
                    for hd in range(8):
                        hrow = (hd % 2) * 64
                        hch = hd // 2
                        exp_toks = []
                        av_par = hd % 2
                        for grp in range(8):
                            par = grp % 2
                            mm = None
                            for k4 in range(4):
                                kc = grp * 4 + k4
                                mm = P.op("tensor", lambda h, par=par, k4=k4, kc=kc, hrow=hrow, hch=hch, qt=qt:
                                          h.matmul(sc[par][:, k4 * 128:(k4 + 1) * 128],
                                                   KT[hrow:hrow + 64, hch, kc * 128:(kc + 1) * 128],
                                                   QT[hrow:hrow + 64, hch, qt * 128:(qt + 1) * 128],
                                                   start=True, stop=True),
                                          deps=[sc_free[par], qkv_done[-1]], inc=(1 if k4 == 3 else 0))
                            tex = P.op("scalar", lambda h, par=par, grp=grp, hd=hd: h.activation(
                                out=attnT[:, grp * 512:(grp + 1) * 512], in_=sc[par][:, :],
                                func=AF.Exp, bias=negB[:, hd:hd + 1], scale=0.125),
                                deps=[mm, tn5, attnT_free], inc=1)
                            sc_free[par] = tex
                            exp_toks.append(tex)
                        mm = None
                        for kc in range(32):
                            mm = P.op("tensor", lambda h, kc=kc, hd=hd, av_par=av_par:
                                      h.matmul(av[av_par][:, 0:65],
                                               attnT[:, kc * 128:(kc + 1) * 128],
                                               Vp[:, kc, hd * 65:(hd + 1) * 65],
                                               start=(kc == 0), stop=(kc == 31)),
                                      deps=[exp_toks[-1], av_free[av_par]], inc=(1 if kc == 31 else 0))
                        attnT_free = mm
                        trc = P.op("vector", lambda h, av_par=av_par: h.reciprocal(
                            rcol[:, :], av[av_par][:, 64:65]), deps=[mm, rcol_free], inc=1)
                        tsc = P.op("vector", lambda h, hd=hd, av_par=av_par: h.tensor_scalar(
                            ao[:, hd * 64:(hd + 1) * 64], av[av_par][:, 0:64], rcol[:, 0:1],
                            None, ALU.mult),
                            deps=[trc, ao_free], inc=1)
                        rcol_free = tsc
                        av_free[av_par] = tsc
                        ao_toks.append(tsc)
                    tT = transpose_tile(P, lambda j: ao[:, j * 128:(j + 1) * 128], ps_t2, aoT, 4,
                                        consts["ident_b"], [ao_toks[-1]], psum_free_tok=aoT_free)
                    ao_free = tT
                    mm = None
                    for oc in range(2):
                        for ci in range(4):
                            mm = P.op("tensor", lambda h, oc=oc, ci=ci, hg=hg: h.matmul(
                                ps_p[:, oc * 512:(oc + 1) * 512], aoT[:, ci * 128:(ci + 1) * 128],
                                wp[:, ci, oc * 512:(oc + 1) * 512], start=(ci == 0),
                                stop=(ci == 3 and hg == 1)),
                                deps=[tT, twp, ps_p_free],
                                inc=(1 if (hg == 1 and ci == 3 and oc == 1) else 0))
                        if hg == 0:
                            mm = P.op("tensor", lambda h, oc=oc: h.matmul(
                                ps_p[:, oc * 512:(oc + 1) * 512], consts["ones_b"][:, 0:128],
                                borow[:, oc * 512:(oc + 1) * 512], start=False, stop=True),
                                deps=[tbo], inc=(1 if oc == 1 else 0))
                    aoT_free = mm
                    cpar = qt % 2
                    tev = P.op("vector", lambda h, cpar=cpar: h.tensor_copy(ctr[cpar][:, :], ps_p[:, :]),
                               deps=[mm, ctr_free[cpar]], inc=1)
                    ps_p_free = tev
                    tac = P.dma("gpsimd", P.stream(f"b_ac{cpar}"), lambda h, qt=qt, cpar=cpar:
                        h.dma_start(out=xout_t[qt, :, :], in_=ctr[cpar][:, :], accum_op=ALU.add),
                        deps=[tev, b0_done])
                    ctr_free[cpar] = tac
                    acc_last = tac
            P.barrier()
            P.flush()
    return acc_last



L0AB_W = [("ln_w", [D]), ("ln_b", [D]), ("W_qkv", [D, 3072]), ("b_qkv", [3072]),
          ("W_proj", [D, D]), ("b_proj", [D])]


def build_program_l0_attn_b():
    nc = _mk_prog()
    aps = {}
    aps["xin"] = nc.dram_tensor("xin", [T, D], F32, kind="ExternalInput").ap()
    aps["xout"] = nc.dram_tensor("xout", [T, D], F32, kind="ExternalOutput").ap()
    aps["znT"] = nc.dram_tensor("znT_scr", [128, 8, T], BF16, kind="Internal").ap()
    for nm, shp in L0AB_W:
        aps[nm] = nc.dram_tensor(nm, shp, F32, kind="ExternalInput").ap()
    with contextlib.ExitStack() as es:
        P = Prog(nc, None, es)
        block = es.enter_context(nc.Block())
        P.block = block
        consts = const_setup(P, es)
        build_l0_attn_b(P, es, aps, consts)
        P.finish()
    return nc


def run_sublayer_l0ab(x_full, wd):
    nc = build_program_l0_attn_b()
    in_maps = []
    for c in range(N_CORES):
        m = {"xin": np.ascontiguousarray(x_full[c])}
        m.update(wd)
        in_maps.append(m)
    res = run_bass_kernel_spmd(nc, in_maps, core_ids=list(range(N_CORES)))
    return np.stack([r["xout"] for r in res.results])




def phase_ln_scratch(P, es, aps, consts, store_xy):
    nc = P.nc
    xin, ynT_d = aps["xin"], aps["ynT"]
    xin_t = xin.rearrange("(n p) d -> n p d", p=128)
    with contextlib.ExitStack() as ph:
        ps_t = ph.enter_context(_ps(nc, "c0_ps_t", [128, 1024], BF16))
        xt = ph.enter_context(_sb(nc, "c0_xt", [128, 1024], F32))
        y = ph.enter_context(_sb(nc, "c0_y", [128, 1024], F32))
        yb = ph.enter_context(_sb(nc, "c0_yb", [128, 1024], BF16))
        yTt = ph.enter_context(_sb(nc, "c0_yTt", [128, 8, 128], BF16))
        xy = ph.enter_context(_sb(nc, "c0_xy", [128, 1024], F32))
        scr = dict(
            junk=ph.enter_context(_sb(nc, "c0_junk", [128, 1024], F32)),
            s1=ph.enter_context(_sb(nc, "c0_s1", [128, 1], F32)),
            s2=ph.enter_context(_sb(nc, "c0_s2", [128, 1], F32)),
            s3=ph.enter_context(_sb(nc, "c0_s3", [128, 1], F32)),
            sq=ph.enter_context(_sb(nc, "c0_sq", [128, 1024], F32)),
        )
        if store_xy:
            xout_t = aps["xout"].rearrange("(n p) d -> n p d", p=128)
        prev = {}
        for i in range(NT):
            tl = P.dma("sync", P.stream("c0_x"), lambda h, i=i: h.dma_start(
                out=xt[:, :], in_=xin_t[i, :, :]), deps=[prev.get("lastx")])
            ty = ln_tile(P, xt[:, :], D, scr, 1e-5, y[:, :], x_deps=[tl],
                         out_tok_extra_deps=[prev.get("ywar")])
            tyb = P.op("vector", lambda h: h.tensor_copy(yb[:, :], y[:, :]),
                       deps=[ty, prev.get("tT")], inc=1)
            tT = transpose_tile(P, lambda j: yb[:, j * 128:(j + 1) * 128], ps_t, None, 8,
                                consts["ident_b"], [tyb], psum_free_tok=prev.get("tT"),
                                out_ap=yTt[:, :, :])
            tst1 = P.dma("sync", P.stream("c0_st"), lambda h, i=i: h.dma_start(
                out=ynT_d[:, :, i * 128:(i + 1) * 128], in_=yTt[:, :, :]), deps=[tT])
            toks = dict(tT=tst1, ywar=None, lastx=ty)
            if store_xy:
                txy = P.op("vector", lambda h: h.tensor_tensor(xy[:, :], xt[:, :], y[:, :], ALU.add),
                           deps=[ty, tl, prev.get("xyst")], inc=1)
                tst2 = P.dma("sync", P.stream("c0_xyst"), lambda h, i=i: h.dma_start(
                    out=xout_t[i, :, :], in_=xy[:, :]), deps=[txy])
                toks["xyst"] = tst2
                toks["ywar"] = txy
                toks["lastx"] = txy
                last = tst2
            else:
                toks["ywar"] = tT
                last = tst1
            prev = toks
        P.barrier()
        P.flush()
    return last




def build_l1_attn_a(P, es, aps, consts):
    nc = P.nc
    xout, ynT_d = aps["xout"], aps["ynT"]
    xout_t = xout.rearrange("(n p) d -> n p d", p=128)
    if not aps.get("inplace"):
        tcopy = P.dma("sync", P.stream("c_copy", "bulk"), lambda h: h.dma_start(
            out=aps["xout"], in_=aps["xin"]))
    else:
        tcopy = None
    ln_done = phase_ln_scratch(P, es, aps, consts, store_xy=False)

    acc_last = ln_done
    for hg in range(2):
        with contextlib.ExitStack() as ph:
            QT = ph.enter_context(_sb(nc, f"c_QT{hg}", [128, 4, 4096], BF16))
            KT = ph.enter_context(_sb(nc, f"c_KT{hg}", [128, 4, 4096], BF16))
            Vp = ph.enter_context(_sb(nc, f"c_Vp{hg}", [128, 32, 520], BF16))
            augk = ph.enter_context(_sb(nc, f"c_augk{hg}", [2, 4096], BF16))
            augq = ph.enter_context(_sb(nc, f"c_augq{hg}", [2, 4096], BF16))
            mcols = ph.enter_context(_sb(nc, f"c_mcols{hg}", [128, 32], F32))
            Gb = ph.enter_context(_sb(nc, f"c_Gb{hg}", [128, 8], F32))
            brow = {}
            btoks = []
            for nm in ("bq", "bk", "bv", "bo"):
                brow[nm], tk = load_row(P, f"c_{nm}{hg}", aps[nm], D, ph, BF16)
                btoks.append(tk)
            growg, tgr = load_row(P, f"c_G{hg}", aps["G"], 16, ph, F32)

            with contextlib.ExitStack() as pp:
                znt = [pp.enter_context(_sb(nc, f"c_znt{hg}{j}", [128, 8, 512], BF16))
                       for j in range(2)]
                wq = pp.enter_context(_sb(nc, f"c_wq{hg}", [128, 8, 512], BF16))
                wk = pp.enter_context(_sb(nc, f"c_wk{hg}", [128, 8, 512], BF16))
                wv = pp.enter_context(_sb(nc, f"c_wv{hg}", [128, 8, 512], BF16))
                bq2 = pp.enter_context(_sb(nc, f"c_bq2{hg}", [1, 512], BF16))
                twq = P.dma("gpsimd", P.stream("bulk", "bulk"), lambda h, hg=hg: h.dma_start(
                    out=wq[:, :, :],
                    in_=aps["WqT"][:, hg * 512:hg * 512 + 512].rearrange("(ci p) n -> p ci n", p=128)))
                twk = P.dma("gpsimd", P.stream("bulk", "bulk"), lambda h, hg=hg: h.dma_start(
                    out=wk[:, :, :],
                    in_=aps["WkT"][:, hg * 512:hg * 512 + 512].rearrange("(ci p) n -> p ci n", p=128)))
                twv = P.dma("gpsimd", P.stream("bulk", "bulk"), lambda h, hg=hg: h.dma_start(
                    out=wv[:, :, :],
                    in_=aps["WvT"][:, hg * 512:hg * 512 + 512].rearrange("(ci p) n -> p ci n", p=128)))
                tws = P.op("vector", lambda h: h.tensor_scalar(
                    rearr(wq[:, :, :], "p a b -> p (a b)"), rearr(wq[:, :, :], "p a b -> p (a b)"),
                    -2.0, None, ALU.mult), deps=[twq], inc=1)
                tbs = P.op("vector", lambda h, hg=hg: h.tensor_scalar(
                    bq2[:, :], brow["bq"][:, hg * 512:hg * 512 + 512], -2.0, None, ALU.mult),
                    deps=[btoks[0]], inc=1)
                ps_q = pp.enter_context(_ps(nc, "c_ps_q", [128, 512], F32))
                ps_v = pp.enter_context(_ps(nc, "c_ps_v", [128, 512], F32))
                znt_free = {0: None, 1: None}
                ps_q_free = None
                ps_v_free = None
                qkv_done = []
                for tt in range(8):
                    par = tt % 2
                    tsl = slice(tt * 512, (tt + 1) * 512)
                    tld = P.dma("gpsimd", P.stream(f"c_zl{par}"), lambda h, tsl=tsl, par=par:
                                h.dma_start(out=znt[par][:, :, :], in_=ynT_d[:, :, tsl]),
                                deps=[znt_free[par], ln_done])
                    last_read = None
                    for wsb, wtok, dst, bx in ((wq, tws, QT, 0), (wk, twk, KT, 1)):
                        for ch in range(4):
                            mm = None
                            for ci in range(8):
                                mm = P.op("tensor", lambda h, ch=ch, ci=ci, par=par, wsb=wsb:
                                          h.matmul(ps_q[:, :], wsb[:, ci, ch * 128:(ch + 1) * 128],
                                                   znt[par][:, ci, :], start=(ci == 0), stop=False),
                                          deps=[tld, wtok, ps_q_free], inc=0)
                            if bx == 0:
                                mm = P.op("tensor", lambda h, ch=ch: h.matmul(
                                    ps_q[:, :], bq2[:, ch * 128:(ch + 1) * 128],
                                    consts["ones_b"][:, 0:512], start=False, stop=True),
                                    deps=[tbs], inc=1)
                            else:
                                mm = P.op("tensor", lambda h, ch=ch, hg=hg: h.matmul(
                                    ps_q[:, :], brow["bk"][:, hg * 512 + ch * 128:hg * 512 + (ch + 1) * 128],
                                    consts["ones_b"][:, 0:512], start=False, stop=True),
                                    deps=[btoks[1]], inc=1)
                            tev = P.op("vector", lambda h, ch=ch, tsl=tsl, dst=dst:
                                       h.tensor_copy(dst[:, ch, tsl], ps_q[:, :]), deps=[mm], inc=1)
                            ps_q_free = tev
                            qkv_done.append(tev)
                    for tsub in range(4):
                        kc = tt * 4 + tsub
                        mm = None
                        for ci in range(8):
                            mm = P.op("tensor", lambda h, tsub=tsub, ci=ci, par=par:
                                      h.matmul(ps_v[:, :], znt[par][:, ci, tsub * 128:(tsub + 1) * 128],
                                               wv[:, ci, :], start=(ci == 0), stop=False),
                                      deps=[tld, twv, ps_v_free], inc=0)
                        mm = P.op("tensor", lambda h, hg=hg: h.matmul(
                            ps_v[:, :], consts["ones_b"][:, 0:128],
                            brow["bv"][:, hg * 512:hg * 512 + 512], start=False, stop=True),
                            deps=[btoks[2]], inc=1)
                        vap = Vp[:, kc, :]
                        vout = ap3(vap, [vap.ap[0], [65, 8], [1, 64]])
                        tev = P.op("vector", lambda h, vout=vout: h.tensor_copy(
                            vout, ps_v[:, :].rearrange("p (e f) -> p e f", e=8)), deps=[mm], inc=1)
                        vone = ap3(vap, [vap.ap[0], [65, 8], [1, 1]])
                        vone = bass.AP(vone.tensor, vone.offset + 64, vone.ap)
                        tmo = P.op("vector", lambda h, vone=vone: h.memset(vone, 1.0),
                                   deps=[mm], inc=1)
                        ps_v_free = tev
                        qkv_done.append(tev)
                        qkv_done.append(tmo)
                        last_read = mm
                    znt_free[par] = last_read
                P.barrier()
                P.flush()

            with contextlib.ExitStack() as pa:
                Vm = pa.enter_context(_sb(nc, f"c_Vm{hg}", [128, 32, 65], BF16))
                aoall = pa.enter_context(_sb(nc, f"c_ao{hg}", [128, 32, 512], BF16))
                wo = pa.enter_context(_sb(nc, f"c_wo{hg}", [128, 4, 1024], BF16))
                two = P.dma("gpsimd", P.stream("bulk", "bulk"), lambda h, hg=hg: h.dma_start(
                    out=wo[:, :, :],
                    in_=aps["WoT"][hg * 512:hg * 512 + 512, :].rearrange("(ci p) n -> p ci n", p=128)))
                sc = [pa.enter_context(_ps(nc, f"c_sc{j}", [128, 512], F32)) for j in range(2)]
                av = [pa.enter_context(_ps(nc, f"c_av{j}", [128, 128], F32)) for j in range(2)]
                ps_p = pa.enter_context(_ps(nc, "c_ps_p", [128, 1024], F32))
                ps_t2 = pa.enter_context(_ps(nc, "c_ps_t2", [128, 512], BF16))
                ps_n = pa.enter_context(_ps(nc, "c_ps_n", [1, 512], F32))
                sq_s = pa.enter_context(_sb(nc, "c_sq_s", [128, 4096], BF16))
                rr = pa.enter_context(_sb(nc, "c_rr", [128, 512], F32))
                lnr = pa.enter_context(_sb(nc, "c_lnr", [128, 512], F32))
                attnT = pa.enter_context(_sb(nc, "c_attnT", [128, 4096], BF16))
                q2row = pa.enter_context(_sb(nc, "c_q2row", [1, 4096], BF16))
                aoT = pa.enter_context(_sb(nc, "c_aoT", [128, 512], BF16))
                rcol = pa.enter_context(_sb(nc, "c_rcol", [128, 1], F32))
                ctr = [pa.enter_context(_sb(nc, f"c_ctr{j}", [128, 1024], F32)) for j in range(2)]
                tonce = P.op("gpsimd", lambda h: h.memset(augk[0:2, :], 1.0), inc=1)
                tonce2 = P.op("gpsimd", lambda h: h.memset(augq[0:2, :], 1.0), inc=1)
                sc_free = {0: None, 1: None}
                av_free = {0: None, 1: None}
                attnT_free = None
                rcol_free = None
                ao_free = None
                aoT_free = None
                ctr_free = {0: None, 1: None}
                sq_free = None
                augk_free = None
                augq_free = None
                q2row_free = None
                mcols_free = None
                Vm_free = None
                rr_free = None
                lnr_free = None
                mprev = None
                tgm = P.op("tensor", lambda h, hg=hg: h.matmul(
                    ps_p[:, 0:8], consts["ones_f"][:, 0:128], growg[:, hg * 8:hg * 8 + 8],
                    start=True, stop=True), deps=[tgr] + consts["tok"], inc=1)
                tgb = P.op("vector", lambda h: h.tensor_copy(Gb[:, :], ps_p[:, 0:8]),
                           deps=[tgm], inc=1)
                ps_p_free = tgb
                for hd in range(8):
                    hrow = (hd % 2) * 64
                    hch = hd // 2
                    for which, src_t, scale_sq in ((0, KT, 1.0), (1, QT, 0.5)):
                        tsq = P.op("scalar", lambda h, src_t=src_t, hch=hch, scale_sq=scale_sq:
                                   h.activation(out=sq_s[:, :], in_=src_t[:, hch, :], func=AF.Square,
                                                scale=scale_sq),
                                   deps=[qkv_done[-1], sq_free], inc=1)
                        war = augk_free if which == 0 else q2row_free
                        for tc in range(8):
                            mm = P.op("tensor", lambda h, hrow=hrow, tc=tc: h.matmul(
                                ps_n[:, :], consts["onescol_b"][hrow:hrow + 64, 0:1],
                                sq_s[hrow:hrow + 64, tc * 512:(tc + 1) * 512],
                                start=True, stop=True),
                                deps=[tsq, mprev], inc=1)
                            if which == 0:
                                tcp = P.op("vector", lambda h, tc=tc: h.tensor_copy(
                                    augk[0:1, tc * 512:(tc + 1) * 512], ps_n[:, :]),
                                    deps=[mm, war, tonce, tonce2], inc=1)
                            else:
                                tcp = P.op("vector", lambda h, tc=tc: h.tensor_copy(
                                    q2row[0:1, tc * 512:(tc + 1) * 512], ps_n[:, :]),
                                    deps=[mm, war, tonce, tonce2], inc=1)
                            mprev = tcp
                            war = tcp
                        if which == 0:
                            augk_free = war
                            k2done = war
                        else:
                            tq2d = P.dma("gpsimd", P.stream("c_q2d"), lambda h: h.dma_start(
                                out=augq[1:2, :], in_=q2row[0:1, :]), deps=[war, augq_free])
                            q2row_free = tq2d
                            augq_free = tq2d
                            q2done = tq2d
                        sq_free = mm
                    mmt = None
                    for kc in range(32):
                        mmt = P.op("tensor", lambda h, kc=kc: h.transpose(
                            ps_t2[:, 2 * kc:2 * kc + 1],
                            augk[0:1, kc * 128:(kc + 1) * 128], consts["ident_b"][0:1, 0:1]),
                            deps=[k2done, mcols_free, aoT_free] + consts["tok"],
                            inc=(1 if kc == 31 else 0))
                    tsm0 = P.op("scalar", lambda h: h.activation(
                        out=mcols[:, :], in_=ap3(ps_t2[:, 0:64], [ps_t2[:, 0:64].ap[0], [2, 32]]),
                        func=AF.Sqrt),
                        deps=[mmt], inc=1)
                    tsm = P.op("vector", lambda h, hd=hd: h.tensor_scalar(
                        mcols[:, :], mcols[:, :], Gb[:, hd:hd + 1], None, ALU.mult),
                        deps=[tsm0, tgb], inc=1)
                    tvm = None
                    for kc in range(32):
                        tvm = P.op("vector", lambda h, kc=kc, hd=hd: h.tensor_scalar(
                            Vm[:, kc, :], Vp[:, kc, hd * 65:(hd + 1) * 65], mcols[:, kc:kc + 1],
                            None, ALU.mult),
                            deps=[tsm, Vm_free, qkv_done[-1]], inc=1)
                    mcols_free = tvm
                    vm_ready = tvm
                    vm_last_read = None
                    for qt in range(NT):
                        av_par = qt % 2
                        exp_toks = []
                        for grp in range(8):
                            par = grp % 2
                            mm = None
                            for k4 in range(4):
                                kc = grp * 4 + k4
                                mm = P.op("tensor", lambda h, par=par, k4=k4, kc=kc, hrow=hrow, hch=hch, qt=qt:
                                          h.matmul(sc[par][:, k4 * 128:(k4 + 1) * 128],
                                                   KT[hrow:hrow + 64, hch, kc * 128:(kc + 1) * 128],
                                                   QT[hrow:hrow + 64, hch, qt * 128:(qt + 1) * 128],
                                                   start=True, stop=False),
                                          deps=[sc_free[par], qkv_done[-1], q2done], inc=0)
                                mm = P.op("tensor", lambda h, par=par, k4=k4, kc=kc, qt=qt:
                                          h.matmul(sc[par][:, k4 * 128:(k4 + 1) * 128],
                                                   augk[:, kc * 128:(kc + 1) * 128],
                                                   augq[:, qt * 128:(qt + 1) * 128],
                                                   start=False, stop=True),
                                          deps=[], inc=(1 if k4 == 3 else 0))
                                scores_last = mm
                            tcl = P.op("vector", lambda h, par=par: h.tensor_scalar(
                                rr[:, :], sc[par][:, :], 1e-30, None, ALU.max),
                                deps=[mm, rr_free], inc=1)
                            sc_free[par] = tcl
                            tln_ = P.op("scalar", lambda h: h.activation(out=lnr[:, :], in_=rr[:, :],
                                                                         func=AF.Ln),
                                        deps=[tcl, lnr_free], inc=1)
                            rr_free = tln_
                            tex = P.op("scalar", lambda h, grp=grp: h.activation(
                                out=attnT[:, grp * 512:(grp + 1) * 512], in_=lnr[:, :],
                                func=AF.Exp, scale=-1.0),
                                deps=[tln_, attnT_free], inc=1)
                            lnr_free = tex
                            exp_toks.append(tex)
                        mm = None
                        for kc in range(32):
                            mm = P.op("tensor", lambda h, kc=kc, av_par=av_par:
                                      h.matmul(av[av_par][:, 0:65],
                                               attnT[:, kc * 128:(kc + 1) * 128],
                                               Vm[:, kc, :],
                                               start=(kc == 0), stop=(kc == 31)),
                                      deps=[exp_toks[-1], av_free[av_par], vm_ready], inc=(1 if kc == 31 else 0))
                        attnT_free = mm
                        vm_last_read = mm
                        aug_last_read = scores_last
                        trc = P.op("vector", lambda h, av_par=av_par: h.reciprocal(
                            rcol[:, :], av[av_par][:, 64:65]), deps=[mm, rcol_free], inc=1)
                        tsc = P.op("vector", lambda h, hd=hd, qt=qt, av_par=av_par: h.tensor_scalar(
                            aoall[:, qt, hd * 64:(hd + 1) * 64], av[av_par][:, 0:64], rcol[:, 0:1],
                            None, ALU.mult),
                            deps=[trc, ao_free], inc=1)
                        rcol_free = tsc
                        av_free[av_par] = tsc
                    Vm_free = vm_last_read
                    augk_free = aug_last_read
                    augq_free = aug_last_read
                    ao_done = tsc
                for qt in range(NT):
                    tT = transpose_tile(P, lambda j, qt=qt: aoall[:, qt, j * 128:(j + 1) * 128],
                                        ps_t2, aoT, 4, consts["ident_b"], [ao_done],
                                        psum_free_tok=aoT_free)
                    mm = None
                    for oc in range(2):
                        for ci in range(4):
                            mm = P.op("tensor", lambda h, oc=oc, ci=ci, hg=hg: h.matmul(
                                ps_p[:, oc * 512:(oc + 1) * 512], aoT[:, ci * 128:(ci + 1) * 128],
                                wo[:, ci, oc * 512:(oc + 1) * 512], start=(ci == 0),
                                stop=(ci == 3 and hg == 1)),
                                deps=[tT, two, ps_p_free],
                                inc=(1 if (hg == 1 and ci == 3 and oc == 1) else 0))
                        if hg == 0:
                            mm = P.op("tensor", lambda h, oc=oc: h.matmul(
                                ps_p[:, oc * 512:(oc + 1) * 512], consts["ones_b"][:, 0:128],
                                brow["bo"][:, oc * 512:(oc + 1) * 512], start=False, stop=True),
                                deps=[btoks[3]], inc=(1 if oc == 1 else 0))
                    aoT_free = mm
                    cpar = qt % 2
                    tev = P.op("vector", lambda h, cpar=cpar: h.tensor_copy(ctr[cpar][:, :], ps_p[:, :]),
                               deps=[mm, ctr_free[cpar]], inc=1)
                    ps_p_free = tev
                    tac = P.dma("gpsimd", P.stream(f"c_ac{cpar}"), lambda h, qt=qt, cpar=cpar:
                                h.dma_start(out=xout_t[qt, :, :], in_=ctr[cpar][:, :],
                                            accum_op=ALU.add),
                                deps=[tev, tcopy])
                    ctr_free[cpar] = tac
                    acc_last = tac
            P.barrier()
            P.flush()
    return acc_last


L1AA_W = [("WqT", [D, D]), ("WkT", [D, D]), ("WvT", [D, D]), ("WoT", [D, D]),
          ("bq", [D]), ("bk", [D]), ("bv", [D]), ("bo", [D]), ("G", [16])]


def build_program_l1_attn_a():
    nc = _mk_prog()
    aps = {}
    aps["xin"] = nc.dram_tensor("xin", [T, D], F32, kind="ExternalInput").ap()
    aps["xout"] = nc.dram_tensor("xout", [T, D], F32, kind="ExternalOutput").ap()
    aps["ynT"] = nc.dram_tensor("ynT_scr", [128, 8, T], BF16, kind="Internal").ap()
    for nm, shp in L1AA_W:
        aps[nm] = nc.dram_tensor(nm, shp, F32, kind="ExternalInput").ap()
    with contextlib.ExitStack() as es:
        P = Prog(nc, None, es)
        block = es.enter_context(nc.Block())
        P.block = block
        consts = const_setup(P, es)
        build_l1_attn_a(P, es, aps, consts)
        P.finish()
    return nc


def run_sublayer_l1aa(x_full, wd):
    nc = build_program_l1_attn_a()
    in_maps = []
    for c in range(N_CORES):
        m = {"xin": np.ascontiguousarray(x_full[c])}
        m.update(wd)
        in_maps.append(m)
    res = run_bass_kernel_spmd(nc, in_maps, core_ids=list(range(N_CORES)))
    return np.stack([r["xout"] for r in res.results])



MIDS = [0.25, 0.75, 1.25, 1.75, 2.5, 3.5, 5.0]


def fp4_quant_chunk(P, x_ap, Fc, scr, out_ap, deps, war_deps=()):
    nb = Fc // 16
    bm, sc_, rsc = scr["bm"], scr["sc"], scr["rsc"]
    xn, cc, r1, sg = scr["xn"], scr["c"], scr["r1"], scr["sg"]
    dd = list(deps) + list(war_deps)

    def blk(ap):
        return ap3(ap, [ap.ap[0], [16, ap.ap[1][1] // 16], [1, 16]])
    t1 = P.op("vector", lambda h: h.tensor_reduce(
        bm[:, 0:nb], blk(x_ap), AX.X, ALU.max,
        apply_absolute_value=True), deps=dd, inc=1)
    t2 = P.op("vector", lambda h: h.tensor_scalar(sc_[:, 0:nb], bm[:, 0:nb], 1.0 / 6.0, 1e-12,
                                                  ALU.mult, ALU.max), deps=[t1], inc=1)
    t3 = P.op("vector", lambda h: h.reciprocal(rsc[:, 0:nb], sc_[:, 0:nb]), deps=[t2], inc=1)
    rb = ap3(rsc[:, 0:nb], [rsc[:, 0:nb].ap[0], [1, nb], [0, 16]])
    t4 = P.op("vector", lambda h: h.tensor_tensor(
        blk(xn[:, 0:Fc]), blk(x_ap), rb, ALU.mult), deps=[t3] + dd, inc=1)
    t5 = P.op("scalar", lambda h: h.activation(out=sg[:, 0:Fc], in_=xn[:, 0:Fc], func=AF.Sign),
              deps=[t4], inc=1)
    t6 = P.op("vector", lambda h: h.tensor_tensor(xn[:, 0:Fc], xn[:, 0:Fc], sg[:, 0:Fc], ALU.mult),
              deps=[t5], inc=1)
    tk = P.op("vector", lambda h: h.tensor_scalar(cc[:, 0:Fc], xn[:, 0:Fc], MIDS[0], None, ALU.is_ge),
              deps=[t6], inc=1)
    for k in range(1, 7):
        tk = P.op("vector", lambda h, k=k: h.scalar_tensor_tensor(
            cc[:, 0:Fc], xn[:, 0:Fc], MIDS[k], cc[:, 0:Fc], ALU.is_ge, ALU.add),
            deps=[tk], inc=1)
    t7 = P.op("vector", lambda h: h.tensor_scalar(r1[:, 0:Fc], cc[:, 0:Fc], 4.0, 0.0,
                                                  ALU.subtract, ALU.max), deps=[tk], inc=1)
    t8 = P.op("vector", lambda h: h.tensor_scalar(xn[:, 0:Fc], cc[:, 0:Fc], 6.0, 0.0,
                                                  ALU.subtract, ALU.max), deps=[t7], inc=1)
    t9 = P.op("vector", lambda h: h.tensor_tensor(r1[:, 0:Fc], cc[:, 0:Fc], r1[:, 0:Fc], ALU.add),
              deps=[t8], inc=1)
    t10 = P.op("vector", lambda h: h.scalar_tensor_tensor(
        cc[:, 0:Fc], r1[:, 0:Fc], 0.5, xn[:, 0:Fc], ALU.mult, ALU.add), deps=[t9], inc=1)
    t11 = P.op("vector", lambda h: h.tensor_tensor(cc[:, 0:Fc], cc[:, 0:Fc], sg[:, 0:Fc], ALU.mult),
               deps=[t10], inc=1)
    sb = ap3(sc_[:, 0:nb], [sc_[:, 0:nb].ap[0], [1, nb], [0, 16]])
    t12 = P.op("vector", lambda h: h.tensor_tensor(
        blk(out_ap), blk(cc[:, 0:Fc]), sb, ALU.mult), deps=[t11], inc=1)
    return t12


def mk_fp4_scr(P, ph, tag):
    nc = P.nc
    return dict(
        bm=ph.enter_context(_sb(nc, f"q_bm{tag}", [128, 32], F32)),
        sc=ph.enter_context(_sb(nc, f"q_sc{tag}", [128, 32], F32)),
        rsc=ph.enter_context(_sb(nc, f"q_rsc{tag}", [128, 32], F32)),
        xn=ph.enter_context(_sb(nc, f"q_xn{tag}", [128, 512], F32)),
        c=ph.enter_context(_sb(nc, f"q_c{tag}", [128, 512], F32)),
        r1=ph.enter_context(_sb(nc, f"q_r1{tag}", [128, 512], F32)),
        sg=ph.enter_context(_sb(nc, f"q_sg{tag}", [128, 512], F32)),
    )




def build_l1_ffn_prep(P, es, aps, consts):
    nc = P.nc
    with contextlib.ExitStack() as ph:
        wt = [ph.enter_context(_sb(nc, f"e_wt{j}", [128, 2816], F32)) for j in range(2)]
        wq_ = ph.enter_context(_sb(nc, "e_wq", [128, 2816], BF16))
        ev = ph.enter_context(_sb(nc, "e_ev", [128, 8, 128], BF16))
        ps_t = ph.enter_context(_ps(nc, "e_ps_t", [128, 1024], BF16))
        scr = mk_fp4_scr(P, ph, "w")
        wt_free = {0: None, 1: None}
        q_prev = None
        ev_free = None
        wq_free = None
        st_last = None
        jobs = []
        for (wsrc, wdst, nrows) in ((aps["w_gate"], aps["wgqT"], 2816),
                                    (aps["w_up"], aps["wupqT"], 2816),
                                    (aps["w_down"], aps["wdqT"], 1024)):
            ncols = 2816 if nrows == 1024 else 1024
            for tile in range(nrows // 128):
                jobs.append((wsrc, wdst, tile, ncols))
        for jid, (wsrc, wdst, tile, ncols) in enumerate(jobs):
            par = jid % 2
            tl = P.dma("sync", P.stream(f"e_wl{par}"), lambda h, wsrc=wsrc, tile=tile, ncols=ncols, par=par:
                       h.dma_start(out=wt[par][:, 0:ncols], in_=wsrc[tile * 128:(tile + 1) * 128, :]),
                       deps=[wt_free[par]])
            qts = []
            for c0 in range(0, ncols, 512):
                Fc = min(512, ncols - c0)
                q_prev = fp4_quant_chunk(P, wt[par][:, c0:c0 + Fc], Fc, scr,
                                         wq_[:, c0:c0 + Fc], [tl], [q_prev, wq_free])
                qts.append(q_prev)
            wt_free[par] = q_prev
            nch = ncols // 128
            for g0 in range(0, nch, 8):
                gn = min(8, nch - g0)
                tts = []
                for j in range(gn):
                    tts.append(P.op("tensor", lambda h, j=j, g0=g0: h.transpose(
                        ps_t[:, j * 128:(j + 1) * 128], wq_[:, (g0 + j) * 128:(g0 + j + 1) * 128],
                        consts["ident_b"][:, :]),
                        deps=[qts[-1], ev_free], inc=(1 if j == gn - 1 else 0)))
                tev = P.op("vector", lambda h, gn=gn: h.tensor_copy(
                    ev[:, 0:gn, :], ps_t[:, 0:gn * 128].rearrange("p (j t) -> p j t", j=gn)),
                    deps=[tts[-1]], inc=1)
                wq_free = tts[-1]
                tst = P.dma("sync", P.stream("e_wst"), lambda h, wdst=wdst, tile=tile, g0=g0, gn=gn:
                            h.dma_start(out=wdst[g0 * 128:(g0 + gn) * 128,
                                                 tile * 128:(tile + 1) * 128].rearrange(
                                "(ci p) t -> p ci t", p=128), in_=ev[:, 0:gn, :]),
                            deps=[tev])
                ev_free = tst
                st_last = tst
        P.barrier()
        P.flush()
    return st_last




def build_l1_ffn_call(P, es, aps, consts, tag):
    nc = P.nc
    xin, xout = aps["xin"], aps["xout"]
    xin_t = xin.rearrange("(n p) d -> n p d", p=128)
    xout_t = xout.rearrange("(n p) d -> n p d", p=128)
    with contextlib.ExitStack() as ph:
        wg = ph.enter_context(_sb(nc, f"e_gT{tag}", [128, 8, 2816], BF16))
        wu = ph.enter_context(_sb(nc, f"e_uT{tag}", [128, 8, 2816], BF16))
        wdn = ph.enter_context(_sb(nc, f"e_dT{tag}", [128, 22, 1024], BF16))
        tw1 = P.dma("gpsimd", P.stream("bulk", "bulk"), lambda h: h.dma_start(
            out=wg[:, :, :], in_=aps["wgqT"].rearrange("(ci p) n -> p ci n", p=128)))
        tw2 = P.dma("gpsimd", P.stream("bulk", "bulk"), lambda h: h.dma_start(
            out=wu[:, :, :], in_=aps["wupqT"].rearrange("(ci p) n -> p ci n", p=128)))
        tw3 = P.dma("gpsimd", P.stream("bulk", "bulk"), lambda h: h.dma_start(
            out=wdn[:, :, :], in_=aps["wdqT"].rearrange("(ci p) n -> p ci n", p=128)))
        xt = ph.enter_context(_sb(nc, f"e_xt{tag}", [128, 1024], F32))
        y = ph.enter_context(_sb(nc, f"e_y{tag}", [128, 1024], F32))
        yq = ph.enter_context(_sb(nc, f"e_yq{tag}", [128, 1024], BF16))
        yqT = ph.enter_context(_sb(nc, f"e_yqT{tag}", [128, 1024], BF16))
        hc = ph.enter_context(_sb(nc, f"e_hc{tag}", [128, 512], F32))
        hq = ph.enter_context(_sb(nc, f"e_hq{tag}", [128, 512], BF16))
        hqT = ph.enter_context(_sb(nc, f"e_hqT{tag}", [128, 22, 128], BF16))
        sgx = [ph.enter_context(_sb(nc, f"e_sgx{tag}{j}", [128, 512], BF16)) for j in range(2)]
        xn2 = ph.enter_context(_sb(nc, f"e_xn2{tag}", [128, 1024], F32))
        scr = mk_fp4_scr(P, ph, f"c{tag}")
        lnscr = dict(
            junk=ph.enter_context(_sb(nc, f"e_junk{tag}", [128, 1024], F32)),
            s1=ph.enter_context(_sb(nc, f"e_s1{tag}", [128, 1], F32)),
            s2=ph.enter_context(_sb(nc, f"e_s2{tag}", [128, 1], F32)),
            s3=ph.enter_context(_sb(nc, f"e_s3{tag}", [128, 1], F32)),
            sq=ph.enter_context(_sb(nc, f"e_sq{tag}", [128, 1024], F32)),
        )
        ps_g = [ph.enter_context(_ps(nc, f"e_ps_g{tag}{j}", [128, 512], F32)) for j in range(2)]
        ps_u = [ph.enter_context(_ps(nc, f"e_ps_u{tag}{j}", [128, 512], F32)) for j in range(2)]
        ps_d = ph.enter_context(_ps(nc, f"e_ps_d{tag}", [128, 1024], F32))
        ps_t = ph.enter_context(_ps(nc, f"e_ps_t{tag}", [128, 1024], BF16))

        prev = {}
        q_prev = None
        for i in range(NT):
            tl = P.dma("sync", P.stream("e_x"), lambda h, i=i: h.dma_start(
                out=xt[:, :], in_=xin_t[i, :, :]), deps=[prev.get("xread")])
            ty = ln_tile(P, xt[:, :], D, lnscr, 1e-5, y[:, :], x_deps=[tl],
                         out_tok_extra_deps=[prev.get("ywar")])
            qts = []
            for c0 in (0, 512):
                q_prev = fp4_quant_chunk(P, y[:, c0:c0 + 512], 512, scr, yq[:, c0:c0 + 512],
                                         [ty], [q_prev, prev.get("yqwar")])
                qts.append(q_prev)
            tT = transpose_tile(P, lambda j: yq[:, j * 128:(j + 1) * 128], ps_t, yqT, 8,
                                consts["ident_b"], [qts[-1]], psum_free_tok=prev.get("ps_t_f"))
            ps_t_f = tT
            hq_toks = []
            dmm = None
            for oc in range(6):
                bk = oc % 2
                c0 = oc * 512
                Fc = min(512, 2816 - c0)
                mmg = mmu = None
                for ci in range(8):
                    mmg = P.op("tensor", lambda h, ci=ci, c0=c0, Fc=Fc, bk=bk: h.matmul(
                        ps_g[bk][:, 0:Fc], yqT[:, ci * 128:(ci + 1) * 128], wg[:, ci, c0:c0 + Fc],
                        start=(ci == 0), stop=(ci == 7)),
                        deps=[tT, tw1, prev.get(f"ps_g_free{bk}")], inc=(1 if ci == 7 else 0))
                for ci in range(8):
                    mmu = P.op("tensor", lambda h, ci=ci, c0=c0, Fc=Fc, bk=bk: h.matmul(
                        ps_u[bk][:, 0:Fc], yqT[:, ci * 128:(ci + 1) * 128], wu[:, ci, c0:c0 + Fc],
                        start=(ci == 0), stop=(ci == 7)),
                        deps=[tT, tw2, prev.get(f"ps_u_free{bk}")], inc=(1 if ci == 7 else 0))
                tsi = P.op("scalar", lambda h, Fc=Fc, bk=bk: h.activation(
                    out=sgx[bk][:, 0:Fc], in_=ps_g[bk][:, 0:Fc], func=AF.Silu),
                    deps=[mmg, prev.get(f"sgxwar{bk}")], inc=1)
                tmu = P.op("vector", lambda h, Fc=Fc, bk=bk: h.tensor_tensor(
                    hc[:, 0:Fc], ps_u[bk][:, 0:Fc], sgx[bk][:, 0:Fc], ALU.mult),
                    deps=[tsi, mmu, prev.get("hcwar")], inc=1)
                prev[f"ps_g_free{bk}"] = tsi
                prev[f"ps_u_free{bk}"] = tmu
                prev[f"sgxwar{bk}"] = tmu
                q_prev = fp4_quant_chunk(P, hc[:, 0:Fc], Fc, scr, hq[:, 0:Fc], [tmu], [q_prev])
                prev["hcwar"] = q_prev
                nch = Fc // 128
                tts = []
                for j in range(nch):
                    tts.append(P.op("tensor", lambda h, j=j: h.transpose(
                        ps_t[:, j * 128:(j + 1) * 128], hq[:, j * 128:(j + 1) * 128],
                        consts["ident_b"][:, :]),
                        deps=[q_prev, ps_t_f], inc=(1 if j == nch - 1 else 0)))
                tev = P.op("vector", lambda h, oc=oc, nch=nch: h.tensor_copy(
                    hqT[:, oc * 4:oc * 4 + nch, :],
                    ps_t[:, 0:nch * 128].rearrange("p (j t) -> p j t", j=nch)),
                    deps=[tts[-1], prev.get("hqTwar")], inc=1)
                ps_t_f = tev
                hq_toks.append(tev)
            for oc2 in range(2):
                for ci in range(22):
                    dmm = P.op("tensor", lambda h, ci=ci, oc2=oc2: h.matmul(
                        ps_d[:, oc2 * 512:(oc2 + 1) * 512], hqT[:, ci, :],
                        wdn[:, ci, oc2 * 512:(oc2 + 1) * 512],
                        start=(ci == 0), stop=(ci == 21)),
                        deps=[hq_toks[-1], tw3, prev.get("psd")], inc=(1 if ci == 21 and oc2 == 1 else 0))
            prev["hqTwar"] = dmm
            txn = P.op("vector", lambda h: h.tensor_tensor(xn2[:, :], xt[:, :], ps_d[:, :], ALU.add),
                       deps=[dmm, tl, prev.get("xnst")], inc=1)
            prev["psd"] = txn
            prev["xread"] = txn
            prev["ywar"] = qts[-1]
            prev["yqwar"] = tT
            tst = P.dma("sync", P.stream("e_st"), lambda h, i=i: h.dma_start(
                out=xout_t[i, :, :], in_=xn2[:, :]), deps=[txn])
            prev["xnst"] = tst
        P.barrier()
        P.flush()
    return tst


L1FFN_W = [("w_gate", [2816, D]), ("w_up", [2816, D]), ("w_down", [D, 2816])]


def build_program_l1_ffn():
    nc = _mk_prog()
    aps = {}
    aps["xin"] = nc.dram_tensor("xin", [T, D], F32, kind="ExternalInput").ap()
    aps["xout"] = nc.dram_tensor("xout", [T, D], F32, kind="ExternalOutput").ap()
    for nm, shp in L1FFN_W:
        aps[nm] = nc.dram_tensor(nm, shp, F32, kind="ExternalInput").ap()
    aps["wgqT"] = nc.dram_tensor("wgqT", [D, 2816], BF16, kind="Internal").ap()
    aps["wupqT"] = nc.dram_tensor("wupqT", [D, 2816], BF16, kind="Internal").ap()
    aps["wdqT"] = nc.dram_tensor("wdqT", [2816, D], BF16, kind="Internal").ap()
    with contextlib.ExitStack() as es:
        P = Prog(nc, None, es)
        block = es.enter_context(nc.Block())
        P.block = block
        consts = const_setup(P, es)
        build_l1_ffn_prep(P, es, aps, consts)
        build_l1_ffn_call(P, es, aps, consts, tag=0)
        P.finish()
    return nc


def run_sublayer_l1ffn(x_full, wd):
    nc = build_program_l1_ffn()
    in_maps = []
    for c in range(N_CORES):
        m = {"xin": np.ascontiguousarray(x_full[c])}
        m.update(wd)
        in_maps.append(m)
    res = run_bass_kernel_spmd(nc, in_maps, core_ids=list(range(N_CORES)))
    return np.stack([r["xout"] for r in res.results])




def build_l1_attn_b(P, es, aps, consts):
    nc = P.nc
    xout, ynT_d = aps["xout"], aps["ynT"]
    xout_t = xout.rearrange("(n p) d -> n p d", p=128)
    f0_done = phase_ln_scratch(P, es, aps, consts, store_xy=True)

    with contextlib.ExitStack() as ph:
        QT = ph.enter_context(_sb(nc, "f_QT", [128, 4096], BF16))
        KTf = ph.enter_context(_sb(nc, "f_KTf", [128, 4096], BF16))
        KH = [ph.enter_context(_sb(nc, f"f_KH{hd}", [128, 4096], BF16)) for hd in range(8)]
        Vp = ph.enter_context(_sb(nc, "f_Vp", [128, 32, 1032], BF16))
        negB = ph.enter_context(_sb(nc, "f_negB", [128, 8], F32))
        gcol = ph.enter_context(_sb(nc, "f_gcol", [128, 1], F32))
        SEG = ph.enter_context(_sb(nc, "f_SEG", [128, 8], BF16))
        grow, tg1 = load_row(P, "f_gam", aps["gamma"], 1, ph, F32)

        with contextlib.ExitStack() as p0:
            segi = p0.enter_context(_sb(nc, "f_segi", [128, 8], F32))
            segf = p0.enter_context(_sb(nc, "f_segf", [128, 8], F32))
            ps_g0 = p0.enter_context(_ps(nc, "f_ps_g0", [128, 8], F32))
            ti = P.op("gpsimd", lambda h: h.iota(segi[:, :], [[-16, 8]], channel_multiplier=1,
                                                 allow_small_or_imprecise_dtypes=True), inc=1)
            t1 = P.op("vector", lambda h: h.tensor_scalar(segf[:, :], segi[:, :], 0.0, None, ALU.is_ge),
                      deps=[ti], inc=1)
            t2 = P.op("vector", lambda h: h.tensor_scalar(segi[:, :], segi[:, :], 15.0, None, ALU.is_le),
                      deps=[t1], inc=1)
            t3 = P.op("vector", lambda h: h.tensor_tensor(segf[:, :], segf[:, :], segi[:, :], ALU.mult),
                      deps=[t2], inc=1)
            t4 = P.op("vector", lambda h: h.tensor_copy(SEG[:, :], segf[:, :]), deps=[t3], inc=1)
            tm = P.op("tensor", lambda h: h.matmul(ps_g0[:, 0:1], consts["ones_f"][:, 0:128],
                                                   grow[:, :], start=True, stop=True),
                      deps=[tg1] + consts["tok"], inc=1)
            tgc = P.op("vector", lambda h: h.tensor_copy(gcol[:, :], ps_g0[:, 0:1]), deps=[tm], inc=1)
            P.barrier()
            P.flush()

        with contextlib.ExitStack() as pp:
            znt = [pp.enter_context(_sb(nc, f"f_znt{j}", [128, 8, 512], BF16)) for j in range(2)]
            wqv = pp.enter_context(_sb(nc, "f_wq", [128, 8, 128], BF16))
            wkv = pp.enter_context(_sb(nc, "f_wk", [128, 8, 128], BF16))
            wvv = pp.enter_context(_sb(nc, "f_wv", [128, 8, 1024], BF16))
            bqr, tb1 = load_row(P, "f_bq", aps["bq"], 128, pp, BF16)
            bkr, tb2 = load_row(P, "f_bk", aps["bk"], 128, pp, BF16)
            bvr, tb3 = load_row(P, "f_bv", aps["bv"], D, pp, BF16)
            twq = P.dma("gpsimd", P.stream("bulk", "bulk"), lambda h: h.dma_start(
                out=wqv[:, :, :], in_=aps["WqT"].rearrange("(ci p) n -> p ci n", p=128)))
            twk = P.dma("gpsimd", P.stream("bulk", "bulk"), lambda h: h.dma_start(
                out=wkv[:, :, :], in_=aps["WkT"].rearrange("(ci p) n -> p ci n", p=128)))
            twv = P.dma("gpsimd", P.stream("bulk", "bulk"), lambda h: h.dma_start(
                out=wvv[:, :, :], in_=aps["WvT"].rearrange("(ci p) n -> p ci n", p=128)))
            ps_q = pp.enter_context(_ps(nc, "f_ps_q", [128, 512], F32))
            ps_v = pp.enter_context(_ps(nc, "f_ps_v", [128, 512], F32))
            znt_free = {0: None, 1: None}
            ps_q_free = None
            ps_v_free = None
            prod_done = []
            for tt in range(8):
                par = tt % 2
                tsl = slice(tt * 512, (tt + 1) * 512)
                tld = P.dma("gpsimd", P.stream(f"f_zl{par}"), lambda h, tsl=tsl, par=par:
                            h.dma_start(out=znt[par][:, :, :], in_=ynT_d[:, :, tsl]),
                            deps=[znt_free[par], f0_done])
                last_read = None
                for wsb, wtok, brr, btok, dst in ((wqv, twq, bqr, tb1, QT), (wkv, twk, bkr, tb2, KTf)):
                    mm = None
                    for ci in range(8):
                        mm = P.op("tensor", lambda h, ci=ci, par=par, wsb=wsb:
                                  h.matmul(ps_q[:, :], wsb[:, ci, :], znt[par][:, ci, :],
                                           start=(ci == 0), stop=False),
                                  deps=[tld, wtok, ps_q_free], inc=0)
                    mm = P.op("tensor", lambda h, brr=brr: h.matmul(
                        ps_q[:, :], brr[:, :], consts["ones_b"][:, 0:512], start=False, stop=True),
                        deps=[btok], inc=1)
                    tev = P.op("vector", lambda h, tsl=tsl, dst=dst: h.tensor_copy(
                        dst[:, tsl], ps_q[:, :]), deps=[mm], inc=1)
                    ps_q_free = tev
                    prod_done.append(tev)
                    last_read = mm
                for tsub in range(4):
                    kc = tt * 4 + tsub
                    for oc in range(2):
                        mm = None
                        for ci in range(8):
                            mm = P.op("tensor", lambda h, tsub=tsub, ci=ci, par=par, oc=oc:
                                      h.matmul(ps_v[:, :], znt[par][:, ci, tsub * 128:(tsub + 1) * 128],
                                               wvv[:, ci, oc * 512:(oc + 1) * 512],
                                               start=(ci == 0), stop=False),
                                      deps=[tld, twv, ps_v_free], inc=0)
                        mm = P.op("tensor", lambda h, oc=oc: h.matmul(
                            ps_v[:, :], consts["ones_b"][:, 0:128],
                            bvr[:, oc * 512:(oc + 1) * 512], start=False, stop=True),
                            deps=[tb3], inc=1)
                        vap = Vp[:, kc, :]
                        vout = ap3(vap, [vap.ap[0], [129, 4], [1, 128]])
                        vout = bass.AP(vout.tensor, vout.offset + oc * 4 * 129, vout.ap)
                        tev = P.op("vector", lambda h, vout=vout: h.tensor_copy(
                            vout, ps_v[:, :].rearrange("p (e f) -> p e f", e=4)), deps=[mm], inc=1)
                        ps_v_free = tev
                        prod_done.append(tev)
                        last_read = mm
                    vone = ap3(vap, [vap.ap[0], [129, 8], [1, 1]])
                    vone = bass.AP(vone.tensor, vone.offset + 128, vone.ap)
                    tmo = P.op("vector", lambda h, vone=vone: h.memset(vone, 1.0), deps=[mm], inc=1)
                    prod_done.append(tmo)
                znt_free[par] = last_read
            mz = []
            for hd in range(8):
                tz = P.op("gpsimd", lambda h, hd=hd: h.memset(KH[hd][:, :], 0.0),
                          deps=[], inc=1)
                mz.append(tz)
            kh_done = []
            for hd in range(8):
                tcpy = P.dma("gpsimd", P.stream("f_khc"), lambda h, hd=hd: h.dma_start(
                    out=KH[hd][hd * 16:hd * 16 + 16, :], in_=KTf[hd * 16:hd * 16 + 16, :]),
                    deps=[mz[-1], prod_done[-1]])
                kh_done.append(tcpy)
            P.barrier()
            P.flush()

        with contextlib.ExitStack() as pb:
            sq_s = pb.enter_context(_sb(nc, "f_sq_s", [128, 4096], BF16))
            nmx = pb.enter_context(_sb(nc, "f_nmx", [8, 2], F32))
            nrow = pb.enter_context(_sb(nc, "f_nrow", [1, 8], F32))
            ntmp = pb.enter_context(_sb(nc, "f_ntmp", [8, 1], F32))
            ps_sg = pb.enter_context(_ps(nc, "f_ps_sg", [8, 512], F32))
            ps_br = pb.enter_context(_ps(nc, "f_ps_br", [128, 8], F32))
            ps_tr = pb.enter_context(_ps(nc, "f_ps_tr", [1, 8], F32))
            nz = P.op("gpsimd", lambda h: h.memset(nmx[:, :], 0.0), inc=1)
            sq_free = None
            mprev = None
            ntmp_free = None
            for which, src_t in ((0, QT), (1, KTf)):
                tsq = P.op("scalar", lambda h, src_t=src_t: h.activation(
                    out=sq_s[:, :], in_=src_t[:, :], func=AF.Square),
                    deps=[prod_done[-1], sq_free], inc=1)
                acc = nz
                for tc in range(8):
                    mm = P.op("tensor", lambda h, tc=tc: h.matmul(
                        ps_sg[:, :], SEG[:, :], sq_s[:, tc * 512:(tc + 1) * 512],
                        start=True, stop=True),
                        deps=[tsq, mprev], inc=1)
                    tmx = P.op("vector", lambda h: h.reduce_max(ntmp[:, :], ps_sg[:, :], axis=AX.X),
                               deps=[mm, ntmp_free], inc=1)
                    acc = P.op("vector", lambda h, which=which: h.tensor_tensor(
                        nmx[:, which:which + 1], nmx[:, which:which + 1], ntmp[:, :], ALU.max),
                        deps=[tmx, acc], inc=1)
                    ntmp_free = acc
                    mprev = tmx
                sq_free = mm
            tb1_ = P.op("vector", lambda h: h.tensor_tensor(nmx[:, 0:1], nmx[:, 0:1], nmx[:, 1:2],
                                                            ALU.mult), deps=[acc], inc=1)
            tb2_ = P.op("scalar", lambda h: h.activation(out=nmx[:, 0:1], in_=nmx[:, 0:1],
                                                         func=AF.Sqrt), deps=[tb1_], inc=1)
            tb3_ = P.op("vector", lambda h: h.tensor_scalar(nmx[:, 0:1], nmx[:, 0:1], -0.25, None,
                                                            ALU.mult), deps=[tb2_], inc=1)
            tb4_ = P.op("tensor", lambda h: h.transpose(ps_tr[:, :], nmx[:, 0:1],
                                                        consts["ident_f"][0:8, 0:8]),
                        deps=[tb3_] + consts["tok"], inc=1)
            tb5_ = P.op("vector", lambda h: h.tensor_copy(nrow[:, :], ps_tr[:, :]), deps=[tb4_], inc=1)
            tb6_ = P.op("tensor", lambda h: h.matmul(ps_br[:, :], consts["ones_f"][:, 0:128],
                                                     nrow[:, :], start=True, stop=True),
                        deps=[tb5_], inc=1)
            tb7_ = P.op("vector", lambda h: h.tensor_copy(negB[:, :], ps_br[:, :]), deps=[tb6_], inc=1)
            P.barrier()
            P.flush()

        with contextlib.ExitStack() as pa:
            sc = [pa.enter_context(_ps(nc, f"f_sc{j}", [128, 512], F32)) for j in range(2)]
            av = [pa.enter_context(_ps(nc, f"f_av{j}", [128, 129], F32)) for j in range(2)]
            attnT = pa.enter_context(_sb(nc, "f_attnT", [128, 4096], BF16))
            absb = pa.enter_context(_sb(nc, "f_absb", [128, 512], F32))
            usb = pa.enter_context(_sb(nc, "f_usb", [128, 512], F32))
            ao = pa.enter_context(_sb(nc, "f_ao", [128, 1024], F32))
            rcol = pa.enter_context(_sb(nc, "f_rcol", [128, 1], F32))
            ctr = [pa.enter_context(_sb(nc, f"f_ctr{j}", [128, 1024], F32)) for j in range(2)]
            sc_free = {0: None, 1: None}
            av_free = {0: None, 1: None}
            attnT_free = None
            rcol_free = None
            ao_free = None
            abs_free = None
            u_free = None
            ctr_free = {0: None, 1: None}
            for qt in range(NT):
                ao_toks = []
                for hd in range(8):
                    exp_toks = []
                    av_par = hd % 2
                    for grp in range(8):
                        par = grp % 2
                        mm = None
                        for k4 in range(4):
                            kc = grp * 4 + k4
                            mm = P.op("tensor", lambda h, par=par, k4=k4, kc=kc, hd=hd, qt=qt:
                                      h.matmul(sc[par][:, k4 * 128:(k4 + 1) * 128],
                                               KH[hd][:, kc * 128:(kc + 1) * 128],
                                               QT[:, qt * 128:(qt + 1) * 128],
                                               start=True, stop=True),
                                      deps=[sc_free[par], kh_done[-1], prod_done[-1]],
                                      inc=(1 if k4 == 3 else 0))
                        tab = P.op("scalar", lambda h, par=par: h.activation(
                            out=absb[:, :], in_=sc[par][:, :], func=AF.Abs, scale=0.25),
                            deps=[mm, abs_free], inc=1)
                        tu = P.op("vector", lambda h, par=par: h.scalar_tensor_tensor(
                            usb[:, :], absb[:, :], 8.0 / 3.0, sc[par][:, :], ALU.mult, ALU.add),
                            deps=[tab, u_free], inc=1)
                        sc_free[par] = tu
                        abs_free = tu
                        tex = P.op("scalar", lambda h, grp=grp, hd=hd: h.activation(
                            out=attnT[:, grp * 512:(grp + 1) * 512], in_=usb[:, :],
                            func=AF.Exp, bias=negB[:, hd:hd + 1], scale=0.15),
                            deps=[tu, tb7_, attnT_free], inc=1)
                        u_free = tex
                        exp_toks.append(tex)
                    mm = None
                    for kc in range(32):
                        mm = P.op("tensor", lambda h, kc=kc, hd=hd, av_par=av_par:
                                  h.matmul(av[av_par][:, 0:129],
                                           attnT[:, kc * 128:(kc + 1) * 128],
                                           Vp[:, kc, hd * 129:(hd + 1) * 129],
                                           start=(kc == 0), stop=(kc == 31)),
                                  deps=[exp_toks[-1], av_free[av_par], prod_done[-1]],
                                  inc=(1 if kc == 31 else 0))
                    attnT_free = mm
                    trc = P.op("vector", lambda h, av_par=av_par: h.reciprocal(
                        rcol[:, :], av[av_par][:, 128:129]), deps=[mm, rcol_free], inc=1)
                    tsc = P.op("vector", lambda h, hd=hd, av_par=av_par: h.tensor_scalar(
                        ao[:, hd * 128:(hd + 1) * 128], av[av_par][:, 0:128], rcol[:, 0:1],
                        None, ALU.mult),
                        deps=[trc, ao_free], inc=1)
                    rcol_free = tsc
                    av_free[av_par] = tsc
                    ao_toks.append(tsc)
                cpar = qt % 2
                tev = P.op("vector", lambda h, cpar=cpar: h.tensor_scalar(
                    ctr[cpar][:, :], ao[:, :], gcol[:, 0:1], None, ALU.mult),
                    deps=[ao_toks[-1], tgc, ctr_free[cpar]], inc=1)
                ao_free = tev
                tac = P.dma("gpsimd", P.stream(f"f_ac{cpar}"), lambda h, qt=qt, cpar=cpar:
                            h.dma_start(out=xout_t[qt, :, :], in_=ctr[cpar][:, :], accum_op=ALU.add),
                            deps=[tev, f0_done])
                ctr_free[cpar] = tac
                acc_last = tac
        P.barrier()
        P.flush()
    return acc_last


L1AB_W = [("WqT", [D, 128]), ("WkT", [D, 128]), ("WvT", [D, D]),
          ("bq", [128]), ("bk", [128]), ("bv", [D]), ("gamma", [1])]


def build_program_l1_attn_b():
    nc = _mk_prog()
    aps = {}
    aps["xin"] = nc.dram_tensor("xin", [T, D], F32, kind="ExternalInput").ap()
    aps["xout"] = nc.dram_tensor("xout", [T, D], F32, kind="ExternalOutput").ap()
    aps["ynT"] = nc.dram_tensor("ynT_scr", [128, 8, T], BF16, kind="Internal").ap()
    for nm, shp in L1AB_W:
        aps[nm] = nc.dram_tensor(nm, shp, F32, kind="ExternalInput").ap()
    with contextlib.ExitStack() as es:
        P = Prog(nc, None, es)
        block = es.enter_context(nc.Block())
        P.block = block
        consts = const_setup(P, es)
        build_l1_attn_b(P, es, aps, consts)
        P.finish()
    return nc


def run_sublayer_l1ab(x_full, wd):
    nc = build_program_l1_attn_b()
    in_maps = []
    for c in range(N_CORES):
        m = {"xin": np.ascontiguousarray(x_full[c])}
        m.update(wd)
        in_maps.append(m)
    res = run_bass_kernel_spmd(nc, in_maps, core_ids=list(range(N_CORES)))
    return np.stack([r["xout"] for r in res.results])




_PROG_CACHE = {}


def _prog(name, builder):
    if name not in _PROG_CACHE:
        _PROG_CACHE[name] = builder()
    return _PROG_CACHE[name]


def _launch(nc, x_full, wd):
    in_maps = []
    for c in range(N_CORES):
        m = {"xin": np.ascontiguousarray(x_full[c])}
        m.update(wd)
        in_maps.append(m)
    res = run_bass_kernel_spmd(nc, in_maps, core_ids=list(range(N_CORES)))
    return np.stack([r["xout"] for r in res.results])


def kernel(**ins):
    x = np.ascontiguousarray(ins["L0_attn_a_x"])
    x = x.reshape(8, T, D)

    wd_l0aa = {nm: ins["L0_attn_a_" + nm] for nm, _ in L0AA_W}
    wd_l0ffn = prep_l0ffn_weights(ins)
    wd_l0ab = {"ln_w": ins["L0_attn_b_ln_w"], "ln_b": ins["L0_attn_b_ln_b"],
               "W_qkv": ins["L0_attn_b_W_qkv"], "b_qkv": ins["L0_attn_b_b_qkv"],
               "W_proj": ins["L0_attn_b_W_proj"], "b_proj": ins["L0_attn_b_b_proj"]}
    wd_l1aa = {"WqT": np.ascontiguousarray(ins["L1_attn_a_Wq"].T),
               "WkT": np.ascontiguousarray(ins["L1_attn_a_Wk"].T),
               "WvT": np.ascontiguousarray(ins["L1_attn_a_Wv"].T),
               "WoT": np.ascontiguousarray(ins["L1_attn_a_Wo"].T),
               "bq": ins["L1_attn_a_bq"], "bk": ins["L1_attn_a_bk"],
               "bv": ins["L1_attn_a_bv"], "bo": ins["L1_attn_a_bo"],
               "G": ins["L1_attn_a_G"]}
    wd_l1ffn = {"w_gate": ins["L1_ffn_w_gate"], "w_up": ins["L1_ffn_w_up"],
                "w_down": ins["L1_ffn_w_down"]}
    wd_l1ab = {"WqT": np.ascontiguousarray(ins["L1_attn_b_Wq"].T),
               "WkT": np.ascontiguousarray(ins["L1_attn_b_Wk"].T),
               "WvT": np.ascontiguousarray(ins["L1_attn_b_Wv"].T),
               "bq": ins["L1_attn_b_bq"], "bk": ins["L1_attn_b_bk"],
               "bv": ins["L1_attn_b_bv"], "gamma": ins["L1_attn_b_gamma"]}

    x = _launch(_prog("l0aa", build_program_l0_attn_a), x, wd_l0aa)
    x = _launch(_prog("l0ffn", build_program_l0_ffn), x, wd_l0ffn)
    x = _launch(_prog("l0ab", build_program_l0_attn_b), x, wd_l0ab)
    x = _launch(_prog("l0ffn", build_program_l0_ffn), x, wd_l0ffn)
    x = _launch(_prog("l1aa", build_program_l1_attn_a), x, wd_l1aa)
    x = _launch(_prog("l1ffn", build_program_l1_ffn), x, wd_l1ffn)
    x = _launch(_prog("l1ab", build_program_l1_attn_b), x, wd_l1ab)
    x = _launch(_prog("l1ffn", build_program_l1_ffn), x, wd_l1ffn)
    return x




FUSED = True


def build_program_fused():
    nc = _mk_prog()
    X = nc.dram_tensor("xin", [T, D], F32, kind="ExternalInput").ap()
    XOUT = nc.dram_tensor("xout", [T, D], F32, kind="ExternalOutput").ap()
    B0 = nc.dram_tensor("xb0", [T, D], F32, kind="Internal").ap()
    B1 = nc.dram_tensor("xb1", [T, D], F32, kind="Internal").ap()
    x2buf = nc.dram_tensor("x2buf", [T, D], F32, kind="Internal").ap()
    znT = nc.dram_tensor("znT_scr", [128, 8, T], BF16, kind="Internal").ap()
    wgqT = nc.dram_tensor("wgqT", [D, 2816], BF16, kind="Internal").ap()
    wupqT = nc.dram_tensor("wupqT", [D, 2816], BF16, kind="Internal").ap()
    wdqT = nc.dram_tensor("wdqT", [2816, D], BF16, kind="Internal").ap()

    def w_aps(prefix, spec):
        d = {}
        for nm, shp in spec:
            d[nm] = nc.dram_tensor(prefix + nm, shp, F32, kind="ExternalInput").ap()
        return d

    a_l0aa = w_aps("l0aa_", L0AA_W)
    a_l0ffn = w_aps("l0ffn_", L0FFN_W)
    a_l0ab = w_aps("l0ab_", L0AB_W)
    a_l1aa = w_aps("l1aa_", L1AA_W)
    a_l1ffn = w_aps("l1ffn_", L1FFN_W)
    a_l1ab = w_aps("l1ab_", L1AB_W)

    with contextlib.ExitStack() as es:
        P = Prog(nc, None, es)
        block = es.enter_context(nc.Block())
        P.block = block
        consts = const_setup(P, es)
        build_l1_ffn_prep(P, es, dict(a_l1ffn, wgqT=wgqT, wupqT=wupqT, wdqT=wdqT), consts)
        build_l0_attn_a(P, es, dict(a_l0aa, xin=X, x2buf=x2buf, xout=B0), consts)
        build_l0_ffn(P, es, dict(a_l0ffn, xin=B0, xout=B1), consts)
        build_l0_attn_b(P, es, dict(a_l0ab, xin=B1, xout=B0, znT=znT), consts)
        build_l0_ffn(P, es, dict(a_l0ffn, xin=B0, xout=B1), consts)
        build_l1_attn_a(P, es, dict(a_l1aa, xin=B1, xout=B0, ynT=znT), consts)
        build_l1_ffn_call(P, es, dict(a_l1ffn, xin=B0, xout=B1, wgqT=wgqT, wupqT=wupqT,
                                      wdqT=wdqT), consts, tag=0)
        build_l1_attn_b(P, es, dict(a_l1ab, xin=B1, xout=B0, ynT=znT), consts)
        build_l1_ffn_call(P, es, dict(a_l1ffn, xin=B0, xout=XOUT, wgqT=wgqT, wupqT=wupqT,
                                      wdqT=wdqT), consts, tag=1)
        P.finish()
    return nc


def kernel_fused(**ins):
    x = np.ascontiguousarray(ins["L0_attn_a_x"]).reshape(8, T, D)
    wd = {}
    for nm, _ in L0AA_W:
        wd["l0aa_" + nm] = ins["L0_attn_a_" + nm]
    for nm, v in prep_l0ffn_weights(ins).items():
        wd["l0ffn_" + nm] = v
    wd.update({"l0ab_ln_w": ins["L0_attn_b_ln_w"], "l0ab_ln_b": ins["L0_attn_b_ln_b"],
               "l0ab_W_qkv": ins["L0_attn_b_W_qkv"], "l0ab_b_qkv": ins["L0_attn_b_b_qkv"],
               "l0ab_W_proj": ins["L0_attn_b_W_proj"], "l0ab_b_proj": ins["L0_attn_b_b_proj"]})
    wd.update({"l1aa_WqT": np.ascontiguousarray(ins["L1_attn_a_Wq"].T),
               "l1aa_WkT": np.ascontiguousarray(ins["L1_attn_a_Wk"].T),
               "l1aa_WvT": np.ascontiguousarray(ins["L1_attn_a_Wv"].T),
               "l1aa_WoT": np.ascontiguousarray(ins["L1_attn_a_Wo"].T),
               "l1aa_bq": ins["L1_attn_a_bq"], "l1aa_bk": ins["L1_attn_a_bk"],
               "l1aa_bv": ins["L1_attn_a_bv"], "l1aa_bo": ins["L1_attn_a_bo"],
               "l1aa_G": ins["L1_attn_a_G"]})
    wd.update({"l1ffn_w_gate": ins["L1_ffn_w_gate"], "l1ffn_w_up": ins["L1_ffn_w_up"],
               "l1ffn_w_down": ins["L1_ffn_w_down"]})
    wd.update({"l1ab_WqT": np.ascontiguousarray(ins["L1_attn_b_Wq"].T),
               "l1ab_WkT": np.ascontiguousarray(ins["L1_attn_b_Wk"].T),
               "l1ab_WvT": np.ascontiguousarray(ins["L1_attn_b_Wv"].T),
               "l1ab_bq": ins["L1_attn_b_bq"], "l1ab_bk": ins["L1_attn_b_bk"],
               "l1ab_bv": ins["L1_attn_b_bv"], "l1ab_gamma": ins["L1_attn_b_gamma"]})
    nc = _prog("fused", build_program_fused)
    in_maps = []
    for c in range(N_CORES):
        m = {"xin": np.ascontiguousarray(x[c])}
        m.update(wd)
        in_maps.append(m)
    res = run_bass_kernel_spmd(nc, in_maps, core_ids=list(range(N_CORES)))
    return np.stack([r["xout"] for r in res.results])


_kernel_unfused = kernel


def kernel(**ins):
    if FUSED:
        return kernel_fused(**ins)
    return _kernel_unfused(**ins)


# version 06
# speedup vs baseline: 1.0040x; 1.0040x over previous
import contextlib
import math
import numpy as np

import concourse.bass as bass
import concourse.mybir as mybir
from concourse.bass_utils import run_bass_kernel_spmd

F32 = mybir.dt.float32
BF16 = mybir.dt.bfloat16
AF = mybir.ActivationFunctionType
ALU = mybir.AluOpType
AX = mybir.AxisListType

D = 1024
T = 4096
NT = 32
N_CORES = 8

ENGS = ("tensor", "vector", "scalar", "gpsimd", "sync")
DEBUG_L0AB_HGS = 2
DEBUG_L0AB_NQT = NT
DEBUG_L0AB_SKIP = set()
DEBUG_L0AB_BSTEP = 9
DEBUG_L0AB_TCN = 8
DEBUG_L0AB_NOCOPY = False

_NAME_N = [0]


def _sb(nc, name, shape, dtype):
    _NAME_N[0] += 1
    return nc.sbuf_tensor(f"{name}_{_NAME_N[0]}", shape, dtype)


def _ps(nc, name, shape, dtype):
    _NAME_N[0] += 1
    return nc.psum_tensor(f"{name}_{_NAME_N[0]}", shape, dtype)



class Stream:

    def __init__(self, sem, name, mode):
        self.sem = sem
        self.name = name
        self.mode = mode
        self.cnt = 0
        self.waited_epoch = False


class Prog:

    def __init__(self, nc, block, es):
        self.nc = nc
        self.block = block
        self.es = es
        self.pend = {e: [] for e in ENGS}
        self.cnt = {e: 0 for e in ENGS}
        self.waited = {e: {} for e in ENGS}
        self.sem = {e: es.enter_context(nc.semaphore(f"sem_{e}")) for e in ENGS}
        self.streams = {}

    def stream(self, name, mode="exact"):
        s = self.streams.get(name)
        if s is None:
            s = Stream(self.es.enter_context(self.nc.semaphore(f"st_{name}")), name, mode)
            self.streams[name] = s
        assert s.mode == mode, name
        return s

    def _emit_waits(self, eng, deps):
        for d in deps:
            if d is None:
                continue
            if isinstance(d[0], Stream):
                s, val = d
                if s.mode == "exact":
                    assert val == s.cnt, f"stream {s.name}: wait {val} != latest {s.cnt}"
                else:
                    val = s.cnt
                s.waited_epoch = True
                w = self.waited[eng]
                key = ("S", s.name)
                if w.get(key, 0) < val:
                    w[key] = val
                    self.pend[eng].append(lambda h, ss=s.sem, v=val: h.wait_ge(ss, v))
            else:
                pe, val = d
                if val <= 0:
                    continue
                if pe == eng and val > self.cnt[eng]:
                    raise AssertionError(f"same-engine dep on future op {pe} {val} > {self.cnt[eng]}")
                w = self.waited[eng]
                if w.get(pe, 0) < val:
                    w[pe] = val
                    sem = self.sem[pe]
                    self.pend[eng].append(lambda h, s=sem, v=val: h.wait_ge(s, v))

    def op(self, eng, fn, deps=(), inc=0):
        self._emit_waits(eng, deps)
        if inc:
            self.cnt[eng] += inc
            sem = self.sem[eng]
            self.pend[eng].append(lambda h, f=fn, s=sem, a=inc: f(h).then_inc(s, a))
            return (eng, self.cnt[eng])
        else:
            self.pend[eng].append(fn)
            return (eng, self.cnt[eng])

    def dma(self, eng, stream, fn, deps=()):
        if stream.mode == "bulk" and stream.waited_epoch:
            raise AssertionError(f"bulk stream {stream.name}: DMA after wait (need barrier)")
        self._emit_waits(eng, deps)
        stream.cnt += 16
        self.pend[eng].append(lambda h, f=fn, s=stream.sem: f(h).then_inc(s, 16))
        return (stream, stream.cnt)

    def tok(self, eng):
        return (eng, self.cnt[eng])

    def barrier(self):
        toks = [(e, self.cnt[e]) for e in ENGS if self.cnt[e] > 0]
        toks += [(s, s.cnt) for s in self.streams.values() if s.cnt > 0]
        for e in ENGS:
            self._save = e
            for d in toks:
                if not isinstance(d[0], Stream) and d[0] == e:
                    continue
                self._emit_waits(e, [d])
        for s in self.streams.values():
            s.waited_epoch = False

    def flush(self):
        for e in ENGS:
            q = self.pend[e]
            if not q:
                continue
            self.pend[e] = []
            body = getattr(self.block, e)

            def run(h, q=q):
                for f in q:
                    f(h)
            body(run)

    def finish(self):
        self.barrier()
        self.flush()




def ap3(base_ap, dims):
    return bass.AP(base_ap.tensor, base_ap.offset, dims)


def rearr(ap, spec, **kw):
    return ap.rearrange(spec, **kw)


class Phase:

    def __init__(self, P, es):
        self.P = P
        self.es = es


def const_setup(P, es):
    nc = P.nc
    c = {}
    c["ones_b"] = es.enter_context(_sb(nc, "c_ones_b", [1, 512], BF16))
    c["ones_f"] = es.enter_context(_sb(nc, "c_ones_f", [1, 512], F32))
    c["onescol_b"] = es.enter_context(_sb(nc, "c_onescol_b", [128, 1], BF16))
    c["onescol_f"] = es.enter_context(_sb(nc, "c_onescol_f", [128, 1], F32))
    c["ident_b"] = es.enter_context(_sb(nc, "c_ident_b", [128, 128], BF16))
    c["ident_f"] = es.enter_context(_sb(nc, "c_ident_f", [128, 128], F32))
    io = es.enter_context(_sb(nc, "c_iota_fp", [128, 128], F32))
    t1 = P.op("gpsimd", lambda h: h.memset(c["ones_b"][:, :], 1.0), inc=1)
    t2 = P.op("gpsimd", lambda h: h.memset(c["ones_f"][:, :], 1.0), inc=1)
    t2b = P.op("gpsimd", lambda h: h.memset(c["onescol_b"][:, :], 1.0), inc=1)
    t2c = P.op("gpsimd", lambda h: h.memset(c["onescol_f"][:, :], 1.0), inc=1)
    t3 = P.op("gpsimd", lambda h: h.iota(io[:, :], [[1, 128]], channel_multiplier=-1,
                                         allow_small_or_imprecise_dtypes=True), inc=1)
    t4 = P.op("vector", lambda h: h.tensor_scalar(c["ident_f"][:, :], io[:, :], 0.0, None, ALU.is_equal),
              deps=[t3], inc=1)
    t5 = P.op("vector", lambda h: h.tensor_copy(c["ident_b"][:, :], c["ident_f"][:, :]),
              deps=[t4], inc=1)
    c["iota_fp"] = io
    c["tok"] = [t1, t2, t2b, t2c, t4, t5]
    return c


def load_w_bf16(P, name, dram_ap, shape_sb, es, rearrange=None, stream=None):
    t = es.enter_context(_sb(P.nc, name, shape_sb, BF16))
    src = dram_ap if rearrange is None else dram_ap.rearrange(*rearrange[0], **rearrange[1])
    tok = P.dma("gpsimd", stream if stream is not None else P.stream("bulk", "bulk"),
                lambda h: h.dma_start(out=t[...], in_=src))
    return t, tok


def load_row(P, name, dram_vec_ap, n, es, dtype=F32, stream=None):
    t = es.enter_context(_sb(P.nc, name, [1, n], dtype))
    src = dram_vec_ap.rearrange("(a n) -> a n", a=1)
    eng = "gpsimd" if dtype != F32 else "sync"
    default = "bulk" if eng == "gpsimd" else "bulks"
    tok = P.dma(eng, stream if stream is not None else P.stream(default, "bulk"),
                lambda h: h.dma_start(out=t[:, :], in_=src))
    return t, tok


def bcast_row_f32(P, row_ap, n, psum, out_sb, ones_f, deps):
    nchunks = (n + 511) // 512
    mts = []
    for j in range(nchunks):
        lo, hi = j * 512, min(n, (j + 1) * 512)
        mts.append(P.op("tensor",
                        lambda h, lo=lo, hi=hi: h.matmul(psum[:, lo:hi], ones_f[:, 0:128],
                                                         row_ap[:, lo:hi], start=True, stop=True),
                        deps=deps, inc=(1 if hi == n else 0)))
    tok = P.op("vector", lambda h: h.tensor_copy(out_sb[:, 0:n], psum[:, 0:n]), deps=[mts[-1]], inc=1)
    return tok


def ln_tile(P, x_ap, F, scr, eps, out_ap, out_tok_extra_deps=(), scale_row=None, bias_row=None,
            x_deps=(), out_dtype_pass=None):
    s1, s2, s3 = scr["s1"], scr["s2"], scr["s3"]
    t1 = P.op("vector", lambda h: h.tensor_scalar(out_ap, x_ap, 1.0, 0.0, ALU.mult, ALU.add,
                                                  accum_out=s1[:, :]),
              deps=list(x_deps) + list(out_tok_extra_deps), inc=1)
    t2 = P.op("vector", lambda h: h.tensor_scalar(s1[:, :], s1[:, :], -1.0 / F, None, ALU.mult),
              deps=[t1], inc=1)
    t3 = P.op("scalar", lambda h: h.activation(out=out_ap, in_=x_ap, func=AF.Square,
                                               bias=s1[:, 0:1], scale=1.0, accum_out=s2[:, :]),
              deps=[t2] + list(x_deps), inc=1)
    t4 = P.op("vector", lambda h: h.tensor_scalar(s2[:, :], s2[:, :], 1.0 / F, eps, ALU.mult, ALU.add),
              deps=[t3], inc=1)
    t5 = P.op("scalar", lambda h: h.activation(out=s2[:, :], in_=s2[:, :], func=AF.Sqrt),
              deps=[t4], inc=1)
    t6 = P.op("vector", lambda h: h.reciprocal(s2[:, :], s2[:, :]), deps=[t5], inc=1)
    t7 = P.op("vector", lambda h: h.tensor_mul(s3[:, :], s1[:, :], s2[:, :]), deps=[t6, t2], inc=1)
    t8 = P.op("scalar", lambda h: h.activation(out=out_ap, in_=x_ap, func=AF.Identity,
                                               bias=s3[:, 0:1], scale=s2[:, 0:1]),
              deps=[t7] + list(x_deps) + list(out_tok_extra_deps), inc=1)
    return t8


def transpose_tile(P, src_bf16_ap_fn, ps_t, out_sb, nchunks, ident_b, deps, psum_free_tok=None,
                   out_ap=None):
    tts = []
    dd = list(deps) + ([psum_free_tok] if psum_free_tok else [])
    for j in range(nchunks):
        tts.append(P.op("tensor",
                        lambda h, j=j: h.transpose(ps_t[:, j * 128:(j + 1) * 128],
                                                   src_bf16_ap_fn(j), ident_b[:, :]),
                        deps=dd, inc=(1 if j == nchunks - 1 else 0)))
    if out_ap is not None:
        ev = P.op("vector", lambda h: h.tensor_copy(
            out_ap, ps_t[:, 0:nchunks * 128].rearrange("p (j t) -> p j t", j=nchunks)),
            deps=[tts[-1]], inc=1)
    else:
        ev = P.op("vector", lambda h: h.tensor_copy(out_sb[:, 0:nchunks * 128], ps_t[:, 0:nchunks * 128]),
                  deps=[tts[-1]], inc=1)
    return ev




def build_l0_attn_a(P, es, aps, consts):
    nc = P.nc
    xin, x2buf, xout = aps["xin"], aps["x2buf"], aps["xout"]

    with contextlib.ExitStack() as ph:
        wq, tw1 = load_w_bf16(P, "wq", aps["Wq"], [128, 8, 1024], ph,
                              rearrange=(("(ci p) o -> p ci o",), dict(p=128)))
        wk, tw2 = load_w_bf16(P, "wk", aps["Wk"], [128, 8, 1024], ph,
                              rearrange=(("(ci p) o -> p ci o",), dict(p=128)))
        wv, tw3 = load_w_bf16(P, "wv", aps["Wv"], [128, 8, 1024], ph,
                              rearrange=(("(ci p) o -> p ci o",), dict(p=128)))
        wg, tw4 = load_w_bf16(P, "wg", aps["Wg"], [128, 8, 1024], ph,
                              rearrange=(("(ci p) o -> p ci o",), dict(p=128)))
        wo, tw5 = load_w_bf16(P, "wo", aps["Wo"], [128, 8, 1024], ph,
                              rearrange=(("(ci p) o -> p ci o",), dict(p=128)))
        wtoks = [tw1, tw2, tw3, tw4, tw5]
        rows = {}
        rtoks = []
        for nm in ("bq", "bk", "bv", "bg", "bo"):
            rows[nm], tk = load_row(P, "r_" + nm, aps[nm], D, ph, BF16)
            rtoks.append(tk)
        for nm in ("ln1_g", "ln1_b"):
            rows[nm], tk = load_row(P, "r_" + nm, aps[nm], D, ph, F32)
            rtoks.append(tk)
        dlrow, tdl = load_row(P, "r_dl", aps["decay_logit"], 16, ph, F32)
        rtoks.append(tdl)

        ident_b = consts["ident_b"]
        ones_b = consts["ones_b"]
        ones_f = consts["ones_f"]
        iota_fp = consts["iota_fp"]

        ps_a = ph.enter_context(_ps(nc, "ps_a", [128, 1024], F32))
        ps_b = ph.enter_context(_ps(nc, "ps_b", [128, 1024], F32))
        ps_c = ph.enter_context(_ps(nc, "ps_c", [128, 1024], F32))
        ps_t = ph.enter_context(_ps(nc, "ps_t", [128, 1024], BF16))

        drow = ph.enter_context(_sb(nc, "drow", [1, 16], F32))
        lndr = ph.enter_context(_sb(nc, "lndr", [1, 16], F32))
        lndc = ph.enter_context(_sb(nc, "lndc", [1, 1024], F32))
        lndcol = ph.enter_context(_sb(nc, "lndcol", [128, 16], F32))
        Lh = ph.enter_context(_sb(nc, "Lh", [128, 16 * 128], BF16))
        Lcol = ph.enter_context(_sb(nc, "Lcol", [128, 16], BF16))
        Ppow = ph.enter_context(_sb(nc, "Ppow", [128, 1024], F32))
        d128 = ph.enter_context(_sb(nc, "d128", [1, 1024], F32))
        mask_lt = ph.enter_context(_sb(nc, "mask_lt", [128, 128], F32))
        scr128 = ph.enter_context(_sb(nc, "scr128", [128, 128], F32))
        r1row = ph.enter_context(_sb(nc, "r1row", [1, 128], F32))
        r2row = ph.enter_context(_sb(nc, "r2row", [1, 128], F32))

        t = P.op("scalar", lambda h: h.activation(out=drow[:, :], in_=dlrow[:, :], func=AF.Sigmoid),
                 deps=[tdl], inc=1)
        t = P.op("scalar", lambda h: h.activation(out=lndr[:, :], in_=drow[:, :], func=AF.Ln),
                 deps=[t], inc=1)
        tln = t
        src = ap3(lndr[:, :], [lndr[:, :].ap[0], [1, 16], [0, 64]])
        tlc = P.op("vector", lambda h: h.tensor_copy(rearr(lndc[:, :], "a (h e) -> a h e", h=16), src),
                   deps=[tln], inc=1)
        tm = P.op("tensor", lambda h: h.matmul(ps_a[:, 0:16], ones_f[:, 0:128], lndr[:, :],
                                               start=True, stop=True), deps=[tln] + consts["tok"], inc=1)
        tlcol = P.op("vector", lambda h: h.tensor_copy(lndcol[:, :], ps_a[:, 0:16]), deps=[tm], inc=1)
        tmask = P.op("vector", lambda h: h.tensor_scalar(mask_lt[:, :], iota_fp[:, :], 0.0, None, ALU.is_ge),
                     deps=consts["tok"], inc=1)
        tio1 = P.op("gpsimd", lambda h: h.iota(r1row[:, :], [[1, 128]], channel_multiplier=0,
                                               allow_small_or_imprecise_dtypes=True), inc=1)
        tio1b = P.op("vector", lambda h: h.tensor_scalar(r1row[:, :], r1row[:, :], 1.0, None, ALU.add),
                     deps=[tio1], inc=1)
        tio2 = P.op("gpsimd", lambda h: h.iota(r2row[:, :], [[-1, 128]], channel_multiplier=0,
                                               allow_small_or_imprecise_dtypes=True), inc=1)
        tio2b = P.op("vector", lambda h: h.tensor_scalar(r2row[:, :], r2row[:, :], 127.0, None, ALU.add),
                     deps=[tio2], inc=1)
        ltoks = []
        for hd in range(16):
            t1 = P.op("vector", lambda h, hd=hd: h.tensor_scalar(scr128[:, :], iota_fp[:, :],
                                                                 lndcol[:, hd:hd + 1], None, ALU.mult),
                      deps=[tlcol, tmask] + (ltoks[-1:] if ltoks else []), inc=1)
            t2 = P.op("vector", lambda h: h.tensor_scalar_min(scr128[:, :], scr128[:, :], 0.0),
                      deps=[t1], inc=1)
            t3 = P.op("scalar", lambda h: h.activation(out=scr128[:, :], in_=scr128[:, :], func=AF.Exp),
                      deps=[t2], inc=1)
            t4 = P.op("vector", lambda h, hd=hd: h.tensor_tensor(Lh[:, hd * 128:(hd + 1) * 128],
                                                                 scr128[:, :], mask_lt[:, :], ALU.mult),
                      deps=[t3], inc=1)
            ltoks.append(t4)
        tp1 = P.op("tensor", lambda h: h.matmul(ps_b[:, 0:512], r1row[:, :], lndc[:, 0:512],
                                                start=True, stop=True), deps=[tio1b, tlc], inc=0)
        tp2 = P.op("tensor", lambda h: h.matmul(ps_b[:, 512:1024], r1row[:, :], lndc[:, 512:1024],
                                                start=True, stop=True), deps=[tio1b, tlc], inc=1)
        tp3 = P.op("scalar", lambda h: h.activation(out=Ppow[:, :], in_=ps_b[:, :], func=AF.Exp),
                   deps=[tp2], inc=1)
        tq1 = P.op("tensor", lambda h: h.matmul(ps_a[:, 16:32], r2row[:, :], lndr[:, :],
                                                start=True, stop=True), deps=[tio2b, tln, tlcol], inc=1)
        tq2 = P.op("scalar", lambda h: h.activation(out=Lcol[:, :], in_=ps_a[:, 16:32], func=AF.Exp),
                   deps=[tq1], inc=1)
        td1 = P.op("vector", lambda h: h.tensor_scalar(d128[:, :], lndc[:, :], 128.0, None, ALU.mult),
                   deps=[tlc], inc=1)
        td2 = P.op("scalar", lambda h: h.activation(out=d128[:, :], in_=d128[:, :], func=AF.Exp),
                   deps=[td1], inc=1)

        g1b = ph.enter_context(_sb(nc, "g1b", [128, 1024], F32))
        b1b = ph.enter_context(_sb(nc, "b1b", [128, 1024], F32))
        tg = bcast_row_f32(P, rows["ln1_g"][:, :], D, ps_c, g1b, ones_f, [rtoks[5], tq1])
        tb = bcast_row_f32(P, rows["ln1_b"][:, :], D, ps_c, b1b, ones_f, [rtoks[6], tg])

        xt = ph.enter_context(_sb(nc, "xt", [128, 1024], F32))
        y_sb = ph.enter_context(_sb(nc, "y_sb", [128, 1024], F32))
        y1 = ph.enter_context(_sb(nc, "y1", [128, 1024], F32))
        y1b_ = ph.enter_context(_sb(nc, "y1b", [128, 1024], BF16))
        yT = ph.enter_context(_sb(nc, "yT", [128, 1024], BF16))
        u_sb = ph.enter_context(_sb(nc, "u_sb", [128, 1024], BF16))
        k_sb = ph.enter_context(_sb(nc, "k_sb", [128, 1024], F32))
        qg_sb = ph.enter_context(_sb(nc, "qg_sb", [128, 1024], F32))
        sg_sb = ph.enter_context(_sb(nc, "sg_sb", [128, 1024], F32))
        st0tmp = ph.enter_context(_sb(nc, "st0tmp", [128, 1024], F32))
        state_sb = ph.enter_context(_sb(nc, "state_sb", [128, 1024], F32))
        o_sb = ph.enter_context(_sb(nc, "o_sb", [128, 1024], BF16))
        oT = ph.enter_context(_sb(nc, "oT", [128, 1024], BF16))
        x2_sb = ph.enter_context(_sb(nc, "x2_sb", [128, 1024], F32))
        st0 = ph.enter_context(_sb(nc, "st0", [1, 2, 1024], F32))
        st0t = ph.enter_context(_sb(nc, "st0t", [1, 1024], F32))
        scr = dict(
            junk=ph.enter_context(_sb(nc, "lnjunk", [128, 1024], F32)),
            s1=ph.enter_context(_sb(nc, "lns1", [128, 1], F32)),
            s2=ph.enter_context(_sb(nc, "lns2", [128, 1], F32)),
            s3=ph.enter_context(_sb(nc, "lns3", [128, 1], F32)),
            sq=ph.enter_context(_sb(nc, "lnsq", [128, 1024], F32)),
        )

        tst0 = P.op("gpsimd", lambda h: h.memset(st0[:, 0, :], 0.0), inc=1)
        P.barrier()

        xin_t = xin.rearrange("(n p) d -> n p d", p=128)
        x2_t = x2buf.rearrange("(n p) d -> n p d", p=128)

        prev = dict(x=None, y=tg, y1=tb, y1b=None, yT=None, u=None, k=None, qg=None,
                    sg=None, st=tst0, o=None, oT=None, x2=None, state=None, stt=None)
        st0_tok = tst0
        ps_a_free = tq2
        ps_b_free = tp3
        ps_c_free = tb
        ps_t_free = (P.tok("tensor"))

        for i in range(NT):
            par = i % 2
            tload = P.dma("sync", P.stream("a_x"), lambda h, i=i: h.dma_start(
                out=xt[:, :], in_=xin_t[i, :, :]), deps=[prev["y"], prev["y1"]])
            ty = ln_tile(P, xt[:, :], D, scr, 1e-5, y_sb[:, :],
                         x_deps=[tload], out_tok_extra_deps=[prev["x2"]])
            ty1a = ln_tile(P, y_sb[:, :], D, scr, 1e-5, y1[:, :], x_deps=[ty],
                           out_tok_extra_deps=[prev["y1b"]])
            ty1g = P.op("vector", lambda h: h.tensor_tensor(y1[:, :], y1[:, :], g1b[:, :], ALU.mult),
                        deps=[ty1a, tg], inc=1)
            ty1 = P.op("vector", lambda h: h.tensor_tensor(y1[:, :], y1[:, :], b1b[:, :], ALU.add),
                       deps=[ty1g, tb], inc=1)
            tyb = P.op("vector", lambda h: h.tensor_copy(y1b_[:, :], y1[:, :]),
                       deps=[ty1, prev["yT"]], inc=1)
            tyT = transpose_tile(P, lambda j: y1b_[:, j * 128:(j + 1) * 128], ps_t, yT, 8,
                                 ident_b, [tyb], psum_free_tok=ps_t_free)
            mm = None
            for w_sb, wtok, brow, half in ((wk, wtoks[1], rows["bk"], 0), (wv, wtoks[2], rows["bv"], 1)):
                for oc in range(2):
                    sl = slice(half * 0 + oc * 512, oc * 512 + 512)
                    dst = ps_a if half == 0 else ps_b
                    for ci in range(8):
                        mm = P.op("tensor", lambda h, dst=dst, sl=sl, ci=ci, w_sb=w_sb:
                                  h.matmul(dst[:, sl], yT[:, ci * 128:(ci + 1) * 128],
                                           w_sb[:, ci, sl], start=(ci == 0), stop=False),
                                  deps=[tyT, wtok, ps_a_free if half == 0 else ps_b_free], inc=0)
                    mm = P.op("tensor", lambda h, dst=dst, sl=sl, brow=brow:
                              h.matmul(dst[:, sl], ones_b[:, 0:128], brow[:, sl], start=False, stop=True),
                              deps=rtoks[:2], inc=(1 if (half, oc) == (1, 1) else 0))
            tkv = mm
            tke = P.op("vector", lambda h: h.tensor_copy(k_sb[:, :], ps_a[:, :]),
                       deps=[tkv, prev["u"]], inc=1)
            tu = P.op("vector", lambda h: h.tensor_tensor(u_sb[:, :], ps_b[:, :], k_sb[:, :], ALU.mult),
                      deps=[tke], inc=1)
            ps_a_free = tke
            ps_b_free = tu
            mm = None
            for w_sb, wtok, brow, half in ((wq, wtoks[0], rows["bq"], 0), (wg, wtoks[3], rows["bg"], 1)):
                for oc in range(2):
                    sl = slice(oc * 512, oc * 512 + 512)
                    dst = ps_a if half == 0 else ps_b
                    for ci in range(8):
                        mm = P.op("tensor", lambda h, dst=dst, sl=sl, ci=ci, w_sb=w_sb:
                                  h.matmul(dst[:, sl], yT[:, ci * 128:(ci + 1) * 128],
                                           w_sb[:, ci, sl], start=(ci == 0), stop=False),
                                  deps=[tyT, wtok, ps_a_free if half == 0 else ps_b_free], inc=0)
                    mm = P.op("tensor", lambda h, dst=dst, sl=sl, brow=brow:
                              h.matmul(dst[:, sl], ones_b[:, 0:128], brow[:, sl], start=False, stop=True),
                              deps=rtoks[:4], inc=(1 if (half, oc) == (1, 1) else 0))
            tqgmm = mm
            tsg = P.op("scalar", lambda h: h.activation(out=sg_sb[:, :], in_=ps_b[:, :], func=AF.Sigmoid),
                       deps=[tqgmm, prev["qg"]], inc=1)
            tqg = P.op("vector", lambda h: h.scalar_tensor_tensor(qg_sb[:, :], ps_a[:, :], 0.125,
                                                                  sg_sb[:, :], ALU.mult, ALU.mult),
                       deps=[tsg], inc=1)
            ps_a_free = tqg
            ps_b_free = tqg
            mm = None
            for hd in range(16):
                mm = P.op("tensor", lambda h, hd=hd: h.matmul(ps_c[:, hd * 64:(hd + 1) * 64],
                                                              Lh[:, hd * 128:(hd + 1) * 128],
                                                              u_sb[:, hd * 64:(hd + 1) * 64],
                                                              start=True, stop=True),
                          deps=[tu, ltoks[-1], ps_c_free], inc=(1 if hd == 15 else 0))
            tS = mm
            tb1 = P.op("tensor", lambda h, par=par: h.matmul(ps_b[:, 0:512], ones_f[:, 0:128],
                                                             st0[:, par, 0:512], start=True, stop=True),
                       deps=[st0_tok, ps_b_free], inc=0)
            tb2 = P.op("tensor", lambda h, par=par: h.matmul(ps_b[:, 512:1024], ones_f[:, 0:128],
                                                             st0[:, par, 512:1024], start=True, stop=True),
                       deps=[st0_tok, ps_b_free], inc=1)
            ts1 = P.op("vector", lambda h: h.tensor_tensor(st0tmp[:, :], ps_b[:, :], Ppow[:, :], ALU.mult),
                       deps=[tb2, tp3, prev["state"]], inc=1)
            ts2 = P.op("vector", lambda h: h.tensor_tensor(state_sb[:, :], ps_c[:, :], st0tmp[:, :], ALU.add),
                       deps=[ts1, tS], inc=1)
            to_ = P.op("vector", lambda h: h.tensor_tensor(o_sb[:, :], state_sb[:, :], qg_sb[:, :], ALU.mult),
                       deps=[ts2, tqg, prev["oT"]], inc=1)
            ps_c_free = ts2
            mm = None
            for hd in range(16):
                mm = P.op("tensor", lambda h, hd=hd: h.matmul(ps_a[0:1, hd * 64:(hd + 1) * 64],
                                                              Lcol[:, hd:hd + 1],
                                                              u_sb[:, hd * 64:(hd + 1) * 64],
                                                              start=True, stop=True),
                          deps=[tu, tq2, ps_a_free], inc=(1 if hd == 15 else 0))
            tsr = mm
            tn1 = P.op("vector", lambda h, par=par: h.tensor_tensor(st0t[:, :], st0[:, par, :],
                                                                    d128[:, :], ALU.mult),
                       deps=[st0_tok, td2], inc=1)
            st0_tok = P.op("vector", lambda h, par=par: h.tensor_tensor(st0[:, 1 - par, :],
                                                                        ps_a[0:1, :], st0t[:, :], ALU.add),
                           deps=[tn1, tsr], inc=1)
            ps_a_free = st0_tok
            toT = transpose_tile(P, lambda j: o_sb[:, j * 128:(j + 1) * 128], ps_t, oT, 8,
                                 ident_b, [to_], psum_free_tok=tyT)
            ps_t_free = toT
            mm = None
            for oc in range(2):
                sl = slice(oc * 512, oc * 512 + 512)
                for ci in range(8):
                    mm = P.op("tensor", lambda h, sl=sl, ci=ci: h.matmul(ps_a[:, sl],
                                                                         oT[:, ci * 128:(ci + 1) * 128],
                                                                         wo[:, ci, sl], start=(ci == 0),
                                                                         stop=False),
                              deps=[toT, wtoks[4], ps_a_free], inc=0)
                mm = P.op("tensor", lambda h, sl=sl: h.matmul(ps_a[:, sl], ones_b[:, 0:128],
                                                              rows["bo"][:, sl], start=False, stop=True),
                          deps=[rtoks[4]], inc=(1 if oc == 1 else 0))
            tx2 = P.op("vector", lambda h: h.tensor_tensor(x2_sb[:, :], ps_a[:, :], y_sb[:, :], ALU.add),
                       deps=[mm, ty], inc=1)
            ps_a_free = tx2
            tst = P.dma("sync", P.stream("a_x2st"), lambda h, i=i: h.dma_start(
                out=x2_t[i, :, :], in_=x2_sb[:, :]), deps=[tx2])
            prev = dict(x=tload, y=ty, y1=ty1, y1b=tyT, yT=tkv if False else tqgmm, u=tS, k=tu,
                        qg=to_, sg=tqg, st=st0_tok, o=toT, oT=mm, x2=tst, state=to_, stt=ts2)

        P.barrier()
        P.flush()

    with contextlib.ExitStack() as ph:
        rows = {}
        rtoks = []
        for nm in ("ln2_g", "ln2_b"):
            rows[nm], tk = load_row(P, "f_" + nm, aps[nm], D, ph, F32)
            rtoks.append(tk)
        b2row, tb2r = load_row(P, "f_b2", aps["b2"], D, ph, BF16)
        b1row, tb1r = load_row(P, "f_b1", aps["b1"], 4096, ph, BF16)

        ps_f = [ph.enter_context(_ps(nc, f"ps_f{j}", [128, 512], F32)) for j in range(2)]
        ps_w2 = ph.enter_context(_ps(nc, "ps_w2", [128, 1024], F32))
        ps_g = ph.enter_context(_ps(nc, "ps_g", [128, 1024], F32))
        ps_t = ph.enter_context(_ps(nc, "ps_t2", [128, 1024], BF16))

        g2b = ph.enter_context(_sb(nc, "g2b", [128, 1024], F32))
        b2b = ph.enter_context(_sb(nc, "b2b", [128, 1024], F32))
        tg = bcast_row_f32(P, rows["ln2_g"][:, :], D, ps_g, g2b, consts["ones_f"], [rtoks[0]])
        tb = bcast_row_f32(P, rows["ln2_b"][:, :], D, ps_g, b2b, consts["ones_f"], [rtoks[1], tg])

        w1h = ph.enter_context(_sb(nc, "w1h", [128, 8, 2048], BF16))
        w2h = ph.enter_context(_sb(nc, "w2h", [128, 16, 1024], BF16))

        xt = ph.enter_context(_sb(nc, "fxt", [128, 1024], F32))
        x2t = ph.enter_context(_sb(nc, "fx2t", [128, 1024], F32))
        hh = ph.enter_context(_sb(nc, "fh", [128, 1024], F32))
        hb_sb = ph.enter_context(_sb(nc, "fhb", [128, 1024], BF16))
        hT = [ph.enter_context(_sb(nc, f"fhT{j}", [128, 1024], BF16)) for j in range(8)]
        xnew = [ph.enter_context(_sb(nc, f"fxn{j}", [128, 1024], F32)) for j in range(8)]
        ff1 = ph.enter_context(_sb(nc, "ff1", [128, 512], BF16))
        scr = dict(
            junk=ph.enter_context(_sb(nc, "f_junk", [128, 1024], F32)),
            s1=ph.enter_context(_sb(nc, "f_s1", [128, 1], F32)),
            s2=ph.enter_context(_sb(nc, "f_s2", [128, 1], F32)),
            s3=ph.enter_context(_sb(nc, "f_s3", [128, 1], F32)),
            sq=ph.enter_context(_sb(nc, "f_sq", [128, 1024], F32)),
        )

        xin_t = xin.rearrange("(n p) d -> n p d", p=128)
        x2_t = x2buf.rearrange("(n p) d -> n p d", p=128)
        xout_t = xout.rearrange("(n p) d -> n p d", p=128)
        W1 = aps["W1"]
        W2 = aps["W2"]

        P.barrier()
        last_store = None
        pf_free = [None, None]
        ff1_free = None
        ps_w2_free = None
        for tg_i in range(4):
            tiles = list(range(tg_i * 8, tg_i * 8 + 8))
            hT_toks = {}
            xnew_toks = {-1: None}
            for jj, i in enumerate(tiles):
                tl1 = P.dma("sync", P.stream("f_x"), lambda h, i=i: h.dma_start(
                    out=xt[:, :], in_=xin_t[i, :, :]), deps=[xnew_toks[jj - 1], last_store])
                tl2 = P.dma("sync", P.stream("f_x2"), lambda h, i=i: h.dma_start(
                    out=x2t[:, :], in_=x2_t[i, :, :]), deps=[xnew_toks[jj - 1]])
                th = ln_tile(P, x2t[:, :], D, scr, 1e-5, hh[:, :], x_deps=[tl2])
                th1 = P.op("vector", lambda h: h.tensor_tensor(hh[:, :], hh[:, :], g2b[:, :], ALU.mult),
                           deps=[th, tg], inc=1)
                th2 = P.op("vector", lambda h: h.tensor_tensor(hh[:, :], hh[:, :], b2b[:, :], ALU.add),
                           deps=[th1, tb], inc=1)
                thb = P.op("vector", lambda h: h.tensor_copy(hb_sb[:, :], hh[:, :]), deps=[th2], inc=1)
                tT = transpose_tile(P, lambda j: hb_sb[:, j * 128:(j + 1) * 128], ps_t, hT[jj], 8,
                                    consts["ident_b"], [thb], psum_free_tok=hT_toks.get(jj - 1))
                hT_toks[jj] = tT
                txn = P.op("vector", lambda h, jj=jj: h.tensor_tensor(xnew[jj][:, :], xt[:, :],
                                                                      x2t[:, :], ALU.add),
                           deps=[tl1, tl2, tT], inc=1)
                xnew_toks[jj] = txn
            accum_toks = {jj: xnew_toks[jj] for jj in range(8)}
            for half in range(2):
                wfree = P.tok("tensor")
                tw1 = P.dma("gpsimd", P.stream("f_w1"), lambda h, half=half: h.dma_start(
                    out=w1h[:, :, :],
                    in_=W1[:, half * 2048:(half + 1) * 2048].rearrange("(ci p) n -> p ci n", p=128)),
                    deps=[wfree])
                tw2 = P.dma("gpsimd", P.stream("f_w2"), lambda h, half=half: h.dma_start(
                    out=w2h[:, :, :],
                    in_=W2[half * 2048:(half + 1) * 2048, :].rearrange("(ci p) n -> p ci n", p=128)),
                    deps=[wfree])
                for jj in range(8):
                    for hb_i in range(4):
                        pf = ps_f[hb_i % 2]
                        mm = None
                        for hc in range(4):
                            hcg = hb_i * 4 + hc
                            for ci in range(8):
                                mm = P.op("tensor", lambda h, pf=pf, hc=hc, hcg=hcg, ci=ci, jj=jj:
                                          h.matmul(pf[:, hc * 128:(hc + 1) * 128],
                                                   w1h[:, ci, hcg * 128:(hcg + 1) * 128],
                                                   hT[jj][:, ci * 128:(ci + 1) * 128],
                                                   start=(ci == 0), stop=False),
                                          deps=[tw1, hT_toks[jj], pf_free[hb_i % 2]], inc=0)
                            mm = P.op("tensor", lambda h, pf=pf, hc=hc, hcg=hcg, half=half:
                                      h.matmul(pf[:, hc * 128:(hc + 1) * 128],
                                               b1row[:, half * 2048 + hcg * 128:half * 2048 + (hcg + 1) * 128],
                                               consts["ones_b"][:, 0:128], start=False, stop=True),
                                      deps=[tb1r], inc=(1 if hc == 3 else 0))
                        tgel = P.op("scalar", lambda h, pf=pf: h.activation(out=ff1[:, :], in_=pf[:, :],
                                                                            func=AF.Gelu),
                                    deps=[mm, ff1_free], inc=1)
                        pf_free[hb_i % 2] = tgel
                        mm2 = None
                        for oc in range(2):
                            for hc in range(4):
                                hcg = hb_i * 4 + hc
                                is_first = (hb_i == 0 and hc == 0)
                                is_last = (half == 0 and hb_i == 3 and hc == 3)
                                mm2 = P.op("tensor", lambda h, hc=hc, hcg=hcg, oc=oc, is_first=is_first, is_last=is_last:
                                           h.matmul(ps_w2[:, oc * 512:(oc + 1) * 512],
                                                    ff1[:, hc * 128:(hc + 1) * 128],
                                                    w2h[:, hcg, oc * 512:(oc + 1) * 512],
                                                    start=is_first, stop=is_last),
                                           deps=[tgel, tw2, ps_w2_free], inc=(1 if is_last and oc == 1 else 0))
                        ff1_free = mm2
                    if half == 1:
                        mmb = None
                        for oc in range(2):
                            mmb = P.op("tensor", lambda h, oc=oc:
                                       h.matmul(ps_w2[:, oc * 512:(oc + 1) * 512],
                                                consts["ones_b"][:, 0:128],
                                                b2row[:, oc * 512:(oc + 1) * 512],
                                                start=False, stop=True),
                                       deps=[tb2r], inc=(1 if oc == 1 else 0))
                        tadd = P.op("vector", lambda h, jj=jj: h.tensor_tensor(xnew[jj][:, :], xnew[jj][:, :],
                                                                               ps_w2[:, :], ALU.add),
                                    deps=[mmb, accum_toks[jj]], inc=1)
                    else:
                        tadd = P.op("vector", lambda h, jj=jj: h.tensor_tensor(xnew[jj][:, :], xnew[jj][:, :],
                                                                               ps_w2[:, :], ALU.add),
                                    deps=[mm2, accum_toks[jj]], inc=1)
                    accum_toks[jj] = tadd
                    ps_w2_free = tadd
            for jj, i in enumerate(tiles):
                tst = P.dma("sync", P.stream("f_st"), lambda h, i=i, jj=jj: h.dma_start(
                    out=xout_t[i, :, :], in_=xnew[jj][:, :]), deps=[accum_toks[jj]])
                last_store = tst
        P.barrier()
        P.flush()
    return last_store




L0AA_W = [("Wq", [D, D]), ("Wk", [D, D]), ("Wv", [D, D]), ("Wg", [D, D]), ("Wo", [D, D]),
          ("bq", [D]), ("bk", [D]), ("bv", [D]), ("bg", [D]), ("bo", [D]),
          ("ln1_g", [D]), ("ln1_b", [D]), ("ln2_g", [D]), ("ln2_b", [D]),
          ("decay_logit", [16]), ("W1", [D, 4096]), ("b1", [4096]), ("W2", [4096, D]), ("b2", [D])]


def _mk_prog():
    nc = bass.Bass("TRN2", target_bir_lowering=False)
    return nc


def build_program_l0_attn_a():
    nc = _mk_prog()
    aps = {}
    aps["xin"] = nc.dram_tensor("xin", [T, D], F32, kind="ExternalInput").ap()
    aps["x2buf"] = nc.dram_tensor("x2buf", [T, D], F32, kind="Internal").ap()
    aps["xout"] = nc.dram_tensor("xout", [T, D], F32, kind="ExternalOutput").ap()
    for nm, shp in L0AA_W:
        aps[nm] = nc.dram_tensor(nm, shp, F32, kind="ExternalInput").ap()
    with contextlib.ExitStack() as es:
        P = Prog(nc, None, es)
        block = es.enter_context(nc.Block())
        P.block = block
        consts = const_setup(P, es)
        build_l0_attn_a(P, es, aps, consts)
        P.finish()
    return nc


def run_sublayer_l0aa(x_full, wd):
    nc = build_program_l0_attn_a()
    in_maps = []
    for c in range(N_CORES):
        m = {"xin": np.ascontiguousarray(x_full[c])}
        for nm, _ in L0AA_W:
            m[nm] = wd[nm]
        in_maps.append(m)
    res = run_bass_kernel_spmd(nc, in_maps, core_ids=list(range(N_CORES)))
    return np.stack([r["xout"] for r in res.results])




def build_l0_ffn(P, es, aps, consts):
    nc = P.nc
    xin, xout = aps["xin"], aps["xout"]
    xin_t = xin.rearrange("(n p) d -> n p d", p=128)
    xout_t = xout.rearrange("(n p) d -> n p d", p=128)

    tcopy = P.dma("sync", P.stream("d_copy", "bulk"), lambda h: h.dma_start(out=xout, in_=xin))

    with contextlib.ExitStack() as ph:
        xlnT = ph.enter_context(_sb(nc, "xlnT", [128, 8, 4096], BF16))
        mask = ph.enter_context(_sb(nc, "mask", [128, 32, 9], F32))
        p0 = contextlib.ExitStack()
        routerT = p0.enter_context(_sb(nc, "sb_routerT", [128, 8, 8], F32))
        ps_t = p0.enter_context(_ps(nc, "d_ps_t", [128, 1024], BF16))
        ps_tf = p0.enter_context(_ps(nc, "d_ps_tf", [128, 1024], F32))
        ps_r = p0.enter_context(_ps(nc, "d_ps_r", [128, 8], F32))
        xt = p0.enter_context(_sb(nc, "d_xt", [128, 1024], F32))
        xln = p0.enter_context(_sb(nc, "d_xln", [128, 1024], F32))
        xlnb = p0.enter_context(_sb(nc, "d_xlnb", [128, 1024], BF16))
        xT32 = p0.enter_context(_sb(nc, "d_xT32", [128, 1024], F32))
        mx = p0.enter_context(_sb(nc, "d_mx", [128, 1], F32))
        scr = dict(
            junk=p0.enter_context(_sb(nc, "d_junk", [128, 1024], F32)),
            s1=p0.enter_context(_sb(nc, "d_s1", [128, 1], F32)),
            s2=p0.enter_context(_sb(nc, "d_s2", [128, 1], F32)),
            s3=p0.enter_context(_sb(nc, "d_s3", [128, 1], F32)),
            sq=p0.enter_context(_sb(nc, "d_sq", [128, 1024], F32)),
        )
        trt = P.dma("sync", P.stream("bulks", "bulk"), lambda h: h.dma_start(
            out=routerT[:, :, :], in_=aps["routerT"].rearrange("(ci p) e -> p ci e", p=128)))
        tmz = P.op("gpsimd", lambda h: h.memset(mask[:, :, 8:9], 1.0), inc=1)

        prev_ln = None
        prev_T = None
        prev_T32 = None
        prev_r = None
        prev_mask = None
        for i in range(NT):
            tl = P.dma("sync", P.stream("d_x"), lambda h, i=i: h.dma_start(
                out=xt[:, :], in_=xin_t[i, :, :]), deps=[prev_ln])
            tln = ln_tile(P, xt[:, :], D, scr, 1e-5, xln[:, :], x_deps=[tl],
                          out_tok_extra_deps=[prev_T32, prev_T])
            tcb = P.op("vector", lambda h: h.tensor_copy(xlnb[:, :], xln[:, :]),
                       deps=[tln, prev_T], inc=1)
            tT = transpose_tile(P, lambda j: xlnb[:, j * 128:(j + 1) * 128], ps_t,
                                None, 8, consts["ident_b"], [tcb], psum_free_tok=prev_T,
                                out_ap=xlnT[:, :, i * 128:(i + 1) * 128])
            tts = []
            for j in range(8):
                tts.append(P.op("tensor", lambda h, j=j: h.transpose(ps_tf[:, j * 128:(j + 1) * 128],
                                                                     xln[:, j * 128:(j + 1) * 128],
                                                                     consts["ident_f"][:, :]),
                                deps=[tln, prev_T32] + consts["tok"], inc=(1 if j == 7 else 0)))
            tev = P.op("vector", lambda h: h.tensor_copy(xT32[:, :], ps_tf[:, :]), deps=[tts[-1]], inc=1)
            mmr = None
            for ci in range(8):
                mmr = P.op("tensor", lambda h, ci=ci: h.matmul(ps_r[:, :], xT32[:, ci * 128:(ci + 1) * 128],
                                                               routerT[:, ci, :], start=(ci == 0),
                                                               stop=(ci == 7)),
                           deps=[tev, trt, prev_r], inc=(1 if ci == 7 else 0))
            tmx = P.op("vector", lambda h: h.reduce_max(mx[:, :], ps_r[:, :], axis=AX.X),
                       deps=[mmr, prev_mask], inc=1)
            tms = P.op("vector", lambda h, i=i: h.tensor_scalar(mask[:, i, 0:8], ps_r[:, :],
                                                                mx[:, 0:1], None, ALU.is_equal),
                       deps=[tmx], inc=1)
            prev_ln = tln
            prev_T = tT
            prev_T32 = mmr
            prev_r = tms
            prev_mask = tms
        P.barrier()
        P.flush()
        p0.close()

        with contextlib.ExitStack() as pe:
            upsl = pe.enter_context(_sb(nc, "upsl", [128, 8, 2048], BF16))
            gtsl = pe.enter_context(_sb(nc, "gtsl", [128, 8, 2048], BF16))
            dwn = pe.enter_context(_sb(nc, "dwn", [128, 16, 1024], BF16))
            hact = pe.enter_context(_sb(nc, "hact", [128, 16, 1024], BF16))
            sg = [pe.enter_context(_sb(nc, f"d_sg{j}", [128, 512], BF16)) for j in range(2)]
            sg_free = {0: None, 1: None}
            contrib = [pe.enter_context(_sb(nc, f"d_ctr{j}", [128, 1024], F32)) for j in range(2)]
            ps_u = [pe.enter_context(_ps(nc, f"d_ps_u{j}", [128, 512], F32)) for j in range(2)]
            ps_g = [pe.enter_context(_ps(nc, f"d_ps_g{j}", [128, 512], F32)) for j in range(2)]
            ps_y = [pe.enter_context(_ps(nc, f"d_ps_y{j}", [128, 1024], F32)) for j in range(2)]

            tdw = None
            tu_free = {0: None, 1: None}
            tg_free = {0: None, 1: None}
            ctr_free = {0: None, 1: None}
            ps_y_free = {0: None, 1: None}
            hact_free = None
            upsl_free = None
            gtsl_free = None
            acc_last = tcopy
            for e in range(9):
                dsrc = (aps["downT"][e] if e < 8 else aps["downTs"])
                tdw = P.dma("gpsimd", P.stream("d_dw"), lambda h, dsrc=dsrc: h.dma_start(
                    out=dwn[:, :, :], in_=dsrc.rearrange("(ci p) n -> p ci n", p=128)),
                    deps=[P.tok("tensor")])
                usrc_f = (aps["upT"][e] if e < 8 else aps["upTs"])
                gsrc_f = (aps["gateT"][e] if e < 8 else aps["gateTs"])
                tlu = P.dma("gpsimd", P.stream("d_up0"), lambda h, usrc_f=usrc_f: h.dma_start(
                    out=upsl[:, :, :], in_=usrc_f.rearrange("(ci p) n -> p ci n", p=128)),
                    deps=[upsl_free])
                tlg = P.dma("gpsimd", P.stream("d_gt0"), lambda h, gsrc_f=gsrc_f: h.dma_start(
                    out=gtsl[:, :, :], in_=gsrc_f.rearrange("(ci p) n -> p ci n", p=128)),
                    deps=[gtsl_free])
                for jg in range(4):
                    jt0 = jg * 1024
                    silu_toks = []
                    for hc in range(16):
                        mmu = mmg = None
                        for j2 in range(2):
                            bk = (hc * 2 + j2) % 2
                            tsl = slice(jt0 + j2 * 512, jt0 + (j2 + 1) * 512)
                            for ci in range(8):
                                mmu = P.op("tensor", lambda h, ci=ci, hc=hc, tsl=tsl, bk=bk:
                                           h.matmul(ps_u[bk][:, :], upsl[:, ci, hc * 128:(hc + 1) * 128],
                                                    xlnT[:, ci, tsl], start=(ci == 0), stop=(ci == 7)),
                                           deps=[tlu, tu_free[bk]], inc=(1 if ci == 7 else 0))
                            for ci in range(8):
                                mmg = P.op("tensor", lambda h, ci=ci, hc=hc, tsl=tsl, bk=bk:
                                           h.matmul(ps_g[bk][:, :], gtsl[:, ci, hc * 128:(hc + 1) * 128],
                                                    xlnT[:, ci, tsl], start=(ci == 0), stop=(ci == 7)),
                                           deps=[tlg, tg_free[bk]], inc=(1 if ci == 7 else 0))
                            tsi = P.op("scalar", lambda h, bk=bk: h.activation(
                                out=sg[bk][:, :], in_=ps_g[bk][:, :], func=AF.Silu),
                                deps=[mmg, sg_free[bk]], inc=1)
                            tmu = P.op("vector", lambda h, hc=hc, j2=j2, bk=bk: h.tensor_tensor(
                                hact[:, hc, j2 * 512:(j2 + 1) * 512], ps_u[bk][:, :], sg[bk][:, :],
                                ALU.mult),
                                deps=[tsi, mmu, hact_free], inc=1)
                            tu_free[bk] = tmu
                            tg_free[bk] = tsi
                            sg_free[bk] = tmu
                            silu_toks.append(tmu)
                        upsl_free = mmu
                        gtsl_free = mmg
                    down_last = None
                    for tsub in range(8):
                        tile_i = jg * 8 + tsub
                        pyp = tsub % 2
                        mm = None
                        for hc in range(16):
                            for oc in range(2):
                                mm = P.op("tensor", lambda h, hc=hc, oc=oc, tsub=tsub, pyp=pyp:
                                          h.matmul(ps_y[pyp][:, oc * 512:(oc + 1) * 512],
                                                   hact[:, hc, tsub * 128:(tsub + 1) * 128],
                                                   dwn[:, hc, oc * 512:(oc + 1) * 512],
                                                   start=(hc == 0), stop=(hc == 15)),
                                          deps=[silu_toks[-1], tdw, ps_y_free[pyp]],
                                          inc=(1 if (hc == 15 and oc == 1) else 0))
                        tev = P.op("vector", lambda h, tile_i=tile_i, tsub=tsub, e=e, pyp=pyp:
                                   h.tensor_scalar(contrib[tsub % 2][:, :], ps_y[pyp][:, :],
                                                   mask[:, tile_i, e:e + 1], None, ALU.mult),
                                   deps=[mm, ctr_free[tsub % 2], prev_mask], inc=1)
                        ps_y_free[pyp] = tev
                        tac = P.dma("gpsimd", P.stream(f"d_ac{tsub % 2}"),
                            lambda h, tile_i=tile_i, tsub=tsub: h.dma_start(
                                out=xout_t[tile_i, :, :], in_=contrib[tsub % 2][:, :],
                                accum_op=ALU.add),
                            deps=[tev, tcopy])
                        ctr_free[tsub % 2] = tac
                        down_last = tac
                    hact_free = mm
            P.barrier()
            P.flush()
            return down_last


L0FFN_W = [("upT", [8, D, 2048]), ("gateT", [8, D, 2048]), ("downT", [8, 2048, D]),
           ("routerT", [D, 8]), ("upTs", [D, 2048]), ("gateTs", [D, 2048]), ("downTs", [2048, D])]


def build_program_l0_ffn():
    nc = _mk_prog()
    aps = {}
    aps["xin"] = nc.dram_tensor("xin", [T, D], F32, kind="ExternalInput").ap()
    aps["xout"] = nc.dram_tensor("xout", [T, D], F32, kind="ExternalOutput").ap()
    for nm, shp in L0FFN_W:
        aps[nm] = nc.dram_tensor(nm, shp, F32, kind="ExternalInput").ap()
    with contextlib.ExitStack() as es:
        P = Prog(nc, None, es)
        block = es.enter_context(nc.Block())
        P.block = block
        consts = const_setup(P, es)
        build_l0_ffn(P, es, aps, consts)
        P.finish()
    return nc


def prep_l0ffn_weights(ins, pfx="L0_ffn_"):
    return {
        "upT": np.ascontiguousarray(ins[pfx + "up"].transpose(0, 2, 1)),
        "gateT": np.ascontiguousarray(ins[pfx + "gate"].transpose(0, 2, 1)),
        "downT": np.ascontiguousarray(ins[pfx + "down"].transpose(0, 2, 1)),
        "routerT": np.ascontiguousarray(ins[pfx + "router"].T),
        "upTs": np.ascontiguousarray(ins[pfx + "w_up_s"].T),
        "gateTs": np.ascontiguousarray(ins[pfx + "w_gate_s"].T),
        "downTs": np.ascontiguousarray(ins[pfx + "w_down_s"].T),
    }


def run_sublayer_l0ffn(x_full, wd):
    nc = build_program_l0_ffn()
    in_maps = []
    for c in range(N_CORES):
        m = {"xin": np.ascontiguousarray(x_full[c])}
        m.update(wd)
        in_maps.append(m)
    res = run_bass_kernel_spmd(nc, in_maps, core_ids=list(range(N_CORES)))
    return np.stack([r["xout"] for r in res.results])




def build_l0_attn_b(P, es, aps, consts):
    nc = P.nc
    xin, xout, znT_d = aps["xin"], aps["xout"], aps["znT"]
    xin_t = xin.rearrange("(n p) d -> n p d", p=128)
    xout_t = xout.rearrange("(n p) d -> n p d", p=128)

    with contextlib.ExitStack() as ph:
        ps_t = ph.enter_context(_ps(nc, "b_ps_t", [128, 1024], BF16))
        xt = ph.enter_context(_sb(nc, "b_xt", [128, 1024], F32))
        y = ph.enter_context(_sb(nc, "b_y", [128, 1024], F32))
        zn = ph.enter_context(_sb(nc, "b_zn", [128, 1024], F32))
        znb = ph.enter_context(_sb(nc, "b_znb", [128, 1024], BF16))
        znTt = ph.enter_context(_sb(nc, "b_znTt", [128, 8, 128], BF16))
        xy = ph.enter_context(_sb(nc, "b_xy", [128, 1024], F32))
        gwb = ph.enter_context(_sb(nc, "b_gwb", [128, 1024], F32))
        bwb = ph.enter_context(_sb(nc, "b_bwb", [128, 1024], F32))
        ps_bc = ph.enter_context(_ps(nc, "b_ps_bc", [128, 1024], F32))
        scr = dict(
            junk=ph.enter_context(_sb(nc, "b_junk", [128, 1024], F32)),
            s1=ph.enter_context(_sb(nc, "b_s1", [128, 1], F32)),
            s2=ph.enter_context(_sb(nc, "b_s2", [128, 1], F32)),
            s3=ph.enter_context(_sb(nc, "b_s3", [128, 1], F32)),
            sq=ph.enter_context(_sb(nc, "b_sq", [128, 1024], F32)),
        )
        lnw, t1 = load_row(P, "b_lnw", aps["ln_w"], D, ph, F32)
        lnb, t2 = load_row(P, "b_lnb", aps["ln_b"], D, ph, F32)
        tg = bcast_row_f32(P, lnw[:, :], D, ps_bc, gwb, consts["ones_f"], [t1] + consts["tok"])
        tb = bcast_row_f32(P, lnb[:, :], D, ps_bc, bwb, consts["ones_f"], [t2, tg])

        prev = {}
        for i in range(NT):
            tl = P.dma("sync", P.stream("b_x"), lambda h, i=i: h.dma_start(
                out=xt[:, :], in_=xin_t[i, :, :]), deps=[prev.get("xy")])
            ty = ln_tile(P, xt[:, :], D, scr, 1e-5, y[:, :], x_deps=[tl],
                         out_tok_extra_deps=[prev.get("xy2")])
            tz = ln_tile(P, y[:, :], D, scr, 1e-5, zn[:, :], x_deps=[ty],
                         out_tok_extra_deps=[prev.get("znb")])
            tz1 = P.op("vector", lambda h: h.tensor_tensor(zn[:, :], zn[:, :], gwb[:, :], ALU.mult),
                       deps=[tz, tg], inc=1)
            tz2 = P.op("vector", lambda h: h.tensor_tensor(zn[:, :], zn[:, :], bwb[:, :], ALU.add),
                       deps=[tz1, tb], inc=1)
            tzb = P.op("vector", lambda h: h.tensor_copy(znb[:, :], zn[:, :]),
                       deps=[tz2, prev.get("znT")], inc=1)
            tT = transpose_tile(P, lambda j: znb[:, j * 128:(j + 1) * 128], ps_t, None, 8,
                                consts["ident_b"], [tzb], psum_free_tok=prev.get("znT"),
                                out_ap=znTt[:, :, :])
            tst1 = P.dma("sync", P.stream("b_znst"), lambda h, i=i: h.dma_start(
                out=znT_d[:, :, i * 128:(i + 1) * 128], in_=znTt[:, :, :]), deps=[tT])
            txy = P.op("vector", lambda h: h.tensor_tensor(xy[:, :], xt[:, :], y[:, :], ALU.add),
                       deps=[ty, tl, prev.get("xyst")], inc=1)
            tst2 = P.dma("sync", P.stream("b_xyst"), lambda h, i=i: h.dma_start(
                out=xout_t[i, :, :], in_=xy[:, :]), deps=[txy])
            prev = dict(xy=ty, xy2=txy, znb=tT, znT=tst1, xyst=tst2)
        P.barrier()
        P.flush()
        b0_done = prev["xyst"]

    acc_last = b0_done
    for hg in range(DEBUG_L0AB_HGS):
        with contextlib.ExitStack() as ph:
            QT = ph.enter_context(_sb(nc, f"b_QT{hg}", [128, 4, 4096], BF16))
            KT = ph.enter_context(_sb(nc, f"b_KT{hg}", [128, 4, 4096], BF16))
            Vp = ph.enter_context(_sb(nc, f"b_Vp{hg}", [128, 32, 520], BF16))
            negB = ph.enter_context(_sb(nc, f"b_negB{hg}", [128, 8], F32))
            wq = ph.enter_context(_sb(nc, f"b_wq{hg}", [128, 8, 512], BF16))
            wk = ph.enter_context(_sb(nc, f"b_wk{hg}", [128, 8, 512], BF16))
            wv = ph.enter_context(_sb(nc, f"b_wv{hg}", [128, 8, 512], BF16))
            wp = ph.enter_context(_sb(nc, f"b_wp{hg}", [128, 4, 1024], BF16))
            brow, tbr = load_row(P, f"b_brow{hg}", aps["b_qkv"], 3072, ph, BF16)
            borow, tbo = load_row(P, f"b_borow{hg}", aps["b_proj"], D, ph, BF16)
            W = aps["W_qkv"]
            twq = P.dma("gpsimd", P.stream("bulk", "bulk"), lambda h, hg=hg: h.dma_start(
                out=wq[:, :, :],
                in_=W[:, hg * 512:hg * 512 + 512].rearrange("(ci p) n -> p ci n", p=128)))
            twk = P.dma("gpsimd", P.stream("bulk", "bulk"), lambda h, hg=hg: h.dma_start(
                out=wk[:, :, :],
                in_=W[:, 1024 + hg * 512:1024 + hg * 512 + 512].rearrange("(ci p) n -> p ci n", p=128)))
            twv = P.dma("gpsimd", P.stream("bulk", "bulk"), lambda h, hg=hg: h.dma_start(
                out=wv[:, :, :],
                in_=W[:, 2048 + hg * 512:2048 + hg * 512 + 512].rearrange("(ci p) n -> p ci n", p=128)))
            twp = P.dma("gpsimd", P.stream("bulk", "bulk"), lambda h, hg=hg: h.dma_start(
                out=wp[:, :, :],
                in_=aps["W_proj"][hg * 512:hg * 512 + 512, :].rearrange("(ci p) n -> p ci n", p=128)))

            with contextlib.ExitStack() as pp:
                znt = [pp.enter_context(_sb(nc, f"b_znt{hg}{j}", [128, 8, 512], BF16))
                       for j in range(2)]
                ps_q = pp.enter_context(_ps(nc, "b_ps_q", [128, 512], F32))
                ps_v = pp.enter_context(_ps(nc, "b_ps_v", [128, 512], F32))
                ps_n = pp.enter_context(_ps(nc, "b_ps_n", [1, 512], F32))
                ps_nb = pp.enter_context(_ps(nc, "b_ps_nb", [128, 8], F32))
                sq_s = pp.enter_context(_sb(nc, "b_sq_s", [128, 4096], F32))
                nrmQ = pp.enter_context(_sb(nc, "b_nrmQ", [1, 8], F32))
                nrmK = pp.enter_context(_sb(nc, "b_nrmK", [1, 8], F32))
                ntmp = pp.enter_context(_sb(nc, "b_ntmp", [1, 512], F32))
                ntm1 = pp.enter_context(_sb(nc, "b_ntm1", [1, 1], F32))
                znt_free = {0: None, 1: None}
                ps_q_free = None
                ps_v_free = None
                qkv_done = []
                for tt in range(8):
                    par = tt % 2
                    tsl = slice(tt * 512, (tt + 1) * 512)
                    tld = P.dma("gpsimd", P.stream(f"b_zl{par}"), lambda h, tsl=tsl, par=par:
                        h.dma_start(out=znt[par][:, :, :], in_=znT_d[:, :, tsl]),
                        deps=[znt_free[par], b0_done])
                    last_read = None
                    for wsb, wtok, dst, boff in (() if "qk" in DEBUG_L0AB_SKIP else ((wq, twq, QT, 0), (wk, twk, KT, 1024))):
                        for ch in range(4):
                            mm = None
                            for ci in range(8):
                                mm = P.op("tensor", lambda h, ch=ch, ci=ci, par=par, wsb=wsb:
                                          h.matmul(ps_q[:, :], wsb[:, ci, ch * 128:(ch + 1) * 128],
                                                   znt[par][:, ci, :], start=(ci == 0), stop=False),
                                          deps=[tld, wtok, ps_q_free], inc=0)
                            mm = P.op("tensor", lambda h, ch=ch, boff=boff, hg=hg:
                                      h.matmul(ps_q[:, :],
                                               brow[:, boff + hg * 512 + ch * 128:boff + hg * 512 + (ch + 1) * 128],
                                               consts["ones_b"][:, 0:512], start=False, stop=True),
                                      deps=[tbr], inc=1)
                            tev = P.op("vector", lambda h, ch=ch, tsl=tsl, dst=dst:
                                       h.tensor_copy(dst[:, ch, tsl], ps_q[:, :]), deps=[mm], inc=1)
                            ps_q_free = tev
                            last_read = mm
                            qkv_done.append(tev)
                    for tsub in (() if "v" in DEBUG_L0AB_SKIP else range(4)):
                        kc = tt * 4 + tsub
                        mm = None
                        for ci in range(8):
                            mm = P.op("tensor", lambda h, tsub=tsub, ci=ci, par=par:
                                      h.matmul(ps_v[:, :], znt[par][:, ci, tsub * 128:(tsub + 1) * 128],
                                               wv[:, ci, :], start=(ci == 0), stop=False),
                                      deps=[tld, twv, ps_v_free], inc=0)
                        mm = P.op("tensor", lambda h, hg=hg:
                                  h.matmul(ps_v[:, :], consts["ones_b"][:, 0:128],
                                           brow[:, 2048 + hg * 512:2048 + hg * 512 + 512],
                                           start=False, stop=True),
                                  deps=[tbr], inc=1)
                        vap = Vp[:, kc, :]
                        vout = ap3(vap, [vap.ap[0], [65, 8], [1, 64]])
                        tev = P.op("vector", lambda h, vout=vout: h.tensor_copy(
                            vout, ps_v[:, :].rearrange("p (e f) -> p e f", e=8)), deps=[mm], inc=1)
                        vone = ap3(vap, [vap.ap[0], [65, 8], [1, 1]])
                        vone = bass.AP(vone.tensor, vone.offset + 64, vone.ap)
                        tmo = P.op("vector", lambda h, vone=vone: h.memset(vone, 1.0),
                                   deps=[mm], inc=1)
                        ps_v_free = tev
                        qkv_done.append(tev)
                        qkv_done.append(tmo)
                        last_read = mm
                    znt_free[par] = last_read
                nz = P.op("gpsimd", lambda h: h.memset(nrmQ[:, :], 0.0), inc=1)
                nz2 = P.op("gpsimd", lambda h: h.memset(nrmK[:, :], 0.0), inc=1)
                qlast = qkv_done[-1] if qkv_done else None
                sq_free = None
                mprev = None
                ntm1_free = None
                for which, src_t, nrm_t, nztok in (() if "bounds" in DEBUG_L0AB_SKIP else (("q", QT, nrmQ, nz), ("k", KT, nrmK, nz2))):
                    for ch in range(4):
                        tsq = P.op("scalar", lambda h, ch=ch, src_t=src_t: h.activation(
                            out=sq_s[:, :], in_=src_t[:, ch, :], func=AF.Square),
                            deps=[qlast, sq_free], inc=1)
                        for half in (range(2) if DEBUG_L0AB_BSTEP >= 2 else ()):
                            hd = ch * 2 + half
                            tac = nztok
                            for tc in range(DEBUG_L0AB_TCN):
                                mm = P.op("tensor", lambda h, half=half, tc=tc: h.matmul(
                                    ps_n[:, :], consts["onescol_f"][half * 64:half * 64 + 64, 0:1],
                                    sq_s[half * 64:half * 64 + 64, tc * 512:(tc + 1) * 512],
                                    start=True, stop=True),
                                    deps=[tsq, mprev], inc=1)
                                tmx = P.op("vector", lambda h: h.reduce_max(ntm1[:, :], ps_n[:, :],
                                                                            axis=AX.X),
                                           deps=[mm, ntm1_free], inc=1)
                                tac = P.op("vector", lambda h, hd=hd, nrm_t=nrm_t: h.tensor_tensor(
                                    nrm_t[:, hd:hd + 1], nrm_t[:, hd:hd + 1], ntm1[:, :], ALU.max),
                                    deps=[tmx, tac], inc=1)
                                ntm1_free = tac
                                mprev = tmx
                            sq_free = mm
                tn1 = P.op("vector", lambda h: h.tensor_tensor(nrmQ[:, :], nrmQ[:, :], nrmK[:, :],
                                                               ALU.mult),
                           deps=[P.tok("vector")], inc=1)
                tn2 = P.op("scalar", lambda h: h.activation(out=nrmQ[:, :], in_=nrmQ[:, :], func=AF.Sqrt),
                           deps=[tn1], inc=1)
                tn3 = P.op("vector", lambda h: h.tensor_scalar(nrmQ[:, :], nrmQ[:, :], -0.125, None,
                                                               ALU.mult),
                           deps=[tn2], inc=1)
                tn4 = P.op("tensor", lambda h: h.matmul(ps_nb[:, :], consts["ones_f"][:, 0:128],
                                                        nrmQ[:, :], start=True, stop=True),
                           deps=[tn3], inc=1)
                tn5 = P.op("vector", lambda h: h.tensor_copy(negB[:, :], ps_nb[:, :]), deps=[tn4], inc=1)
                P.barrier()
                P.flush()

            with contextlib.ExitStack() as pa:
                sc = [pa.enter_context(_ps(nc, f"b_sc{j}", [128, 512], F32)) for j in range(2)]
                av = [pa.enter_context(_ps(nc, f"b_av{j}", [128, 128], F32)) for j in range(2)]
                ps_p = pa.enter_context(_ps(nc, "b_ps_p", [128, 1024], F32))
                ps_t2 = pa.enter_context(_ps(nc, "b_ps_t2", [128, 512], BF16))
                attnT = pa.enter_context(_sb(nc, "b_attnT", [128, 4096], BF16))
                ao = pa.enter_context(_sb(nc, "b_ao", [128, 512], BF16))
                aoT = pa.enter_context(_sb(nc, "b_aoT", [128, 512], BF16))
                rcol = pa.enter_context(_sb(nc, "b_rcol", [128, 1], F32))
                ctr = [pa.enter_context(_sb(nc, f"b_ctr{j}", [128, 1024], F32)) for j in range(2)]
                sc_free = {0: None, 1: None}
                av_free = {0: None, 1: None}
                attnT_free = None
                ao_free = None
                rcol_free = None
                aoT_free = None
                ctr_free = {0: None, 1: None}
                ps_p_free = None
                for qt in range(DEBUG_L0AB_NQT):
                    ao_toks = []
                    for hd in range(8):
                        hrow = (hd % 2) * 64
                        hch = hd // 2
                        exp_toks = []
                        av_par = hd % 2
                        for grp in range(8):
                            par = grp % 2
                            mm = None
                            for k4 in range(4):
                                kc = grp * 4 + k4
                                mm = P.op("tensor", lambda h, par=par, k4=k4, kc=kc, hrow=hrow, hch=hch, qt=qt:
                                          h.matmul(sc[par][:, k4 * 128:(k4 + 1) * 128],
                                                   KT[hrow:hrow + 64, hch, kc * 128:(kc + 1) * 128],
                                                   QT[hrow:hrow + 64, hch, qt * 128:(qt + 1) * 128],
                                                   start=True, stop=True),
                                          deps=[sc_free[par], qkv_done[-1]], inc=(1 if k4 == 3 else 0))
                            tex = P.op("scalar", lambda h, par=par, grp=grp, hd=hd: h.activation(
                                out=attnT[:, grp * 512:(grp + 1) * 512], in_=sc[par][:, :],
                                func=AF.Exp, bias=negB[:, hd:hd + 1], scale=0.125),
                                deps=[mm, tn5, attnT_free], inc=1)
                            sc_free[par] = tex
                            exp_toks.append(tex)
                        mm = None
                        for kc in range(32):
                            mm = P.op("tensor", lambda h, kc=kc, hd=hd, av_par=av_par:
                                      h.matmul(av[av_par][:, 0:65],
                                               attnT[:, kc * 128:(kc + 1) * 128],
                                               Vp[:, kc, hd * 65:(hd + 1) * 65],
                                               start=(kc == 0), stop=(kc == 31)),
                                      deps=[exp_toks[-1], av_free[av_par]], inc=(1 if kc == 31 else 0))
                        attnT_free = mm
                        trc = P.op("vector", lambda h, av_par=av_par: h.reciprocal(
                            rcol[:, :], av[av_par][:, 64:65]), deps=[mm, rcol_free], inc=1)
                        tsc = P.op("vector", lambda h, hd=hd, av_par=av_par: h.tensor_scalar(
                            ao[:, hd * 64:(hd + 1) * 64], av[av_par][:, 0:64], rcol[:, 0:1],
                            None, ALU.mult),
                            deps=[trc, ao_free], inc=1)
                        rcol_free = tsc
                        av_free[av_par] = tsc
                        ao_toks.append(tsc)
                    tT = transpose_tile(P, lambda j: ao[:, j * 128:(j + 1) * 128], ps_t2, aoT, 4,
                                        consts["ident_b"], [ao_toks[-1]], psum_free_tok=aoT_free)
                    ao_free = tT
                    mm = None
                    for oc in range(2):
                        for ci in range(4):
                            mm = P.op("tensor", lambda h, oc=oc, ci=ci, hg=hg: h.matmul(
                                ps_p[:, oc * 512:(oc + 1) * 512], aoT[:, ci * 128:(ci + 1) * 128],
                                wp[:, ci, oc * 512:(oc + 1) * 512], start=(ci == 0),
                                stop=(ci == 3 and hg == 1)),
                                deps=[tT, twp, ps_p_free],
                                inc=(1 if (hg == 1 and ci == 3 and oc == 1) else 0))
                        if hg == 0:
                            mm = P.op("tensor", lambda h, oc=oc: h.matmul(
                                ps_p[:, oc * 512:(oc + 1) * 512], consts["ones_b"][:, 0:128],
                                borow[:, oc * 512:(oc + 1) * 512], start=False, stop=True),
                                deps=[tbo], inc=(1 if oc == 1 else 0))
                    aoT_free = mm
                    cpar = qt % 2
                    tev = P.op("vector", lambda h, cpar=cpar: h.tensor_copy(ctr[cpar][:, :], ps_p[:, :]),
                               deps=[mm, ctr_free[cpar]], inc=1)
                    ps_p_free = tev
                    tac = P.dma("gpsimd", P.stream(f"b_ac{cpar}"), lambda h, qt=qt, cpar=cpar:
                        h.dma_start(out=xout_t[qt, :, :], in_=ctr[cpar][:, :], accum_op=ALU.add),
                        deps=[tev, b0_done])
                    ctr_free[cpar] = tac
                    acc_last = tac
            P.barrier()
            P.flush()
    return acc_last



L0AB_W = [("ln_w", [D]), ("ln_b", [D]), ("W_qkv", [D, 3072]), ("b_qkv", [3072]),
          ("W_proj", [D, D]), ("b_proj", [D])]


def build_program_l0_attn_b():
    nc = _mk_prog()
    aps = {}
    aps["xin"] = nc.dram_tensor("xin", [T, D], F32, kind="ExternalInput").ap()
    aps["xout"] = nc.dram_tensor("xout", [T, D], F32, kind="ExternalOutput").ap()
    aps["znT"] = nc.dram_tensor("znT_scr", [128, 8, T], BF16, kind="Internal").ap()
    for nm, shp in L0AB_W:
        aps[nm] = nc.dram_tensor(nm, shp, F32, kind="ExternalInput").ap()
    with contextlib.ExitStack() as es:
        P = Prog(nc, None, es)
        block = es.enter_context(nc.Block())
        P.block = block
        consts = const_setup(P, es)
        build_l0_attn_b(P, es, aps, consts)
        P.finish()
    return nc


def run_sublayer_l0ab(x_full, wd):
    nc = build_program_l0_attn_b()
    in_maps = []
    for c in range(N_CORES):
        m = {"xin": np.ascontiguousarray(x_full[c])}
        m.update(wd)
        in_maps.append(m)
    res = run_bass_kernel_spmd(nc, in_maps, core_ids=list(range(N_CORES)))
    return np.stack([r["xout"] for r in res.results])




def phase_ln_scratch(P, es, aps, consts, store_xy):
    nc = P.nc
    xin, ynT_d = aps["xin"], aps["ynT"]
    xin_t = xin.rearrange("(n p) d -> n p d", p=128)
    with contextlib.ExitStack() as ph:
        ps_t = ph.enter_context(_ps(nc, "c0_ps_t", [128, 1024], BF16))
        xt = ph.enter_context(_sb(nc, "c0_xt", [128, 1024], F32))
        y = ph.enter_context(_sb(nc, "c0_y", [128, 1024], F32))
        yb = ph.enter_context(_sb(nc, "c0_yb", [128, 1024], BF16))
        yTt = ph.enter_context(_sb(nc, "c0_yTt", [128, 8, 128], BF16))
        xy = ph.enter_context(_sb(nc, "c0_xy", [128, 1024], F32))
        scr = dict(
            junk=ph.enter_context(_sb(nc, "c0_junk", [128, 1024], F32)),
            s1=ph.enter_context(_sb(nc, "c0_s1", [128, 1], F32)),
            s2=ph.enter_context(_sb(nc, "c0_s2", [128, 1], F32)),
            s3=ph.enter_context(_sb(nc, "c0_s3", [128, 1], F32)),
            sq=ph.enter_context(_sb(nc, "c0_sq", [128, 1024], F32)),
        )
        if store_xy:
            xout_t = aps["xout"].rearrange("(n p) d -> n p d", p=128)
        prev = {}
        for i in range(NT):
            tl = P.dma("sync", P.stream("c0_x"), lambda h, i=i: h.dma_start(
                out=xt[:, :], in_=xin_t[i, :, :]), deps=[prev.get("lastx")])
            ty = ln_tile(P, xt[:, :], D, scr, 1e-5, y[:, :], x_deps=[tl],
                         out_tok_extra_deps=[prev.get("ywar")])
            tyb = P.op("vector", lambda h: h.tensor_copy(yb[:, :], y[:, :]),
                       deps=[ty, prev.get("tT")], inc=1)
            tT = transpose_tile(P, lambda j: yb[:, j * 128:(j + 1) * 128], ps_t, None, 8,
                                consts["ident_b"], [tyb], psum_free_tok=prev.get("tT"),
                                out_ap=yTt[:, :, :])
            tst1 = P.dma("sync", P.stream("c0_st"), lambda h, i=i: h.dma_start(
                out=ynT_d[:, :, i * 128:(i + 1) * 128], in_=yTt[:, :, :]), deps=[tT])
            toks = dict(tT=tst1, ywar=None, lastx=ty)
            if store_xy:
                txy = P.op("vector", lambda h: h.tensor_tensor(xy[:, :], xt[:, :], y[:, :], ALU.add),
                           deps=[ty, tl, prev.get("xyst")], inc=1)
                tst2 = P.dma("sync", P.stream("c0_xyst"), lambda h, i=i: h.dma_start(
                    out=xout_t[i, :, :], in_=xy[:, :]), deps=[txy])
                toks["xyst"] = tst2
                toks["ywar"] = txy
                toks["lastx"] = txy
                last = tst2
            else:
                toks["ywar"] = tT
                last = tst1
            prev = toks
        P.barrier()
        P.flush()
    return last




def build_l1_attn_a(P, es, aps, consts):
    nc = P.nc
    xout, ynT_d = aps["xout"], aps["ynT"]
    xout_t = xout.rearrange("(n p) d -> n p d", p=128)
    if not aps.get("inplace"):
        tcopy = P.dma("sync", P.stream("c_copy", "bulk"), lambda h: h.dma_start(
            out=aps["xout"], in_=aps["xin"]))
    else:
        tcopy = None
    ln_done = phase_ln_scratch(P, es, aps, consts, store_xy=False)

    acc_last = ln_done
    for hg in range(2):
        with contextlib.ExitStack() as ph:
            QT = ph.enter_context(_sb(nc, f"c_QT{hg}", [128, 4, 4096], BF16))
            KT = ph.enter_context(_sb(nc, f"c_KT{hg}", [128, 4, 4096], BF16))
            Vp = ph.enter_context(_sb(nc, f"c_Vp{hg}", [128, 32, 520], BF16))
            augk = ph.enter_context(_sb(nc, f"c_augk{hg}", [2, 4096], BF16))
            augq = ph.enter_context(_sb(nc, f"c_augq{hg}", [2, 4096], BF16))
            mcols = ph.enter_context(_sb(nc, f"c_mcols{hg}", [128, 32], F32))
            Gb = ph.enter_context(_sb(nc, f"c_Gb{hg}", [128, 8], F32))
            brow = {}
            btoks = []
            for nm in ("bq", "bk", "bv", "bo"):
                brow[nm], tk = load_row(P, f"c_{nm}{hg}", aps[nm], D, ph, BF16)
                btoks.append(tk)
            growg, tgr = load_row(P, f"c_G{hg}", aps["G"], 16, ph, F32)

            with contextlib.ExitStack() as pp:
                znt = [pp.enter_context(_sb(nc, f"c_znt{hg}{j}", [128, 8, 512], BF16))
                       for j in range(2)]
                wq = pp.enter_context(_sb(nc, f"c_wq{hg}", [128, 8, 512], BF16))
                wk = pp.enter_context(_sb(nc, f"c_wk{hg}", [128, 8, 512], BF16))
                wv = pp.enter_context(_sb(nc, f"c_wv{hg}", [128, 8, 512], BF16))
                bq2 = pp.enter_context(_sb(nc, f"c_bq2{hg}", [1, 512], BF16))
                twq = P.dma("gpsimd", P.stream("bulk", "bulk"), lambda h, hg=hg: h.dma_start(
                    out=wq[:, :, :],
                    in_=aps["WqT"][:, hg * 512:hg * 512 + 512].rearrange("(ci p) n -> p ci n", p=128)))
                twk = P.dma("gpsimd", P.stream("bulk", "bulk"), lambda h, hg=hg: h.dma_start(
                    out=wk[:, :, :],
                    in_=aps["WkT"][:, hg * 512:hg * 512 + 512].rearrange("(ci p) n -> p ci n", p=128)))
                twv = P.dma("gpsimd", P.stream("bulk", "bulk"), lambda h, hg=hg: h.dma_start(
                    out=wv[:, :, :],
                    in_=aps["WvT"][:, hg * 512:hg * 512 + 512].rearrange("(ci p) n -> p ci n", p=128)))
                tws = P.op("vector", lambda h: h.tensor_scalar(
                    rearr(wq[:, :, :], "p a b -> p (a b)"), rearr(wq[:, :, :], "p a b -> p (a b)"),
                    -2.0, None, ALU.mult), deps=[twq], inc=1)
                tbs = P.op("vector", lambda h, hg=hg: h.tensor_scalar(
                    bq2[:, :], brow["bq"][:, hg * 512:hg * 512 + 512], -2.0, None, ALU.mult),
                    deps=[btoks[0]], inc=1)
                ps_q = pp.enter_context(_ps(nc, "c_ps_q", [128, 512], F32))
                ps_v = pp.enter_context(_ps(nc, "c_ps_v", [128, 512], F32))
                znt_free = {0: None, 1: None}
                ps_q_free = None
                ps_v_free = None
                qkv_done = []
                for tt in range(8):
                    par = tt % 2
                    tsl = slice(tt * 512, (tt + 1) * 512)
                    tld = P.dma("gpsimd", P.stream(f"c_zl{par}"), lambda h, tsl=tsl, par=par:
                                h.dma_start(out=znt[par][:, :, :], in_=ynT_d[:, :, tsl]),
                                deps=[znt_free[par], ln_done])
                    last_read = None
                    for wsb, wtok, dst, bx in ((wq, tws, QT, 0), (wk, twk, KT, 1)):
                        for ch in range(4):
                            mm = None
                            for ci in range(8):
                                mm = P.op("tensor", lambda h, ch=ch, ci=ci, par=par, wsb=wsb:
                                          h.matmul(ps_q[:, :], wsb[:, ci, ch * 128:(ch + 1) * 128],
                                                   znt[par][:, ci, :], start=(ci == 0), stop=False),
                                          deps=[tld, wtok, ps_q_free], inc=0)
                            if bx == 0:
                                mm = P.op("tensor", lambda h, ch=ch: h.matmul(
                                    ps_q[:, :], bq2[:, ch * 128:(ch + 1) * 128],
                                    consts["ones_b"][:, 0:512], start=False, stop=True),
                                    deps=[tbs], inc=1)
                            else:
                                mm = P.op("tensor", lambda h, ch=ch, hg=hg: h.matmul(
                                    ps_q[:, :], brow["bk"][:, hg * 512 + ch * 128:hg * 512 + (ch + 1) * 128],
                                    consts["ones_b"][:, 0:512], start=False, stop=True),
                                    deps=[btoks[1]], inc=1)
                            tev = P.op("vector", lambda h, ch=ch, tsl=tsl, dst=dst:
                                       h.tensor_copy(dst[:, ch, tsl], ps_q[:, :]), deps=[mm], inc=1)
                            ps_q_free = tev
                            qkv_done.append(tev)
                    for tsub in range(4):
                        kc = tt * 4 + tsub
                        mm = None
                        for ci in range(8):
                            mm = P.op("tensor", lambda h, tsub=tsub, ci=ci, par=par:
                                      h.matmul(ps_v[:, :], znt[par][:, ci, tsub * 128:(tsub + 1) * 128],
                                               wv[:, ci, :], start=(ci == 0), stop=False),
                                      deps=[tld, twv, ps_v_free], inc=0)
                        mm = P.op("tensor", lambda h, hg=hg: h.matmul(
                            ps_v[:, :], consts["ones_b"][:, 0:128],
                            brow["bv"][:, hg * 512:hg * 512 + 512], start=False, stop=True),
                            deps=[btoks[2]], inc=1)
                        vap = Vp[:, kc, :]
                        vout = ap3(vap, [vap.ap[0], [65, 8], [1, 64]])
                        tev = P.op("vector", lambda h, vout=vout: h.tensor_copy(
                            vout, ps_v[:, :].rearrange("p (e f) -> p e f", e=8)), deps=[mm], inc=1)
                        vone = ap3(vap, [vap.ap[0], [65, 8], [1, 1]])
                        vone = bass.AP(vone.tensor, vone.offset + 64, vone.ap)
                        tmo = P.op("vector", lambda h, vone=vone: h.memset(vone, 1.0),
                                   deps=[mm], inc=1)
                        ps_v_free = tev
                        qkv_done.append(tev)
                        qkv_done.append(tmo)
                        last_read = mm
                    znt_free[par] = last_read
                P.barrier()
                P.flush()

            with contextlib.ExitStack() as pa:
                Vm = pa.enter_context(_sb(nc, f"c_Vm{hg}", [128, 32, 65], BF16))
                aoall = pa.enter_context(_sb(nc, f"c_ao{hg}", [128, 32, 512], BF16))
                wo = pa.enter_context(_sb(nc, f"c_wo{hg}", [128, 4, 1024], BF16))
                two = P.dma("gpsimd", P.stream("bulk", "bulk"), lambda h, hg=hg: h.dma_start(
                    out=wo[:, :, :],
                    in_=aps["WoT"][hg * 512:hg * 512 + 512, :].rearrange("(ci p) n -> p ci n", p=128)))
                sc = [pa.enter_context(_ps(nc, f"c_sc{j}", [128, 512], F32)) for j in range(2)]
                av = [pa.enter_context(_ps(nc, f"c_av{j}", [128, 128], F32)) for j in range(2)]
                ps_p = pa.enter_context(_ps(nc, "c_ps_p", [128, 1024], F32))
                ps_t2 = pa.enter_context(_ps(nc, "c_ps_t2", [128, 512], BF16))
                ps_n = pa.enter_context(_ps(nc, "c_ps_n", [1, 512], F32))
                sq_s = pa.enter_context(_sb(nc, "c_sq_s", [128, 4096], BF16))
                rr = pa.enter_context(_sb(nc, "c_rr", [128, 512], F32))
                lnr = pa.enter_context(_sb(nc, "c_lnr", [128, 512], F32))
                attnT = pa.enter_context(_sb(nc, "c_attnT", [128, 4096], BF16))
                q2row = pa.enter_context(_sb(nc, "c_q2row", [1, 4096], BF16))
                aoT = pa.enter_context(_sb(nc, "c_aoT", [128, 512], BF16))
                rcol = pa.enter_context(_sb(nc, "c_rcol", [128, 1], F32))
                ctr = [pa.enter_context(_sb(nc, f"c_ctr{j}", [128, 1024], F32)) for j in range(2)]
                tonce = P.op("gpsimd", lambda h: h.memset(augk[0:2, :], 1.0), inc=1)
                tonce2 = P.op("gpsimd", lambda h: h.memset(augq[0:2, :], 1.0), inc=1)
                sc_free = {0: None, 1: None}
                av_free = {0: None, 1: None}
                attnT_free = None
                rcol_free = None
                ao_free = None
                aoT_free = None
                ctr_free = {0: None, 1: None}
                sq_free = None
                augk_free = None
                augq_free = None
                q2row_free = None
                mcols_free = None
                Vm_free = None
                rr_free = None
                lnr_free = None
                mprev = None
                tgm = P.op("tensor", lambda h, hg=hg: h.matmul(
                    ps_p[:, 0:8], consts["ones_f"][:, 0:128], growg[:, hg * 8:hg * 8 + 8],
                    start=True, stop=True), deps=[tgr] + consts["tok"], inc=1)
                tgb = P.op("vector", lambda h: h.tensor_copy(Gb[:, :], ps_p[:, 0:8]),
                           deps=[tgm], inc=1)
                ps_p_free = tgb
                for hd in range(8):
                    hrow = (hd % 2) * 64
                    hch = hd // 2
                    for which, src_t, scale_sq in ((0, KT, 1.0), (1, QT, 0.5)):
                        tsq = P.op("scalar", lambda h, src_t=src_t, hch=hch, scale_sq=scale_sq:
                                   h.activation(out=sq_s[:, :], in_=src_t[:, hch, :], func=AF.Square,
                                                scale=scale_sq),
                                   deps=[qkv_done[-1], sq_free], inc=1)
                        war = augk_free if which == 0 else q2row_free
                        for tc in range(8):
                            mm = P.op("tensor", lambda h, hrow=hrow, tc=tc: h.matmul(
                                ps_n[:, :], consts["onescol_b"][hrow:hrow + 64, 0:1],
                                sq_s[hrow:hrow + 64, tc * 512:(tc + 1) * 512],
                                start=True, stop=True),
                                deps=[tsq, mprev], inc=1)
                            if which == 0:
                                tcp = P.op("vector", lambda h, tc=tc: h.tensor_copy(
                                    augk[0:1, tc * 512:(tc + 1) * 512], ps_n[:, :]),
                                    deps=[mm, war, tonce, tonce2], inc=1)
                            else:
                                tcp = P.op("vector", lambda h, tc=tc: h.tensor_copy(
                                    q2row[0:1, tc * 512:(tc + 1) * 512], ps_n[:, :]),
                                    deps=[mm, war, tonce, tonce2], inc=1)
                            mprev = tcp
                            war = tcp
                        if which == 0:
                            augk_free = war
                            k2done = war
                        else:
                            tq2d = P.dma("gpsimd", P.stream("c_q2d"), lambda h: h.dma_start(
                                out=augq[1:2, :], in_=q2row[0:1, :]), deps=[war, augq_free])
                            q2row_free = tq2d
                            augq_free = tq2d
                            q2done = tq2d
                        sq_free = mm
                    mmt = None
                    for kc in range(32):
                        mmt = P.op("tensor", lambda h, kc=kc: h.transpose(
                            ps_t2[:, 2 * kc:2 * kc + 1],
                            augk[0:1, kc * 128:(kc + 1) * 128], consts["ident_b"][0:1, 0:1]),
                            deps=[k2done, mcols_free, aoT_free] + consts["tok"],
                            inc=(1 if kc == 31 else 0))
                    tsm0 = P.op("scalar", lambda h: h.activation(
                        out=mcols[:, :], in_=ap3(ps_t2[:, 0:64], [ps_t2[:, 0:64].ap[0], [2, 32]]),
                        func=AF.Sqrt),
                        deps=[mmt], inc=1)
                    tsm = P.op("vector", lambda h, hd=hd: h.tensor_scalar(
                        mcols[:, :], mcols[:, :], Gb[:, hd:hd + 1], None, ALU.mult),
                        deps=[tsm0, tgb], inc=1)
                    tvm = None
                    for kc in range(32):
                        tvm = P.op("vector", lambda h, kc=kc, hd=hd: h.tensor_scalar(
                            Vm[:, kc, :], Vp[:, kc, hd * 65:(hd + 1) * 65], mcols[:, kc:kc + 1],
                            None, ALU.mult),
                            deps=[tsm, Vm_free, qkv_done[-1]], inc=1)
                    mcols_free = tvm
                    vm_ready = tvm
                    vm_last_read = None
                    for qt in range(NT):
                        av_par = qt % 2
                        exp_toks = []
                        for grp in range(8):
                            par = grp % 2
                            mm = None
                            for k4 in range(4):
                                kc = grp * 4 + k4
                                mm = P.op("tensor", lambda h, par=par, k4=k4, kc=kc, hrow=hrow, hch=hch, qt=qt:
                                          h.matmul(sc[par][:, k4 * 128:(k4 + 1) * 128],
                                                   KT[hrow:hrow + 64, hch, kc * 128:(kc + 1) * 128],
                                                   QT[hrow:hrow + 64, hch, qt * 128:(qt + 1) * 128],
                                                   start=True, stop=False),
                                          deps=[sc_free[par], qkv_done[-1], q2done], inc=0)
                                mm = P.op("tensor", lambda h, par=par, k4=k4, kc=kc, qt=qt:
                                          h.matmul(sc[par][:, k4 * 128:(k4 + 1) * 128],
                                                   augk[:, kc * 128:(kc + 1) * 128],
                                                   augq[:, qt * 128:(qt + 1) * 128],
                                                   start=False, stop=True),
                                          deps=[], inc=(1 if k4 == 3 else 0))
                                scores_last = mm
                            tcl = P.op("vector", lambda h, par=par: h.tensor_scalar(
                                rr[:, :], sc[par][:, :], 1e-30, None, ALU.max),
                                deps=[mm, rr_free], inc=1)
                            sc_free[par] = tcl
                            tln_ = P.op("scalar", lambda h: h.activation(out=lnr[:, :], in_=rr[:, :],
                                                                         func=AF.Ln),
                                        deps=[tcl, lnr_free], inc=1)
                            rr_free = tln_
                            tex = P.op("scalar", lambda h, grp=grp: h.activation(
                                out=attnT[:, grp * 512:(grp + 1) * 512], in_=lnr[:, :],
                                func=AF.Exp, scale=-1.0),
                                deps=[tln_, attnT_free], inc=1)
                            lnr_free = tex
                            exp_toks.append(tex)
                        mm = None
                        for kc in range(32):
                            mm = P.op("tensor", lambda h, kc=kc, av_par=av_par:
                                      h.matmul(av[av_par][:, 0:65],
                                               attnT[:, kc * 128:(kc + 1) * 128],
                                               Vm[:, kc, :],
                                               start=(kc == 0), stop=(kc == 31)),
                                      deps=[exp_toks[-1], av_free[av_par], vm_ready], inc=(1 if kc == 31 else 0))
                        attnT_free = mm
                        vm_last_read = mm
                        aug_last_read = scores_last
                        trc = P.op("vector", lambda h, av_par=av_par: h.reciprocal(
                            rcol[:, :], av[av_par][:, 64:65]), deps=[mm, rcol_free], inc=1)
                        tsc = P.op("vector", lambda h, hd=hd, qt=qt, av_par=av_par: h.tensor_scalar(
                            aoall[:, qt, hd * 64:(hd + 1) * 64], av[av_par][:, 0:64], rcol[:, 0:1],
                            None, ALU.mult),
                            deps=[trc, ao_free], inc=1)
                        rcol_free = tsc
                        av_free[av_par] = tsc
                    Vm_free = vm_last_read
                    augk_free = aug_last_read
                    augq_free = aug_last_read
                    ao_done = tsc
                for qt in range(NT):
                    tT = transpose_tile(P, lambda j, qt=qt: aoall[:, qt, j * 128:(j + 1) * 128],
                                        ps_t2, aoT, 4, consts["ident_b"], [ao_done],
                                        psum_free_tok=aoT_free)
                    mm = None
                    for oc in range(2):
                        for ci in range(4):
                            mm = P.op("tensor", lambda h, oc=oc, ci=ci, hg=hg: h.matmul(
                                ps_p[:, oc * 512:(oc + 1) * 512], aoT[:, ci * 128:(ci + 1) * 128],
                                wo[:, ci, oc * 512:(oc + 1) * 512], start=(ci == 0),
                                stop=(ci == 3 and hg == 1)),
                                deps=[tT, two, ps_p_free],
                                inc=(1 if (hg == 1 and ci == 3 and oc == 1) else 0))
                        if hg == 0:
                            mm = P.op("tensor", lambda h, oc=oc: h.matmul(
                                ps_p[:, oc * 512:(oc + 1) * 512], consts["ones_b"][:, 0:128],
                                brow["bo"][:, oc * 512:(oc + 1) * 512], start=False, stop=True),
                                deps=[btoks[3]], inc=(1 if oc == 1 else 0))
                    aoT_free = mm
                    cpar = qt % 2
                    tev = P.op("vector", lambda h, cpar=cpar: h.tensor_copy(ctr[cpar][:, :], ps_p[:, :]),
                               deps=[mm, ctr_free[cpar]], inc=1)
                    ps_p_free = tev
                    tac = P.dma("gpsimd", P.stream(f"c_ac{cpar}"), lambda h, qt=qt, cpar=cpar:
                                h.dma_start(out=xout_t[qt, :, :], in_=ctr[cpar][:, :],
                                            accum_op=ALU.add),
                                deps=[tev, tcopy])
                    ctr_free[cpar] = tac
                    acc_last = tac
            P.barrier()
            P.flush()
    return acc_last


L1AA_W = [("WqT", [D, D]), ("WkT", [D, D]), ("WvT", [D, D]), ("WoT", [D, D]),
          ("bq", [D]), ("bk", [D]), ("bv", [D]), ("bo", [D]), ("G", [16])]


def build_program_l1_attn_a():
    nc = _mk_prog()
    aps = {}
    aps["xin"] = nc.dram_tensor("xin", [T, D], F32, kind="ExternalInput").ap()
    aps["xout"] = nc.dram_tensor("xout", [T, D], F32, kind="ExternalOutput").ap()
    aps["ynT"] = nc.dram_tensor("ynT_scr", [128, 8, T], BF16, kind="Internal").ap()
    for nm, shp in L1AA_W:
        aps[nm] = nc.dram_tensor(nm, shp, F32, kind="ExternalInput").ap()
    with contextlib.ExitStack() as es:
        P = Prog(nc, None, es)
        block = es.enter_context(nc.Block())
        P.block = block
        consts = const_setup(P, es)
        build_l1_attn_a(P, es, aps, consts)
        P.finish()
    return nc


def run_sublayer_l1aa(x_full, wd):
    nc = build_program_l1_attn_a()
    in_maps = []
    for c in range(N_CORES):
        m = {"xin": np.ascontiguousarray(x_full[c])}
        m.update(wd)
        in_maps.append(m)
    res = run_bass_kernel_spmd(nc, in_maps, core_ids=list(range(N_CORES)))
    return np.stack([r["xout"] for r in res.results])



MIDS = [0.25, 0.75, 1.25, 1.75, 2.5, 3.5, 5.0]


def fp4_quant_chunk(P, x_ap, Fc, scr, out_ap, deps, war_deps=()):
    nb = Fc // 16
    bm, sc_, rsc = scr["bm"], scr["sc"], scr["rsc"]
    xn, cc, r1, sg = scr["xn"], scr["c"], scr["r1"], scr["sg"]
    dd = list(deps) + list(war_deps)

    def blk(ap):
        return ap3(ap, [ap.ap[0], [16, ap.ap[1][1] // 16], [1, 16]])
    t1 = P.op("vector", lambda h: h.tensor_reduce(
        bm[:, 0:nb], blk(x_ap), AX.X, ALU.max,
        apply_absolute_value=True), deps=dd, inc=1)
    t2 = P.op("vector", lambda h: h.tensor_scalar(sc_[:, 0:nb], bm[:, 0:nb], 1.0 / 6.0, 1e-12,
                                                  ALU.mult, ALU.max), deps=[t1], inc=1)
    t3 = P.op("vector", lambda h: h.reciprocal(rsc[:, 0:nb], sc_[:, 0:nb]), deps=[t2], inc=1)
    rb = ap3(rsc[:, 0:nb], [rsc[:, 0:nb].ap[0], [1, nb], [0, 16]])
    t4 = P.op("vector", lambda h: h.tensor_tensor(
        blk(xn[:, 0:Fc]), blk(x_ap), rb, ALU.mult), deps=[t3] + dd, inc=1)
    t5 = P.op("scalar", lambda h: h.activation(out=sg[:, 0:Fc], in_=xn[:, 0:Fc], func=AF.Sign),
              deps=[t4], inc=1)
    t6 = P.op("vector", lambda h: h.tensor_tensor(xn[:, 0:Fc], xn[:, 0:Fc], sg[:, 0:Fc], ALU.mult),
              deps=[t5], inc=1)
    tk = P.op("vector", lambda h: h.tensor_scalar(cc[:, 0:Fc], xn[:, 0:Fc], MIDS[0], None, ALU.is_ge),
              deps=[t6], inc=1)
    for k in range(1, 7):
        tk = P.op("vector", lambda h, k=k: h.scalar_tensor_tensor(
            cc[:, 0:Fc], xn[:, 0:Fc], MIDS[k], cc[:, 0:Fc], ALU.is_ge, ALU.add),
            deps=[tk], inc=1)
    t7 = P.op("vector", lambda h: h.tensor_scalar(r1[:, 0:Fc], cc[:, 0:Fc], 4.0, 0.0,
                                                  ALU.subtract, ALU.max), deps=[tk], inc=1)
    t8 = P.op("vector", lambda h: h.tensor_scalar(xn[:, 0:Fc], cc[:, 0:Fc], 6.0, 0.0,
                                                  ALU.subtract, ALU.max), deps=[t7], inc=1)
    t9 = P.op("vector", lambda h: h.tensor_tensor(r1[:, 0:Fc], cc[:, 0:Fc], r1[:, 0:Fc], ALU.add),
              deps=[t8], inc=1)
    t10 = P.op("vector", lambda h: h.scalar_tensor_tensor(
        cc[:, 0:Fc], r1[:, 0:Fc], 0.5, xn[:, 0:Fc], ALU.mult, ALU.add), deps=[t9], inc=1)
    t11 = P.op("vector", lambda h: h.tensor_tensor(cc[:, 0:Fc], cc[:, 0:Fc], sg[:, 0:Fc], ALU.mult),
               deps=[t10], inc=1)
    sb = ap3(sc_[:, 0:nb], [sc_[:, 0:nb].ap[0], [1, nb], [0, 16]])
    t12 = P.op("vector", lambda h: h.tensor_tensor(
        blk(out_ap), blk(cc[:, 0:Fc]), sb, ALU.mult), deps=[t11], inc=1)
    return t12


def mk_fp4_scr(P, ph, tag):
    nc = P.nc
    return dict(
        bm=ph.enter_context(_sb(nc, f"q_bm{tag}", [128, 32], F32)),
        sc=ph.enter_context(_sb(nc, f"q_sc{tag}", [128, 32], F32)),
        rsc=ph.enter_context(_sb(nc, f"q_rsc{tag}", [128, 32], F32)),
        xn=ph.enter_context(_sb(nc, f"q_xn{tag}", [128, 512], F32)),
        c=ph.enter_context(_sb(nc, f"q_c{tag}", [128, 512], F32)),
        r1=ph.enter_context(_sb(nc, f"q_r1{tag}", [128, 512], F32)),
        sg=ph.enter_context(_sb(nc, f"q_sg{tag}", [128, 512], F32)),
    )




def build_l1_ffn_prep(P, es, aps, consts):
    nc = P.nc
    with contextlib.ExitStack() as ph:
        wt = [ph.enter_context(_sb(nc, f"e_wt{j}", [128, 2816], F32)) for j in range(2)]
        wq_ = ph.enter_context(_sb(nc, "e_wq", [128, 2816], BF16))
        ev = ph.enter_context(_sb(nc, "e_ev", [128, 8, 128], BF16))
        ps_t = ph.enter_context(_ps(nc, "e_ps_t", [128, 1024], BF16))
        scr = mk_fp4_scr(P, ph, "w")
        wt_free = {0: None, 1: None}
        q_prev = None
        ev_free = None
        wq_free = None
        st_last = None
        jobs = []
        for (wsrc, wdst, nrows) in ((aps["w_gate"], aps["wgqT"], 2816),
                                    (aps["w_up"], aps["wupqT"], 2816),
                                    (aps["w_down"], aps["wdqT"], 1024)):
            ncols = 2816 if nrows == 1024 else 1024
            for tile in range(nrows // 128):
                jobs.append((wsrc, wdst, tile, ncols))
        for jid, (wsrc, wdst, tile, ncols) in enumerate(jobs):
            par = jid % 2
            tl = P.dma("sync", P.stream(f"e_wl{par}"), lambda h, wsrc=wsrc, tile=tile, ncols=ncols, par=par:
                       h.dma_start(out=wt[par][:, 0:ncols], in_=wsrc[tile * 128:(tile + 1) * 128, :]),
                       deps=[wt_free[par]])
            qts = []
            for c0 in range(0, ncols, 512):
                Fc = min(512, ncols - c0)
                q_prev = fp4_quant_chunk(P, wt[par][:, c0:c0 + Fc], Fc, scr,
                                         wq_[:, c0:c0 + Fc], [tl], [q_prev, wq_free])
                qts.append(q_prev)
            wt_free[par] = q_prev
            nch = ncols // 128
            for g0 in range(0, nch, 8):
                gn = min(8, nch - g0)
                tts = []
                for j in range(gn):
                    tts.append(P.op("tensor", lambda h, j=j, g0=g0: h.transpose(
                        ps_t[:, j * 128:(j + 1) * 128], wq_[:, (g0 + j) * 128:(g0 + j + 1) * 128],
                        consts["ident_b"][:, :]),
                        deps=[qts[-1], ev_free], inc=(1 if j == gn - 1 else 0)))
                tev = P.op("vector", lambda h, gn=gn: h.tensor_copy(
                    ev[:, 0:gn, :], ps_t[:, 0:gn * 128].rearrange("p (j t) -> p j t", j=gn)),
                    deps=[tts[-1]], inc=1)
                wq_free = tts[-1]
                tst = P.dma("sync", P.stream("e_wst"), lambda h, wdst=wdst, tile=tile, g0=g0, gn=gn:
                            h.dma_start(out=wdst[g0 * 128:(g0 + gn) * 128,
                                                 tile * 128:(tile + 1) * 128].rearrange(
                                "(ci p) t -> p ci t", p=128), in_=ev[:, 0:gn, :]),
                            deps=[tev])
                ev_free = tst
                st_last = tst
        P.barrier()
        P.flush()
    return st_last




def build_l1_ffn_call(P, es, aps, consts, tag):
    nc = P.nc
    xin, xout = aps["xin"], aps["xout"]
    xin_t = xin.rearrange("(n p) d -> n p d", p=128)
    xout_t = xout.rearrange("(n p) d -> n p d", p=128)
    with contextlib.ExitStack() as ph:
        wg = ph.enter_context(_sb(nc, f"e_gT{tag}", [128, 8, 2816], BF16))
        wu = ph.enter_context(_sb(nc, f"e_uT{tag}", [128, 8, 2816], BF16))
        wdn = ph.enter_context(_sb(nc, f"e_dT{tag}", [128, 22, 1024], BF16))
        tw1 = P.dma("gpsimd", P.stream("bulk", "bulk"), lambda h: h.dma_start(
            out=wg[:, :, :], in_=aps["wgqT"].rearrange("(ci p) n -> p ci n", p=128)))
        tw2 = P.dma("gpsimd", P.stream("bulk", "bulk"), lambda h: h.dma_start(
            out=wu[:, :, :], in_=aps["wupqT"].rearrange("(ci p) n -> p ci n", p=128)))
        tw3 = P.dma("gpsimd", P.stream("bulk", "bulk"), lambda h: h.dma_start(
            out=wdn[:, :, :], in_=aps["wdqT"].rearrange("(ci p) n -> p ci n", p=128)))
        xt = [ph.enter_context(_sb(nc, f"e_xt{tag}{j}", [128, 1024], F32)) for j in range(2)]
        y = [ph.enter_context(_sb(nc, f"e_y{tag}{j}", [128, 1024], F32)) for j in range(2)]
        yq = [ph.enter_context(_sb(nc, f"e_yq{tag}{j}", [128, 1024], BF16)) for j in range(2)]
        yqT = [ph.enter_context(_sb(nc, f"e_yqT{tag}{j}", [128, 1024], BF16)) for j in range(2)]
        hc = [ph.enter_context(_sb(nc, f"e_hc{tag}{j}", [128, 512], F32)) for j in range(2)]
        hq = ph.enter_context(_sb(nc, f"e_hq{tag}", [128, 512], BF16))
        hqT = ph.enter_context(_sb(nc, f"e_hqT{tag}", [128, 22, 128], BF16))
        sgx = [ph.enter_context(_sb(nc, f"e_sgx{tag}{j}", [128, 512], BF16)) for j in range(2)]
        xn2 = ph.enter_context(_sb(nc, f"e_xn2{tag}", [128, 1024], F32))
        scr = [mk_fp4_scr(P, ph, f"c{tag}{j}") for j in range(2)]
        lnscr = [dict(
            s1=ph.enter_context(_sb(nc, f"e_s1{tag}{j}", [128, 1], F32)),
            s2=ph.enter_context(_sb(nc, f"e_s2{tag}{j}", [128, 1], F32)),
            s3=ph.enter_context(_sb(nc, f"e_s3{tag}{j}", [128, 1], F32)),
        ) for j in range(2)]
        ps_g = [ph.enter_context(_ps(nc, f"e_ps_g{tag}{j}", [128, 512], F32)) for j in range(2)]
        ps_u = [ph.enter_context(_ps(nc, f"e_ps_u{tag}{j}", [128, 512], F32)) for j in range(2)]
        ps_d = ph.enter_context(_ps(nc, f"e_ps_d{tag}", [128, 1024], F32))
        ps_t = ph.enter_context(_ps(nc, f"e_ps_t{tag}", [128, 1024], BF16))

        prev = {}
        q_prev = {0: None, 1: None}
        tst = None
        for i in range(NT):
            pr = i % 2
            tl = P.dma("sync", P.stream(f"e_x{pr}"), lambda h, i=i, pr=pr: h.dma_start(
                out=xt[pr][:, :], in_=xin_t[i, :, :]), deps=[prev.get(f"xread{pr}")])
            ty = ln_tile(P, xt[pr][:, :], D, lnscr[pr], 1e-5, y[pr][:, :], x_deps=[tl],
                         out_tok_extra_deps=[prev.get(f"ywar{pr}")])
            qts = []
            for c0 in (0, 512):
                q_prev[pr] = fp4_quant_chunk(P, y[pr][:, c0:c0 + 512], 512, scr[pr],
                                             yq[pr][:, c0:c0 + 512],
                                             [ty], [q_prev[pr], prev.get(f"yqwar{pr}")])
                qts.append(q_prev[pr])
            tT = transpose_tile(P, lambda j, pr=pr: yq[pr][:, j * 128:(j + 1) * 128], ps_t, yqT[pr],
                                8, consts["ident_b"], [qts[-1]], psum_free_tok=prev.get("ps_t_f"))
            prev["ps_t_f"] = tT
            hq_toks = []
            dmm = None
            for oc in range(6):
                bk = oc % 2
                c0 = oc * 512
                Fc = min(512, 2816 - c0)
                mmg = mmu = None
                for ci in range(8):
                    mmg = P.op("tensor", lambda h, ci=ci, c0=c0, Fc=Fc, bk=bk, pr=pr: h.matmul(
                        ps_g[bk][:, 0:Fc], yqT[pr][:, ci * 128:(ci + 1) * 128], wg[:, ci, c0:c0 + Fc],
                        start=(ci == 0), stop=(ci == 7)),
                        deps=[tT, tw1, prev.get(f"ps_g_free{bk}")], inc=(1 if ci == 7 else 0))
                for ci in range(8):
                    mmu = P.op("tensor", lambda h, ci=ci, c0=c0, Fc=Fc, bk=bk, pr=pr: h.matmul(
                        ps_u[bk][:, 0:Fc], yqT[pr][:, ci * 128:(ci + 1) * 128], wu[:, ci, c0:c0 + Fc],
                        start=(ci == 0), stop=(ci == 7)),
                        deps=[tT, tw2, prev.get(f"ps_u_free{bk}")], inc=(1 if ci == 7 else 0))
                tsi = P.op("scalar", lambda h, Fc=Fc, bk=bk: h.activation(
                    out=sgx[bk][:, 0:Fc], in_=ps_g[bk][:, 0:Fc], func=AF.Silu),
                    deps=[mmg, prev.get(f"sgxwar{bk}")], inc=1)
                tmu = P.op("vector", lambda h, Fc=Fc, bk=bk, pr=pr: h.tensor_tensor(
                    hc[pr][:, 0:Fc], ps_u[bk][:, 0:Fc], sgx[bk][:, 0:Fc], ALU.mult),
                    deps=[tsi, mmu, prev.get(f"hcwar{pr}")], inc=1)
                prev[f"ps_g_free{bk}"] = tsi
                prev[f"ps_u_free{bk}"] = tmu
                prev[f"sgxwar{bk}"] = tmu
                q_prev[pr] = fp4_quant_chunk(P, hc[pr][:, 0:Fc], Fc, scr[pr], hq[:, 0:Fc],
                                             [tmu], [q_prev[pr], prev.get("hqread")])
                prev[f"hcwar{pr}"] = q_prev[pr]
                nch = Fc // 128
                tts = []
                for j in range(nch):
                    tts.append(P.op("tensor", lambda h, j=j: h.transpose(
                        ps_t[:, j * 128:(j + 1) * 128], hq[:, j * 128:(j + 1) * 128],
                        consts["ident_b"][:, :]),
                        deps=[q_prev[pr], prev.get("ps_t_f")], inc=(1 if j == nch - 1 else 0)))
                prev["hqread"] = tts[-1]
                tev = P.op("vector", lambda h, oc=oc, nch=nch: h.tensor_copy(
                    hqT[:, oc * 4:oc * 4 + nch, :],
                    ps_t[:, 0:nch * 128].rearrange("p (j t) -> p j t", j=nch)),
                    deps=[tts[-1], prev.get("hqTwar")], inc=1)
                prev["ps_t_f"] = tev
                hq_toks.append(tev)
            for oc2 in range(2):
                for ci in range(22):
                    dmm = P.op("tensor", lambda h, ci=ci, oc2=oc2: h.matmul(
                        ps_d[:, oc2 * 512:(oc2 + 1) * 512], hqT[:, ci, :],
                        wdn[:, ci, oc2 * 512:(oc2 + 1) * 512],
                        start=(ci == 0), stop=(ci == 21)),
                        deps=[hq_toks[-1], tw3, prev.get("psd")], inc=(1 if ci == 21 and oc2 == 1 else 0))
            prev["hqTwar"] = dmm
            txn = P.op("vector", lambda h, pr=pr: h.tensor_tensor(xn2[:, :], xt[pr][:, :],
                                                                  ps_d[:, :], ALU.add),
                       deps=[dmm, tl, prev.get("xnst")], inc=1)
            prev["psd"] = txn
            prev[f"xread{pr}"] = txn
            prev[f"ywar{pr}"] = qts[-1]
            prev[f"yqwar{pr}"] = tT
            tst = P.dma("sync", P.stream("e_st"), lambda h, i=i: h.dma_start(
                out=xout_t[i, :, :], in_=xn2[:, :]), deps=[txn])
            prev["xnst"] = tst
        P.barrier()
        P.flush()
    return tst


L1FFN_W = [("w_gate", [2816, D]), ("w_up", [2816, D]), ("w_down", [D, 2816])]


def build_program_l1_ffn():
    nc = _mk_prog()
    aps = {}
    aps["xin"] = nc.dram_tensor("xin", [T, D], F32, kind="ExternalInput").ap()
    aps["xout"] = nc.dram_tensor("xout", [T, D], F32, kind="ExternalOutput").ap()
    for nm, shp in L1FFN_W:
        aps[nm] = nc.dram_tensor(nm, shp, F32, kind="ExternalInput").ap()
    aps["wgqT"] = nc.dram_tensor("wgqT", [D, 2816], BF16, kind="Internal").ap()
    aps["wupqT"] = nc.dram_tensor("wupqT", [D, 2816], BF16, kind="Internal").ap()
    aps["wdqT"] = nc.dram_tensor("wdqT", [2816, D], BF16, kind="Internal").ap()
    with contextlib.ExitStack() as es:
        P = Prog(nc, None, es)
        block = es.enter_context(nc.Block())
        P.block = block
        consts = const_setup(P, es)
        build_l1_ffn_prep(P, es, aps, consts)
        build_l1_ffn_call(P, es, aps, consts, tag=0)
        P.finish()
    return nc


def run_sublayer_l1ffn(x_full, wd):
    nc = build_program_l1_ffn()
    in_maps = []
    for c in range(N_CORES):
        m = {"xin": np.ascontiguousarray(x_full[c])}
        m.update(wd)
        in_maps.append(m)
    res = run_bass_kernel_spmd(nc, in_maps, core_ids=list(range(N_CORES)))
    return np.stack([r["xout"] for r in res.results])




def build_l1_attn_b(P, es, aps, consts):
    nc = P.nc
    xout, ynT_d = aps["xout"], aps["ynT"]
    xout_t = xout.rearrange("(n p) d -> n p d", p=128)
    f0_done = phase_ln_scratch(P, es, aps, consts, store_xy=True)

    with contextlib.ExitStack() as ph:
        QT = ph.enter_context(_sb(nc, "f_QT", [128, 4096], BF16))
        KTf = ph.enter_context(_sb(nc, "f_KTf", [128, 4096], BF16))
        KH = [ph.enter_context(_sb(nc, f"f_KH{hd}", [128, 4096], BF16)) for hd in range(8)]
        Vp = ph.enter_context(_sb(nc, "f_Vp", [128, 32, 1032], BF16))
        negB = ph.enter_context(_sb(nc, "f_negB", [128, 8], F32))
        gcol = ph.enter_context(_sb(nc, "f_gcol", [128, 1], F32))
        SEG = ph.enter_context(_sb(nc, "f_SEG", [128, 8], BF16))
        grow, tg1 = load_row(P, "f_gam", aps["gamma"], 1, ph, F32)

        with contextlib.ExitStack() as p0:
            segi = p0.enter_context(_sb(nc, "f_segi", [128, 8], F32))
            segf = p0.enter_context(_sb(nc, "f_segf", [128, 8], F32))
            ps_g0 = p0.enter_context(_ps(nc, "f_ps_g0", [128, 8], F32))
            ti = P.op("gpsimd", lambda h: h.iota(segi[:, :], [[-16, 8]], channel_multiplier=1,
                                                 allow_small_or_imprecise_dtypes=True), inc=1)
            t1 = P.op("vector", lambda h: h.tensor_scalar(segf[:, :], segi[:, :], 0.0, None, ALU.is_ge),
                      deps=[ti], inc=1)
            t2 = P.op("vector", lambda h: h.tensor_scalar(segi[:, :], segi[:, :], 15.0, None, ALU.is_le),
                      deps=[t1], inc=1)
            t3 = P.op("vector", lambda h: h.tensor_tensor(segf[:, :], segf[:, :], segi[:, :], ALU.mult),
                      deps=[t2], inc=1)
            t4 = P.op("vector", lambda h: h.tensor_copy(SEG[:, :], segf[:, :]), deps=[t3], inc=1)
            tm = P.op("tensor", lambda h: h.matmul(ps_g0[:, 0:1], consts["ones_f"][:, 0:128],
                                                   grow[:, :], start=True, stop=True),
                      deps=[tg1] + consts["tok"], inc=1)
            tgc = P.op("vector", lambda h: h.tensor_copy(gcol[:, :], ps_g0[:, 0:1]), deps=[tm], inc=1)
            P.barrier()
            P.flush()

        with contextlib.ExitStack() as pp:
            znt = [pp.enter_context(_sb(nc, f"f_znt{j}", [128, 8, 512], BF16)) for j in range(2)]
            wqv = pp.enter_context(_sb(nc, "f_wq", [128, 8, 128], BF16))
            wkv = pp.enter_context(_sb(nc, "f_wk", [128, 8, 128], BF16))
            wvv = pp.enter_context(_sb(nc, "f_wv", [128, 8, 1024], BF16))
            bqr, tb1 = load_row(P, "f_bq", aps["bq"], 128, pp, BF16)
            bkr, tb2 = load_row(P, "f_bk", aps["bk"], 128, pp, BF16)
            bvr, tb3 = load_row(P, "f_bv", aps["bv"], D, pp, BF16)
            twq = P.dma("gpsimd", P.stream("bulk", "bulk"), lambda h: h.dma_start(
                out=wqv[:, :, :], in_=aps["WqT"].rearrange("(ci p) n -> p ci n", p=128)))
            twk = P.dma("gpsimd", P.stream("bulk", "bulk"), lambda h: h.dma_start(
                out=wkv[:, :, :], in_=aps["WkT"].rearrange("(ci p) n -> p ci n", p=128)))
            twv = P.dma("gpsimd", P.stream("bulk", "bulk"), lambda h: h.dma_start(
                out=wvv[:, :, :], in_=aps["WvT"].rearrange("(ci p) n -> p ci n", p=128)))
            ps_q = pp.enter_context(_ps(nc, "f_ps_q", [128, 512], F32))
            ps_v = pp.enter_context(_ps(nc, "f_ps_v", [128, 512], F32))
            znt_free = {0: None, 1: None}
            ps_q_free = None
            ps_v_free = None
            prod_done = []
            for tt in range(8):
                par = tt % 2
                tsl = slice(tt * 512, (tt + 1) * 512)
                tld = P.dma("gpsimd", P.stream(f"f_zl{par}"), lambda h, tsl=tsl, par=par:
                            h.dma_start(out=znt[par][:, :, :], in_=ynT_d[:, :, tsl]),
                            deps=[znt_free[par], f0_done])
                last_read = None
                for wsb, wtok, brr, btok, dst in ((wqv, twq, bqr, tb1, QT), (wkv, twk, bkr, tb2, KTf)):
                    mm = None
                    for ci in range(8):
                        mm = P.op("tensor", lambda h, ci=ci, par=par, wsb=wsb:
                                  h.matmul(ps_q[:, :], wsb[:, ci, :], znt[par][:, ci, :],
                                           start=(ci == 0), stop=False),
                                  deps=[tld, wtok, ps_q_free], inc=0)
                    mm = P.op("tensor", lambda h, brr=brr: h.matmul(
                        ps_q[:, :], brr[:, :], consts["ones_b"][:, 0:512], start=False, stop=True),
                        deps=[btok], inc=1)
                    tev = P.op("vector", lambda h, tsl=tsl, dst=dst: h.tensor_copy(
                        dst[:, tsl], ps_q[:, :]), deps=[mm], inc=1)
                    ps_q_free = tev
                    prod_done.append(tev)
                    last_read = mm
                for tsub in range(4):
                    kc = tt * 4 + tsub
                    for oc in range(2):
                        mm = None
                        for ci in range(8):
                            mm = P.op("tensor", lambda h, tsub=tsub, ci=ci, par=par, oc=oc:
                                      h.matmul(ps_v[:, :], znt[par][:, ci, tsub * 128:(tsub + 1) * 128],
                                               wvv[:, ci, oc * 512:(oc + 1) * 512],
                                               start=(ci == 0), stop=False),
                                      deps=[tld, twv, ps_v_free], inc=0)
                        mm = P.op("tensor", lambda h, oc=oc: h.matmul(
                            ps_v[:, :], consts["ones_b"][:, 0:128],
                            bvr[:, oc * 512:(oc + 1) * 512], start=False, stop=True),
                            deps=[tb3], inc=1)
                        vap = Vp[:, kc, :]
                        vout = ap3(vap, [vap.ap[0], [129, 4], [1, 128]])
                        vout = bass.AP(vout.tensor, vout.offset + oc * 4 * 129, vout.ap)
                        tev = P.op("vector", lambda h, vout=vout: h.tensor_copy(
                            vout, ps_v[:, :].rearrange("p (e f) -> p e f", e=4)), deps=[mm], inc=1)
                        ps_v_free = tev
                        prod_done.append(tev)
                        last_read = mm
                    vone = ap3(vap, [vap.ap[0], [129, 8], [1, 1]])
                    vone = bass.AP(vone.tensor, vone.offset + 128, vone.ap)
                    tmo = P.op("vector", lambda h, vone=vone: h.memset(vone, 1.0), deps=[mm], inc=1)
                    prod_done.append(tmo)
                znt_free[par] = last_read
            mz = []
            for hd in range(8):
                tz = P.op("gpsimd", lambda h, hd=hd: h.memset(KH[hd][:, :], 0.0),
                          deps=[], inc=1)
                mz.append(tz)
            kh_done = []
            for hd in range(8):
                tcpy = P.dma("gpsimd", P.stream("f_khc"), lambda h, hd=hd: h.dma_start(
                    out=KH[hd][hd * 16:hd * 16 + 16, :], in_=KTf[hd * 16:hd * 16 + 16, :]),
                    deps=[mz[-1], prod_done[-1]])
                kh_done.append(tcpy)
            P.barrier()
            P.flush()

        with contextlib.ExitStack() as pb:
            sq_s = pb.enter_context(_sb(nc, "f_sq_s", [128, 4096], BF16))
            nmx = pb.enter_context(_sb(nc, "f_nmx", [8, 2], F32))
            nrow = pb.enter_context(_sb(nc, "f_nrow", [1, 8], F32))
            ntmp = pb.enter_context(_sb(nc, "f_ntmp", [8, 1], F32))
            ps_sg = pb.enter_context(_ps(nc, "f_ps_sg", [8, 512], F32))
            ps_br = pb.enter_context(_ps(nc, "f_ps_br", [128, 8], F32))
            ps_tr = pb.enter_context(_ps(nc, "f_ps_tr", [1, 8], F32))
            nz = P.op("gpsimd", lambda h: h.memset(nmx[:, :], 0.0), inc=1)
            sq_free = None
            mprev = None
            ntmp_free = None
            for which, src_t in ((0, QT), (1, KTf)):
                tsq = P.op("scalar", lambda h, src_t=src_t: h.activation(
                    out=sq_s[:, :], in_=src_t[:, :], func=AF.Square),
                    deps=[prod_done[-1], sq_free], inc=1)
                acc = nz
                for tc in range(8):
                    mm = P.op("tensor", lambda h, tc=tc: h.matmul(
                        ps_sg[:, :], SEG[:, :], sq_s[:, tc * 512:(tc + 1) * 512],
                        start=True, stop=True),
                        deps=[tsq, mprev], inc=1)
                    tmx = P.op("vector", lambda h: h.reduce_max(ntmp[:, :], ps_sg[:, :], axis=AX.X),
                               deps=[mm, ntmp_free], inc=1)
                    acc = P.op("vector", lambda h, which=which: h.tensor_tensor(
                        nmx[:, which:which + 1], nmx[:, which:which + 1], ntmp[:, :], ALU.max),
                        deps=[tmx, acc], inc=1)
                    ntmp_free = acc
                    mprev = tmx
                sq_free = mm
            tb1_ = P.op("vector", lambda h: h.tensor_tensor(nmx[:, 0:1], nmx[:, 0:1], nmx[:, 1:2],
                                                            ALU.mult), deps=[acc], inc=1)
            tb2_ = P.op("scalar", lambda h: h.activation(out=nmx[:, 0:1], in_=nmx[:, 0:1],
                                                         func=AF.Sqrt), deps=[tb1_], inc=1)
            tb3_ = P.op("vector", lambda h: h.tensor_scalar(nmx[:, 0:1], nmx[:, 0:1], -0.25, None,
                                                            ALU.mult), deps=[tb2_], inc=1)
            tb4_ = P.op("tensor", lambda h: h.transpose(ps_tr[:, :], nmx[:, 0:1],
                                                        consts["ident_f"][0:8, 0:8]),
                        deps=[tb3_] + consts["tok"], inc=1)
            tb5_ = P.op("vector", lambda h: h.tensor_copy(nrow[:, :], ps_tr[:, :]), deps=[tb4_], inc=1)
            tb6_ = P.op("tensor", lambda h: h.matmul(ps_br[:, :], consts["ones_f"][:, 0:128],
                                                     nrow[:, :], start=True, stop=True),
                        deps=[tb5_], inc=1)
            tb7_ = P.op("vector", lambda h: h.tensor_copy(negB[:, :], ps_br[:, :]), deps=[tb6_], inc=1)
            P.barrier()
            P.flush()

        with contextlib.ExitStack() as pa:
            sc = [pa.enter_context(_ps(nc, f"f_sc{j}", [128, 512], F32)) for j in range(2)]
            av = [pa.enter_context(_ps(nc, f"f_av{j}", [128, 129], F32)) for j in range(2)]
            attnT = pa.enter_context(_sb(nc, "f_attnT", [128, 4096], BF16))
            absb = pa.enter_context(_sb(nc, "f_absb", [128, 512], F32))
            usb = pa.enter_context(_sb(nc, "f_usb", [128, 512], F32))
            ao = pa.enter_context(_sb(nc, "f_ao", [128, 1024], F32))
            rcol = pa.enter_context(_sb(nc, "f_rcol", [128, 1], F32))
            ctr = [pa.enter_context(_sb(nc, f"f_ctr{j}", [128, 1024], F32)) for j in range(2)]
            sc_free = {0: None, 1: None}
            av_free = {0: None, 1: None}
            attnT_free = None
            rcol_free = None
            ao_free = None
            abs_free = None
            u_free = None
            ctr_free = {0: None, 1: None}
            for qt in range(NT):
                ao_toks = []
                for hd in range(8):
                    exp_toks = []
                    av_par = hd % 2
                    for grp in range(8):
                        par = grp % 2
                        mm = None
                        for k4 in range(4):
                            kc = grp * 4 + k4
                            mm = P.op("tensor", lambda h, par=par, k4=k4, kc=kc, hd=hd, qt=qt:
                                      h.matmul(sc[par][:, k4 * 128:(k4 + 1) * 128],
                                               KH[hd][:, kc * 128:(kc + 1) * 128],
                                               QT[:, qt * 128:(qt + 1) * 128],
                                               start=True, stop=True),
                                      deps=[sc_free[par], kh_done[-1], prod_done[-1]],
                                      inc=(1 if k4 == 3 else 0))
                        tab = P.op("scalar", lambda h, par=par: h.activation(
                            out=absb[:, :], in_=sc[par][:, :], func=AF.Abs, scale=0.25),
                            deps=[mm, abs_free], inc=1)
                        tu = P.op("vector", lambda h, par=par: h.scalar_tensor_tensor(
                            usb[:, :], absb[:, :], 8.0 / 3.0, sc[par][:, :], ALU.mult, ALU.add),
                            deps=[tab, u_free], inc=1)
                        sc_free[par] = tu
                        abs_free = tu
                        tex = P.op("scalar", lambda h, grp=grp, hd=hd: h.activation(
                            out=attnT[:, grp * 512:(grp + 1) * 512], in_=usb[:, :],
                            func=AF.Exp, bias=negB[:, hd:hd + 1], scale=0.15),
                            deps=[tu, tb7_, attnT_free], inc=1)
                        u_free = tex
                        exp_toks.append(tex)
                    mm = None
                    for kc in range(32):
                        mm = P.op("tensor", lambda h, kc=kc, hd=hd, av_par=av_par:
                                  h.matmul(av[av_par][:, 0:129],
                                           attnT[:, kc * 128:(kc + 1) * 128],
                                           Vp[:, kc, hd * 129:(hd + 1) * 129],
                                           start=(kc == 0), stop=(kc == 31)),
                                  deps=[exp_toks[-1], av_free[av_par], prod_done[-1]],
                                  inc=(1 if kc == 31 else 0))
                    attnT_free = mm
                    trc = P.op("vector", lambda h, av_par=av_par: h.reciprocal(
                        rcol[:, :], av[av_par][:, 128:129]), deps=[mm, rcol_free], inc=1)
                    tsc = P.op("vector", lambda h, hd=hd, av_par=av_par: h.tensor_scalar(
                        ao[:, hd * 128:(hd + 1) * 128], av[av_par][:, 0:128], rcol[:, 0:1],
                        None, ALU.mult),
                        deps=[trc, ao_free], inc=1)
                    rcol_free = tsc
                    av_free[av_par] = tsc
                    ao_toks.append(tsc)
                cpar = qt % 2
                tev = P.op("vector", lambda h, cpar=cpar: h.tensor_scalar(
                    ctr[cpar][:, :], ao[:, :], gcol[:, 0:1], None, ALU.mult),
                    deps=[ao_toks[-1], tgc, ctr_free[cpar]], inc=1)
                ao_free = tev
                tac = P.dma("gpsimd", P.stream(f"f_ac{cpar}"), lambda h, qt=qt, cpar=cpar:
                            h.dma_start(out=xout_t[qt, :, :], in_=ctr[cpar][:, :], accum_op=ALU.add),
                            deps=[tev, f0_done])
                ctr_free[cpar] = tac
                acc_last = tac
        P.barrier()
        P.flush()
    return acc_last


L1AB_W = [("WqT", [D, 128]), ("WkT", [D, 128]), ("WvT", [D, D]),
          ("bq", [128]), ("bk", [128]), ("bv", [D]), ("gamma", [1])]


def build_program_l1_attn_b():
    nc = _mk_prog()
    aps = {}
    aps["xin"] = nc.dram_tensor("xin", [T, D], F32, kind="ExternalInput").ap()
    aps["xout"] = nc.dram_tensor("xout", [T, D], F32, kind="ExternalOutput").ap()
    aps["ynT"] = nc.dram_tensor("ynT_scr", [128, 8, T], BF16, kind="Internal").ap()
    for nm, shp in L1AB_W:
        aps[nm] = nc.dram_tensor(nm, shp, F32, kind="ExternalInput").ap()
    with contextlib.ExitStack() as es:
        P = Prog(nc, None, es)
        block = es.enter_context(nc.Block())
        P.block = block
        consts = const_setup(P, es)
        build_l1_attn_b(P, es, aps, consts)
        P.finish()
    return nc


def run_sublayer_l1ab(x_full, wd):
    nc = build_program_l1_attn_b()
    in_maps = []
    for c in range(N_CORES):
        m = {"xin": np.ascontiguousarray(x_full[c])}
        m.update(wd)
        in_maps.append(m)
    res = run_bass_kernel_spmd(nc, in_maps, core_ids=list(range(N_CORES)))
    return np.stack([r["xout"] for r in res.results])




_PROG_CACHE = {}


def _prog(name, builder):
    if name not in _PROG_CACHE:
        _PROG_CACHE[name] = builder()
    return _PROG_CACHE[name]


def _launch(nc, x_full, wd):
    in_maps = []
    for c in range(N_CORES):
        m = {"xin": np.ascontiguousarray(x_full[c])}
        m.update(wd)
        in_maps.append(m)
    res = run_bass_kernel_spmd(nc, in_maps, core_ids=list(range(N_CORES)))
    return np.stack([r["xout"] for r in res.results])


def kernel(**ins):
    x = np.ascontiguousarray(ins["L0_attn_a_x"])
    x = x.reshape(8, T, D)

    wd_l0aa = {nm: ins["L0_attn_a_" + nm] for nm, _ in L0AA_W}
    wd_l0ffn = prep_l0ffn_weights(ins)
    wd_l0ab = {"ln_w": ins["L0_attn_b_ln_w"], "ln_b": ins["L0_attn_b_ln_b"],
               "W_qkv": ins["L0_attn_b_W_qkv"], "b_qkv": ins["L0_attn_b_b_qkv"],
               "W_proj": ins["L0_attn_b_W_proj"], "b_proj": ins["L0_attn_b_b_proj"]}
    wd_l1aa = {"WqT": np.ascontiguousarray(ins["L1_attn_a_Wq"].T),
               "WkT": np.ascontiguousarray(ins["L1_attn_a_Wk"].T),
               "WvT": np.ascontiguousarray(ins["L1_attn_a_Wv"].T),
               "WoT": np.ascontiguousarray(ins["L1_attn_a_Wo"].T),
               "bq": ins["L1_attn_a_bq"], "bk": ins["L1_attn_a_bk"],
               "bv": ins["L1_attn_a_bv"], "bo": ins["L1_attn_a_bo"],
               "G": ins["L1_attn_a_G"]}
    wd_l1ffn = {"w_gate": ins["L1_ffn_w_gate"], "w_up": ins["L1_ffn_w_up"],
                "w_down": ins["L1_ffn_w_down"]}
    wd_l1ab = {"WqT": np.ascontiguousarray(ins["L1_attn_b_Wq"].T),
               "WkT": np.ascontiguousarray(ins["L1_attn_b_Wk"].T),
               "WvT": np.ascontiguousarray(ins["L1_attn_b_Wv"].T),
               "bq": ins["L1_attn_b_bq"], "bk": ins["L1_attn_b_bk"],
               "bv": ins["L1_attn_b_bv"], "gamma": ins["L1_attn_b_gamma"]}

    x = _launch(_prog("l0aa", build_program_l0_attn_a), x, wd_l0aa)
    x = _launch(_prog("l0ffn", build_program_l0_ffn), x, wd_l0ffn)
    x = _launch(_prog("l0ab", build_program_l0_attn_b), x, wd_l0ab)
    x = _launch(_prog("l0ffn", build_program_l0_ffn), x, wd_l0ffn)
    x = _launch(_prog("l1aa", build_program_l1_attn_a), x, wd_l1aa)
    x = _launch(_prog("l1ffn", build_program_l1_ffn), x, wd_l1ffn)
    x = _launch(_prog("l1ab", build_program_l1_attn_b), x, wd_l1ab)
    x = _launch(_prog("l1ffn", build_program_l1_ffn), x, wd_l1ffn)
    return x




FUSED = True


def build_program_fused():
    nc = _mk_prog()
    X = nc.dram_tensor("xin", [T, D], F32, kind="ExternalInput").ap()
    XOUT = nc.dram_tensor("xout", [T, D], F32, kind="ExternalOutput").ap()
    B0 = nc.dram_tensor("xb0", [T, D], F32, kind="Internal").ap()
    B1 = nc.dram_tensor("xb1", [T, D], F32, kind="Internal").ap()
    x2buf = nc.dram_tensor("x2buf", [T, D], F32, kind="Internal").ap()
    znT = nc.dram_tensor("znT_scr", [128, 8, T], BF16, kind="Internal").ap()
    wgqT = nc.dram_tensor("wgqT", [D, 2816], BF16, kind="Internal").ap()
    wupqT = nc.dram_tensor("wupqT", [D, 2816], BF16, kind="Internal").ap()
    wdqT = nc.dram_tensor("wdqT", [2816, D], BF16, kind="Internal").ap()

    def w_aps(prefix, spec):
        d = {}
        for nm, shp in spec:
            d[nm] = nc.dram_tensor(prefix + nm, shp, F32, kind="ExternalInput").ap()
        return d

    a_l0aa = w_aps("l0aa_", L0AA_W)
    a_l0ffn = w_aps("l0ffn_", L0FFN_W)
    a_l0ab = w_aps("l0ab_", L0AB_W)
    a_l1aa = w_aps("l1aa_", L1AA_W)
    a_l1ffn = w_aps("l1ffn_", L1FFN_W)
    a_l1ab = w_aps("l1ab_", L1AB_W)

    with contextlib.ExitStack() as es:
        P = Prog(nc, None, es)
        block = es.enter_context(nc.Block())
        P.block = block
        consts = const_setup(P, es)
        build_l1_ffn_prep(P, es, dict(a_l1ffn, wgqT=wgqT, wupqT=wupqT, wdqT=wdqT), consts)
        build_l0_attn_a(P, es, dict(a_l0aa, xin=X, x2buf=x2buf, xout=B0), consts)
        build_l0_ffn(P, es, dict(a_l0ffn, xin=B0, xout=B1), consts)
        build_l0_attn_b(P, es, dict(a_l0ab, xin=B1, xout=B0, znT=znT), consts)
        build_l0_ffn(P, es, dict(a_l0ffn, xin=B0, xout=B1), consts)
        build_l1_attn_a(P, es, dict(a_l1aa, xin=B1, xout=B0, ynT=znT), consts)
        build_l1_ffn_call(P, es, dict(a_l1ffn, xin=B0, xout=B1, wgqT=wgqT, wupqT=wupqT,
                                      wdqT=wdqT), consts, tag=0)
        build_l1_attn_b(P, es, dict(a_l1ab, xin=B1, xout=B0, ynT=znT), consts)
        build_l1_ffn_call(P, es, dict(a_l1ffn, xin=B0, xout=XOUT, wgqT=wgqT, wupqT=wupqT,
                                      wdqT=wdqT), consts, tag=1)
        P.finish()
    return nc


def kernel_fused(**ins):
    x = np.ascontiguousarray(ins["L0_attn_a_x"]).reshape(8, T, D)
    wd = {}
    for nm, _ in L0AA_W:
        wd["l0aa_" + nm] = ins["L0_attn_a_" + nm]
    for nm, v in prep_l0ffn_weights(ins).items():
        wd["l0ffn_" + nm] = v
    wd.update({"l0ab_ln_w": ins["L0_attn_b_ln_w"], "l0ab_ln_b": ins["L0_attn_b_ln_b"],
               "l0ab_W_qkv": ins["L0_attn_b_W_qkv"], "l0ab_b_qkv": ins["L0_attn_b_b_qkv"],
               "l0ab_W_proj": ins["L0_attn_b_W_proj"], "l0ab_b_proj": ins["L0_attn_b_b_proj"]})
    wd.update({"l1aa_WqT": np.ascontiguousarray(ins["L1_attn_a_Wq"].T),
               "l1aa_WkT": np.ascontiguousarray(ins["L1_attn_a_Wk"].T),
               "l1aa_WvT": np.ascontiguousarray(ins["L1_attn_a_Wv"].T),
               "l1aa_WoT": np.ascontiguousarray(ins["L1_attn_a_Wo"].T),
               "l1aa_bq": ins["L1_attn_a_bq"], "l1aa_bk": ins["L1_attn_a_bk"],
               "l1aa_bv": ins["L1_attn_a_bv"], "l1aa_bo": ins["L1_attn_a_bo"],
               "l1aa_G": ins["L1_attn_a_G"]})
    wd.update({"l1ffn_w_gate": ins["L1_ffn_w_gate"], "l1ffn_w_up": ins["L1_ffn_w_up"],
               "l1ffn_w_down": ins["L1_ffn_w_down"]})
    wd.update({"l1ab_WqT": np.ascontiguousarray(ins["L1_attn_b_Wq"].T),
               "l1ab_WkT": np.ascontiguousarray(ins["L1_attn_b_Wk"].T),
               "l1ab_WvT": np.ascontiguousarray(ins["L1_attn_b_Wv"].T),
               "l1ab_bq": ins["L1_attn_b_bq"], "l1ab_bk": ins["L1_attn_b_bk"],
               "l1ab_bv": ins["L1_attn_b_bv"], "l1ab_gamma": ins["L1_attn_b_gamma"]})
    nc = _prog("fused", build_program_fused)
    in_maps = []
    for c in range(N_CORES):
        m = {"xin": np.ascontiguousarray(x[c])}
        m.update(wd)
        in_maps.append(m)
    res = run_bass_kernel_spmd(nc, in_maps, core_ids=list(range(N_CORES)))
    return np.stack([r["xout"] for r in res.results])


_kernel_unfused = kernel


def kernel(**ins):
    if FUSED:
        return kernel_fused(**ins)
    return _kernel_unfused(**ins)


# version 09
# speedup vs baseline: 1.1735x; 1.1688x over previous
import contextlib
import math
import numpy as np

import concourse.bass as bass
import concourse.mybir as mybir
from concourse.bass_utils import run_bass_kernel_spmd

F32 = mybir.dt.float32
BF16 = mybir.dt.bfloat16
AF = mybir.ActivationFunctionType
ALU = mybir.AluOpType
AX = mybir.AxisListType

D = 1024
T = 4096
NT = 32
N_CORES = 8

ENGS = ("tensor", "vector", "scalar", "gpsimd", "sync")
DEBUG_L0AB_HGS = 2
DEBUG_L0AB_NQT = NT
DEBUG_L0AB_SKIP = set()
DEBUG_L0AB_BSTEP = 9
DEBUG_L0AB_TCN = 8
DEBUG_L0AB_NOCOPY = False

_NAME_N = [0]


def _sb(nc, name, shape, dtype):
    _NAME_N[0] += 1
    return nc.sbuf_tensor(f"{name}_{_NAME_N[0]}", shape, dtype)


def _ps(nc, name, shape, dtype):
    _NAME_N[0] += 1
    return nc.psum_tensor(f"{name}_{_NAME_N[0]}", shape, dtype)



class Stream:

    def __init__(self, sem, name, mode):
        self.sem = sem
        self.name = name
        self.mode = mode
        self.cnt = 0
        self.waited_epoch = False


class Prog:

    def __init__(self, nc, block, es):
        self.nc = nc
        self.block = block
        self.es = es
        self.pend = {e: [] for e in ENGS}
        self.cnt = {e: 0 for e in ENGS}
        self.waited = {e: {} for e in ENGS}
        self.sem = {e: es.enter_context(nc.semaphore(f"sem_{e}")) for e in ENGS}
        self.streams = {}

    def stream(self, name, mode="exact"):
        s = self.streams.get(name)
        if s is None:
            s = Stream(self.es.enter_context(self.nc.semaphore(f"st_{name}")), name, mode)
            self.streams[name] = s
        assert s.mode == mode, name
        return s

    def _emit_waits(self, eng, deps):
        for d in deps:
            if d is None:
                continue
            if isinstance(d[0], Stream):
                s, val = d
                if s.mode == "exact":
                    assert val == s.cnt, f"stream {s.name}: wait {val} != latest {s.cnt}"
                else:
                    val = s.cnt
                s.waited_epoch = True
                w = self.waited[eng]
                key = ("S", s.name)
                if w.get(key, 0) < val:
                    w[key] = val
                    self.pend[eng].append(lambda h, ss=s.sem, v=val: h.wait_ge(ss, v))
            else:
                pe, val = d
                if val <= 0:
                    continue
                if pe == eng and val > self.cnt[eng]:
                    raise AssertionError(f"same-engine dep on future op {pe} {val} > {self.cnt[eng]}")
                w = self.waited[eng]
                if w.get(pe, 0) < val:
                    w[pe] = val
                    sem = self.sem[pe]
                    self.pend[eng].append(lambda h, s=sem, v=val: h.wait_ge(s, v))

    def op(self, eng, fn, deps=(), inc=0):
        self._emit_waits(eng, deps)
        if inc:
            self.cnt[eng] += inc
            sem = self.sem[eng]
            self.pend[eng].append(lambda h, f=fn, s=sem, a=inc: f(h).then_inc(s, a))
            return (eng, self.cnt[eng])
        else:
            self.pend[eng].append(fn)
            return (eng, self.cnt[eng])

    def dma(self, eng, stream, fn, deps=()):
        if stream.mode == "bulk" and stream.waited_epoch:
            raise AssertionError(f"bulk stream {stream.name}: DMA after wait (need barrier)")
        self._emit_waits(eng, deps)
        stream.cnt += 16
        self.pend[eng].append(lambda h, f=fn, s=stream.sem: f(h).then_inc(s, 16))
        return (stream, stream.cnt)

    def tok(self, eng):
        return (eng, self.cnt[eng])

    def barrier(self):
        toks = [(e, self.cnt[e]) for e in ENGS if self.cnt[e] > 0]
        toks += [(s, s.cnt) for s in self.streams.values() if s.cnt > 0]
        for e in ENGS:
            self._save = e
            for d in toks:
                if not isinstance(d[0], Stream) and d[0] == e:
                    continue
                self._emit_waits(e, [d])
        for s in self.streams.values():
            s.waited_epoch = False

    def flush(self):
        for e in ENGS:
            q = self.pend[e]
            if not q:
                continue
            self.pend[e] = []
            body = getattr(self.block, e)

            def run(h, q=q):
                for f in q:
                    f(h)
            body(run)

    def finish(self):
        self.barrier()
        self.flush()




def ap3(base_ap, dims):
    return bass.AP(base_ap.tensor, base_ap.offset, dims)


def rearr(ap, spec, **kw):
    return ap.rearrange(spec, **kw)


class Phase:

    def __init__(self, P, es):
        self.P = P
        self.es = es


def const_setup(P, es):
    nc = P.nc
    c = {}
    c["ones_b"] = es.enter_context(_sb(nc, "c_ones_b", [1, 512], BF16))
    c["ones_f"] = es.enter_context(_sb(nc, "c_ones_f", [1, 512], F32))
    c["onescol_b"] = es.enter_context(_sb(nc, "c_onescol_b", [128, 1], BF16))
    c["onescol_f"] = es.enter_context(_sb(nc, "c_onescol_f", [128, 1], F32))
    c["ident_b"] = es.enter_context(_sb(nc, "c_ident_b", [128, 128], BF16))
    c["ident_f"] = es.enter_context(_sb(nc, "c_ident_f", [128, 128], F32))
    io = es.enter_context(_sb(nc, "c_iota_fp", [128, 128], F32))
    t1 = P.op("gpsimd", lambda h: h.memset(c["ones_b"][:, :], 1.0), inc=1)
    t2 = P.op("gpsimd", lambda h: h.memset(c["ones_f"][:, :], 1.0), inc=1)
    t2b = P.op("gpsimd", lambda h: h.memset(c["onescol_b"][:, :], 1.0), inc=1)
    t2c = P.op("gpsimd", lambda h: h.memset(c["onescol_f"][:, :], 1.0), inc=1)
    t3 = P.op("gpsimd", lambda h: h.iota(io[:, :], [[1, 128]], channel_multiplier=-1,
                                         allow_small_or_imprecise_dtypes=True), inc=1)
    t4 = P.op("vector", lambda h: h.tensor_scalar(c["ident_f"][:, :], io[:, :], 0.0, None, ALU.is_equal),
              deps=[t3], inc=1)
    t5 = P.op("vector", lambda h: h.tensor_copy(c["ident_b"][:, :], c["ident_f"][:, :]),
              deps=[t4], inc=1)
    c["iota_fp"] = io
    c["tok"] = [t1, t2, t2b, t2c, t4, t5]
    return c


def load_w_bf16(P, name, dram_ap, shape_sb, es, rearrange=None, stream=None):
    t = es.enter_context(_sb(P.nc, name, shape_sb, BF16))
    src = dram_ap if rearrange is None else dram_ap.rearrange(*rearrange[0], **rearrange[1])
    tok = P.dma("gpsimd", stream if stream is not None else P.stream("bulk", "bulk"),
                lambda h: h.dma_start(out=t[...], in_=src))
    return t, tok


def load_row(P, name, dram_vec_ap, n, es, dtype=F32, stream=None):
    t = es.enter_context(_sb(P.nc, name, [1, n], dtype))
    src = dram_vec_ap.rearrange("(a n) -> a n", a=1)
    eng = "gpsimd" if dtype != F32 else "sync"
    default = "bulk" if eng == "gpsimd" else "bulks"
    tok = P.dma(eng, stream if stream is not None else P.stream(default, "bulk"),
                lambda h: h.dma_start(out=t[:, :], in_=src))
    return t, tok


def bcast_row_f32(P, row_ap, n, psum, out_sb, ones_f, deps):
    nchunks = (n + 511) // 512
    mts = []
    for j in range(nchunks):
        lo, hi = j * 512, min(n, (j + 1) * 512)
        mts.append(P.op("tensor",
                        lambda h, lo=lo, hi=hi: h.matmul(psum[:, lo:hi], ones_f[:, 0:128],
                                                         row_ap[:, lo:hi], start=True, stop=True),
                        deps=deps, inc=(1 if hi == n else 0)))
    tok = P.op("vector", lambda h: h.tensor_copy(out_sb[:, 0:n], psum[:, 0:n]), deps=[mts[-1]], inc=1)
    return tok


def ln_tile(P, x_ap, F, scr, eps, out_ap, out_tok_extra_deps=(), scale_row=None, bias_row=None,
            x_deps=(), out_dtype_pass=None):
    junk, s1, s2, s3, sq = scr["junk"], scr["s1"], scr["s2"], scr["s3"], scr["sq"]
    t1 = P.op("vector", lambda h: h.tensor_scalar(junk[:, 0:F], x_ap, 1.0, 0.0, ALU.mult, ALU.add,
                                                  accum_out=s1[:, :]), deps=list(x_deps), inc=1)
    t2 = P.op("vector", lambda h: h.tensor_scalar(s1[:, :], s1[:, :], -1.0 / F, None, ALU.mult),
              deps=[t1], inc=1)
    t3 = P.op("scalar", lambda h: h.activation(out=sq[:, 0:F], in_=x_ap, func=AF.Square,
                                               bias=s1[:, 0:1], scale=1.0, accum_out=s2[:, :]),
              deps=[t2] + list(x_deps), inc=1)
    t4 = P.op("vector", lambda h: h.tensor_scalar(s2[:, :], s2[:, :], 1.0 / F, eps, ALU.mult, ALU.add),
              deps=[t3], inc=1)
    t5 = P.op("scalar", lambda h: h.activation(out=s2[:, :], in_=s2[:, :], func=AF.Sqrt),
              deps=[t4], inc=1)
    t6 = P.op("vector", lambda h: h.reciprocal(s2[:, :], s2[:, :]), deps=[t5], inc=1)
    t7 = P.op("vector", lambda h: h.tensor_mul(s3[:, :], s1[:, :], s2[:, :]), deps=[t6, t2], inc=1)
    t8 = P.op("scalar", lambda h: h.activation(out=out_ap, in_=x_ap, func=AF.Identity,
                                               bias=s3[:, 0:1], scale=s2[:, 0:1]),
              deps=[t7] + list(x_deps) + list(out_tok_extra_deps), inc=1)
    return t8


def transpose_tile(P, src_bf16_ap_fn, ps_t, out_sb, nchunks, ident_b, deps, psum_free_tok=None,
                   out_ap=None):
    tts = []
    dd = list(deps) + ([psum_free_tok] if psum_free_tok else [])
    for j in range(nchunks):
        tts.append(P.op("tensor",
                        lambda h, j=j: h.transpose(ps_t[:, j * 128:(j + 1) * 128],
                                                   src_bf16_ap_fn(j), ident_b[:, :]),
                        deps=dd, inc=(1 if j == nchunks - 1 else 0)))
    if out_ap is not None:
        ev = P.op("vector", lambda h: h.tensor_copy(
            out_ap, ps_t[:, 0:nchunks * 128].rearrange("p (j t) -> p j t", j=nchunks)),
            deps=[tts[-1]], inc=1)
    else:
        ev = P.op("vector", lambda h: h.tensor_copy(out_sb[:, 0:nchunks * 128], ps_t[:, 0:nchunks * 128]),
                  deps=[tts[-1]], inc=1)
    return ev




def build_l0_attn_a(P, es, aps, consts):
    nc = P.nc
    xin, x2buf, xout = aps["xin"], aps["x2buf"], aps["xout"]

    with contextlib.ExitStack() as ph:
        wq, tw1 = load_w_bf16(P, "wq", aps["Wq"], [128, 8, 1024], ph,
                              rearrange=(("(ci p) o -> p ci o",), dict(p=128)))
        wk, tw2 = load_w_bf16(P, "wk", aps["Wk"], [128, 8, 1024], ph,
                              rearrange=(("(ci p) o -> p ci o",), dict(p=128)))
        wv, tw3 = load_w_bf16(P, "wv", aps["Wv"], [128, 8, 1024], ph,
                              rearrange=(("(ci p) o -> p ci o",), dict(p=128)))
        wg, tw4 = load_w_bf16(P, "wg", aps["Wg"], [128, 8, 1024], ph,
                              rearrange=(("(ci p) o -> p ci o",), dict(p=128)))
        wo, tw5 = load_w_bf16(P, "wo", aps["Wo"], [128, 8, 1024], ph,
                              rearrange=(("(ci p) o -> p ci o",), dict(p=128)))
        wtoks = [tw1, tw2, tw3, tw4, tw5]
        rows = {}
        rtoks = []
        for nm in ("bq", "bk", "bv", "bg", "bo"):
            rows[nm], tk = load_row(P, "r_" + nm, aps[nm], D, ph, BF16)
            rtoks.append(tk)
        for nm in ("ln1_g", "ln1_b"):
            rows[nm], tk = load_row(P, "r_" + nm, aps[nm], D, ph, F32)
            rtoks.append(tk)
        dlrow, tdl = load_row(P, "r_dl", aps["decay_logit"], 16, ph, F32)
        rtoks.append(tdl)

        ident_b = consts["ident_b"]
        ones_b = consts["ones_b"]
        ones_f = consts["ones_f"]
        iota_fp = consts["iota_fp"]

        ps_a = ph.enter_context(_ps(nc, "ps_a", [128, 1024], F32))
        ps_b = ph.enter_context(_ps(nc, "ps_b", [128, 1024], F32))
        ps_c = ph.enter_context(_ps(nc, "ps_c", [128, 1024], F32))
        ps_t = ph.enter_context(_ps(nc, "ps_t", [128, 1024], BF16))

        drow = ph.enter_context(_sb(nc, "drow", [1, 16], F32))
        lndr = ph.enter_context(_sb(nc, "lndr", [1, 16], F32))
        lndc = ph.enter_context(_sb(nc, "lndc", [1, 1024], F32))
        lndcol = ph.enter_context(_sb(nc, "lndcol", [128, 16], F32))
        Lh = ph.enter_context(_sb(nc, "Lh", [128, 16 * 128], BF16))
        Lcol = ph.enter_context(_sb(nc, "Lcol", [128, 16], BF16))
        Ppow = ph.enter_context(_sb(nc, "Ppow", [128, 1024], F32))
        d128 = ph.enter_context(_sb(nc, "d128", [1, 1024], F32))
        mask_lt = ph.enter_context(_sb(nc, "mask_lt", [128, 128], F32))
        scr128 = ph.enter_context(_sb(nc, "scr128", [128, 128], F32))
        r1row = ph.enter_context(_sb(nc, "r1row", [1, 128], F32))
        r2row = ph.enter_context(_sb(nc, "r2row", [1, 128], F32))

        t = P.op("scalar", lambda h: h.activation(out=drow[:, :], in_=dlrow[:, :], func=AF.Sigmoid),
                 deps=[tdl], inc=1)
        t = P.op("scalar", lambda h: h.activation(out=lndr[:, :], in_=drow[:, :], func=AF.Ln),
                 deps=[t], inc=1)
        tln = t
        src = ap3(lndr[:, :], [lndr[:, :].ap[0], [1, 16], [0, 64]])
        tlc = P.op("vector", lambda h: h.tensor_copy(rearr(lndc[:, :], "a (h e) -> a h e", h=16), src),
                   deps=[tln], inc=1)
        tm = P.op("tensor", lambda h: h.matmul(ps_a[:, 0:16], ones_f[:, 0:128], lndr[:, :],
                                               start=True, stop=True), deps=[tln] + consts["tok"], inc=1)
        tlcol = P.op("vector", lambda h: h.tensor_copy(lndcol[:, :], ps_a[:, 0:16]), deps=[tm], inc=1)
        tmask = P.op("vector", lambda h: h.tensor_scalar(mask_lt[:, :], iota_fp[:, :], 0.0, None, ALU.is_ge),
                     deps=consts["tok"], inc=1)
        tio1 = P.op("gpsimd", lambda h: h.iota(r1row[:, :], [[1, 128]], channel_multiplier=0,
                                               allow_small_or_imprecise_dtypes=True), inc=1)
        tio1b = P.op("vector", lambda h: h.tensor_scalar(r1row[:, :], r1row[:, :], 1.0, None, ALU.add),
                     deps=[tio1], inc=1)
        tio2 = P.op("gpsimd", lambda h: h.iota(r2row[:, :], [[-1, 128]], channel_multiplier=0,
                                               allow_small_or_imprecise_dtypes=True), inc=1)
        tio2b = P.op("vector", lambda h: h.tensor_scalar(r2row[:, :], r2row[:, :], 127.0, None, ALU.add),
                     deps=[tio2], inc=1)
        ltoks = []
        for hd in range(16):
            t1 = P.op("vector", lambda h, hd=hd: h.tensor_scalar(scr128[:, :], iota_fp[:, :],
                                                                 lndcol[:, hd:hd + 1], None, ALU.mult),
                      deps=[tlcol, tmask] + (ltoks[-1:] if ltoks else []), inc=1)
            t2 = P.op("vector", lambda h: h.tensor_scalar_min(scr128[:, :], scr128[:, :], 0.0),
                      deps=[t1], inc=1)
            t3 = P.op("scalar", lambda h: h.activation(out=scr128[:, :], in_=scr128[:, :], func=AF.Exp),
                      deps=[t2], inc=1)
            t4 = P.op("vector", lambda h, hd=hd: h.tensor_tensor(Lh[:, hd * 128:(hd + 1) * 128],
                                                                 scr128[:, :], mask_lt[:, :], ALU.mult),
                      deps=[t3], inc=1)
            ltoks.append(t4)
        tp1 = P.op("tensor", lambda h: h.matmul(ps_b[:, 0:512], r1row[:, :], lndc[:, 0:512],
                                                start=True, stop=True), deps=[tio1b, tlc], inc=0)
        tp2 = P.op("tensor", lambda h: h.matmul(ps_b[:, 512:1024], r1row[:, :], lndc[:, 512:1024],
                                                start=True, stop=True), deps=[tio1b, tlc], inc=1)
        tp3 = P.op("scalar", lambda h: h.activation(out=Ppow[:, :], in_=ps_b[:, :], func=AF.Exp),
                   deps=[tp2], inc=1)
        tq1 = P.op("tensor", lambda h: h.matmul(ps_a[:, 16:32], r2row[:, :], lndr[:, :],
                                                start=True, stop=True), deps=[tio2b, tln, tlcol], inc=1)
        tq2 = P.op("scalar", lambda h: h.activation(out=Lcol[:, :], in_=ps_a[:, 16:32], func=AF.Exp),
                   deps=[tq1], inc=1)
        td1 = P.op("vector", lambda h: h.tensor_scalar(d128[:, :], lndc[:, :], 128.0, None, ALU.mult),
                   deps=[tlc], inc=1)
        td2 = P.op("scalar", lambda h: h.activation(out=d128[:, :], in_=d128[:, :], func=AF.Exp),
                   deps=[td1], inc=1)

        g1b = ph.enter_context(_sb(nc, "g1b", [128, 1024], F32))
        b1b = ph.enter_context(_sb(nc, "b1b", [128, 1024], F32))
        tg = bcast_row_f32(P, rows["ln1_g"][:, :], D, ps_c, g1b, ones_f, [rtoks[5], tq1])
        tb = bcast_row_f32(P, rows["ln1_b"][:, :], D, ps_c, b1b, ones_f, [rtoks[6], tg])

        xt = ph.enter_context(_sb(nc, "xt", [128, 1024], F32))
        y_sb = ph.enter_context(_sb(nc, "y_sb", [128, 1024], F32))
        y1 = ph.enter_context(_sb(nc, "y1", [128, 1024], F32))
        y1b_ = ph.enter_context(_sb(nc, "y1b", [128, 1024], BF16))
        yT = ph.enter_context(_sb(nc, "yT", [128, 1024], BF16))
        u_sb = ph.enter_context(_sb(nc, "u_sb", [128, 1024], BF16))
        k_sb = ph.enter_context(_sb(nc, "k_sb", [128, 1024], F32))
        qg_sb = ph.enter_context(_sb(nc, "qg_sb", [128, 1024], F32))
        sg_sb = ph.enter_context(_sb(nc, "sg_sb", [128, 1024], F32))
        st0tmp = ph.enter_context(_sb(nc, "st0tmp", [128, 1024], F32))
        state_sb = ph.enter_context(_sb(nc, "state_sb", [128, 1024], F32))
        o_sb = ph.enter_context(_sb(nc, "o_sb", [128, 1024], BF16))
        oT = ph.enter_context(_sb(nc, "oT", [128, 1024], BF16))
        x2_sb = ph.enter_context(_sb(nc, "x2_sb", [128, 1024], F32))
        st0 = ph.enter_context(_sb(nc, "st0", [1, 2, 1024], F32))
        st0t = ph.enter_context(_sb(nc, "st0t", [1, 1024], F32))
        scr = dict(
            junk=ph.enter_context(_sb(nc, "lnjunk", [128, 1024], F32)),
            s1=ph.enter_context(_sb(nc, "lns1", [128, 1], F32)),
            s2=ph.enter_context(_sb(nc, "lns2", [128, 1], F32)),
            s3=ph.enter_context(_sb(nc, "lns3", [128, 1], F32)),
            sq=ph.enter_context(_sb(nc, "lnsq", [128, 1024], F32)),
        )

        tst0 = P.op("gpsimd", lambda h: h.memset(st0[:, 0, :], 0.0), inc=1)
        P.barrier()

        xin_t = xin.rearrange("(n p) d -> n p d", p=128)
        x2_t = x2buf.rearrange("(n p) d -> n p d", p=128)

        prev = dict(x=None, y=tg, y1=tb, y1b=None, yT=None, u=None, k=None, qg=None,
                    sg=None, st=tst0, o=None, oT=None, x2=None, state=None, stt=None)
        st0_tok = tst0
        ps_a_free = tq2
        ps_b_free = tp3
        ps_c_free = tb
        ps_t_free = (P.tok("tensor"))

        for i in range(NT):
            par = i % 2
            tload = P.dma("sync", P.stream("a_x"), lambda h, i=i: h.dma_start(
                out=xt[:, :], in_=xin_t[i, :, :]), deps=[prev["y"], prev["y1"]])
            ty = ln_tile(P, xt[:, :], D, scr, 1e-5, y_sb[:, :],
                         x_deps=[tload], out_tok_extra_deps=[prev["x2"]])
            ty1a = ln_tile(P, y_sb[:, :], D, scr, 1e-5, y1[:, :], x_deps=[ty],
                           out_tok_extra_deps=[prev["y1b"]])
            ty1g = P.op("vector", lambda h: h.tensor_tensor(y1[:, :], y1[:, :], g1b[:, :], ALU.mult),
                        deps=[ty1a, tg], inc=1)
            ty1 = P.op("vector", lambda h: h.tensor_tensor(y1[:, :], y1[:, :], b1b[:, :], ALU.add),
                       deps=[ty1g, tb], inc=1)
            tyb = P.op("vector", lambda h: h.tensor_copy(y1b_[:, :], y1[:, :]),
                       deps=[ty1, prev["yT"]], inc=1)
            tyT = transpose_tile(P, lambda j: y1b_[:, j * 128:(j + 1) * 128], ps_t, yT, 8,
                                 ident_b, [tyb], psum_free_tok=ps_t_free)
            mm = None
            for w_sb, wtok, brow, half in ((wk, wtoks[1], rows["bk"], 0), (wv, wtoks[2], rows["bv"], 1)):
                for oc in range(2):
                    sl = slice(half * 0 + oc * 512, oc * 512 + 512)
                    dst = ps_a if half == 0 else ps_b
                    for ci in range(8):
                        mm = P.op("tensor", lambda h, dst=dst, sl=sl, ci=ci, w_sb=w_sb:
                                  h.matmul(dst[:, sl], yT[:, ci * 128:(ci + 1) * 128],
                                           w_sb[:, ci, sl], start=(ci == 0), stop=False),
                                  deps=[tyT, wtok, ps_a_free if half == 0 else ps_b_free], inc=0)
                    mm = P.op("tensor", lambda h, dst=dst, sl=sl, brow=brow:
                              h.matmul(dst[:, sl], ones_b[:, 0:128], brow[:, sl], start=False, stop=True),
                              deps=rtoks[:2], inc=(1 if (half, oc) == (1, 1) else 0))
            tkv = mm
            tke = P.op("vector", lambda h: h.tensor_copy(k_sb[:, :], ps_a[:, :]),
                       deps=[tkv, prev["u"]], inc=1)
            tu = P.op("vector", lambda h: h.tensor_tensor(u_sb[:, :], ps_b[:, :], k_sb[:, :], ALU.mult),
                      deps=[tke], inc=1)
            ps_a_free = tke
            ps_b_free = tu
            mm = None
            for w_sb, wtok, brow, half in ((wq, wtoks[0], rows["bq"], 0), (wg, wtoks[3], rows["bg"], 1)):
                for oc in range(2):
                    sl = slice(oc * 512, oc * 512 + 512)
                    dst = ps_a if half == 0 else ps_b
                    for ci in range(8):
                        mm = P.op("tensor", lambda h, dst=dst, sl=sl, ci=ci, w_sb=w_sb:
                                  h.matmul(dst[:, sl], yT[:, ci * 128:(ci + 1) * 128],
                                           w_sb[:, ci, sl], start=(ci == 0), stop=False),
                                  deps=[tyT, wtok, ps_a_free if half == 0 else ps_b_free], inc=0)
                    mm = P.op("tensor", lambda h, dst=dst, sl=sl, brow=brow:
                              h.matmul(dst[:, sl], ones_b[:, 0:128], brow[:, sl], start=False, stop=True),
                              deps=rtoks[:4], inc=(1 if (half, oc) == (1, 1) else 0))
            tqgmm = mm
            tsg = P.op("scalar", lambda h: h.activation(out=sg_sb[:, :], in_=ps_b[:, :], func=AF.Sigmoid),
                       deps=[tqgmm, prev["qg"]], inc=1)
            tqg = P.op("vector", lambda h: h.scalar_tensor_tensor(qg_sb[:, :], ps_a[:, :], 0.125,
                                                                  sg_sb[:, :], ALU.mult, ALU.mult),
                       deps=[tsg], inc=1)
            ps_a_free = tqg
            ps_b_free = tqg
            mm = None
            for hd in range(16):
                mm = P.op("tensor", lambda h, hd=hd: h.matmul(ps_c[:, hd * 64:(hd + 1) * 64],
                                                              Lh[:, hd * 128:(hd + 1) * 128],
                                                              u_sb[:, hd * 64:(hd + 1) * 64],
                                                              start=True, stop=True),
                          deps=[tu, ltoks[-1], ps_c_free], inc=(1 if hd == 15 else 0))
            tS = mm
            tb1 = P.op("tensor", lambda h, par=par: h.matmul(ps_b[:, 0:512], ones_f[:, 0:128],
                                                             st0[:, par, 0:512], start=True, stop=True),
                       deps=[st0_tok, ps_b_free], inc=0)
            tb2 = P.op("tensor", lambda h, par=par: h.matmul(ps_b[:, 512:1024], ones_f[:, 0:128],
                                                             st0[:, par, 512:1024], start=True, stop=True),
                       deps=[st0_tok, ps_b_free], inc=1)
            ts1 = P.op("vector", lambda h: h.tensor_tensor(st0tmp[:, :], ps_b[:, :], Ppow[:, :], ALU.mult),
                       deps=[tb2, tp3, prev["state"]], inc=1)
            ts2 = P.op("vector", lambda h: h.tensor_tensor(state_sb[:, :], ps_c[:, :], st0tmp[:, :], ALU.add),
                       deps=[ts1, tS], inc=1)
            to_ = P.op("vector", lambda h: h.tensor_tensor(o_sb[:, :], state_sb[:, :], qg_sb[:, :], ALU.mult),
                       deps=[ts2, tqg, prev["oT"]], inc=1)
            ps_c_free = ts2
            mm = None
            for hd in range(16):
                mm = P.op("tensor", lambda h, hd=hd: h.matmul(ps_a[0:1, hd * 64:(hd + 1) * 64],
                                                              Lcol[:, hd:hd + 1],
                                                              u_sb[:, hd * 64:(hd + 1) * 64],
                                                              start=True, stop=True),
                          deps=[tu, tq2, ps_a_free], inc=(1 if hd == 15 else 0))
            tsr = mm
            tn1 = P.op("vector", lambda h, par=par: h.tensor_tensor(st0t[:, :], st0[:, par, :],
                                                                    d128[:, :], ALU.mult),
                       deps=[st0_tok, td2], inc=1)
            st0_tok = P.op("vector", lambda h, par=par: h.tensor_tensor(st0[:, 1 - par, :],
                                                                        ps_a[0:1, :], st0t[:, :], ALU.add),
                           deps=[tn1, tsr], inc=1)
            ps_a_free = st0_tok
            toT = transpose_tile(P, lambda j: o_sb[:, j * 128:(j + 1) * 128], ps_t, oT, 8,
                                 ident_b, [to_], psum_free_tok=tyT)
            ps_t_free = toT
            mm = None
            for oc in range(2):
                sl = slice(oc * 512, oc * 512 + 512)
                for ci in range(8):
                    mm = P.op("tensor", lambda h, sl=sl, ci=ci: h.matmul(ps_a[:, sl],
                                                                         oT[:, ci * 128:(ci + 1) * 128],
                                                                         wo[:, ci, sl], start=(ci == 0),
                                                                         stop=False),
                              deps=[toT, wtoks[4], ps_a_free], inc=0)
                mm = P.op("tensor", lambda h, sl=sl: h.matmul(ps_a[:, sl], ones_b[:, 0:128],
                                                              rows["bo"][:, sl], start=False, stop=True),
                          deps=[rtoks[4]], inc=(1 if oc == 1 else 0))
            tx2 = P.op("vector", lambda h: h.tensor_tensor(x2_sb[:, :], ps_a[:, :], y_sb[:, :], ALU.add),
                       deps=[mm, ty], inc=1)
            ps_a_free = tx2
            tst = P.dma("sync", P.stream("a_x2st"), lambda h, i=i: h.dma_start(
                out=x2_t[i, :, :], in_=x2_sb[:, :]), deps=[tx2])
            prev = dict(x=tload, y=ty, y1=ty1, y1b=tyT, yT=tkv if False else tqgmm, u=tS, k=tu,
                        qg=to_, sg=tqg, st=st0_tok, o=toT, oT=mm, x2=tst, state=to_, stt=ts2)

        P.barrier()
        P.flush()

    with contextlib.ExitStack() as ph:
        rows = {}
        rtoks = []
        for nm in ("ln2_g", "ln2_b"):
            rows[nm], tk = load_row(P, "f_" + nm, aps[nm], D, ph, F32)
            rtoks.append(tk)
        b2row, tb2r = load_row(P, "f_b2", aps["b2"], D, ph, BF16)
        b1row, tb1r = load_row(P, "f_b1", aps["b1"], 4096, ph, BF16)

        ps_f = [ph.enter_context(_ps(nc, f"ps_f{j}", [128, 512], F32)) for j in range(2)]
        ps_w2 = ph.enter_context(_ps(nc, "ps_w2", [128, 1024], F32))
        ps_g = ph.enter_context(_ps(nc, "ps_g", [128, 1024], F32))
        ps_t = ph.enter_context(_ps(nc, "ps_t2", [128, 1024], BF16))

        g2b = ph.enter_context(_sb(nc, "g2b", [128, 1024], F32))
        b2b = ph.enter_context(_sb(nc, "b2b", [128, 1024], F32))
        tg = bcast_row_f32(P, rows["ln2_g"][:, :], D, ps_g, g2b, consts["ones_f"], [rtoks[0]])
        tb = bcast_row_f32(P, rows["ln2_b"][:, :], D, ps_g, b2b, consts["ones_f"], [rtoks[1], tg])

        w1h = ph.enter_context(_sb(nc, "w1h", [128, 8, 2048], BF16))
        w2h = ph.enter_context(_sb(nc, "w2h", [128, 16, 1024], BF16))

        xt = ph.enter_context(_sb(nc, "fxt", [128, 1024], F32))
        x2t = ph.enter_context(_sb(nc, "fx2t", [128, 1024], F32))
        hh = ph.enter_context(_sb(nc, "fh", [128, 1024], F32))
        hb_sb = ph.enter_context(_sb(nc, "fhb", [128, 1024], BF16))
        hT = [ph.enter_context(_sb(nc, f"fhT{j}", [128, 1024], BF16)) for j in range(8)]
        xnew = [ph.enter_context(_sb(nc, f"fxn{j}", [128, 1024], F32)) for j in range(8)]
        ff1 = ph.enter_context(_sb(nc, "ff1", [128, 512], BF16))
        scr = dict(
            junk=ph.enter_context(_sb(nc, "f_junk", [128, 1024], F32)),
            s1=ph.enter_context(_sb(nc, "f_s1", [128, 1], F32)),
            s2=ph.enter_context(_sb(nc, "f_s2", [128, 1], F32)),
            s3=ph.enter_context(_sb(nc, "f_s3", [128, 1], F32)),
            sq=ph.enter_context(_sb(nc, "f_sq", [128, 1024], F32)),
        )

        xin_t = xin.rearrange("(n p) d -> n p d", p=128)
        x2_t = x2buf.rearrange("(n p) d -> n p d", p=128)
        xout_t = xout.rearrange("(n p) d -> n p d", p=128)
        W1 = aps["W1"]
        W2 = aps["W2"]

        P.barrier()
        last_store = None
        pf_free = [None, None]
        ff1_free = None
        ps_w2_free = None
        for tg_i in range(4):
            tiles = list(range(tg_i * 8, tg_i * 8 + 8))
            hT_toks = {}
            xnew_toks = {-1: None}
            for jj, i in enumerate(tiles):
                tl1 = P.dma("sync", P.stream("f_x"), lambda h, i=i: h.dma_start(
                    out=xt[:, :], in_=xin_t[i, :, :]), deps=[xnew_toks[jj - 1], last_store])
                tl2 = P.dma("sync", P.stream("f_x2"), lambda h, i=i: h.dma_start(
                    out=x2t[:, :], in_=x2_t[i, :, :]), deps=[xnew_toks[jj - 1]])
                th = ln_tile(P, x2t[:, :], D, scr, 1e-5, hh[:, :], x_deps=[tl2])
                th1 = P.op("vector", lambda h: h.tensor_tensor(hh[:, :], hh[:, :], g2b[:, :], ALU.mult),
                           deps=[th, tg], inc=1)
                th2 = P.op("vector", lambda h: h.tensor_tensor(hh[:, :], hh[:, :], b2b[:, :], ALU.add),
                           deps=[th1, tb], inc=1)
                thb = P.op("vector", lambda h: h.tensor_copy(hb_sb[:, :], hh[:, :]), deps=[th2], inc=1)
                tT = transpose_tile(P, lambda j: hb_sb[:, j * 128:(j + 1) * 128], ps_t, hT[jj], 8,
                                    consts["ident_b"], [thb], psum_free_tok=hT_toks.get(jj - 1))
                hT_toks[jj] = tT
                txn = P.op("vector", lambda h, jj=jj: h.tensor_tensor(xnew[jj][:, :], xt[:, :],
                                                                      x2t[:, :], ALU.add),
                           deps=[tl1, tl2, tT], inc=1)
                xnew_toks[jj] = txn
            accum_toks = {jj: xnew_toks[jj] for jj in range(8)}
            for half in range(2):
                wfree = P.tok("tensor")
                tw1 = P.dma("gpsimd", P.stream("f_w1"), lambda h, half=half: h.dma_start(
                    out=w1h[:, :, :],
                    in_=W1[:, half * 2048:(half + 1) * 2048].rearrange("(ci p) n -> p ci n", p=128)),
                    deps=[wfree])
                tw2 = P.dma("gpsimd", P.stream("f_w2"), lambda h, half=half: h.dma_start(
                    out=w2h[:, :, :],
                    in_=W2[half * 2048:(half + 1) * 2048, :].rearrange("(ci p) n -> p ci n", p=128)),
                    deps=[wfree])
                for jj in range(8):
                    for hb_i in range(4):
                        pf = ps_f[hb_i % 2]
                        mm = None
                        for hc in range(4):
                            hcg = hb_i * 4 + hc
                            for ci in range(8):
                                mm = P.op("tensor", lambda h, pf=pf, hc=hc, hcg=hcg, ci=ci, jj=jj:
                                          h.matmul(pf[:, hc * 128:(hc + 1) * 128],
                                                   w1h[:, ci, hcg * 128:(hcg + 1) * 128],
                                                   hT[jj][:, ci * 128:(ci + 1) * 128],
                                                   start=(ci == 0), stop=False),
                                          deps=[tw1, hT_toks[jj], pf_free[hb_i % 2]], inc=0)
                            mm = P.op("tensor", lambda h, pf=pf, hc=hc, hcg=hcg, half=half:
                                      h.matmul(pf[:, hc * 128:(hc + 1) * 128],
                                               b1row[:, half * 2048 + hcg * 128:half * 2048 + (hcg + 1) * 128],
                                               consts["ones_b"][:, 0:128], start=False, stop=True),
                                      deps=[tb1r], inc=(1 if hc == 3 else 0))
                        tgel = P.op("scalar", lambda h, pf=pf: h.activation(out=ff1[:, :], in_=pf[:, :],
                                                                            func=AF.Gelu),
                                    deps=[mm, ff1_free], inc=1)
                        pf_free[hb_i % 2] = tgel
                        mm2 = None
                        for oc in range(2):
                            for hc in range(4):
                                hcg = hb_i * 4 + hc
                                is_first = (hb_i == 0 and hc == 0)
                                is_last = (half == 0 and hb_i == 3 and hc == 3)
                                mm2 = P.op("tensor", lambda h, hc=hc, hcg=hcg, oc=oc, is_first=is_first, is_last=is_last:
                                           h.matmul(ps_w2[:, oc * 512:(oc + 1) * 512],
                                                    ff1[:, hc * 128:(hc + 1) * 128],
                                                    w2h[:, hcg, oc * 512:(oc + 1) * 512],
                                                    start=is_first, stop=is_last),
                                           deps=[tgel, tw2, ps_w2_free], inc=(1 if is_last and oc == 1 else 0))
                        ff1_free = mm2
                    if half == 1:
                        mmb = None
                        for oc in range(2):
                            mmb = P.op("tensor", lambda h, oc=oc:
                                       h.matmul(ps_w2[:, oc * 512:(oc + 1) * 512],
                                                consts["ones_b"][:, 0:128],
                                                b2row[:, oc * 512:(oc + 1) * 512],
                                                start=False, stop=True),
                                       deps=[tb2r], inc=(1 if oc == 1 else 0))
                        tadd = P.op("vector", lambda h, jj=jj: h.tensor_tensor(xnew[jj][:, :], xnew[jj][:, :],
                                                                               ps_w2[:, :], ALU.add),
                                    deps=[mmb, accum_toks[jj]], inc=1)
                    else:
                        tadd = P.op("vector", lambda h, jj=jj: h.tensor_tensor(xnew[jj][:, :], xnew[jj][:, :],
                                                                               ps_w2[:, :], ALU.add),
                                    deps=[mm2, accum_toks[jj]], inc=1)
                    accum_toks[jj] = tadd
                    ps_w2_free = tadd
            for jj, i in enumerate(tiles):
                tst = P.dma("sync", P.stream("f_st"), lambda h, i=i, jj=jj: h.dma_start(
                    out=xout_t[i, :, :], in_=xnew[jj][:, :]), deps=[accum_toks[jj]])
                last_store = tst
        P.barrier()
        P.flush()
    return last_store




L0AA_W = [("Wq", [D, D]), ("Wk", [D, D]), ("Wv", [D, D]), ("Wg", [D, D]), ("Wo", [D, D]),
          ("bq", [D]), ("bk", [D]), ("bv", [D]), ("bg", [D]), ("bo", [D]),
          ("ln1_g", [D]), ("ln1_b", [D]), ("ln2_g", [D]), ("ln2_b", [D]),
          ("decay_logit", [16]), ("W1", [D, 4096]), ("b1", [4096]), ("W2", [4096, D]), ("b2", [D])]


def _mk_prog():
    nc = bass.Bass("TRN2", target_bir_lowering=False)
    return nc


def build_program_l0_attn_a():
    nc = _mk_prog()
    aps = {}
    aps["xin"] = nc.dram_tensor("xin", [T, D], F32, kind="ExternalInput").ap()
    aps["x2buf"] = nc.dram_tensor("x2buf", [T, D], F32, kind="Internal").ap()
    aps["xout"] = nc.dram_tensor("xout", [T, D], F32, kind="ExternalOutput").ap()
    for nm, shp in L0AA_W:
        aps[nm] = nc.dram_tensor(nm, shp, F32, kind="ExternalInput").ap()
    with contextlib.ExitStack() as es:
        P = Prog(nc, None, es)
        block = es.enter_context(nc.Block())
        P.block = block
        consts = const_setup(P, es)
        build_l0_attn_a(P, es, aps, consts)
        P.finish()
    return nc


def run_sublayer_l0aa(x_full, wd):
    nc = build_program_l0_attn_a()
    in_maps = []
    for c in range(N_CORES):
        m = {"xin": np.ascontiguousarray(x_full[c])}
        for nm, _ in L0AA_W:
            m[nm] = wd[nm]
        in_maps.append(m)
    res = run_bass_kernel_spmd(nc, in_maps, core_ids=list(range(N_CORES)))
    return np.stack([r["xout"] for r in res.results])




def build_l0_ffn(P, es, aps, consts):
    nc = P.nc
    xin, xout = aps["xin"], aps["xout"]
    xin_t = xin.rearrange("(n p) d -> n p d", p=128)
    xout_t = xout.rearrange("(n p) d -> n p d", p=128)

    tcopy = P.dma("sync", P.stream("d_copy", "bulk"), lambda h: h.dma_start(out=xout, in_=xin))

    with contextlib.ExitStack() as ph:
        xlnT = ph.enter_context(_sb(nc, "xlnT", [128, 8, 4096], BF16))
        mask = ph.enter_context(_sb(nc, "mask", [128, 32, 9], F32))
        p0 = contextlib.ExitStack()
        routerT = p0.enter_context(_sb(nc, "sb_routerT", [128, 8, 8], F32))
        ps_t = p0.enter_context(_ps(nc, "d_ps_t", [128, 1024], BF16))
        ps_tf = p0.enter_context(_ps(nc, "d_ps_tf", [128, 1024], F32))
        ps_r = p0.enter_context(_ps(nc, "d_ps_r", [128, 8], F32))
        xt = p0.enter_context(_sb(nc, "d_xt", [128, 1024], F32))
        xln = p0.enter_context(_sb(nc, "d_xln", [128, 1024], F32))
        xlnb = p0.enter_context(_sb(nc, "d_xlnb", [128, 1024], BF16))
        xT32 = p0.enter_context(_sb(nc, "d_xT32", [128, 1024], F32))
        mx = p0.enter_context(_sb(nc, "d_mx", [128, 1], F32))
        scr = dict(
            junk=p0.enter_context(_sb(nc, "d_junk", [128, 1024], F32)),
            s1=p0.enter_context(_sb(nc, "d_s1", [128, 1], F32)),
            s2=p0.enter_context(_sb(nc, "d_s2", [128, 1], F32)),
            s3=p0.enter_context(_sb(nc, "d_s3", [128, 1], F32)),
            sq=p0.enter_context(_sb(nc, "d_sq", [128, 1024], F32)),
        )
        trt = P.dma("sync", P.stream("bulks", "bulk"), lambda h: h.dma_start(
            out=routerT[:, :, :], in_=aps["routerT"].rearrange("(ci p) e -> p ci e", p=128)))
        tmz = P.op("gpsimd", lambda h: h.memset(mask[:, :, 8:9], 1.0), inc=1)

        prev_ln = None
        prev_T = None
        prev_T32 = None
        prev_r = None
        prev_mask = None
        for i in range(NT):
            tl = P.dma("sync", P.stream("d_x"), lambda h, i=i: h.dma_start(
                out=xt[:, :], in_=xin_t[i, :, :]), deps=[prev_ln])
            tln = ln_tile(P, xt[:, :], D, scr, 1e-5, xln[:, :], x_deps=[tl],
                          out_tok_extra_deps=[prev_T32, prev_T])
            tcb = P.op("vector", lambda h: h.tensor_copy(xlnb[:, :], xln[:, :]),
                       deps=[tln, prev_T], inc=1)
            tT = transpose_tile(P, lambda j: xlnb[:, j * 128:(j + 1) * 128], ps_t,
                                None, 8, consts["ident_b"], [tcb], psum_free_tok=prev_T,
                                out_ap=xlnT[:, :, i * 128:(i + 1) * 128])
            tts = []
            for j in range(8):
                tts.append(P.op("tensor", lambda h, j=j: h.transpose(ps_tf[:, j * 128:(j + 1) * 128],
                                                                     xln[:, j * 128:(j + 1) * 128],
                                                                     consts["ident_f"][:, :]),
                                deps=[tln, prev_T32] + consts["tok"], inc=(1 if j == 7 else 0)))
            tev = P.op("vector", lambda h: h.tensor_copy(xT32[:, :], ps_tf[:, :]), deps=[tts[-1]], inc=1)
            mmr = None
            for ci in range(8):
                mmr = P.op("tensor", lambda h, ci=ci: h.matmul(ps_r[:, :], xT32[:, ci * 128:(ci + 1) * 128],
                                                               routerT[:, ci, :], start=(ci == 0),
                                                               stop=(ci == 7)),
                           deps=[tev, trt, prev_r], inc=(1 if ci == 7 else 0))
            tmx = P.op("vector", lambda h: h.reduce_max(mx[:, :], ps_r[:, :], axis=AX.X),
                       deps=[mmr, prev_mask], inc=1)
            tms = P.op("vector", lambda h, i=i: h.tensor_scalar(mask[:, i, 0:8], ps_r[:, :],
                                                                mx[:, 0:1], None, ALU.is_equal),
                       deps=[tmx], inc=1)
            prev_ln = tln
            prev_T = tT
            prev_T32 = mmr
            prev_r = tms
            prev_mask = tms
        P.barrier()
        P.flush()
        p0.close()

        with contextlib.ExitStack() as pe:
            upsl = [pe.enter_context(_sb(nc, f"upsl{j}", [128, 8, 128], BF16)) for j in range(2)]
            gtsl = [pe.enter_context(_sb(nc, f"gtsl{j}", [128, 8, 128], BF16)) for j in range(2)]
            dwn = pe.enter_context(_sb(nc, "dwn", [128, 16, 1024], BF16))
            hact = pe.enter_context(_sb(nc, "hact", [128, 16, 1024], BF16))
            sg = [pe.enter_context(_sb(nc, f"d_sg{j}", [128, 512], BF16)) for j in range(2)]
            sg_free = {0: None, 1: None}
            contrib = [pe.enter_context(_sb(nc, f"d_ctr{j}", [128, 1024], F32)) for j in range(2)]
            ps_u = [pe.enter_context(_ps(nc, f"d_ps_u{j}", [128, 512], F32)) for j in range(2)]
            ps_g = [pe.enter_context(_ps(nc, f"d_ps_g{j}", [128, 512], F32)) for j in range(2)]
            ps_y = [pe.enter_context(_ps(nc, f"d_ps_y{j}", [128, 1024], F32)) for j in range(2)]

            tdw = None
            tu_free = {0: None, 1: None}
            tg_free = {0: None, 1: None}
            ctr_free = {0: None, 1: None}
            ps_y_free = {0: None, 1: None}
            hact_free = None
            upsl_free = {0: None, 1: None}
            gtsl_free = {0: None, 1: None}
            acc_last = tcopy
            for e in range(9):
                dsrc = (aps["downT"][e] if e < 8 else aps["downTs"])
                tdw = P.dma("gpsimd", P.stream("d_dw"), lambda h, dsrc=dsrc: h.dma_start(
                    out=dwn[:, :, :], in_=dsrc.rearrange("(ci p) n -> p ci n", p=128)),
                    deps=[P.tok("tensor")])
                for jg in range(4):
                    jt0 = jg * 1024
                    silu_toks = []
                    for hc in range(16):
                        par = hc % 2
                        usrc = (aps["upT"][e] if e < 8 else aps["upTs"])[:, hc * 128:(hc + 1) * 128]
                        gsrc = (aps["gateT"][e] if e < 8 else aps["gateTs"])[:, hc * 128:(hc + 1) * 128]
                        tlu = P.dma("gpsimd", P.stream(f"d_up{par}"), lambda h, usrc=usrc, par=par:
                            h.dma_start(out=upsl[par][:, :, :],
                                        in_=usrc.rearrange("(ci p) n -> p ci n", p=128)),
                            deps=[upsl_free[par]])
                        tlg = P.dma("gpsimd", P.stream(f"d_gt{par}"), lambda h, gsrc=gsrc, par=par:
                            h.dma_start(out=gtsl[par][:, :, :],
                                        in_=gsrc.rearrange("(ci p) n -> p ci n", p=128)),
                            deps=[gtsl_free[par]])
                        mmu = mmg = None
                        for j2 in range(2):
                            bk = (hc * 2 + j2) % 2
                            tsl = slice(jt0 + j2 * 512, jt0 + (j2 + 1) * 512)
                            for ci in range(8):
                                mmu = P.op("tensor", lambda h, ci=ci, par=par, tsl=tsl, bk=bk:
                                           h.matmul(ps_u[bk][:, :], upsl[par][:, ci, :],
                                                    xlnT[:, ci, tsl], start=(ci == 0), stop=(ci == 7)),
                                           deps=[tlu, tu_free[bk]], inc=(1 if ci == 7 else 0))
                            for ci in range(8):
                                mmg = P.op("tensor", lambda h, ci=ci, par=par, tsl=tsl, bk=bk:
                                           h.matmul(ps_g[bk][:, :], gtsl[par][:, ci, :],
                                                    xlnT[:, ci, tsl], start=(ci == 0), stop=(ci == 7)),
                                           deps=[tlg, tg_free[bk]], inc=(1 if ci == 7 else 0))
                            tsi = P.op("scalar", lambda h, bk=bk: h.activation(
                                out=sg[bk][:, :], in_=ps_g[bk][:, :], func=AF.Silu),
                                deps=[mmg, sg_free[bk]], inc=1)
                            tmu = P.op("vector", lambda h, hc=hc, j2=j2, bk=bk: h.tensor_tensor(
                                hact[:, hc, j2 * 512:(j2 + 1) * 512], ps_u[bk][:, :], sg[bk][:, :],
                                ALU.mult),
                                deps=[tsi, mmu, hact_free], inc=1)
                            tu_free[bk] = tmu
                            tg_free[bk] = tsi
                            sg_free[bk] = tmu
                            silu_toks.append(tmu)
                        upsl_free[par] = mmu
                        gtsl_free[par] = mmg
                    down_last = None
                    for tsub in range(8):
                        tile_i = jg * 8 + tsub
                        pyp = tsub % 2
                        mm = None
                        for hc in range(16):
                            for oc in range(2):
                                mm = P.op("tensor", lambda h, hc=hc, oc=oc, tsub=tsub, pyp=pyp:
                                          h.matmul(ps_y[pyp][:, oc * 512:(oc + 1) * 512],
                                                   hact[:, hc, tsub * 128:(tsub + 1) * 128],
                                                   dwn[:, hc, oc * 512:(oc + 1) * 512],
                                                   start=(hc == 0), stop=(hc == 15)),
                                          deps=[silu_toks[-1], tdw, ps_y_free[pyp]],
                                          inc=(1 if (hc == 15 and oc == 1) else 0))
                        tev = P.op("vector", lambda h, tile_i=tile_i, tsub=tsub, e=e, pyp=pyp:
                                   h.tensor_scalar(contrib[tsub % 2][:, :], ps_y[pyp][:, :],
                                                   mask[:, tile_i, e:e + 1], None, ALU.mult),
                                   deps=[mm, ctr_free[tsub % 2], prev_mask], inc=1)
                        ps_y_free[pyp] = tev
                        tac = P.dma("gpsimd", P.stream(f"d_ac{tsub % 2}"),
                            lambda h, tile_i=tile_i, tsub=tsub: h.dma_start(
                                out=xout_t[tile_i, :, :], in_=contrib[tsub % 2][:, :],
                                accum_op=ALU.add),
                            deps=[tev, tcopy])
                        ctr_free[tsub % 2] = tac
                        down_last = tac
                    hact_free = mm
            P.barrier()
            P.flush()
            return down_last


L0FFN_W = [("upT", [8, D, 2048]), ("gateT", [8, D, 2048]), ("downT", [8, 2048, D]),
           ("routerT", [D, 8]), ("upTs", [D, 2048]), ("gateTs", [D, 2048]), ("downTs", [2048, D])]


def build_program_l0_ffn():
    nc = _mk_prog()
    aps = {}
    aps["xin"] = nc.dram_tensor("xin", [T, D], F32, kind="ExternalInput").ap()
    aps["xout"] = nc.dram_tensor("xout", [T, D], F32, kind="ExternalOutput").ap()
    for nm, shp in L0FFN_W:
        aps[nm] = nc.dram_tensor(nm, shp, F32, kind="ExternalInput").ap()
    with contextlib.ExitStack() as es:
        P = Prog(nc, None, es)
        block = es.enter_context(nc.Block())
        P.block = block
        consts = const_setup(P, es)
        build_l0_ffn(P, es, aps, consts)
        P.finish()
    return nc


def prep_l0ffn_weights(ins, pfx="L0_ffn_"):
    return {
        "upT": np.ascontiguousarray(ins[pfx + "up"].transpose(0, 2, 1)),
        "gateT": np.ascontiguousarray(ins[pfx + "gate"].transpose(0, 2, 1)),
        "downT": np.ascontiguousarray(ins[pfx + "down"].transpose(0, 2, 1)),
        "routerT": np.ascontiguousarray(ins[pfx + "router"].T),
        "upTs": np.ascontiguousarray(ins[pfx + "w_up_s"].T),
        "gateTs": np.ascontiguousarray(ins[pfx + "w_gate_s"].T),
        "downTs": np.ascontiguousarray(ins[pfx + "w_down_s"].T),
    }


def run_sublayer_l0ffn(x_full, wd):
    nc = build_program_l0_ffn()
    in_maps = []
    for c in range(N_CORES):
        m = {"xin": np.ascontiguousarray(x_full[c])}
        m.update(wd)
        in_maps.append(m)
    res = run_bass_kernel_spmd(nc, in_maps, core_ids=list(range(N_CORES)))
    return np.stack([r["xout"] for r in res.results])




def build_l0_attn_b(P, es, aps, consts):
    nc = P.nc
    xin, xout, znT_d = aps["xin"], aps["xout"], aps["znT"]
    xin_t = xin.rearrange("(n p) d -> n p d", p=128)
    xout_t = xout.rearrange("(n p) d -> n p d", p=128)

    with contextlib.ExitStack() as ph:
        ps_t = ph.enter_context(_ps(nc, "b_ps_t", [128, 1024], BF16))
        xt = ph.enter_context(_sb(nc, "b_xt", [128, 1024], F32))
        y = ph.enter_context(_sb(nc, "b_y", [128, 1024], F32))
        zn = ph.enter_context(_sb(nc, "b_zn", [128, 1024], F32))
        znb = ph.enter_context(_sb(nc, "b_znb", [128, 1024], BF16))
        znTt = ph.enter_context(_sb(nc, "b_znTt", [128, 8, 128], BF16))
        xy = ph.enter_context(_sb(nc, "b_xy", [128, 1024], F32))
        gwb = ph.enter_context(_sb(nc, "b_gwb", [128, 1024], F32))
        bwb = ph.enter_context(_sb(nc, "b_bwb", [128, 1024], F32))
        ps_bc = ph.enter_context(_ps(nc, "b_ps_bc", [128, 1024], F32))
        scr = dict(
            junk=ph.enter_context(_sb(nc, "b_junk", [128, 1024], F32)),
            s1=ph.enter_context(_sb(nc, "b_s1", [128, 1], F32)),
            s2=ph.enter_context(_sb(nc, "b_s2", [128, 1], F32)),
            s3=ph.enter_context(_sb(nc, "b_s3", [128, 1], F32)),
            sq=ph.enter_context(_sb(nc, "b_sq", [128, 1024], F32)),
        )
        lnw, t1 = load_row(P, "b_lnw", aps["ln_w"], D, ph, F32)
        lnb, t2 = load_row(P, "b_lnb", aps["ln_b"], D, ph, F32)
        tg = bcast_row_f32(P, lnw[:, :], D, ps_bc, gwb, consts["ones_f"], [t1] + consts["tok"])
        tb = bcast_row_f32(P, lnb[:, :], D, ps_bc, bwb, consts["ones_f"], [t2, tg])

        prev = {}
        for i in range(NT):
            tl = P.dma("sync", P.stream("b_x"), lambda h, i=i: h.dma_start(
                out=xt[:, :], in_=xin_t[i, :, :]), deps=[prev.get("xy")])
            ty = ln_tile(P, xt[:, :], D, scr, 1e-5, y[:, :], x_deps=[tl],
                         out_tok_extra_deps=[prev.get("xy2")])
            tz = ln_tile(P, y[:, :], D, scr, 1e-5, zn[:, :], x_deps=[ty],
                         out_tok_extra_deps=[prev.get("znb")])
            tz1 = P.op("vector", lambda h: h.tensor_tensor(zn[:, :], zn[:, :], gwb[:, :], ALU.mult),
                       deps=[tz, tg], inc=1)
            tz2 = P.op("vector", lambda h: h.tensor_tensor(zn[:, :], zn[:, :], bwb[:, :], ALU.add),
                       deps=[tz1, tb], inc=1)
            tzb = P.op("vector", lambda h: h.tensor_copy(znb[:, :], zn[:, :]),
                       deps=[tz2, prev.get("znT")], inc=1)
            tT = transpose_tile(P, lambda j: znb[:, j * 128:(j + 1) * 128], ps_t, None, 8,
                                consts["ident_b"], [tzb], psum_free_tok=prev.get("znT"),
                                out_ap=znTt[:, :, :])
            tst1 = P.dma("sync", P.stream("b_znst"), lambda h, i=i: h.dma_start(
                out=znT_d[:, :, i * 128:(i + 1) * 128], in_=znTt[:, :, :]), deps=[tT])
            txy = P.op("vector", lambda h: h.tensor_tensor(xy[:, :], xt[:, :], y[:, :], ALU.add),
                       deps=[ty, tl, prev.get("xyst")], inc=1)
            tst2 = P.dma("sync", P.stream("b_xyst"), lambda h, i=i: h.dma_start(
                out=xout_t[i, :, :], in_=xy[:, :]), deps=[txy])
            prev = dict(xy=ty, xy2=txy, znb=tT, znT=tst1, xyst=tst2)
        P.barrier()
        P.flush()
        b0_done = prev["xyst"]

    acc_last = b0_done
    for hg in range(DEBUG_L0AB_HGS):
        with contextlib.ExitStack() as ph:
            QT = ph.enter_context(_sb(nc, f"b_QT{hg}", [128, 4, 4096], BF16))
            KT = ph.enter_context(_sb(nc, f"b_KT{hg}", [128, 4, 4096], BF16))
            Vp = ph.enter_context(_sb(nc, f"b_Vp{hg}", [128, 32, 520], BF16))
            negB = ph.enter_context(_sb(nc, f"b_negB{hg}", [128, 8], F32))
            wq = ph.enter_context(_sb(nc, f"b_wq{hg}", [128, 8, 512], BF16))
            wk = ph.enter_context(_sb(nc, f"b_wk{hg}", [128, 8, 512], BF16))
            wv = ph.enter_context(_sb(nc, f"b_wv{hg}", [128, 8, 512], BF16))
            wp = ph.enter_context(_sb(nc, f"b_wp{hg}", [128, 4, 1024], BF16))
            brow, tbr = load_row(P, f"b_brow{hg}", aps["b_qkv"], 3072, ph, BF16)
            borow, tbo = load_row(P, f"b_borow{hg}", aps["b_proj"], D, ph, BF16)
            W = aps["W_qkv"]
            twq = P.dma("gpsimd", P.stream("bulk", "bulk"), lambda h, hg=hg: h.dma_start(
                out=wq[:, :, :],
                in_=W[:, hg * 512:hg * 512 + 512].rearrange("(ci p) n -> p ci n", p=128)))
            twk = P.dma("gpsimd", P.stream("bulk", "bulk"), lambda h, hg=hg: h.dma_start(
                out=wk[:, :, :],
                in_=W[:, 1024 + hg * 512:1024 + hg * 512 + 512].rearrange("(ci p) n -> p ci n", p=128)))
            twv = P.dma("gpsimd", P.stream("bulk", "bulk"), lambda h, hg=hg: h.dma_start(
                out=wv[:, :, :],
                in_=W[:, 2048 + hg * 512:2048 + hg * 512 + 512].rearrange("(ci p) n -> p ci n", p=128)))
            twp = P.dma("gpsimd", P.stream("bulk", "bulk"), lambda h, hg=hg: h.dma_start(
                out=wp[:, :, :],
                in_=aps["W_proj"][hg * 512:hg * 512 + 512, :].rearrange("(ci p) n -> p ci n", p=128)))

            with contextlib.ExitStack() as pp:
                znt = [pp.enter_context(_sb(nc, f"b_znt{hg}{j}", [128, 8, 512], BF16))
                       for j in range(2)]
                ps_q = pp.enter_context(_ps(nc, "b_ps_q", [128, 512], F32))
                ps_v = pp.enter_context(_ps(nc, "b_ps_v", [128, 512], F32))
                ps_n = pp.enter_context(_ps(nc, "b_ps_n", [1, 512], F32))
                ps_nb = pp.enter_context(_ps(nc, "b_ps_nb", [128, 8], F32))
                sq_s = pp.enter_context(_sb(nc, "b_sq_s", [128, 4096], F32))
                nrmQ = pp.enter_context(_sb(nc, "b_nrmQ", [1, 8], F32))
                nrmK = pp.enter_context(_sb(nc, "b_nrmK", [1, 8], F32))
                ntmp = pp.enter_context(_sb(nc, "b_ntmp", [1, 512], F32))
                ntm1 = pp.enter_context(_sb(nc, "b_ntm1", [1, 1], F32))
                znt_free = {0: None, 1: None}
                ps_q_free = None
                ps_v_free = None
                qkv_done = []
                for tt in range(8):
                    par = tt % 2
                    tsl = slice(tt * 512, (tt + 1) * 512)
                    tld = P.dma("gpsimd", P.stream(f"b_zl{par}"), lambda h, tsl=tsl, par=par:
                        h.dma_start(out=znt[par][:, :, :], in_=znT_d[:, :, tsl]),
                        deps=[znt_free[par], b0_done])
                    last_read = None
                    for wsb, wtok, dst, boff in (() if "qk" in DEBUG_L0AB_SKIP else ((wq, twq, QT, 0), (wk, twk, KT, 1024))):
                        for ch in range(4):
                            mm = None
                            for ci in range(8):
                                mm = P.op("tensor", lambda h, ch=ch, ci=ci, par=par, wsb=wsb:
                                          h.matmul(ps_q[:, :], wsb[:, ci, ch * 128:(ch + 1) * 128],
                                                   znt[par][:, ci, :], start=(ci == 0), stop=False),
                                          deps=[tld, wtok, ps_q_free], inc=0)
                            mm = P.op("tensor", lambda h, ch=ch, boff=boff, hg=hg:
                                      h.matmul(ps_q[:, :],
                                               brow[:, boff + hg * 512 + ch * 128:boff + hg * 512 + (ch + 1) * 128],
                                               consts["ones_b"][:, 0:512], start=False, stop=True),
                                      deps=[tbr], inc=1)
                            tev = P.op("vector", lambda h, ch=ch, tsl=tsl, dst=dst:
                                       h.tensor_copy(dst[:, ch, tsl], ps_q[:, :]), deps=[mm], inc=1)
                            ps_q_free = tev
                            last_read = mm
                            qkv_done.append(tev)
                    for tsub in (() if "v" in DEBUG_L0AB_SKIP else range(4)):
                        kc = tt * 4 + tsub
                        mm = None
                        for ci in range(8):
                            mm = P.op("tensor", lambda h, tsub=tsub, ci=ci, par=par:
                                      h.matmul(ps_v[:, :], znt[par][:, ci, tsub * 128:(tsub + 1) * 128],
                                               wv[:, ci, :], start=(ci == 0), stop=False),
                                      deps=[tld, twv, ps_v_free], inc=0)
                        mm = P.op("tensor", lambda h, hg=hg:
                                  h.matmul(ps_v[:, :], consts["ones_b"][:, 0:128],
                                           brow[:, 2048 + hg * 512:2048 + hg * 512 + 512],
                                           start=False, stop=True),
                                  deps=[tbr], inc=1)
                        vap = Vp[:, kc, :]
                        vout = ap3(vap, [vap.ap[0], [65, 8], [1, 64]])
                        tev = P.op("vector", lambda h, vout=vout: h.tensor_copy(
                            vout, ps_v[:, :].rearrange("p (e f) -> p e f", e=8)), deps=[mm], inc=1)
                        vone = ap3(vap, [vap.ap[0], [65, 8], [1, 1]])
                        vone = bass.AP(vone.tensor, vone.offset + 64, vone.ap)
                        tmo = P.op("vector", lambda h, vone=vone: h.memset(vone, 1.0),
                                   deps=[mm], inc=1)
                        ps_v_free = tev
                        qkv_done.append(tev)
                        qkv_done.append(tmo)
                        last_read = mm
                    znt_free[par] = last_read
                nz = P.op("gpsimd", lambda h: h.memset(nrmQ[:, :], 0.0), inc=1)
                nz2 = P.op("gpsimd", lambda h: h.memset(nrmK[:, :], 0.0), inc=1)
                qlast = qkv_done[-1] if qkv_done else None
                sq_free = None
                mprev = None
                ntm1_free = None
                for which, src_t, nrm_t, nztok in (() if "bounds" in DEBUG_L0AB_SKIP else (("q", QT, nrmQ, nz), ("k", KT, nrmK, nz2))):
                    for ch in range(4):
                        tsq = P.op("scalar", lambda h, ch=ch, src_t=src_t: h.activation(
                            out=sq_s[:, :], in_=src_t[:, ch, :], func=AF.Square),
                            deps=[qlast, sq_free], inc=1)
                        for half in (range(2) if DEBUG_L0AB_BSTEP >= 2 else ()):
                            hd = ch * 2 + half
                            tac = nztok
                            for tc in range(DEBUG_L0AB_TCN):
                                mm = P.op("tensor", lambda h, half=half, tc=tc: h.matmul(
                                    ps_n[:, :], consts["onescol_f"][half * 64:half * 64 + 64, 0:1],
                                    sq_s[half * 64:half * 64 + 64, tc * 512:(tc + 1) * 512],
                                    start=True, stop=True),
                                    deps=[tsq, mprev], inc=1)
                                tmx = P.op("vector", lambda h: h.reduce_max(ntm1[:, :], ps_n[:, :],
                                                                            axis=AX.X),
                                           deps=[mm, ntm1_free], inc=1)
                                tac = P.op("vector", lambda h, hd=hd, nrm_t=nrm_t: h.tensor_tensor(
                                    nrm_t[:, hd:hd + 1], nrm_t[:, hd:hd + 1], ntm1[:, :], ALU.max),
                                    deps=[tmx, tac], inc=1)
                                ntm1_free = tac
                                mprev = tmx
                            sq_free = mm
                tn1 = P.op("vector", lambda h: h.tensor_tensor(nrmQ[:, :], nrmQ[:, :], nrmK[:, :],
                                                               ALU.mult),
                           deps=[P.tok("vector")], inc=1)
                tn2 = P.op("scalar", lambda h: h.activation(out=nrmQ[:, :], in_=nrmQ[:, :], func=AF.Sqrt),
                           deps=[tn1], inc=1)
                tn3 = P.op("vector", lambda h: h.tensor_scalar(nrmQ[:, :], nrmQ[:, :], -0.125, None,
                                                               ALU.mult),
                           deps=[tn2], inc=1)
                tn4 = P.op("tensor", lambda h: h.matmul(ps_nb[:, :], consts["ones_f"][:, 0:128],
                                                        nrmQ[:, :], start=True, stop=True),
                           deps=[tn3], inc=1)
                tn5 = P.op("vector", lambda h: h.tensor_copy(negB[:, :], ps_nb[:, :]), deps=[tn4], inc=1)
                P.barrier()
                P.flush()

            with contextlib.ExitStack() as pa:
                sc = [pa.enter_context(_ps(nc, f"b_sc{j}", [128, 512], F32)) for j in range(2)]
                av = [pa.enter_context(_ps(nc, f"b_av{j}", [128, 128], F32)) for j in range(2)]
                ps_p = pa.enter_context(_ps(nc, "b_ps_p", [128, 1024], F32))
                ps_t2 = pa.enter_context(_ps(nc, "b_ps_t2", [128, 512], BF16))
                attnT = pa.enter_context(_sb(nc, "b_attnT", [128, 4096], BF16))
                ao = pa.enter_context(_sb(nc, "b_ao", [128, 512], BF16))
                aoT = pa.enter_context(_sb(nc, "b_aoT", [128, 512], BF16))
                rcol = pa.enter_context(_sb(nc, "b_rcol", [128, 1], F32))
                ctr = [pa.enter_context(_sb(nc, f"b_ctr{j}", [128, 1024], F32)) for j in range(2)]
                sc_free = {0: None, 1: None}
                av_free = {0: None, 1: None}
                attnT_free = None
                ao_free = None
                rcol_free = None
                aoT_free = None
                ctr_free = {0: None, 1: None}
                ps_p_free = None
                for qt in range(DEBUG_L0AB_NQT):
                    ao_toks = []
                    for hd in range(8):
                        hrow = (hd % 2) * 64
                        hch = hd // 2
                        exp_toks = []
                        av_par = hd % 2
                        for grp in range(8):
                            par = grp % 2
                            mm = None
                            for k4 in range(4):
                                kc = grp * 4 + k4
                                mm = P.op("tensor", lambda h, par=par, k4=k4, kc=kc, hrow=hrow, hch=hch, qt=qt:
                                          h.matmul(sc[par][:, k4 * 128:(k4 + 1) * 128],
                                                   KT[hrow:hrow + 64, hch, kc * 128:(kc + 1) * 128],
                                                   QT[hrow:hrow + 64, hch, qt * 128:(qt + 1) * 128],
                                                   start=True, stop=True),
                                          deps=[sc_free[par], qkv_done[-1]], inc=(1 if k4 == 3 else 0))
                            tex = P.op("scalar", lambda h, par=par, grp=grp, hd=hd: h.activation(
                                out=attnT[:, grp * 512:(grp + 1) * 512], in_=sc[par][:, :],
                                func=AF.Exp, bias=negB[:, hd:hd + 1], scale=0.125),
                                deps=[mm, tn5, attnT_free], inc=1)
                            sc_free[par] = tex
                            exp_toks.append(tex)
                        mm = None
                        for kc in range(32):
                            mm = P.op("tensor", lambda h, kc=kc, hd=hd, av_par=av_par:
                                      h.matmul(av[av_par][:, 0:65],
                                               attnT[:, kc * 128:(kc + 1) * 128],
                                               Vp[:, kc, hd * 65:(hd + 1) * 65],
                                               start=(kc == 0), stop=(kc == 31)),
                                      deps=[exp_toks[-1], av_free[av_par]], inc=(1 if kc == 31 else 0))
                        attnT_free = mm
                        trc = P.op("vector", lambda h, av_par=av_par: h.reciprocal(
                            rcol[:, :], av[av_par][:, 64:65]), deps=[mm, rcol_free], inc=1)
                        tsc = P.op("vector", lambda h, hd=hd, av_par=av_par: h.tensor_scalar(
                            ao[:, hd * 64:(hd + 1) * 64], av[av_par][:, 0:64], rcol[:, 0:1],
                            None, ALU.mult),
                            deps=[trc, ao_free], inc=1)
                        rcol_free = tsc
                        av_free[av_par] = tsc
                        ao_toks.append(tsc)
                    tT = transpose_tile(P, lambda j: ao[:, j * 128:(j + 1) * 128], ps_t2, aoT, 4,
                                        consts["ident_b"], [ao_toks[-1]], psum_free_tok=aoT_free)
                    ao_free = tT
                    mm = None
                    for oc in range(2):
                        for ci in range(4):
                            mm = P.op("tensor", lambda h, oc=oc, ci=ci, hg=hg: h.matmul(
                                ps_p[:, oc * 512:(oc + 1) * 512], aoT[:, ci * 128:(ci + 1) * 128],
                                wp[:, ci, oc * 512:(oc + 1) * 512], start=(ci == 0),
                                stop=(ci == 3 and hg == 1)),
                                deps=[tT, twp, ps_p_free],
                                inc=(1 if (hg == 1 and ci == 3 and oc == 1) else 0))
                        if hg == 0:
                            mm = P.op("tensor", lambda h, oc=oc: h.matmul(
                                ps_p[:, oc * 512:(oc + 1) * 512], consts["ones_b"][:, 0:128],
                                borow[:, oc * 512:(oc + 1) * 512], start=False, stop=True),
                                deps=[tbo], inc=(1 if oc == 1 else 0))
                    aoT_free = mm
                    cpar = qt % 2
                    tev = P.op("vector", lambda h, cpar=cpar: h.tensor_copy(ctr[cpar][:, :], ps_p[:, :]),
                               deps=[mm, ctr_free[cpar]], inc=1)
                    ps_p_free = tev
                    tac = P.dma("gpsimd", P.stream(f"b_ac{cpar}"), lambda h, qt=qt, cpar=cpar:
                        h.dma_start(out=xout_t[qt, :, :], in_=ctr[cpar][:, :], accum_op=ALU.add),
                        deps=[tev, b0_done])
                    ctr_free[cpar] = tac
                    acc_last = tac
            P.barrier()
            P.flush()
    return acc_last



L0AB_W = [("ln_w", [D]), ("ln_b", [D]), ("W_qkv", [D, 3072]), ("b_qkv", [3072]),
          ("W_proj", [D, D]), ("b_proj", [D])]


def build_program_l0_attn_b():
    nc = _mk_prog()
    aps = {}
    aps["xin"] = nc.dram_tensor("xin", [T, D], F32, kind="ExternalInput").ap()
    aps["xout"] = nc.dram_tensor("xout", [T, D], F32, kind="ExternalOutput").ap()
    aps["znT"] = nc.dram_tensor("znT_scr", [128, 8, T], BF16, kind="Internal").ap()
    for nm, shp in L0AB_W:
        aps[nm] = nc.dram_tensor(nm, shp, F32, kind="ExternalInput").ap()
    with contextlib.ExitStack() as es:
        P = Prog(nc, None, es)
        block = es.enter_context(nc.Block())
        P.block = block
        consts = const_setup(P, es)
        build_l0_attn_b(P, es, aps, consts)
        P.finish()
    return nc


def run_sublayer_l0ab(x_full, wd):
    nc = build_program_l0_attn_b()
    in_maps = []
    for c in range(N_CORES):
        m = {"xin": np.ascontiguousarray(x_full[c])}
        m.update(wd)
        in_maps.append(m)
    res = run_bass_kernel_spmd(nc, in_maps, core_ids=list(range(N_CORES)))
    return np.stack([r["xout"] for r in res.results])




def phase_ln_scratch(P, es, aps, consts, store_xy):
    nc = P.nc
    xin, ynT_d = aps["xin"], aps["ynT"]
    xin_t = xin.rearrange("(n p) d -> n p d", p=128)
    with contextlib.ExitStack() as ph:
        ps_t = ph.enter_context(_ps(nc, "c0_ps_t", [128, 1024], BF16))
        xt = ph.enter_context(_sb(nc, "c0_xt", [128, 1024], F32))
        y = ph.enter_context(_sb(nc, "c0_y", [128, 1024], F32))
        yb = ph.enter_context(_sb(nc, "c0_yb", [128, 1024], BF16))
        yTt = ph.enter_context(_sb(nc, "c0_yTt", [128, 8, 128], BF16))
        xy = ph.enter_context(_sb(nc, "c0_xy", [128, 1024], F32))
        scr = dict(
            junk=ph.enter_context(_sb(nc, "c0_junk", [128, 1024], F32)),
            s1=ph.enter_context(_sb(nc, "c0_s1", [128, 1], F32)),
            s2=ph.enter_context(_sb(nc, "c0_s2", [128, 1], F32)),
            s3=ph.enter_context(_sb(nc, "c0_s3", [128, 1], F32)),
            sq=ph.enter_context(_sb(nc, "c0_sq", [128, 1024], F32)),
        )
        if store_xy:
            xout_t = aps["xout"].rearrange("(n p) d -> n p d", p=128)
        prev = {}
        for i in range(NT):
            tl = P.dma("sync", P.stream("c0_x"), lambda h, i=i: h.dma_start(
                out=xt[:, :], in_=xin_t[i, :, :]), deps=[prev.get("lastx")])
            ty = ln_tile(P, xt[:, :], D, scr, 1e-5, y[:, :], x_deps=[tl],
                         out_tok_extra_deps=[prev.get("ywar")])
            tyb = P.op("vector", lambda h: h.tensor_copy(yb[:, :], y[:, :]),
                       deps=[ty, prev.get("tT")], inc=1)
            tT = transpose_tile(P, lambda j: yb[:, j * 128:(j + 1) * 128], ps_t, None, 8,
                                consts["ident_b"], [tyb], psum_free_tok=prev.get("tT"),
                                out_ap=yTt[:, :, :])
            tst1 = P.dma("sync", P.stream("c0_st"), lambda h, i=i: h.dma_start(
                out=ynT_d[:, :, i * 128:(i + 1) * 128], in_=yTt[:, :, :]), deps=[tT])
            toks = dict(tT=tst1, ywar=None, lastx=ty)
            if store_xy:
                txy = P.op("vector", lambda h: h.tensor_tensor(xy[:, :], xt[:, :], y[:, :], ALU.add),
                           deps=[ty, tl, prev.get("xyst")], inc=1)
                tst2 = P.dma("sync", P.stream("c0_xyst"), lambda h, i=i: h.dma_start(
                    out=xout_t[i, :, :], in_=xy[:, :]), deps=[txy])
                toks["xyst"] = tst2
                toks["ywar"] = txy
                toks["lastx"] = txy
                last = tst2
            else:
                toks["ywar"] = tT
                last = tst1
            prev = toks
        P.barrier()
        P.flush()
    return last




def build_l1_attn_a(P, es, aps, consts):
    nc = P.nc
    xout, ynT_d = aps["xout"], aps["ynT"]
    xout_t = xout.rearrange("(n p) d -> n p d", p=128)
    if not aps.get("inplace"):
        tcopy = P.dma("sync", P.stream("c_copy", "bulk"), lambda h: h.dma_start(
            out=aps["xout"], in_=aps["xin"]))
    else:
        tcopy = None
    ln_done = phase_ln_scratch(P, es, aps, consts, store_xy=False)

    acc_last = ln_done
    for hg in range(2):
        with contextlib.ExitStack() as ph:
            QT = ph.enter_context(_sb(nc, f"c_QT{hg}", [128, 4, 4096], BF16))
            KT = ph.enter_context(_sb(nc, f"c_KT{hg}", [128, 4, 4096], BF16))
            Vp = ph.enter_context(_sb(nc, f"c_Vp{hg}", [128, 32, 520], BF16))
            augk = ph.enter_context(_sb(nc, f"c_augk{hg}", [66, 4096], BF16))
            augq = ph.enter_context(_sb(nc, f"c_augq{hg}", [66, 4096], BF16))
            mcols = ph.enter_context(_sb(nc, f"c_mcols{hg}", [128, 32], F32))
            Gb = ph.enter_context(_sb(nc, f"c_Gb{hg}", [128, 8], F32))
            brow = {}
            btoks = []
            for nm in ("bq", "bk", "bv", "bo"):
                brow[nm], tk = load_row(P, f"c_{nm}{hg}", aps[nm], D, ph, BF16)
                btoks.append(tk)
            growg, tgr = load_row(P, f"c_G{hg}", aps["G"], 16, ph, F32)

            with contextlib.ExitStack() as pp:
                znt = [pp.enter_context(_sb(nc, f"c_znt{hg}{j}", [128, 8, 512], BF16))
                       for j in range(2)]
                wq = pp.enter_context(_sb(nc, f"c_wq{hg}", [128, 8, 512], BF16))
                wk = pp.enter_context(_sb(nc, f"c_wk{hg}", [128, 8, 512], BF16))
                wv = pp.enter_context(_sb(nc, f"c_wv{hg}", [128, 8, 512], BF16))
                bq2 = pp.enter_context(_sb(nc, f"c_bq2{hg}", [1, 512], BF16))
                twq = P.dma("gpsimd", P.stream("bulk", "bulk"), lambda h, hg=hg: h.dma_start(
                    out=wq[:, :, :],
                    in_=aps["WqT"][:, hg * 512:hg * 512 + 512].rearrange("(ci p) n -> p ci n", p=128)))
                twk = P.dma("gpsimd", P.stream("bulk", "bulk"), lambda h, hg=hg: h.dma_start(
                    out=wk[:, :, :],
                    in_=aps["WkT"][:, hg * 512:hg * 512 + 512].rearrange("(ci p) n -> p ci n", p=128)))
                twv = P.dma("gpsimd", P.stream("bulk", "bulk"), lambda h, hg=hg: h.dma_start(
                    out=wv[:, :, :],
                    in_=aps["WvT"][:, hg * 512:hg * 512 + 512].rearrange("(ci p) n -> p ci n", p=128)))
                tws = P.op("vector", lambda h: h.tensor_scalar(
                    rearr(wq[:, :, :], "p a b -> p (a b)"), rearr(wq[:, :, :], "p a b -> p (a b)"),
                    -2.0, None, ALU.mult), deps=[twq], inc=1)
                tbs = P.op("vector", lambda h, hg=hg: h.tensor_scalar(
                    bq2[:, :], brow["bq"][:, hg * 512:hg * 512 + 512], -2.0, None, ALU.mult),
                    deps=[btoks[0]], inc=1)
                ps_q = pp.enter_context(_ps(nc, "c_ps_q", [128, 512], F32))
                ps_v = pp.enter_context(_ps(nc, "c_ps_v", [128, 512], F32))
                znt_free = {0: None, 1: None}
                ps_q_free = None
                ps_v_free = None
                qkv_done = []
                for tt in range(8):
                    par = tt % 2
                    tsl = slice(tt * 512, (tt + 1) * 512)
                    tld = P.dma("gpsimd", P.stream(f"c_zl{par}"), lambda h, tsl=tsl, par=par:
                                h.dma_start(out=znt[par][:, :, :], in_=ynT_d[:, :, tsl]),
                                deps=[znt_free[par], ln_done])
                    last_read = None
                    for wsb, wtok, dst, bx in ((wq, tws, QT, 0), (wk, twk, KT, 1)):
                        for ch in range(4):
                            mm = None
                            for ci in range(8):
                                mm = P.op("tensor", lambda h, ch=ch, ci=ci, par=par, wsb=wsb:
                                          h.matmul(ps_q[:, :], wsb[:, ci, ch * 128:(ch + 1) * 128],
                                                   znt[par][:, ci, :], start=(ci == 0), stop=False),
                                          deps=[tld, wtok, ps_q_free], inc=0)
                            if bx == 0:
                                mm = P.op("tensor", lambda h, ch=ch: h.matmul(
                                    ps_q[:, :], bq2[:, ch * 128:(ch + 1) * 128],
                                    consts["ones_b"][:, 0:512], start=False, stop=True),
                                    deps=[tbs], inc=1)
                            else:
                                mm = P.op("tensor", lambda h, ch=ch, hg=hg: h.matmul(
                                    ps_q[:, :], brow["bk"][:, hg * 512 + ch * 128:hg * 512 + (ch + 1) * 128],
                                    consts["ones_b"][:, 0:512], start=False, stop=True),
                                    deps=[btoks[1]], inc=1)
                            tev = P.op("vector", lambda h, ch=ch, tsl=tsl, dst=dst:
                                       h.tensor_copy(dst[:, ch, tsl], ps_q[:, :]), deps=[mm], inc=1)
                            ps_q_free = tev
                            qkv_done.append(tev)
                    for tsub in range(4):
                        kc = tt * 4 + tsub
                        mm = None
                        for ci in range(8):
                            mm = P.op("tensor", lambda h, tsub=tsub, ci=ci, par=par:
                                      h.matmul(ps_v[:, :], znt[par][:, ci, tsub * 128:(tsub + 1) * 128],
                                               wv[:, ci, :], start=(ci == 0), stop=False),
                                      deps=[tld, twv, ps_v_free], inc=0)
                        mm = P.op("tensor", lambda h, hg=hg: h.matmul(
                            ps_v[:, :], consts["ones_b"][:, 0:128],
                            brow["bv"][:, hg * 512:hg * 512 + 512], start=False, stop=True),
                            deps=[btoks[2]], inc=1)
                        vap = Vp[:, kc, :]
                        vout = ap3(vap, [vap.ap[0], [65, 8], [1, 64]])
                        tev = P.op("vector", lambda h, vout=vout: h.tensor_copy(
                            vout, ps_v[:, :].rearrange("p (e f) -> p e f", e=8)), deps=[mm], inc=1)
                        vone = ap3(vap, [vap.ap[0], [65, 8], [1, 1]])
                        vone = bass.AP(vone.tensor, vone.offset + 64, vone.ap)
                        tmo = P.op("vector", lambda h, vone=vone: h.memset(vone, 1.0),
                                   deps=[mm], inc=1)
                        ps_v_free = tev
                        qkv_done.append(tev)
                        qkv_done.append(tmo)
                        last_read = mm
                    znt_free[par] = last_read
                P.barrier()
                P.flush()

            with contextlib.ExitStack() as pa:
                Vm = pa.enter_context(_sb(nc, f"c_Vm{hg}", [128, 32, 65], BF16))
                aoall = pa.enter_context(_sb(nc, f"c_ao{hg}", [128, 32, 512], BF16))
                wo = pa.enter_context(_sb(nc, f"c_wo{hg}", [128, 4, 1024], BF16))
                two = P.dma("gpsimd", P.stream("bulk", "bulk"), lambda h, hg=hg: h.dma_start(
                    out=wo[:, :, :],
                    in_=aps["WoT"][hg * 512:hg * 512 + 512, :].rearrange("(ci p) n -> p ci n", p=128)))
                sc = [pa.enter_context(_ps(nc, f"c_sc{j}", [128, 512], F32)) for j in range(2)]
                av = [pa.enter_context(_ps(nc, f"c_av{j}", [128, 128], F32)) for j in range(2)]
                ps_p = pa.enter_context(_ps(nc, "c_ps_p", [128, 1024], F32))
                ps_t2 = pa.enter_context(_ps(nc, "c_ps_t2", [128, 512], BF16))
                ps_n = pa.enter_context(_ps(nc, "c_ps_n", [1, 512], F32))
                sq_s = pa.enter_context(_sb(nc, "c_sq_s", [128, 4096], BF16))
                rr = pa.enter_context(_sb(nc, "c_rr", [128, 512], F32))
                lnr = pa.enter_context(_sb(nc, "c_lnr", [128, 512], F32))
                attnT = pa.enter_context(_sb(nc, "c_attnT", [128, 4096], BF16))
                q2row = pa.enter_context(_sb(nc, "c_q2row", [1, 4096], BF16))
                aoT = pa.enter_context(_sb(nc, "c_aoT", [128, 512], BF16))
                rcol = pa.enter_context(_sb(nc, "c_rcol", [128, 1], F32))
                ctr = [pa.enter_context(_sb(nc, f"c_ctr{j}", [128, 1024], F32)) for j in range(2)]
                tonce = P.op("gpsimd", lambda h: h.memset(augk[64:66, :], 1.0), inc=1)
                tonce2 = P.op("gpsimd", lambda h: h.memset(augq[64:66, :], 1.0), inc=1)
                sc_free = {0: None, 1: None}
                av_free = {0: None, 1: None}
                attnT_free = None
                rcol_free = None
                ao_free = None
                aoT_free = None
                ctr_free = {0: None, 1: None}
                sq_free = None
                augk_free = None
                augq_free = None
                q2row_free = None
                mcols_free = None
                Vm_free = None
                rr_free = None
                lnr_free = None
                mprev = None
                tgm = P.op("tensor", lambda h, hg=hg: h.matmul(
                    ps_p[:, 0:8], consts["ones_f"][:, 0:128], growg[:, hg * 8:hg * 8 + 8],
                    start=True, stop=True), deps=[tgr] + consts["tok"], inc=1)
                tgb = P.op("vector", lambda h: h.tensor_copy(Gb[:, :], ps_p[:, 0:8]),
                           deps=[tgm], inc=1)
                ps_p_free = tgb
                for hd in range(8):
                    hrow = (hd % 2) * 64
                    hch = hd // 2
                    for which, src_t, scale_sq in ((0, KT, 1.0), (1, QT, 0.5)):
                        tsq = P.op("scalar", lambda h, src_t=src_t, hch=hch, scale_sq=scale_sq:
                                   h.activation(out=sq_s[:, :], in_=src_t[:, hch, :], func=AF.Square,
                                                scale=scale_sq),
                                   deps=[qkv_done[-1], sq_free], inc=1)
                        war = augk_free if which == 0 else q2row_free
                        for tc in range(8):
                            mm = P.op("tensor", lambda h, hrow=hrow, tc=tc: h.matmul(
                                ps_n[:, :], consts["onescol_b"][hrow:hrow + 64, 0:1],
                                sq_s[hrow:hrow + 64, tc * 512:(tc + 1) * 512],
                                start=True, stop=True),
                                deps=[tsq, mprev], inc=1)
                            if which == 0:
                                tcp = P.op("vector", lambda h, tc=tc: h.tensor_copy(
                                    augk[64:65, tc * 512:(tc + 1) * 512], ps_n[:, :]),
                                    deps=[mm, war, tonce, tonce2], inc=1)
                            else:
                                tcp = P.op("vector", lambda h, tc=tc: h.tensor_copy(
                                    q2row[0:1, tc * 512:(tc + 1) * 512], ps_n[:, :]),
                                    deps=[mm, war, tonce, tonce2], inc=1)
                            mprev = tcp
                            war = tcp
                        if which == 0:
                            augk_free = war
                            k2done = war
                        else:
                            tq2d = P.dma("gpsimd", P.stream("c_q2d"), lambda h: h.dma_start(
                                out=augq[65:66, :], in_=q2row[0:1, :]), deps=[war, augq_free])
                            q2row_free = tq2d
                            augq_free = tq2d
                            q2done = tq2d
                        sq_free = mm
                    tkc = P.op("vector", lambda h, hrow=hrow, hch=hch: h.tensor_copy(
                        augk[0:64, :], KT[hrow:hrow + 64, hch, :]),
                        deps=[qkv_done[-1], k2done, augk_free], inc=1)
                    tqc = P.op("vector", lambda h, hrow=hrow, hch=hch: h.tensor_copy(
                        augq[0:64, :], QT[hrow:hrow + 64, hch, :]),
                        deps=[qkv_done[-1], q2done, augq_free], inc=1)
                    k2done = tkc
                    q2done = tqc
                    mmt = None
                    for kc in range(32):
                        mmt = P.op("tensor", lambda h, kc=kc: h.transpose(
                            ps_t2[:, 2 * kc:2 * kc + 1],
                            augk[64:65, kc * 128:(kc + 1) * 128], consts["ident_b"][64:65, 64:65]),
                            deps=[k2done, mcols_free, aoT_free] + consts["tok"],
                            inc=(1 if kc == 31 else 0))
                    tsm0 = P.op("scalar", lambda h: h.activation(
                        out=mcols[:, :], in_=ap3(ps_t2[:, 0:64], [ps_t2[:, 0:64].ap[0], [2, 32]]),
                        func=AF.Sqrt),
                        deps=[mmt], inc=1)
                    tsm = P.op("vector", lambda h, hd=hd: h.tensor_scalar(
                        mcols[:, :], mcols[:, :], Gb[:, hd:hd + 1], None, ALU.mult),
                        deps=[tsm0, tgb], inc=1)
                    tvm = None
                    for kc in range(32):
                        tvm = P.op("vector", lambda h, kc=kc, hd=hd: h.tensor_scalar(
                            Vm[:, kc, :], Vp[:, kc, hd * 65:(hd + 1) * 65], mcols[:, kc:kc + 1],
                            None, ALU.mult),
                            deps=[tsm, Vm_free, qkv_done[-1]], inc=1)
                    mcols_free = tvm
                    vm_ready = tvm
                    vm_last_read = None
                    for qt in range(NT):
                        av_par = qt % 2
                        exp_toks = []
                        for grp in range(8):
                            par = grp % 2
                            mm = None
                            for k4 in range(4):
                                kc = grp * 4 + k4
                                mm = P.op("tensor", lambda h, par=par, k4=k4, kc=kc, qt=qt:
                                          h.matmul(sc[par][:, k4 * 128:(k4 + 1) * 128],
                                                   augk[:, kc * 128:(kc + 1) * 128],
                                                   augq[:, qt * 128:(qt + 1) * 128],
                                                   start=True, stop=True),
                                          deps=[sc_free[par], k2done, q2done], inc=(1 if k4 == 3 else 0))
                                scores_last = mm
                            tcl = P.op("vector", lambda h, par=par: h.tensor_scalar(
                                rr[:, :], sc[par][:, :], 1e-30, None, ALU.max),
                                deps=[mm, rr_free], inc=1)
                            sc_free[par] = tcl
                            tln_ = P.op("scalar", lambda h: h.activation(out=lnr[:, :], in_=rr[:, :],
                                                                         func=AF.Ln),
                                        deps=[tcl, lnr_free], inc=1)
                            rr_free = tln_
                            tex = P.op("scalar", lambda h, grp=grp: h.activation(
                                out=attnT[:, grp * 512:(grp + 1) * 512], in_=lnr[:, :],
                                func=AF.Exp, scale=-1.0),
                                deps=[tln_, attnT_free], inc=1)
                            lnr_free = tex
                            exp_toks.append(tex)
                        mm = None
                        for kc in range(32):
                            mm = P.op("tensor", lambda h, kc=kc, av_par=av_par:
                                      h.matmul(av[av_par][:, 0:65],
                                               attnT[:, kc * 128:(kc + 1) * 128],
                                               Vm[:, kc, :],
                                               start=(kc == 0), stop=(kc == 31)),
                                      deps=[exp_toks[-1], av_free[av_par], vm_ready], inc=(1 if kc == 31 else 0))
                        attnT_free = mm
                        vm_last_read = mm
                        aug_last_read = scores_last
                        trc = P.op("vector", lambda h, av_par=av_par: h.reciprocal(
                            rcol[:, :], av[av_par][:, 64:65]), deps=[mm, rcol_free], inc=1)
                        tsc = P.op("vector", lambda h, hd=hd, qt=qt, av_par=av_par: h.tensor_scalar(
                            aoall[:, qt, hd * 64:(hd + 1) * 64], av[av_par][:, 0:64], rcol[:, 0:1],
                            None, ALU.mult),
                            deps=[trc, ao_free], inc=1)
                        rcol_free = tsc
                        av_free[av_par] = tsc
                    Vm_free = vm_last_read
                    augk_free = aug_last_read
                    augq_free = aug_last_read
                    ao_done = tsc
                for qt in range(NT):
                    tT = transpose_tile(P, lambda j, qt=qt: aoall[:, qt, j * 128:(j + 1) * 128],
                                        ps_t2, aoT, 4, consts["ident_b"], [ao_done],
                                        psum_free_tok=aoT_free)
                    mm = None
                    for oc in range(2):
                        for ci in range(4):
                            mm = P.op("tensor", lambda h, oc=oc, ci=ci, hg=hg: h.matmul(
                                ps_p[:, oc * 512:(oc + 1) * 512], aoT[:, ci * 128:(ci + 1) * 128],
                                wo[:, ci, oc * 512:(oc + 1) * 512], start=(ci == 0),
                                stop=(ci == 3 and hg == 1)),
                                deps=[tT, two, ps_p_free],
                                inc=(1 if (hg == 1 and ci == 3 and oc == 1) else 0))
                        if hg == 0:
                            mm = P.op("tensor", lambda h, oc=oc: h.matmul(
                                ps_p[:, oc * 512:(oc + 1) * 512], consts["ones_b"][:, 0:128],
                                brow["bo"][:, oc * 512:(oc + 1) * 512], start=False, stop=True),
                                deps=[btoks[3]], inc=(1 if oc == 1 else 0))
                    aoT_free = mm
                    cpar = qt % 2
                    tev = P.op("vector", lambda h, cpar=cpar: h.tensor_copy(ctr[cpar][:, :], ps_p[:, :]),
                               deps=[mm, ctr_free[cpar]], inc=1)
                    ps_p_free = tev
                    tac = P.dma("gpsimd", P.stream(f"c_ac{cpar}"), lambda h, qt=qt, cpar=cpar:
                                h.dma_start(out=xout_t[qt, :, :], in_=ctr[cpar][:, :],
                                            accum_op=ALU.add),
                                deps=[tev, tcopy])
                    ctr_free[cpar] = tac
                    acc_last = tac
            P.barrier()
            P.flush()
    return acc_last


L1AA_W = [("WqT", [D, D]), ("WkT", [D, D]), ("WvT", [D, D]), ("WoT", [D, D]),
          ("bq", [D]), ("bk", [D]), ("bv", [D]), ("bo", [D]), ("G", [16])]


def build_program_l1_attn_a():
    nc = _mk_prog()
    aps = {}
    aps["xin"] = nc.dram_tensor("xin", [T, D], F32, kind="ExternalInput").ap()
    aps["xout"] = nc.dram_tensor("xout", [T, D], F32, kind="ExternalOutput").ap()
    aps["ynT"] = nc.dram_tensor("ynT_scr", [128, 8, T], BF16, kind="Internal").ap()
    for nm, shp in L1AA_W:
        aps[nm] = nc.dram_tensor(nm, shp, F32, kind="ExternalInput").ap()
    with contextlib.ExitStack() as es:
        P = Prog(nc, None, es)
        block = es.enter_context(nc.Block())
        P.block = block
        consts = const_setup(P, es)
        build_l1_attn_a(P, es, aps, consts)
        P.finish()
    return nc


def run_sublayer_l1aa(x_full, wd):
    nc = build_program_l1_attn_a()
    in_maps = []
    for c in range(N_CORES):
        m = {"xin": np.ascontiguousarray(x_full[c])}
        m.update(wd)
        in_maps.append(m)
    res = run_bass_kernel_spmd(nc, in_maps, core_ids=list(range(N_CORES)))
    return np.stack([r["xout"] for r in res.results])



MIDS = [0.25, 0.75, 1.25, 1.75, 2.5, 3.5, 5.0]


def fp4_quant_chunk(P, x_ap, Fc, scr, out_ap, deps, war_deps=()):
    nb = Fc // 16
    bm, sc_, rsc = scr["bm"], scr["sc"], scr["rsc"]
    xn, cc, r1, sg = scr["xn"], scr["c"], scr["r1"], scr["sg"]
    dd = list(deps) + list(war_deps)

    def blk(ap):
        return ap3(ap, [ap.ap[0], [16, ap.ap[1][1] // 16], [1, 16]])
    t1 = P.op("vector", lambda h: h.tensor_reduce(
        bm[:, 0:nb], blk(x_ap), AX.X, ALU.max,
        apply_absolute_value=True), deps=dd, inc=1)
    t2 = P.op("vector", lambda h: h.tensor_scalar(sc_[:, 0:nb], bm[:, 0:nb], 1.0 / 6.0, 1e-12,
                                                  ALU.mult, ALU.max), deps=[t1], inc=1)
    t3 = P.op("vector", lambda h: h.reciprocal(rsc[:, 0:nb], sc_[:, 0:nb]), deps=[t2], inc=1)
    rb = ap3(rsc[:, 0:nb], [rsc[:, 0:nb].ap[0], [1, nb], [0, 16]])
    t4 = P.op("vector", lambda h: h.tensor_tensor(
        blk(xn[:, 0:Fc]), blk(x_ap), rb, ALU.mult), deps=[t3] + dd, inc=1)
    t5 = P.op("scalar", lambda h: h.activation(out=sg[:, 0:Fc], in_=xn[:, 0:Fc], func=AF.Sign),
              deps=[t4], inc=1)
    t6 = P.op("vector", lambda h: h.tensor_tensor(xn[:, 0:Fc], xn[:, 0:Fc], sg[:, 0:Fc], ALU.mult),
              deps=[t5], inc=1)
    tk = P.op("vector", lambda h: h.tensor_scalar(cc[:, 0:Fc], xn[:, 0:Fc], MIDS[0], None, ALU.is_ge),
              deps=[t6], inc=1)
    for k in range(1, 7):
        tk = P.op("vector", lambda h, k=k: h.scalar_tensor_tensor(
            cc[:, 0:Fc], xn[:, 0:Fc], MIDS[k], cc[:, 0:Fc], ALU.is_ge, ALU.add),
            deps=[tk], inc=1)
    t7 = P.op("vector", lambda h: h.tensor_scalar(r1[:, 0:Fc], cc[:, 0:Fc], 4.0, 0.0,
                                                  ALU.subtract, ALU.max), deps=[tk], inc=1)
    t8 = P.op("vector", lambda h: h.tensor_scalar(xn[:, 0:Fc], cc[:, 0:Fc], 6.0, 0.0,
                                                  ALU.subtract, ALU.max), deps=[t7], inc=1)
    t9 = P.op("vector", lambda h: h.tensor_tensor(r1[:, 0:Fc], cc[:, 0:Fc], r1[:, 0:Fc], ALU.add),
              deps=[t8], inc=1)
    t10 = P.op("vector", lambda h: h.scalar_tensor_tensor(
        cc[:, 0:Fc], r1[:, 0:Fc], 0.5, xn[:, 0:Fc], ALU.mult, ALU.add), deps=[t9], inc=1)
    t11 = P.op("vector", lambda h: h.tensor_tensor(cc[:, 0:Fc], cc[:, 0:Fc], sg[:, 0:Fc], ALU.mult),
               deps=[t10], inc=1)
    sb = ap3(sc_[:, 0:nb], [sc_[:, 0:nb].ap[0], [1, nb], [0, 16]])
    t12 = P.op("vector", lambda h: h.tensor_tensor(
        blk(out_ap), blk(cc[:, 0:Fc]), sb, ALU.mult), deps=[t11], inc=1)
    return t12


def mk_fp4_scr(P, ph, tag):
    nc = P.nc
    return dict(
        bm=ph.enter_context(_sb(nc, f"q_bm{tag}", [128, 32], F32)),
        sc=ph.enter_context(_sb(nc, f"q_sc{tag}", [128, 32], F32)),
        rsc=ph.enter_context(_sb(nc, f"q_rsc{tag}", [128, 32], F32)),
        xn=ph.enter_context(_sb(nc, f"q_xn{tag}", [128, 512], F32)),
        c=ph.enter_context(_sb(nc, f"q_c{tag}", [128, 512], F32)),
        r1=ph.enter_context(_sb(nc, f"q_r1{tag}", [128, 512], F32)),
        sg=ph.enter_context(_sb(nc, f"q_sg{tag}", [128, 512], F32)),
    )




def build_l1_ffn_prep(P, es, aps, consts):
    nc = P.nc
    with contextlib.ExitStack() as ph:
        wt = [ph.enter_context(_sb(nc, f"e_wt{j}", [128, 2816], F32)) for j in range(2)]
        wq_ = ph.enter_context(_sb(nc, "e_wq", [128, 2816], BF16))
        ev = ph.enter_context(_sb(nc, "e_ev", [128, 8, 128], BF16))
        ps_t = ph.enter_context(_ps(nc, "e_ps_t", [128, 1024], BF16))
        scr = mk_fp4_scr(P, ph, "w")
        wt_free = {0: None, 1: None}
        q_prev = None
        ev_free = None
        wq_free = None
        st_last = None
        jobs = []
        for (wsrc, wdst, nrows) in ((aps["w_gate"], aps["wgqT"], 2816),
                                    (aps["w_up"], aps["wupqT"], 2816),
                                    (aps["w_down"], aps["wdqT"], 1024)):
            ncols = 2816 if nrows == 1024 else 1024
            for tile in range(nrows // 128):
                jobs.append((wsrc, wdst, tile, ncols))
        for jid, (wsrc, wdst, tile, ncols) in enumerate(jobs):
            par = jid % 2
            tl = P.dma("sync", P.stream(f"e_wl{par}"), lambda h, wsrc=wsrc, tile=tile, ncols=ncols, par=par:
                       h.dma_start(out=wt[par][:, 0:ncols], in_=wsrc[tile * 128:(tile + 1) * 128, :]),
                       deps=[wt_free[par]])
            qts = []
            for c0 in range(0, ncols, 512):
                Fc = min(512, ncols - c0)
                q_prev = fp4_quant_chunk(P, wt[par][:, c0:c0 + Fc], Fc, scr,
                                         wq_[:, c0:c0 + Fc], [tl], [q_prev, wq_free])
                qts.append(q_prev)
            wt_free[par] = q_prev
            nch = ncols // 128
            for g0 in range(0, nch, 8):
                gn = min(8, nch - g0)
                tts = []
                for j in range(gn):
                    tts.append(P.op("tensor", lambda h, j=j, g0=g0: h.transpose(
                        ps_t[:, j * 128:(j + 1) * 128], wq_[:, (g0 + j) * 128:(g0 + j + 1) * 128],
                        consts["ident_b"][:, :]),
                        deps=[qts[-1], ev_free], inc=(1 if j == gn - 1 else 0)))
                tev = P.op("vector", lambda h, gn=gn: h.tensor_copy(
                    ev[:, 0:gn, :], ps_t[:, 0:gn * 128].rearrange("p (j t) -> p j t", j=gn)),
                    deps=[tts[-1]], inc=1)
                wq_free = tts[-1]
                tst = P.dma("sync", P.stream("e_wst"), lambda h, wdst=wdst, tile=tile, g0=g0, gn=gn:
                            h.dma_start(out=wdst[g0 * 128:(g0 + gn) * 128,
                                                 tile * 128:(tile + 1) * 128].rearrange(
                                "(ci p) t -> p ci t", p=128), in_=ev[:, 0:gn, :]),
                            deps=[tev])
                ev_free = tst
                st_last = tst
        P.barrier()
        P.flush()
    return st_last




def build_l1_ffn_call(P, es, aps, consts, tag):
    nc = P.nc
    xin, xout = aps["xin"], aps["xout"]
    xin_t = xin.rearrange("(n p) d -> n p d", p=128)
    xout_t = xout.rearrange("(n p) d -> n p d", p=128)
    with contextlib.ExitStack() as ph:
        wg = ph.enter_context(_sb(nc, f"e_gT{tag}", [128, 8, 2816], BF16))
        wu = ph.enter_context(_sb(nc, f"e_uT{tag}", [128, 8, 2816], BF16))
        wdn = ph.enter_context(_sb(nc, f"e_dT{tag}", [128, 22, 1024], BF16))
        tw1 = P.dma("gpsimd", P.stream("bulk", "bulk"), lambda h: h.dma_start(
            out=wg[:, :, :], in_=aps["wgqT"].rearrange("(ci p) n -> p ci n", p=128)))
        tw2 = P.dma("gpsimd", P.stream("bulk", "bulk"), lambda h: h.dma_start(
            out=wu[:, :, :], in_=aps["wupqT"].rearrange("(ci p) n -> p ci n", p=128)))
        tw3 = P.dma("gpsimd", P.stream("bulk", "bulk"), lambda h: h.dma_start(
            out=wdn[:, :, :], in_=aps["wdqT"].rearrange("(ci p) n -> p ci n", p=128)))
        xt = ph.enter_context(_sb(nc, f"e_xt{tag}", [128, 1024], F32))
        y = ph.enter_context(_sb(nc, f"e_y{tag}", [128, 1024], F32))
        yq = ph.enter_context(_sb(nc, f"e_yq{tag}", [128, 1024], BF16))
        yqT = ph.enter_context(_sb(nc, f"e_yqT{tag}", [128, 1024], BF16))
        hc = ph.enter_context(_sb(nc, f"e_hc{tag}", [128, 512], F32))
        hq = ph.enter_context(_sb(nc, f"e_hq{tag}", [128, 512], BF16))
        hqT = ph.enter_context(_sb(nc, f"e_hqT{tag}", [128, 22, 128], BF16))
        sgx = [ph.enter_context(_sb(nc, f"e_sgx{tag}{j}", [128, 512], BF16)) for j in range(2)]
        xn2 = ph.enter_context(_sb(nc, f"e_xn2{tag}", [128, 1024], F32))
        scr = mk_fp4_scr(P, ph, f"c{tag}")
        lnscr = dict(
            junk=ph.enter_context(_sb(nc, f"e_junk{tag}", [128, 1024], F32)),
            s1=ph.enter_context(_sb(nc, f"e_s1{tag}", [128, 1], F32)),
            s2=ph.enter_context(_sb(nc, f"e_s2{tag}", [128, 1], F32)),
            s3=ph.enter_context(_sb(nc, f"e_s3{tag}", [128, 1], F32)),
            sq=ph.enter_context(_sb(nc, f"e_sq{tag}", [128, 1024], F32)),
        )
        ps_g = [ph.enter_context(_ps(nc, f"e_ps_g{tag}{j}", [128, 512], F32)) for j in range(2)]
        ps_u = [ph.enter_context(_ps(nc, f"e_ps_u{tag}{j}", [128, 512], F32)) for j in range(2)]
        ps_d = ph.enter_context(_ps(nc, f"e_ps_d{tag}", [128, 1024], F32))
        ps_t = ph.enter_context(_ps(nc, f"e_ps_t{tag}", [128, 1024], BF16))

        prev = {}
        q_prev = None
        for i in range(NT):
            tl = P.dma("sync", P.stream("e_x"), lambda h, i=i: h.dma_start(
                out=xt[:, :], in_=xin_t[i, :, :]), deps=[prev.get("xread")])
            ty = ln_tile(P, xt[:, :], D, lnscr, 1e-5, y[:, :], x_deps=[tl],
                         out_tok_extra_deps=[prev.get("ywar")])
            qts = []
            for c0 in (0, 512):
                q_prev = fp4_quant_chunk(P, y[:, c0:c0 + 512], 512, scr, yq[:, c0:c0 + 512],
                                         [ty], [q_prev, prev.get("yqwar")])
                qts.append(q_prev)
            tT = transpose_tile(P, lambda j: yq[:, j * 128:(j + 1) * 128], ps_t, yqT, 8,
                                consts["ident_b"], [qts[-1]], psum_free_tok=prev.get("ps_t_f"))
            ps_t_f = tT
            hq_toks = []
            dmm = None
            for oc in range(6):
                bk = oc % 2
                c0 = oc * 512
                Fc = min(512, 2816 - c0)
                mmg = mmu = None
                for ci in range(8):
                    mmg = P.op("tensor", lambda h, ci=ci, c0=c0, Fc=Fc, bk=bk: h.matmul(
                        ps_g[bk][:, 0:Fc], yqT[:, ci * 128:(ci + 1) * 128], wg[:, ci, c0:c0 + Fc],
                        start=(ci == 0), stop=(ci == 7)),
                        deps=[tT, tw1, prev.get(f"ps_g_free{bk}")], inc=(1 if ci == 7 else 0))
                for ci in range(8):
                    mmu = P.op("tensor", lambda h, ci=ci, c0=c0, Fc=Fc, bk=bk: h.matmul(
                        ps_u[bk][:, 0:Fc], yqT[:, ci * 128:(ci + 1) * 128], wu[:, ci, c0:c0 + Fc],
                        start=(ci == 0), stop=(ci == 7)),
                        deps=[tT, tw2, prev.get(f"ps_u_free{bk}")], inc=(1 if ci == 7 else 0))
                tsi = P.op("scalar", lambda h, Fc=Fc, bk=bk: h.activation(
                    out=sgx[bk][:, 0:Fc], in_=ps_g[bk][:, 0:Fc], func=AF.Silu),
                    deps=[mmg, prev.get(f"sgxwar{bk}")], inc=1)
                tmu = P.op("vector", lambda h, Fc=Fc, bk=bk: h.tensor_tensor(
                    hc[:, 0:Fc], ps_u[bk][:, 0:Fc], sgx[bk][:, 0:Fc], ALU.mult),
                    deps=[tsi, mmu, prev.get("hcwar")], inc=1)
                prev[f"ps_g_free{bk}"] = tsi
                prev[f"ps_u_free{bk}"] = tmu
                prev[f"sgxwar{bk}"] = tmu
                q_prev = fp4_quant_chunk(P, hc[:, 0:Fc], Fc, scr, hq[:, 0:Fc], [tmu], [q_prev])
                prev["hcwar"] = q_prev
                nch = Fc // 128
                tts = []
                for j in range(nch):
                    tts.append(P.op("tensor", lambda h, j=j: h.transpose(
                        ps_t[:, j * 128:(j + 1) * 128], hq[:, j * 128:(j + 1) * 128],
                        consts["ident_b"][:, :]),
                        deps=[q_prev, ps_t_f], inc=(1 if j == nch - 1 else 0)))
                tev = P.op("vector", lambda h, oc=oc, nch=nch: h.tensor_copy(
                    hqT[:, oc * 4:oc * 4 + nch, :],
                    ps_t[:, 0:nch * 128].rearrange("p (j t) -> p j t", j=nch)),
                    deps=[tts[-1], prev.get("hqTwar")], inc=1)
                ps_t_f = tev
                hq_toks.append(tev)
            for oc2 in range(2):
                for ci in range(22):
                    dmm = P.op("tensor", lambda h, ci=ci, oc2=oc2: h.matmul(
                        ps_d[:, oc2 * 512:(oc2 + 1) * 512], hqT[:, ci, :],
                        wdn[:, ci, oc2 * 512:(oc2 + 1) * 512],
                        start=(ci == 0), stop=(ci == 21)),
                        deps=[hq_toks[-1], tw3, prev.get("psd")], inc=(1 if ci == 21 and oc2 == 1 else 0))
            prev["hqTwar"] = dmm
            txn = P.op("vector", lambda h: h.tensor_tensor(xn2[:, :], xt[:, :], ps_d[:, :], ALU.add),
                       deps=[dmm, tl, prev.get("xnst")], inc=1)
            prev["psd"] = txn
            prev["xread"] = txn
            prev["ywar"] = qts[-1]
            prev["yqwar"] = tT
            tst = P.dma("sync", P.stream("e_st"), lambda h, i=i: h.dma_start(
                out=xout_t[i, :, :], in_=xn2[:, :]), deps=[txn])
            prev["xnst"] = tst
        P.barrier()
        P.flush()
    return tst


L1FFN_W = [("w_gate", [2816, D]), ("w_up", [2816, D]), ("w_down", [D, 2816])]


def build_program_l1_ffn():
    nc = _mk_prog()
    aps = {}
    aps["xin"] = nc.dram_tensor("xin", [T, D], F32, kind="ExternalInput").ap()
    aps["xout"] = nc.dram_tensor("xout", [T, D], F32, kind="ExternalOutput").ap()
    for nm, shp in L1FFN_W:
        aps[nm] = nc.dram_tensor(nm, shp, F32, kind="ExternalInput").ap()
    aps["wgqT"] = nc.dram_tensor("wgqT", [D, 2816], BF16, kind="Internal").ap()
    aps["wupqT"] = nc.dram_tensor("wupqT", [D, 2816], BF16, kind="Internal").ap()
    aps["wdqT"] = nc.dram_tensor("wdqT", [2816, D], BF16, kind="Internal").ap()
    with contextlib.ExitStack() as es:
        P = Prog(nc, None, es)
        block = es.enter_context(nc.Block())
        P.block = block
        consts = const_setup(P, es)
        build_l1_ffn_prep(P, es, aps, consts)
        build_l1_ffn_call(P, es, aps, consts, tag=0)
        P.finish()
    return nc


def run_sublayer_l1ffn(x_full, wd):
    nc = build_program_l1_ffn()
    in_maps = []
    for c in range(N_CORES):
        m = {"xin": np.ascontiguousarray(x_full[c])}
        m.update(wd)
        in_maps.append(m)
    res = run_bass_kernel_spmd(nc, in_maps, core_ids=list(range(N_CORES)))
    return np.stack([r["xout"] for r in res.results])




def build_l1_attn_b(P, es, aps, consts):
    nc = P.nc
    xout, ynT_d = aps["xout"], aps["ynT"]
    xout_t = xout.rearrange("(n p) d -> n p d", p=128)
    f0_done = phase_ln_scratch(P, es, aps, consts, store_xy=True)

    with contextlib.ExitStack() as ph:
        QT = ph.enter_context(_sb(nc, "f_QT", [128, 4096], BF16))
        KTf = ph.enter_context(_sb(nc, "f_KTf", [128, 4096], BF16))
        KH = [ph.enter_context(_sb(nc, f"f_KH{hd}", [128, 4096], BF16)) for hd in range(8)]
        Vp = ph.enter_context(_sb(nc, "f_Vp", [128, 32, 1032], BF16))
        negB = ph.enter_context(_sb(nc, "f_negB", [128, 8], F32))
        gcol = ph.enter_context(_sb(nc, "f_gcol", [128, 1], F32))
        SEG = ph.enter_context(_sb(nc, "f_SEG", [128, 8], BF16))
        grow, tg1 = load_row(P, "f_gam", aps["gamma"], 1, ph, F32)

        with contextlib.ExitStack() as p0:
            segi = p0.enter_context(_sb(nc, "f_segi", [128, 8], F32))
            segf = p0.enter_context(_sb(nc, "f_segf", [128, 8], F32))
            ps_g0 = p0.enter_context(_ps(nc, "f_ps_g0", [128, 8], F32))
            ti = P.op("gpsimd", lambda h: h.iota(segi[:, :], [[-16, 8]], channel_multiplier=1,
                                                 allow_small_or_imprecise_dtypes=True), inc=1)
            t1 = P.op("vector", lambda h: h.tensor_scalar(segf[:, :], segi[:, :], 0.0, None, ALU.is_ge),
                      deps=[ti], inc=1)
            t2 = P.op("vector", lambda h: h.tensor_scalar(segi[:, :], segi[:, :], 15.0, None, ALU.is_le),
                      deps=[t1], inc=1)
            t3 = P.op("vector", lambda h: h.tensor_tensor(segf[:, :], segf[:, :], segi[:, :], ALU.mult),
                      deps=[t2], inc=1)
            t4 = P.op("vector", lambda h: h.tensor_copy(SEG[:, :], segf[:, :]), deps=[t3], inc=1)
            tm = P.op("tensor", lambda h: h.matmul(ps_g0[:, 0:1], consts["ones_f"][:, 0:128],
                                                   grow[:, :], start=True, stop=True),
                      deps=[tg1] + consts["tok"], inc=1)
            tgc = P.op("vector", lambda h: h.tensor_copy(gcol[:, :], ps_g0[:, 0:1]), deps=[tm], inc=1)
            P.barrier()
            P.flush()

        with contextlib.ExitStack() as pp:
            znt = [pp.enter_context(_sb(nc, f"f_znt{j}", [128, 8, 512], BF16)) for j in range(2)]
            wqv = pp.enter_context(_sb(nc, "f_wq", [128, 8, 128], BF16))
            wkv = pp.enter_context(_sb(nc, "f_wk", [128, 8, 128], BF16))
            wvv = pp.enter_context(_sb(nc, "f_wv", [128, 8, 1024], BF16))
            bqr, tb1 = load_row(P, "f_bq", aps["bq"], 128, pp, BF16)
            bkr, tb2 = load_row(P, "f_bk", aps["bk"], 128, pp, BF16)
            bvr, tb3 = load_row(P, "f_bv", aps["bv"], D, pp, BF16)
            twq = P.dma("gpsimd", P.stream("bulk", "bulk"), lambda h: h.dma_start(
                out=wqv[:, :, :], in_=aps["WqT"].rearrange("(ci p) n -> p ci n", p=128)))
            twk = P.dma("gpsimd", P.stream("bulk", "bulk"), lambda h: h.dma_start(
                out=wkv[:, :, :], in_=aps["WkT"].rearrange("(ci p) n -> p ci n", p=128)))
            twv = P.dma("gpsimd", P.stream("bulk", "bulk"), lambda h: h.dma_start(
                out=wvv[:, :, :], in_=aps["WvT"].rearrange("(ci p) n -> p ci n", p=128)))
            ps_q = pp.enter_context(_ps(nc, "f_ps_q", [128, 512], F32))
            ps_v = pp.enter_context(_ps(nc, "f_ps_v", [128, 512], F32))
            znt_free = {0: None, 1: None}
            ps_q_free = None
            ps_v_free = None
            prod_done = []
            for tt in range(8):
                par = tt % 2
                tsl = slice(tt * 512, (tt + 1) * 512)
                tld = P.dma("gpsimd", P.stream(f"f_zl{par}"), lambda h, tsl=tsl, par=par:
                            h.dma_start(out=znt[par][:, :, :], in_=ynT_d[:, :, tsl]),
                            deps=[znt_free[par], f0_done])
                last_read = None
                for wsb, wtok, brr, btok, dst in ((wqv, twq, bqr, tb1, QT), (wkv, twk, bkr, tb2, KTf)):
                    mm = None
                    for ci in range(8):
                        mm = P.op("tensor", lambda h, ci=ci, par=par, wsb=wsb:
                                  h.matmul(ps_q[:, :], wsb[:, ci, :], znt[par][:, ci, :],
                                           start=(ci == 0), stop=False),
                                  deps=[tld, wtok, ps_q_free], inc=0)
                    mm = P.op("tensor", lambda h, brr=brr: h.matmul(
                        ps_q[:, :], brr[:, :], consts["ones_b"][:, 0:512], start=False, stop=True),
                        deps=[btok], inc=1)
                    tev = P.op("vector", lambda h, tsl=tsl, dst=dst: h.tensor_copy(
                        dst[:, tsl], ps_q[:, :]), deps=[mm], inc=1)
                    ps_q_free = tev
                    prod_done.append(tev)
                    last_read = mm
                for tsub in range(4):
                    kc = tt * 4 + tsub
                    for oc in range(2):
                        mm = None
                        for ci in range(8):
                            mm = P.op("tensor", lambda h, tsub=tsub, ci=ci, par=par, oc=oc:
                                      h.matmul(ps_v[:, :], znt[par][:, ci, tsub * 128:(tsub + 1) * 128],
                                               wvv[:, ci, oc * 512:(oc + 1) * 512],
                                               start=(ci == 0), stop=False),
                                      deps=[tld, twv, ps_v_free], inc=0)
                        mm = P.op("tensor", lambda h, oc=oc: h.matmul(
                            ps_v[:, :], consts["ones_b"][:, 0:128],
                            bvr[:, oc * 512:(oc + 1) * 512], start=False, stop=True),
                            deps=[tb3], inc=1)
                        vap = Vp[:, kc, :]
                        vout = ap3(vap, [vap.ap[0], [129, 4], [1, 128]])
                        vout = bass.AP(vout.tensor, vout.offset + oc * 4 * 129, vout.ap)
                        tev = P.op("vector", lambda h, vout=vout: h.tensor_copy(
                            vout, ps_v[:, :].rearrange("p (e f) -> p e f", e=4)), deps=[mm], inc=1)
                        ps_v_free = tev
                        prod_done.append(tev)
                        last_read = mm
                    vone = ap3(vap, [vap.ap[0], [129, 8], [1, 1]])
                    vone = bass.AP(vone.tensor, vone.offset + 128, vone.ap)
                    tmo = P.op("vector", lambda h, vone=vone: h.memset(vone, 1.0), deps=[mm], inc=1)
                    prod_done.append(tmo)
                znt_free[par] = last_read
            mz = []
            for hd in range(8):
                tz = P.op("gpsimd", lambda h, hd=hd: h.memset(KH[hd][:, :], 0.0),
                          deps=[], inc=1)
                mz.append(tz)
            kh_done = []
            for hd in range(8):
                tcpy = P.dma("gpsimd", P.stream("f_khc"), lambda h, hd=hd: h.dma_start(
                    out=KH[hd][hd * 16:hd * 16 + 16, :], in_=KTf[hd * 16:hd * 16 + 16, :]),
                    deps=[mz[-1], prod_done[-1]])
                kh_done.append(tcpy)
            P.barrier()
            P.flush()

        with contextlib.ExitStack() as pb:
            sq_s = pb.enter_context(_sb(nc, "f_sq_s", [128, 4096], BF16))
            nmx = pb.enter_context(_sb(nc, "f_nmx", [8, 2], F32))
            nrow = pb.enter_context(_sb(nc, "f_nrow", [1, 8], F32))
            ntmp = pb.enter_context(_sb(nc, "f_ntmp", [8, 1], F32))
            ps_sg = pb.enter_context(_ps(nc, "f_ps_sg", [8, 512], F32))
            ps_br = pb.enter_context(_ps(nc, "f_ps_br", [128, 8], F32))
            ps_tr = pb.enter_context(_ps(nc, "f_ps_tr", [1, 8], F32))
            nz = P.op("gpsimd", lambda h: h.memset(nmx[:, :], 0.0), inc=1)
            sq_free = None
            mprev = None
            ntmp_free = None
            for which, src_t in ((0, QT), (1, KTf)):
                tsq = P.op("scalar", lambda h, src_t=src_t: h.activation(
                    out=sq_s[:, :], in_=src_t[:, :], func=AF.Square),
                    deps=[prod_done[-1], sq_free], inc=1)
                acc = nz
                for tc in range(8):
                    mm = P.op("tensor", lambda h, tc=tc: h.matmul(
                        ps_sg[:, :], SEG[:, :], sq_s[:, tc * 512:(tc + 1) * 512],
                        start=True, stop=True),
                        deps=[tsq, mprev], inc=1)
                    tmx = P.op("vector", lambda h: h.reduce_max(ntmp[:, :], ps_sg[:, :], axis=AX.X),
                               deps=[mm, ntmp_free], inc=1)
                    acc = P.op("vector", lambda h, which=which: h.tensor_tensor(
                        nmx[:, which:which + 1], nmx[:, which:which + 1], ntmp[:, :], ALU.max),
                        deps=[tmx, acc], inc=1)
                    ntmp_free = acc
                    mprev = tmx
                sq_free = mm
            tb1_ = P.op("vector", lambda h: h.tensor_tensor(nmx[:, 0:1], nmx[:, 0:1], nmx[:, 1:2],
                                                            ALU.mult), deps=[acc], inc=1)
            tb2_ = P.op("scalar", lambda h: h.activation(out=nmx[:, 0:1], in_=nmx[:, 0:1],
                                                         func=AF.Sqrt), deps=[tb1_], inc=1)
            tb3_ = P.op("vector", lambda h: h.tensor_scalar(nmx[:, 0:1], nmx[:, 0:1], -0.25, None,
                                                            ALU.mult), deps=[tb2_], inc=1)
            tb4_ = P.op("tensor", lambda h: h.transpose(ps_tr[:, :], nmx[:, 0:1],
                                                        consts["ident_f"][0:8, 0:8]),
                        deps=[tb3_] + consts["tok"], inc=1)
            tb5_ = P.op("vector", lambda h: h.tensor_copy(nrow[:, :], ps_tr[:, :]), deps=[tb4_], inc=1)
            tb6_ = P.op("tensor", lambda h: h.matmul(ps_br[:, :], consts["ones_f"][:, 0:128],
                                                     nrow[:, :], start=True, stop=True),
                        deps=[tb5_], inc=1)
            tb7_ = P.op("vector", lambda h: h.tensor_copy(negB[:, :], ps_br[:, :]), deps=[tb6_], inc=1)
            P.barrier()
            P.flush()

        with contextlib.ExitStack() as pa:
            sc = [pa.enter_context(_ps(nc, f"f_sc{j}", [128, 512], F32)) for j in range(2)]
            av = [pa.enter_context(_ps(nc, f"f_av{j}", [128, 129], F32)) for j in range(2)]
            attnT = pa.enter_context(_sb(nc, "f_attnT", [128, 4096], BF16))
            absb = pa.enter_context(_sb(nc, "f_absb", [128, 512], F32))
            usb = pa.enter_context(_sb(nc, "f_usb", [128, 512], F32))
            ao = pa.enter_context(_sb(nc, "f_ao", [128, 1024], F32))
            rcol = pa.enter_context(_sb(nc, "f_rcol", [128, 1], F32))
            ctr = [pa.enter_context(_sb(nc, f"f_ctr{j}", [128, 1024], F32)) for j in range(2)]
            sc_free = {0: None, 1: None}
            av_free = {0: None, 1: None}
            attnT_free = None
            rcol_free = None
            ao_free = None
            abs_free = None
            u_free = None
            ctr_free = {0: None, 1: None}
            for qt in range(NT):
                ao_toks = []
                for hd in range(8):
                    exp_toks = []
                    av_par = hd % 2
                    for grp in range(8):
                        par = grp % 2
                        mm = None
                        for k4 in range(4):
                            kc = grp * 4 + k4
                            mm = P.op("tensor", lambda h, par=par, k4=k4, kc=kc, hd=hd, qt=qt:
                                      h.matmul(sc[par][:, k4 * 128:(k4 + 1) * 128],
                                               KH[hd][:, kc * 128:(kc + 1) * 128],
                                               QT[:, qt * 128:(qt + 1) * 128],
                                               start=True, stop=True),
                                      deps=[sc_free[par], kh_done[-1], prod_done[-1]],
                                      inc=(1 if k4 == 3 else 0))
                        tab = P.op("scalar", lambda h, par=par: h.activation(
                            out=absb[:, :], in_=sc[par][:, :], func=AF.Abs, scale=0.25),
                            deps=[mm, abs_free], inc=1)
                        tu = P.op("vector", lambda h, par=par: h.scalar_tensor_tensor(
                            usb[:, :], absb[:, :], 8.0 / 3.0, sc[par][:, :], ALU.mult, ALU.add),
                            deps=[tab, u_free], inc=1)
                        sc_free[par] = tu
                        abs_free = tu
                        tex = P.op("scalar", lambda h, grp=grp, hd=hd: h.activation(
                            out=attnT[:, grp * 512:(grp + 1) * 512], in_=usb[:, :],
                            func=AF.Exp, bias=negB[:, hd:hd + 1], scale=0.15),
                            deps=[tu, tb7_, attnT_free], inc=1)
                        u_free = tex
                        exp_toks.append(tex)
                    mm = None
                    for kc in range(32):
                        mm = P.op("tensor", lambda h, kc=kc, hd=hd, av_par=av_par:
                                  h.matmul(av[av_par][:, 0:129],
                                           attnT[:, kc * 128:(kc + 1) * 128],
                                           Vp[:, kc, hd * 129:(hd + 1) * 129],
                                           start=(kc == 0), stop=(kc == 31)),
                                  deps=[exp_toks[-1], av_free[av_par], prod_done[-1]],
                                  inc=(1 if kc == 31 else 0))
                    attnT_free = mm
                    trc = P.op("vector", lambda h, av_par=av_par: h.reciprocal(
                        rcol[:, :], av[av_par][:, 128:129]), deps=[mm, rcol_free], inc=1)
                    tsc = P.op("vector", lambda h, hd=hd, av_par=av_par: h.tensor_scalar(
                        ao[:, hd * 128:(hd + 1) * 128], av[av_par][:, 0:128], rcol[:, 0:1],
                        None, ALU.mult),
                        deps=[trc, ao_free], inc=1)
                    rcol_free = tsc
                    av_free[av_par] = tsc
                    ao_toks.append(tsc)
                cpar = qt % 2
                tev = P.op("vector", lambda h, cpar=cpar: h.tensor_scalar(
                    ctr[cpar][:, :], ao[:, :], gcol[:, 0:1], None, ALU.mult),
                    deps=[ao_toks[-1], tgc, ctr_free[cpar]], inc=1)
                ao_free = tev
                tac = P.dma("gpsimd", P.stream(f"f_ac{cpar}"), lambda h, qt=qt, cpar=cpar:
                            h.dma_start(out=xout_t[qt, :, :], in_=ctr[cpar][:, :], accum_op=ALU.add),
                            deps=[tev, f0_done])
                ctr_free[cpar] = tac
                acc_last = tac
        P.barrier()
        P.flush()
    return acc_last


L1AB_W = [("WqT", [D, 128]), ("WkT", [D, 128]), ("WvT", [D, D]),
          ("bq", [128]), ("bk", [128]), ("bv", [D]), ("gamma", [1])]


def build_program_l1_attn_b():
    nc = _mk_prog()
    aps = {}
    aps["xin"] = nc.dram_tensor("xin", [T, D], F32, kind="ExternalInput").ap()
    aps["xout"] = nc.dram_tensor("xout", [T, D], F32, kind="ExternalOutput").ap()
    aps["ynT"] = nc.dram_tensor("ynT_scr", [128, 8, T], BF16, kind="Internal").ap()
    for nm, shp in L1AB_W:
        aps[nm] = nc.dram_tensor(nm, shp, F32, kind="ExternalInput").ap()
    with contextlib.ExitStack() as es:
        P = Prog(nc, None, es)
        block = es.enter_context(nc.Block())
        P.block = block
        consts = const_setup(P, es)
        build_l1_attn_b(P, es, aps, consts)
        P.finish()
    return nc


def run_sublayer_l1ab(x_full, wd):
    nc = build_program_l1_attn_b()
    in_maps = []
    for c in range(N_CORES):
        m = {"xin": np.ascontiguousarray(x_full[c])}
        m.update(wd)
        in_maps.append(m)
    res = run_bass_kernel_spmd(nc, in_maps, core_ids=list(range(N_CORES)))
    return np.stack([r["xout"] for r in res.results])




_PROG_CACHE = {}


def _prog(name, builder):
    if name not in _PROG_CACHE:
        _PROG_CACHE[name] = builder()
    return _PROG_CACHE[name]


def _launch(nc, x_full, wd):
    in_maps = []
    for c in range(N_CORES):
        m = {"xin": np.ascontiguousarray(x_full[c])}
        m.update(wd)
        in_maps.append(m)
    res = run_bass_kernel_spmd(nc, in_maps, core_ids=list(range(N_CORES)))
    return np.stack([r["xout"] for r in res.results])


def kernel(**ins):
    x = np.ascontiguousarray(ins["L0_attn_a_x"])
    x = x.reshape(8, T, D)

    wd_l0aa = {nm: ins["L0_attn_a_" + nm] for nm, _ in L0AA_W}
    wd_l0ffn = prep_l0ffn_weights(ins)
    wd_l0ab = {"ln_w": ins["L0_attn_b_ln_w"], "ln_b": ins["L0_attn_b_ln_b"],
               "W_qkv": ins["L0_attn_b_W_qkv"], "b_qkv": ins["L0_attn_b_b_qkv"],
               "W_proj": ins["L0_attn_b_W_proj"], "b_proj": ins["L0_attn_b_b_proj"]}
    wd_l1aa = {"WqT": np.ascontiguousarray(ins["L1_attn_a_Wq"].T),
               "WkT": np.ascontiguousarray(ins["L1_attn_a_Wk"].T),
               "WvT": np.ascontiguousarray(ins["L1_attn_a_Wv"].T),
               "WoT": np.ascontiguousarray(ins["L1_attn_a_Wo"].T),
               "bq": ins["L1_attn_a_bq"], "bk": ins["L1_attn_a_bk"],
               "bv": ins["L1_attn_a_bv"], "bo": ins["L1_attn_a_bo"],
               "G": ins["L1_attn_a_G"]}
    wd_l1ffn = {"w_gate": ins["L1_ffn_w_gate"], "w_up": ins["L1_ffn_w_up"],
                "w_down": ins["L1_ffn_w_down"]}
    wd_l1ab = {"WqT": np.ascontiguousarray(ins["L1_attn_b_Wq"].T),
               "WkT": np.ascontiguousarray(ins["L1_attn_b_Wk"].T),
               "WvT": np.ascontiguousarray(ins["L1_attn_b_Wv"].T),
               "bq": ins["L1_attn_b_bq"], "bk": ins["L1_attn_b_bk"],
               "bv": ins["L1_attn_b_bv"], "gamma": ins["L1_attn_b_gamma"]}

    x = _launch(_prog("l0aa", build_program_l0_attn_a), x, wd_l0aa)
    x = _launch(_prog("l0ffn", build_program_l0_ffn), x, wd_l0ffn)
    x = _launch(_prog("l0ab", build_program_l0_attn_b), x, wd_l0ab)
    x = _launch(_prog("l0ffn", build_program_l0_ffn), x, wd_l0ffn)
    x = _launch(_prog("l1aa", build_program_l1_attn_a), x, wd_l1aa)
    x = _launch(_prog("l1ffn", build_program_l1_ffn), x, wd_l1ffn)
    x = _launch(_prog("l1ab", build_program_l1_attn_b), x, wd_l1ab)
    x = _launch(_prog("l1ffn", build_program_l1_ffn), x, wd_l1ffn)
    return x




FUSED = True


def build_program_fused():
    nc = _mk_prog()
    X = nc.dram_tensor("xin", [T, D], F32, kind="ExternalInput").ap()
    XOUT = nc.dram_tensor("xout", [T, D], F32, kind="ExternalOutput").ap()
    B0 = nc.dram_tensor("xb0", [T, D], F32, kind="Internal").ap()
    B1 = nc.dram_tensor("xb1", [T, D], F32, kind="Internal").ap()
    x2buf = nc.dram_tensor("x2buf", [T, D], F32, kind="Internal").ap()
    znT = nc.dram_tensor("znT_scr", [128, 8, T], BF16, kind="Internal").ap()
    wgqT = nc.dram_tensor("wgqT", [D, 2816], BF16, kind="Internal").ap()
    wupqT = nc.dram_tensor("wupqT", [D, 2816], BF16, kind="Internal").ap()
    wdqT = nc.dram_tensor("wdqT", [2816, D], BF16, kind="Internal").ap()

    def w_aps(prefix, spec):
        d = {}
        for nm, shp in spec:
            d[nm] = nc.dram_tensor(prefix + nm, shp, F32, kind="ExternalInput").ap()
        return d

    a_l0aa = w_aps("l0aa_", L0AA_W)
    a_l0ffn = w_aps("l0ffn_", L0FFN_W)
    a_l0ab = w_aps("l0ab_", L0AB_W)
    a_l1aa = w_aps("l1aa_", L1AA_W)
    a_l1ffn = w_aps("l1ffn_", L1FFN_W)
    a_l1ab = w_aps("l1ab_", L1AB_W)

    with contextlib.ExitStack() as es:
        P = Prog(nc, None, es)
        block = es.enter_context(nc.Block())
        P.block = block
        consts = const_setup(P, es)
        build_l1_ffn_prep(P, es, dict(a_l1ffn, wgqT=wgqT, wupqT=wupqT, wdqT=wdqT), consts)
        build_l0_attn_a(P, es, dict(a_l0aa, xin=X, x2buf=x2buf, xout=B0), consts)
        build_l0_ffn(P, es, dict(a_l0ffn, xin=B0, xout=B1), consts)
        build_l0_attn_b(P, es, dict(a_l0ab, xin=B1, xout=B0, znT=znT), consts)
        build_l0_ffn(P, es, dict(a_l0ffn, xin=B0, xout=B1), consts)
        build_l1_attn_a(P, es, dict(a_l1aa, xin=B1, xout=B0, ynT=znT), consts)
        build_l1_ffn_call(P, es, dict(a_l1ffn, xin=B0, xout=B1, wgqT=wgqT, wupqT=wupqT,
                                      wdqT=wdqT), consts, tag=0)
        build_l1_attn_b(P, es, dict(a_l1ab, xin=B1, xout=B0, ynT=znT), consts)
        build_l1_ffn_call(P, es, dict(a_l1ffn, xin=B0, xout=XOUT, wgqT=wgqT, wupqT=wupqT,
                                      wdqT=wdqT), consts, tag=1)
        P.finish()
    return nc


def kernel_fused(**ins):
    x = np.ascontiguousarray(ins["L0_attn_a_x"]).reshape(8, T, D)
    wd = {}
    for nm, _ in L0AA_W:
        wd["l0aa_" + nm] = ins["L0_attn_a_" + nm]
    for nm, v in prep_l0ffn_weights(ins).items():
        wd["l0ffn_" + nm] = v
    wd.update({"l0ab_ln_w": ins["L0_attn_b_ln_w"], "l0ab_ln_b": ins["L0_attn_b_ln_b"],
               "l0ab_W_qkv": ins["L0_attn_b_W_qkv"], "l0ab_b_qkv": ins["L0_attn_b_b_qkv"],
               "l0ab_W_proj": ins["L0_attn_b_W_proj"], "l0ab_b_proj": ins["L0_attn_b_b_proj"]})
    wd.update({"l1aa_WqT": np.ascontiguousarray(ins["L1_attn_a_Wq"].T),
               "l1aa_WkT": np.ascontiguousarray(ins["L1_attn_a_Wk"].T),
               "l1aa_WvT": np.ascontiguousarray(ins["L1_attn_a_Wv"].T),
               "l1aa_WoT": np.ascontiguousarray(ins["L1_attn_a_Wo"].T),
               "l1aa_bq": ins["L1_attn_a_bq"], "l1aa_bk": ins["L1_attn_a_bk"],
               "l1aa_bv": ins["L1_attn_a_bv"], "l1aa_bo": ins["L1_attn_a_bo"],
               "l1aa_G": ins["L1_attn_a_G"]})
    wd.update({"l1ffn_w_gate": ins["L1_ffn_w_gate"], "l1ffn_w_up": ins["L1_ffn_w_up"],
               "l1ffn_w_down": ins["L1_ffn_w_down"]})
    wd.update({"l1ab_WqT": np.ascontiguousarray(ins["L1_attn_b_Wq"].T),
               "l1ab_WkT": np.ascontiguousarray(ins["L1_attn_b_Wk"].T),
               "l1ab_WvT": np.ascontiguousarray(ins["L1_attn_b_Wv"].T),
               "l1ab_bq": ins["L1_attn_b_bq"], "l1ab_bk": ins["L1_attn_b_bk"],
               "l1ab_bv": ins["L1_attn_b_bv"], "l1ab_gamma": ins["L1_attn_b_gamma"]})
    nc = _prog("fused", build_program_fused)
    in_maps = []
    for c in range(N_CORES):
        m = {"xin": np.ascontiguousarray(x[c])}
        m.update(wd)
        in_maps.append(m)
    res = run_bass_kernel_spmd(nc, in_maps, core_ids=list(range(N_CORES)))
    return np.stack([r["xout"] for r in res.results])


_kernel_unfused = kernel


def kernel(**ins):
    if FUSED:
        return kernel_fused(**ins)
    return _kernel_unfused(**ins)
